# version 03
# baseline (speedup 1.0000x reference)
import numpy as np
import ml_dtypes
from contextlib import ExitStack
import concourse.bass as bass
import concourse.mybir as mybir
from concourse.bass_utils import run_bass_kernel_spmd

F32 = mybir.dt.float32
BF16 = mybir.dt.bfloat16
U32 = mybir.dt.uint32
AF = mybir.ActivationFunctionType
ALU = mybir.AluOpType
AX = mybir.AxisListType

T = 4096
D = 1024
NCORE = 8
NT = T // 128
ND = D // 128
TG = 512
NTG = T // TG

bfdt = ml_dtypes.bfloat16


def _bf(x):
    return np.ascontiguousarray(np.asarray(x, dtype=bfdt))


def _f32(x):
    return np.ascontiguousarray(np.asarray(x, dtype=np.float32))


class SemC:
    def __init__(self, nc, name):
        self.h = nc.semaphore(name).__enter__()
        self.n = 0

    def inc(self, instr, k=1):
        instr.then_inc(self.h, k)
        self.n += k
        return self.n

    def wait(self, eng, v=None):
        eng.wait_ge(self.h, self.n if v is None else v)



def host_consts():
    c = {}
    ident = np.eye(128, dtype=np.float32)
    c["ident"] = _bf(ident)
    c["identf"] = _f32(ident)
    ROT = 32
    inv_freq = 1.0 / (10000.0 ** (np.arange(0, ROT, 2, dtype=np.float64) / ROT))
    t_ar = np.arange(T, dtype=np.float64)
    freqs = t_ar[:, None] * inv_freq[None, :]
    cosr = np.repeat(np.cos(freqs), 2, axis=-1)
    sinr = np.repeat(np.sin(freqs), 2, axis=-1)
    SCALE = 64.0 ** (-0.5)
    cq = np.ones((64, T)); sq_ = np.zeros((64, T))
    cq[:32, :] = cosr.T; sq_[:32, :] = sinr.T
    c["c0q"] = _bf(cq * SCALE); c["s0q"] = _bf(sq_ * SCALE)
    c["c0k"] = _bf(cq); c["s0k"] = _bf(sq_)
    R0 = np.zeros((64, 64), dtype=np.float32)
    for i in range(16):
        R0[2 * i + 1, 2 * i] = -1.0
        R0[2 * i, 2 * i + 1] = 1.0
    c["R0"] = _bf(R0)
    DH = 64
    inv_f1 = 1.0 / (10000.0 ** (np.arange(0, DH, 2, dtype=np.float64) / DH))
    fr1 = t_ar[:, None] * inv_f1[None, :]
    fr1 = np.concatenate([fr1, fr1], axis=-1)
    base_scale = (np.arange(0, DH, 2, dtype=np.float64) + 0.4 * DH) / (1.4 * DH)
    power = (t_ar - T // 2) / (256 // 2)
    scal = base_scale[None, :] ** power[:, None]
    scal = np.concatenate([scal, scal], axis=-1)
    cos1 = np.cos(fr1); sin1 = np.sin(fr1)
    QKS = 8.0
    csq = (cos1 * scal * QKS).T; ssq = (sin1 * scal * QKS).T
    csk = (cos1 / scal).T; ssk = (sin1 / scal).T
    c["c1q"] = _bf(np.concatenate([csq, csq], axis=0))
    c["s1q"] = _bf(np.concatenate([ssq, ssq], axis=0))
    c["c1k"] = _bf(np.concatenate([csk, csk], axis=0))
    c["s1k"] = _bf(np.concatenate([ssk, ssk], axis=0))
    R1 = np.zeros((64, 64), dtype=np.float32)
    for i in range(32):
        R1[i + 32, i] = -1.0
        R1[i, i + 32] = 1.0
    R1s = np.zeros((128, 128), dtype=np.float32)
    R1s[:64, :64] = R1; R1s[64:, 64:] = R1
    c["R1"] = _bf(R1s)
    ii = np.arange(128)
    c["mLOW"] = _f32((ii[:, None] >= ii[None, :]).astype(np.float32))
    c["mUP"] = _f32((ii[:, None] <= ii[None, :]).astype(np.float32))
    c["ones_1x128"] = _bf(np.ones((1, 128), np.float32))
    c["ones_64x1"] = _bf(np.ones((64, 1), np.float32))
    c["ones_128x1"] = _bf(np.ones((128, 1), np.float32))
    c["ones_1x64"] = _bf(np.ones((1, 64), np.float32))
    S2 = np.zeros((2, 128), np.float32)
    S2[0, :64] = 1.0; S2[1, 64:] = 1.0
    c["S2"] = _bf(S2)
    iw1 = np.zeros((128, 16), np.float32)
    for dc in range(8):
        for hh in range(2):
            iw1[:, dc * 2 + hh] = (dc * 128 + ii) * 2 + hh
    c["iota_w1"] = _f32(iw1)
    iw2 = np.zeros((128, 32), np.float32)
    for fc in range(32):
        iw2[:, fc] = fc * 128 + ii
    c["iota_w2"] = _f32(iw2)
    c["iota8"] = _f32(np.arange(8, dtype=np.float32).reshape(1, 8))
    S2T = np.zeros((128, 2), np.float32)
    S2T[:64, 0] = 1.0; S2T[64:, 1] = 1.0
    c["S2T"] = _bf(S2T)
    ID2 = np.zeros((64, 128), np.float32)
    for p in range(128):
        ID2[p % 64, p] = 1.0
    c["ID2"] = _bf(ID2)
    jj = np.arange(384)
    rr_ = np.arange(128)
    c["Bband"] = _f32(((jj[None, :] >= rr_[:, None]) &
                       (jj[None, :] <= rr_[:, None] + 256)).astype(np.float32))
    return c


CONST_SPECS = {
    "ident": ([128, 128], BF16), "identf": ([128, 128], F32),
    "c0q": ([64, T], BF16), "s0q": ([64, T], BF16),
    "c0k": ([64, T], BF16), "s0k": ([64, T], BF16), "R0": ([64, 64], BF16),
    "c1q": ([128, T], BF16), "s1q": ([128, T], BF16), "c1k": ([128, T], BF16),
    "s1k": ([128, T], BF16), "R1": ([128, 128], BF16),
    "mLOW": ([128, 128], F32), "mUP": ([128, 128], F32),
    "ones_1x128": ([1, 128], BF16), "ones_64x1": ([64, 1], BF16),
    "ones_128x1": ([128, 1], BF16), "ones_1x64": ([1, 64], BF16), "S2": ([2, 128], BF16),
    "iota_w1": ([128, 16], F32), "iota_w2": ([128, 32], F32), "iota8": ([1, 8], F32),
    "S2T": ([128, 2], BF16), "ID2": ([64, 128], BF16), "Bband": ([128, 384], F32),
}


class FusedEnv:
    def __init__(self, nc, block, sems, cst):
        self.nc = nc
        self.block = block
        self.sems = sems
        self.cst = cst
        self.pref = ""

    def barrier(self):
        ld, vq, sc, mm, stq = self.sems
        vals = (mm.n, vq.n, sc.n, stq.n, ld.n)

        def _bar(e, vals=vals):
            e.wait_ge(mm.h, vals[0])
            e.wait_ge(vq.h, vals[1])
            e.wait_ge(sc.h, vals[2])
            e.wait_ge(stq.h, vals[3])
            e.wait_ge(ld.h, vals[4])
        self.block.tensor(_bar)
        self.block.vector(_bar)
        self.block.scalar(_bar)
        self.block.gpsimd(_bar)
        self.block.sync(_bar)


def _tmm(t, out, in_, ident):
    return t.matmul(out, in_, ident, start=True, stop=True)


def declare_consts(nc, names):
    return {n: nc.dram_tensor("cst_" + n, CONST_SPECS[n][0], CONST_SPECS[n][1],
                              kind="ExternalInput").ap() for n in names}



def build_ln0(env=None, io=None):
    if env is None:
        nc = bass.Bass("TRN2", target_bir_lowering=False)
        x = nc.dram_tensor("x", [T, D], F32, kind="ExternalInput").ap()
        xn0f = nc.dram_tensor("xn0f", [D, T], BF16, kind="ExternalOutput").ap()
        cst = declare_consts(nc, ["ident"])
    else:
        nc = env.nc
        cst = env.cst
        x, xn0f = io
    P_ = "A" if env else ""
    with ExitStack() as es:
        ident = es.enter_context(nc.sbuf_tensor(P_ + "ident", [128, 128], BF16))
        xt = es.enter_context(nc.sbuf_tensor(P_ + "xt", [128, 2 * D], F32))
        xsq = es.enter_context(nc.sbuf_tensor(P_ + "xsq", [128, D], F32))
        st0 = es.enter_context(nc.sbuf_tensor(P_ + "st0", [128, 8], F32))
        xnt = es.enter_context(nc.sbuf_tensor(P_ + "xnt", [128, 2 * D], BF16))
        tev = es.enter_context(nc.sbuf_tensor(P_ + "tev", [128, 4 * 128], BF16))
        pt = es.enter_context(nc.psum_tensor(P_ + "pt", [128, 2 * 512], F32))
        if env is None:
            block = es.enter_context(nc.Block())
            ld = SemC(nc, "ld"); vq = SemC(nc, "vq"); sc = SemC(nc, "sc")
            stq = SemC(nc, "stq")
        else:
            block = env.block
            ld, vq, sc, _mm, stq = env.sems
        tp = SemC(nc, P_ + "tp"); ev = SemC(nc, P_ + "ev")

        @block.sync
        def _(sy):
            sy.dma_start(out=ident[:], in_=cst["ident"]).then_inc(ld.h, 16)
        ld.n += 16
        LDB = ld.n

        v_end = [0] * NT
        s_end = [0] * NT
        ev_at = {}
        st_at = {}
        for tt in range(NT):
            bo = (tt % 2) * D

            @block.sync
            def _(sy, tt=tt, bo=bo):
                if tt >= 2:
                    sy.wait_ge(vq.h, v_end[tt - 2])
                    sy.wait_ge(sc.h, s_end[tt - 2])
                sy.dma_start(out=xt[:, bo:bo + D],
                             in_=x[tt * 128:(tt + 1) * 128, :]).then_inc(ld.h, 16)
            ld.n += 16

            @block.vector
            def _(v, tt=tt, bo=bo):
                v.wait_ge(ld.h, LDB + 16 * (tt + 1))
                vq.inc(v.reduce_sum(st0[:, 0:1], xt[:, bo:bo + D], axis=AX.X))

            @block.scalar
            def _(s, tt=tt, bo=bo):
                s.wait_ge(ld.h, LDB + 16 * (tt + 1))
                if tt >= 1:
                    s.wait_ge(vq.h, v_end[tt - 1])
                sc.inc(s.activation(xsq[:], xt[:, bo:bo + D], AF.Square,
                                    accum_out=st0[:, 2:3]))
            s_end[tt] = sc.n

            @block.vector
            def _(v, tt=tt, bo=bo):
                vq.wait(v)
                vq.inc(v.tensor_scalar(out=st0[:, 1:2], in0=st0[:, 0:1],
                                       scalar1=-1.0 / D, scalar2=None, op0=ALU.mult))
                vq.wait(v)
                vq.inc(v.tensor_tensor(out=st0[:, 3:4], in0=st0[:, 1:2],
                                       in1=st0[:, 1:2], op=ALU.mult))
                sc.wait(v)
                vq.wait(v)
                vq.inc(v.tensor_scalar(out=st0[:, 4:5], in0=st0[:, 2:3],
                                       scalar1=1.0 / D, scalar2=st0[:, 3:4],
                                       op0=ALU.mult, op1=ALU.subtract))
                vq.wait(v)
                vq.inc(v.tensor_scalar(out=st0[:, 5:6], in0=st0[:, 4:5],
                                       scalar1=1e-5, scalar2=None, op0=ALU.add))
                vq.wait(v)
                vq.inc(v.reciprocal(st0[:, 6:7], st0[:, 5:6]))

            @block.scalar
            def _(s, tt=tt):
                vq.wait(s)
                sc.inc(s.sqrt(st0[:, 7:8], st0[:, 6:7]))

            @block.vector
            def _(v, tt=tt, bo=bo):
                sc.wait(v)
                vq.inc(v.tensor_scalar(out=xnt[:, bo:bo + D], in0=xt[:, bo:bo + D],
                                       scalar1=st0[:, 1:2], scalar2=st0[:, 7:8],
                                       op0=ALU.add, op1=ALU.mult))
            v_end[tt] = vq.n

            for dc in range(ND):
                k = tt * ND + dc
                pb = k % 2
                eb = k % 4

                @block.tensor
                def _(t, tt=tt, dc=dc, pb=pb, bo=bo, k=k):
                    t.wait_ge(vq.h, v_end[tt])
                    if k >= 2:
                        t.wait_ge(ev.h, ev_at[k - 2])
                    tp.inc(_tmm(t, pt[:, pb * 512:pb * 512 + 128],
                                xnt[:, bo + dc * 128:bo + (dc + 1) * 128], ident[:]))

                @block.scalar
                def _(s, pb=pb, eb=eb, k=k):
                    tp.wait(s)
                    if k >= 4:
                        s.wait_ge(stq.h, st_at[k - 4])
                    ev.inc(s.copy(tev[:, eb * 128:(eb + 1) * 128],
                                  pt[:, pb * 512:pb * 512 + 128]))
                ev_at[k] = ev.n

                @block.sync
                def _(sy, tt=tt, dc=dc, eb=eb):
                    ev.wait(sy)
                    sy.dma_start(out=xn0f[dc * 128:(dc + 1) * 128, tt * 128:(tt + 1) * 128],
                                 in_=tev[:, eb * 128:(eb + 1) * 128]).then_inc(stq.h, 16)
                stq.n += 16
                st_at[k] = stq.n

        @block.sync
        def _(sy):
            sy.wait_ge(stq.h, stq.n)
    return nc



def build_l0attn(env=None, io=None):
    if env is None:
        nc = bass.Bass("TRN2", target_bir_lowering=False)
        xn0f = nc.dram_tensor("xn0f", [D, T], BF16, kind="ExternalInput").ap()
        Wq = nc.dram_tensor("Wq", [D, D], F32, kind="ExternalInput").ap()
        Wk = nc.dram_tensor("Wk", [D, 512], F32, kind="ExternalInput").ap()
        Wv = nc.dram_tensor("Wv", [D, 512], F32, kind="ExternalInput").ap()
        Wo = nc.dram_tensor("Wo", [D, D], F32, kind="ExternalInput").ap()
        bo = nc.dram_tensor("bo", [8, 128], F32, kind="ExternalInput").ap()
        s0f = nc.dram_tensor("s0f", [D, T], F32, kind="ExternalOutput").ap()
        cst = declare_consts(nc, ["identf", "c0q", "s0q", "c0k", "s0k", "R0",
                                  "ones_64x1", "ones_1x128", "ones_1x64"])
    else:
        nc = env.nc
        cst = env.cst
        xn0f, Wq, Wk, Wv, Wo, bo, s0f = io
    P_ = "B" if env else ""
    oat = nc.dram_tensor(P_ + "oat", [D, T], BF16, kind="Internal").ap()
    kscr = nc.dram_tensor(P_ + "kscr", [512, T], BF16, kind="Internal").ap()
    NH, NKV = 16, 8
    with ExitStack() as es:
        def mk_sb(stack):
            def sb(name, shape, dt):
                return stack.enter_context(nc.sbuf_tensor(P_ + name, shape, dt))
            return sb

        sb = mk_sb(es)
        xn = sb("xn", [128, ND * T], BF16)
        cqt = sb("cq", [64, T], BF16)
        sqt = sb("sq", [64, T], BF16)
        ckt = sb("ck", [64, T], BF16)
        skt = sb("sk", [64, T], BF16)
        R0s = sb("R0s", [64, 64], BF16)
        identf = sb("identf", [128, 128], F32)
        ones64 = sb("ones64", [64, 1], BF16)
        ones1x128 = sb("ones1x128", [1, 128], BF16)
        ones1x64 = sb("ones1x64", [1, 64], BF16)
        vaug = sb("vaug", [128, NT * 520], BF16)
        mk2_all = sb("mk2_all", [1, NKV], F32)
        macc = sb("macc", [1, 4], F32)
        chbf = sb("chbf", [1, 1], BF16)
        bias_h = sb("bias_h", [128, 1], F32)
        rr = sb("rr", [1, 512], BF16)
        rb = sb("rb", [64, 2 * 512], BF16)
        oh = sb("oh", [64, 2 * 512], BF16)
        bo_raw = sb("bo_raw", [8, 128], F32)
        bo_t = sb("bo_t", [128, 8], F32)
        pp = es.enter_context(nc.psum_tensor(P_ + "pp", [128, 8 * 512], F32))
        if env is None:
            block = es.enter_context(nc.Block())
            ld = SemC(nc, "ld"); vq = SemC(nc, "vq"); sc = SemC(nc, "sc")
            mm = SemC(nc, "mm"); stq = SemC(nc, "stq")
        else:
            block = env.block
            ld, vq, sc, mm, stq = env.sems

        es23 = ExitStack()
        sb23 = mk_sb(es23)
        kcur = sb23("kcur", [64, T], BF16)
        qf = sb23("qf", [64, T], BF16)
        raw = sb23("raw", [64, 2 * 512], BF16)
        rt1 = sb23("rt1", [64, 2 * 512], BF16)
        rt2 = sb23("rt2", [64, 2 * 512], BF16)
        psb = sb23("psb", [128, 4 * 512], BF16)

        es_wv = ExitStack()
        wv_sb = mk_sb(es_wv)("wv_sb", [128, ND * 512], BF16)

        def bank(i, lo=0, hi=512):
            return pp[:, i * 512 + lo:i * 512 + hi]

        @block.sync
        def _(sy):
            for nm, dst in (("c0q", cqt), ("s0q", sqt), ("c0k", ckt), ("s0k", skt),
                            ("R0", R0s), ("ones_64x1", ones64), ("identf", identf),
                            ("ones_1x128", ones1x128), ("ones_1x64", ones1x64)):
                sy.dma_start(out=dst[:], in_=cst[nm]).then_inc(ld.h, 16)
            sy.dma_start(out=xn[:].rearrange("p (c t) -> p c t", c=ND),
                         in_=xn0f.rearrange("(c p) t -> p c t", p=128)).then_inc(ld.h, 16)
            sy.dma_start(out=bo_raw[:], in_=bo).then_inc(ld.h, 16)
        ld.n += 16 * 11

        @block.gpsimd
        def _(g):
            for i in range(ND):
                g.dma_start(out=wv_sb[:, i * 512:(i + 1) * 512],
                            in_=Wv[i * 128:(i + 1) * 128, :]).then_inc(ld.h, 16)
        ld.n += 16 * ND
        LDALL = ld.n

        @block.tensor
        def _(t):
            t.wait_ge(ld.h, LDALL)
            mm.inc(t.matmul(bank(7, 8, 16), bo_raw[:], identf[0:8, 0:8],
                            start=True, stop=True))
        mm_bo = mm.n

        @block.vector
        def _(v):
            v.wait_ge(mm.h, mm_bo)
            vq.inc(v.tensor_copy(bo_t[:], bank(7, 8, 16)))
            vq.inc(v.memset(vaug[:], 0.0))
        v_init = vq.n

        sc_v = {}
        for tt in range(NT):
            @block.tensor
            def _(t, tt=tt):
                t.wait_ge(ld.h, LDALL)
                if tt >= 2:
                    t.wait_ge(sc.h, sc_v[tt - 2])
                for dc in range(ND):
                    mm.inc(t.matmul(bank(tt % 2),
                                    xn[:, dc * T + tt * 128:dc * T + (tt + 1) * 128],
                                    wv_sb[:, dc * 512:(dc + 1) * 512],
                                    start=(dc == 0), stop=(dc == ND - 1)))

            @block.scalar
            def _(s, tt=tt):
                mm.wait(s)
                if tt == 0:
                    s.wait_ge(vq.h, v_init)
                dst = vaug[:, tt * 520:(tt + 1) * 520].rearrange("p (h o) -> p h o", h=8)[:, :, 0:64]
                src = bank(tt % 2).rearrange("p (h o) -> p h o", h=8)
                sc.inc(s.copy(dst, src))
            sc_v[tt] = sc.n

        @block.vector
        def _(v):
            sc.wait(v)
            vq.inc(v.memset(
                vaug[:].rearrange("p (t o) -> p t o", t=NT * 8)[:, :, 64:65], 1.0))
        v_vaug_done = vq.n

        def proj_rope_head(w_sb, stride, off, ct, st_, dst_sl, wld_v):
            rope_v = {}
            raw_ev = {}
            for tg in range(NTG):
                b = tg % 2

                @block.tensor
                def _(t, tg=tg, b=b):
                    t.wait_ge(ld.h, wld_v)
                    t.wait_ge(sc.h, sc.n)
                    if tg >= 2:
                        t.wait_ge(vq.h, rope_v[tg - 2])
                        t.wait_ge(sc.h, raw_ev[tg - 2])
                    for dc in range(ND):
                        mm.inc(t.matmul(bank(4 + b)[0:64, :],
                                        w_sb[:, dc * stride + off:dc * stride + off + 64],
                                        xn[:, dc * T + tg * TG:dc * T + (tg + 1) * TG],
                                        start=(dc == 0), stop=(dc == ND - 1)))

                @block.scalar
                def _(s, tg=tg, b=b):
                    mm.wait(s)
                    if tg >= 2:
                        s.wait_ge(vq.h, rope_v[tg - 2])
                    sc.inc(s.copy(raw[:, b * 512:(b + 1) * 512], bank(4 + b)[0:64, :]))
                raw_ev[tg] = sc.n

                @block.tensor
                def _(t, tg=tg, b=b):
                    sc.wait(t)
                    mm.inc(t.matmul(bank(6 + b)[0:64, :], R0s[:],
                                    raw[:, b * 512:(b + 1) * 512], start=True, stop=True))
                r_done = mm.n

                @block.vector
                def _(v, tg=tg, b=b, r_done=r_done, raw_done=raw_ev[tg]):
                    v.wait_ge(sc.h, raw_done)
                    vq.wait(v)
                    vq.inc(v.tensor_tensor(out=rt1[:, b * 512:(b + 1) * 512],
                                           in0=raw[:, b * 512:(b + 1) * 512],
                                           in1=ct[:, tg * TG:(tg + 1) * TG], op=ALU.mult))
                    v.wait_ge(mm.h, r_done)
                    vq.wait(v)
                    vq.inc(v.tensor_tensor(out=rt2[:, b * 512:(b + 1) * 512],
                                           in0=bank(6 + b)[0:64, :],
                                           in1=st_[:, tg * TG:(tg + 1) * TG], op=ALU.mult))
                    vq.wait(v)
                    vq.inc(v.tensor_tensor(
                        out=dst_sl(tg),
                        in0=rt1[:, b * 512:(b + 1) * 512],
                        in1=rt2[:, b * 512:(b + 1) * 512], op=ALU.add))
                rope_v[tg] = vq.n

        def head_norm_bound(src_fn):
            @block.vector
            def _(v):
                vq.wait(v)
                vq.inc(v.memset(macc[:, 0:1], 0.0))
            for tg in range(NTG):
                @block.vector
                def _(v, tg=tg):
                    if tg >= 1:
                        v.wait_ge(mm.h, mm.n)
                    vq.wait(v)
                    vq.inc(v.tensor_tensor(out=rt1[:, 0:512], in0=src_fn(tg),
                                           in1=src_fn(tg), op=ALU.mult))
                v_sq = vq.n

                @block.tensor
                def _(t, v_sq=v_sq):
                    t.wait_ge(vq.h, v_sq)
                    mm.inc(t.matmul(bank(7)[0:1, :], ones64[:], rt1[:, 0:512],
                                    start=True, stop=True))
                m_mm = mm.n

                @block.vector
                def _(v, m_mm=m_mm):
                    v.wait_ge(mm.h, m_mm)
                    vq.wait(v)
                    vq.inc(v.reduce_max(macc[:, 1:2], bank(7)[0:1, :], axis=AX.X))
                    vq.wait(v)
                    vq.inc(v.tensor_tensor(out=macc[:, 0:1], in0=macc[:, 0:1],
                                           in1=macc[:, 1:2], op=ALU.max))

        es_wv.close()
        es_wk = ExitStack()
        wk_sb = mk_sb(es_wk)("wk_sb", [128, ND * 512], BF16)

        @block.gpsimd
        def _(g):
            g.wait_ge(mm.h, mm.n)
            for i in range(ND):
                g.dma_start(out=wk_sb[:, i * 512:(i + 1) * 512],
                            in_=Wk[i * 128:(i + 1) * 128, :]).then_inc(ld.h, 16)
        ld.n += 16 * ND
        WKLD = ld.n

        kscr_w = {}
        for j in range(NKV):
            if j >= 1:
                @block.vector
                def _(v, j=j):
                    v.wait_ge(stq.h, kscr_w[j - 1])
            proj_rope_head(wk_sb, 512, j * 64, ckt, skt,
                           lambda tg: kcur[:, tg * TG:(tg + 1) * TG], WKLD)
            head_norm_bound(lambda tg: kcur[:, tg * TG:(tg + 1) * TG])

            @block.vector
            def _(v, j=j):
                vq.wait(v)
                vq.inc(v.tensor_copy(mk2_all[:, j:j + 1], macc[:, 0:1]))
            v_kd = vq.n

            @block.sync
            def _(sy, j=j, v_kd=v_kd):
                sy.wait_ge(vq.h, v_kd)
                sy.dma_start(out=kscr[j * 64:(j + 1) * 64, :],
                             in_=kcur[:]).then_inc(stq.h, 16)
            stq.n += 16
            kscr_w[j] = stq.n

        es_wk.close()

        es_wq = ExitStack()
        wq_sb = mk_sb(es_wq)("wq_sb", [128, ND * D], BF16)

        @block.gpsimd
        def _(g):
            g.wait_ge(mm.h, mm.n)
            for i in range(ND):
                g.dma_start(out=wq_sb[:, i * D:(i + 1) * D],
                            in_=Wq[i * 128:(i + 1) * 128, :]).then_inc(ld.h, 16)
        ld.n += 16 * ND
        WQLD = ld.n

        oat_w = {}
        for h in range(NH):
            j = h // 2
            if h % 2 == 0:
                @block.sync
                def _(sy, j=j):
                    sy.wait_ge(mm.h, mm.n)
                    sy.wait_ge(stq.h, kscr_w[NKV - 1])
                    sy.dma_start(out=kcur[:], in_=kscr[j * 64:(j + 1) * 64, :]
                                 ).then_inc(ld.h, 16)
                ld.n += 16
            KLD = ld.n
            proj_rope_head(wq_sb, D, h * 64, cqt, sqt,
                           lambda tg: qf[:, tg * TG:(tg + 1) * TG], WQLD)

            head_norm_bound(lambda tg: qf[:, tg * TG:(tg + 1) * TG])

            @block.vector
            def _(v, j=j):
                vq.wait(v)
                vq.inc(v.tensor_tensor(out=macc[:, 2:3], in0=macc[:, 0:1],
                                       in1=mk2_all[:, j:j + 1], op=ALU.mult))
            v_c2 = vq.n

            @block.scalar
            def _(s, v_c2=v_c2):
                s.wait_ge(vq.h, v_c2)
                sc.inc(s.sqrt(macc[:, 3:4], macc[:, 2:3]))
            sc_cs = sc.n

            @block.vector
            def _(v, sc_cs=sc_cs):
                v.wait_ge(sc.h, sc_cs)
                vq.wait(v)
                vq.inc(v.tensor_copy(chbf[:], macc[:, 3:4]))
            v_mq = vq.n

            @block.tensor
            def _(t, v_mq=v_mq):
                t.wait_ge(vq.h, v_mq)
                t.wait_ge(sc.h, sc.n)
                mm.inc(t.matmul(bank(7, 4, 5), ones1x128[:], chbf[:],
                                start=True, stop=True))
            mm_c2 = mm.n

            @block.scalar
            def _(s, mm_c2=mm_c2):
                s.wait_ge(mm.h, mm_c2)
                sc.inc(s.mul(bias_h[:], bank(7, 4, 5), -1.0))
            sc_bias = sc.n

            for qg in range(NTG):
                sc_base = sc.n
                sc_exp = [sc_base + kt + 1 for kt in range(NT)]

                mm_qk = [0] * NT
                mm_av = [0] * NT
                cnt = mm.n
                for kt in range(NT):
                    cnt += 1; mm_qk[kt] = cnt
                    if kt >= 1:
                        cnt += 1; mm_av[kt - 1] = cnt
                cnt += 1; mm_av[NT - 1] = cnt

                @block.tensor
                def _(t, h=h, j=j, qg=qg, sc_exp=sc_exp, sc_bias=sc_bias, KLD=KLD):
                    t.wait_ge(sc.h, sc_bias)
                    t.wait_ge(ld.h, KLD)
                    for kt in range(NT):
                        if kt >= 3:
                            t.wait_ge(sc.h, sc_exp[kt - 3])
                        mm.inc(t.matmul(bank(kt % 3),
                                        kcur[:, kt * 128:(kt + 1) * 128],
                                        qf[:, qg * TG:(qg + 1) * TG],
                                        start=True, stop=True))
                        if kt >= 1:
                            t.wait_ge(sc.h, sc_exp[kt - 1])
                            mm.inc(t.matmul(
                                bank(3)[0:65, :],
                                vaug[:, (kt - 1) * 520 + j * 65:(kt - 1) * 520 + j * 65 + 65],
                                psb[:, ((kt - 1) % 4) * 512:((kt - 1) % 4) * 512 + 512],
                                start=(kt - 1 == 0), stop=False))
                    t.wait_ge(sc.h, sc_exp[NT - 1])
                    mm.inc(t.matmul(bank(3)[0:65, :],
                                    vaug[:, (NT - 1) * 520 + j * 65:(NT - 1) * 520 + j * 65 + 65],
                                    psb[:, ((NT - 1) % 4) * 512:((NT - 1) % 4) * 512 + 512],
                                    start=False, stop=True))
                assert mm.n == cnt, (mm.n, cnt)

                @block.scalar
                def _(s, qg=qg, mm_qk=mm_qk, mm_av=mm_av):
                    for kt in range(NT):
                        s.wait_ge(mm.h, mm_qk[kt])
                        if kt >= 4:
                            s.wait_ge(mm.h, mm_av[kt - 4])
                        sc.inc(s.activation(psb[:, (kt % 4) * 512:(kt % 4) * 512 + 512],
                                            bank(kt % 3), AF.Exp, bias=bias_h[:, 0:1]))
                assert sc.n == sc_exp[-1], (sc.n, sc_exp[-1])

                @block.vector
                def _(v, mm_last=mm_av[NT - 1]):
                    v.wait_ge(mm.h, mm_last)
                    vq.wait(v)
                    with nc.allow_low_precision(reason="softmax denom bf16 for matmul bcast"):
                        vq.inc(v.reciprocal(rr[:], bank(3)[64:65, :]))
                v_rr = vq.n

                @block.tensor
                def _(t, v_rr=v_rr):
                    t.wait_ge(vq.h, v_rr)
                    mm.inc(t.matmul(bank(5)[0:64, :], ones1x64[:], rr[:],
                                    start=True, stop=True))
                mm_rb = mm.n

                @block.scalar
                def _(s, qg=qg, h=h, mm_rb=mm_rb):
                    s.wait_ge(mm.h, mm_rb)
                    key = (h, qg - 2) if qg >= 2 else (h - 1, qg + NTG - 2)
                    if key in oat_w:
                        s.wait_ge(stq.h, oat_w[key])
                    sc.inc(s.copy(rb[:, (qg % 2) * 512:(qg % 2) * 512 + 512],
                                  bank(5)[0:64, :]))
                sc_rb = sc.n

                @block.vector
                def _(v, qg=qg, sc_rb=sc_rb):
                    v.wait_ge(sc.h, sc_rb)
                    vq.wait(v)
                    vq.inc(v.tensor_tensor(out=oh[:, (qg % 2) * 512:(qg % 2) * 512 + 512],
                                           in0=bank(3)[0:64, :],
                                           in1=rb[:, (qg % 2) * 512:(qg % 2) * 512 + 512],
                                           op=ALU.mult))
                v_oh = vq.n

                @block.sync
                def _(sy, h=h, qg=qg, v_oh=v_oh):
                    sy.wait_ge(vq.h, v_oh)
                    sy.dma_start(out=oat[h * 64:(h + 1) * 64, qg * TG:(qg + 1) * TG],
                                 in_=oh[:, (qg % 2) * 512:(qg % 2) * 512 + 512]
                                 ).then_inc(stq.h, 16)
                stq.n += 16
                oat_w[(h, qg)] = stq.n

                @block.tensor
                def _(t, v_oh=v_oh):
                    t.wait_ge(vq.h, v_oh)
        B3_DONE = oat_w[(NH - 1, NTG - 1)]
        B3_MM = mm.n

        es_wq.close()
        es23.close()
        es4 = ExitStack()
        sb4 = mk_sb(es4)
        att = sb4("att", [128, ND * 512], BF16)
        wo_sb = sb4("wo_sb", [128, ND * D], BF16)
        s0e = sb4("s0e", [128, 2 * 512], F32)

        @block.gpsimd
        def _(g):
            g.wait_ge(mm.h, B3_MM)
            g.wait_ge(stq.h, B3_DONE)
            for i in range(ND):
                g.dma_start(out=wo_sb[:, i * D:(i + 1) * D],
                            in_=Wo[i * 128:(i + 1) * 128, :]).then_inc(ld.h, 16)
        ld.n += 16 * ND
        WOLD = ld.n

        s0_ev = {}
        s0_st = {}
        att_ld = {}
        for tg in range(NTG):
            @block.sync
            def _(sy, tg=tg):
                if tg == 0:
                    sy.wait_ge(stq.h, B3_DONE)
                sy.dma_start(out=att[:].rearrange("p (c t) -> p c t", c=ND),
                             in_=oat.rearrange("(c p) t -> p c t", p=128)
                             [:, :, tg * TG:(tg + 1) * TG]).then_inc(ld.h, 16)
            ld.n += 16
            att_ld[tg] = ld.n

            for dt in range(ND):
                k = tg * ND + dt

                @block.tensor
                def _(t, tg=tg, dt=dt, k=k):
                    t.wait_ge(ld.h, att_ld[tg])
                    if k >= 2:
                        t.wait_ge(vq.h, s0_ev[k - 2])
                    for ic in range(ND):
                        mm.inc(t.matmul(bank(k % 2),
                                        wo_sb[:, ic * D + dt * 128:ic * D + (dt + 1) * 128],
                                        att[:, ic * 512:(ic + 1) * 512],
                                        start=(ic == 0), stop=(ic == ND - 1)))

                @block.vector
                def _(v, dt=dt, k=k):
                    v.wait_ge(mm.h, mm.n)
                    if k >= 4:
                        v.wait_ge(stq.h, s0_st[k - 4])
                    vq.wait(v)
                    vq.inc(v.tensor_scalar(out=s0e[:, (k % 2) * 512:(k % 2) * 512 + 512],
                                           in0=bank(k % 2), scalar1=bo_t[:, dt:dt + 1],
                                           scalar2=None, op0=ALU.add))
                s0_ev[k] = vq.n

                @block.sync
                def _(sy, tg=tg, dt=dt, k=k):
                    sy.wait_ge(vq.h, s0_ev[k])
                    sy.dma_start(out=s0f[dt * 128:(dt + 1) * 128, tg * TG:(tg + 1) * TG],
                                 in_=s0e[:, (k % 2) * 512:(k % 2) * 512 + 512]
                                 ).then_inc(stq.h, 16)
                stq.n += 16
                s0_st[k] = stq.n

            @block.sync
            def _(sy):
                sy.wait_ge(mm.h, mm.n)

        @block.sync
        def _(sy):
            sy.wait_ge(stq.h, stq.n)
        es4.close()
    return nc



def build_l0moe(env=None, io=None):
    if env is None:
        nc = bass.Bass("TRN2", target_bir_lowering=False)
        xn0f = nc.dram_tensor("xn0f", [D, T], BF16, kind="ExternalInput").ap()
        x = nc.dram_tensor("x", [T, D], F32, kind="ExternalInput").ap()
        s0f = nc.dram_tensor("s0f", [D, T], F32, kind="ExternalInput").ap()
        Wr = nc.dram_tensor("Wr", [D, 8], F32, kind="ExternalInput").ap()
        br = nc.dram_tensor("br", [1, 8], F32, kind="ExternalInput").ap()
        W1m = nc.dram_tensor("W1m", [8, D, 4096], F32, kind="ExternalInput").ap()
        b1m = nc.dram_tensor("b1m", [8, 4096], F32, kind="ExternalInput").ap()
        W2m = nc.dram_tensor("W2m", [8, 4096, D], F32, kind="ExternalInput").ap()
        b2m = nc.dram_tensor("b2m", [8, D], F32, kind="ExternalInput").ap()
        x1 = nc.dram_tensor("x1", [T, D], F32, kind="ExternalOutput").ap()
        cst = declare_consts(nc, ["ident", "identf", "ones_1x128", "iota_w1",
                                  "iota_w2", "iota8"])
    else:
        nc = env.nc
        cst = env.cst
        xn0f, x, s0f, Wr, br, W1m, b1m, W2m, b2m, x1 = io
    P_ = "C" if env else ""
    NS = 256
    NSG = T // NS
    with ExitStack() as es:
        def sb(name, shape, dt):
            return es.enter_context(nc.sbuf_tensor(P_ + name, shape, dt))

        w1_sb = sb("w1_sb", [128, ND * 4096], BF16)
        w2_sb = sb("w2_sb", [128, 32 * D], BF16)
        xns = sb("xns", [128, 2 * ND * NS], BF16)
        s0s = sb("s0s", [128, 2 * ND * NS], F32)
        xacc = sb("xacc", [128, 4 * D], F32)
        h_sb = sb("h_sb", [128, 2 * NS], BF16)
        g_sb = sb("g_sb", [128, NS], F32)
        sg_sb = sb("sg_sb", [128, NS], F32)
        mean_bf = sb("mean_bf", [128, ND], BF16)
        wr_sb = sb("wr_sb", [128, ND * 8], BF16)
        br_sb = sb("br_sb", [1, 8], F32)
        io8 = sb("io8", [1, 8], F32)
        iow1 = sb("iow1", [128, 16], F32)
        iow2 = sb("iow2", [128, 32], F32)
        onesb = sb("onesb", [1, 128], BF16)
        identf = sb("identf", [128, 128], F32)
        sc8 = sb("sc8", [1, 8], F32)
        mx1 = sb("mx1", [1, 1], F32)
        ohot = sb("ohot", [1, 8], F32)
        e_val = sb("e_val", [1, 1], F32)
        e_bf = sb("e_bf", [1, 1], BF16)
        e_f = sb("e_f", [128, 1], F32)
        e_sc = sb("e_sc", [128, 2], F32)
        idx1f = sb("idx1f", [128, 16], F32)
        idx2f = sb("idx2f", [128, 32], F32)
        idx1u = sb("idx1u", [128, 16], U32)
        idx2u = sb("idx2u", [128, 32], U32)
        b1bf = sb("b1bf", [8, 4096], BF16)
        b2bf = sb("b2bf", [8, D], BF16)
        b1row = sb("b1row", [1, 4096], F32)
        b2row = sb("b2row", [1, D], F32)
        b1t = sb("b1t", [128, 32], F32)
        b2t = sb("b2t", [128, 8], F32)
        xnc = sb("xnc", [128, ND], F32)
        oh_bfr = sb("oh_bfr", [1, 8], BF16)
        oh8 = sb("oh8", [8, 1], BF16)
        identb = sb("identb", [1, 1], BF16)
        pp = es.enter_context(nc.psum_tensor(P_ + "pp", [128, 6 * 512], F32))
        ppt = es.enter_context(nc.psum_tensor(P_ + "ppt", [128, 2 * 512], F32))
        if env is None:
            block = es.enter_context(nc.Block())
            ld = SemC(nc, "ld"); vq = SemC(nc, "vq"); sc = SemC(nc, "sc")
            mm = SemC(nc, "mm"); stq = SemC(nc, "stq")
        else:
            block = env.block
            ld, vq, sc, mm, stq = env.sems
        gat = SemC(nc, P_ + "gat")

        def bank(i, lo=0, hi=512):
            return pp[:, i * 512 + lo:i * 512 + hi]

        @block.sync
        def _(sy):
            for nm, dst in (("identf", identf), ("ones_1x128", onesb),
                            ("iota_w1", iow1), ("iota_w2", iow2), ("iota8", io8)):
                sy.dma_start(out=dst[:], in_=cst[nm]).then_inc(ld.h, 16)
            sy.dma_start(out=br_sb[:], in_=br).then_inc(ld.h, 16)
            sy.dma_start(out=identb[:], in_=cst["ident"][0:1, 0:1]).then_inc(ld.h, 16)
        ld.n += 16 * 7

        @block.gpsimd
        def _(g):
            for i in range(ND):
                g.dma_start(out=wr_sb[:, i * 8:(i + 1) * 8],
                            in_=Wr[i * 128:(i + 1) * 128, :]).then_inc(ld.h, 16)
            g.dma_start(out=b1bf[:], in_=b1m).then_inc(ld.h, 16)
            g.dma_start(out=b2bf[:], in_=b2m).then_inc(ld.h, 16)
        ld.n += 16 * 10
        LD0 = ld.n

        mean_v = {}
        for dc in range(ND):
            @block.sync
            def _(sy, dc=dc):
                if dc >= 1:
                    sy.wait_ge(vq.h, mean_v[dc - 1])
                sy.dma_start(out=xns[:, 0:T],
                             in_=xn0f[dc * 128:(dc + 1) * 128, :]).then_inc(ld.h, 16)
            ld.n += 16

            @block.vector
            def _(v, dc=dc):
                v.wait_ge(ld.h, LD0 + 16 * (dc + 1))
                vq.wait(v)
                vq.inc(v.reduce_sum(xnc[:, dc:dc + 1], xns[:, 0:T], axis=AX.X))
                vq.wait(v)
                vq.inc(v.tensor_scalar(out=mean_bf[:, dc:dc + 1], in0=xnc[:, dc:dc + 1],
                                       scalar1=1.0 / T, scalar2=None, op0=ALU.mult))
            mean_v[dc] = vq.n
        MEAN_V = vq.n

        @block.tensor
        def _(t):
            t.wait_ge(vq.h, MEAN_V)
            for dc in range(ND):
                mm.inc(t.matmul(bank(0, 0, 8)[0:1, :], mean_bf[:, dc:dc + 1],
                                wr_sb[:, dc * 8:(dc + 1) * 8],
                                start=(dc == 0), stop=(dc == ND - 1)))
        mm_sc = mm.n

        @block.vector
        def _(v):
            v.wait_ge(mm.h, mm_sc)
            vq.wait(v)
            vq.inc(v.tensor_tensor(out=sc8[:], in0=bank(0, 0, 8)[0:1, :],
                                   in1=br_sb[:], op=ALU.add))
            vq.wait(v)
            vq.inc(v.reduce_max(mx1[:], sc8[:], axis=AX.X))
            vq.wait(v)
            vq.inc(v.tensor_scalar(out=ohot[:], in0=sc8[:], scalar1=mx1[:, 0:1],
                                   scalar2=None, op0=ALU.is_ge))
            vq.wait(v)
            vq.inc(v.tensor_tensor(out=ohot[:], in0=ohot[:], in1=io8[:], op=ALU.mult))
            vq.wait(v)
            vq.inc(v.reduce_sum(e_val[:], ohot[:], axis=AX.X))
            vq.wait(v)
            vq.inc(v.tensor_copy(e_bf[:], e_val[:]))
        v_e = vq.n

        @block.tensor
        def _(t):
            t.wait_ge(vq.h, v_e)
            mm.inc(t.matmul(bank(1, 0, 1), onesb[:], e_bf[:], start=True, stop=True))
        mm_e = mm.n

        @block.vector
        def _(v):
            v.wait_ge(mm.h, mm_e)
            vq.wait(v)
            vq.inc(v.tensor_copy(e_f[:], bank(1, 0, 1)))
            vq.wait(v)
            vq.inc(v.tensor_scalar(out=e_sc[:, 0:1], in0=e_f[:], scalar1=2048.0,
                                   scalar2=None, op0=ALU.mult))
            vq.wait(v)
            vq.inc(v.tensor_scalar(out=e_sc[:, 1:2], in0=e_f[:], scalar1=4096.0,
                                   scalar2=None, op0=ALU.mult))
            vq.wait(v)
            vq.inc(v.tensor_scalar(out=idx1f[:], in0=iow1[:], scalar1=e_sc[:, 0:1],
                                   scalar2=None, op0=ALU.add))
            vq.wait(v)
            vq.inc(v.tensor_scalar(out=idx2f[:], in0=iow2[:], scalar1=e_sc[:, 1:2],
                                   scalar2=None, op0=ALU.add))
            vq.wait(v)
            vq.inc(v.tensor_copy(idx1u[:], idx1f[:]))
            vq.wait(v)
            vq.inc(v.tensor_copy(idx2u[:], idx2f[:]))
        v_idx = vq.n

        w1v = W1m.rearrange("e d (h c) -> (e d h) c", c=2048)
        w2v = W2m.rearrange("e f c -> (e f) c")
        @block.gpsimd
        def _(g):
            g.wait_ge(vq.h, v_idx)
            for dc in range(ND):
                for hh in range(2):
                    g.indirect_dma_start(
                        out=w1_sb[:, dc * 4096 + hh * 2048:dc * 4096 + (hh + 1) * 2048],
                        out_offset=None, in_=w1v,
                        in_offset=bass.IndirectOffsetOnAxis(
                            ap=idx1u[:, dc * 2 + hh:dc * 2 + hh + 1], axis=0),
                    ).then_inc(gat.h, 16)
            for fc in range(32):
                g.indirect_dma_start(
                    out=w2_sb[:, fc * D:(fc + 1) * D],
                    out_offset=None, in_=w2v,
                    in_offset=bass.IndirectOffsetOnAxis(
                        ap=idx2u[:, fc:fc + 1], axis=0),
                ).then_inc(gat.h, 16)
        gat.n += 16 * 48
        GAT_ALL = gat.n

        @block.vector
        def _(v):
            vq.wait(v)
            vq.inc(v.tensor_scalar(out=ohot[:], in0=sc8[:], scalar1=mx1[:, 0:1],
                                   scalar2=None, op0=ALU.is_ge))
            vq.wait(v)
            vq.inc(v.tensor_copy(oh_bfr[:], ohot[:]))
        v_ohbf = vq.n

        @block.tensor
        def _(t):
            t.wait_ge(vq.h, v_ohbf)
            mm.inc(t.matmul(bank(1, 8, 9)[0:8, :], oh_bfr[:], identb[0:1, 0:1],
                            start=True, stop=True))
        mm_oht = mm.n

        @block.vector
        def _(v):
            v.wait_ge(mm.h, mm_oht)
            vq.wait(v)
            vq.inc(v.tensor_copy(oh8[:], bank(1, 8, 9)[0:8, :]))
        v_oh8 = vq.n

        @block.tensor
        def _(t):
            t.wait_ge(vq.h, v_oh8)
            t.wait_ge(ld.h, LD0)
            for c in range(32):
                mm.inc(t.matmul(bank(2, c, c + 1), b1bf[:, c * 128:(c + 1) * 128],
                                oh8[:], start=True, stop=True))
            for c in range(ND):
                mm.inc(t.matmul(bank(3, c, c + 1), b2bf[:, c * 128:(c + 1) * 128],
                                oh8[:], start=True, stop=True))
        mm_bias = mm.n

        @block.vector
        def _(v):
            v.wait_ge(mm.h, mm_bias)
            vq.wait(v)
            vq.inc(v.tensor_copy(b1t[:], bank(2, 0, 32)))
            vq.wait(v)
            vq.inc(v.tensor_copy(b2t[:], bank(3, 0, 8)))
        v_bias = vq.n

        xs_ld = {}
        out_st = {}
        hv_ev = {}
        sg_end_v = {}
        for sg in range(NSG):
            b = sg % 2
            t0 = sg * NS

            @block.sync
            def _(sy, sg=sg, b=b, t0=t0):
                if sg < 2:
                    sy.wait_ge(vq.h, MEAN_V)
                if sg >= 2:
                    sy.wait_ge(stq.h, out_st[sg - 2])
                    sy.wait_ge(vq.h, sg_end_v[sg - 2])
                sy.dma_start(out=xns[:, b * 2048:(b + 1) * 2048]
                             .rearrange("p (c t) -> p c t", c=ND),
                             in_=xn0f.rearrange("(c p) t -> p c t", p=128)
                             [:, :, t0:t0 + NS]).then_inc(ld.h, 16)
                sy.dma_start(out=s0s[:, b * 2048:(b + 1) * 2048]
                             .rearrange("p (c t) -> p c t", c=ND),
                             in_=s0f.rearrange("(c p) t -> p c t", p=128)
                             [:, :, t0:t0 + NS]).then_inc(ld.h, 16)
                sy.dma_start(out=xacc[:, b * 2 * D:(b + 1) * 2 * D]
                             .rearrange("p (w d) -> p w d", w=2),
                             in_=x.rearrange("(w p) d -> p w d", p=128)
                             [:, 2 * sg:2 * sg + 2, :]).then_inc(ld.h, 16)
            ld.n += 48
            xs_ld[sg] = ld.n

            z_ev = {}
            for fc in range(32):
                @block.tensor
                def _(t, sg=sg, b=b, fc=fc):
                    t.wait_ge(ld.h, xs_ld[sg])
                    if sg == 0 and fc == 0:
                        t.wait_ge(gat.h, GAT_ALL)
                        t.wait_ge(vq.h, v_bias)
                    if fc >= 2:
                        t.wait_ge(sc.h, hv_ev[(sg, fc - 2)])
                    elif sg >= 1:
                        t.wait_ge(sc.h, hv_ev[(sg - 1, 30 + fc)])
                    for dc in range(ND):
                        mm.inc(t.matmul(bank(fc % 2, 0, 256),
                                        w1_sb[:, dc * 4096 + fc * 128:dc * 4096 + (fc + 1) * 128],
                                        xns[:, b * 2048 + dc * 256:b * 2048 + (dc + 1) * 256],
                                        start=(dc == 0), stop=(dc == ND - 1)))

                @block.scalar
                def _(s, sg=sg, fc=fc):
                    mm.wait(s)
                    sc.inc(s.activation(h_sb[:, (fc % 2) * 256:(fc % 2) * 256 + 256],
                                        bank(fc % 2, 0, 256),
                                        AF.Relu, bias=b1t[:, fc:fc + 1]))
                hv_ev[(sg, fc)] = sc.n

                @block.tensor
                def _(t, sg=sg, fc=fc):
                    t.wait_ge(sc.h, hv_ev[(sg, fc)])
                    for dt in range(ND):
                        mm.inc(t.matmul(bank(2 + dt // 2, (dt % 2) * 256, (dt % 2) * 256 + 256),
                                        w2_sb[:, fc * D + dt * 128:fc * D + (dt + 1) * 128],
                                        h_sb[:, (fc % 2) * 256:(fc % 2) * 256 + 256],
                                        start=(fc == 0 and dt % 2 == 0),
                                        stop=(fc == 31 and dt % 2 == 1),
                                        skip_group_check=True))
            mm_z = mm.n

            vg_done = {}
            prev_tp = 0
            for dt in range(ND):
                @block.scalar
                def _(s, sg=sg, dt=dt):
                    s.wait_ge(mm.h, mm_z)
                    if dt >= 1:
                        s.wait_ge(vq.h, vg_done[(sg, dt - 1)])
                    elif sg >= 1:
                        s.wait_ge(vq.h, sg_end_v[sg - 1])
                    sc.inc(s.activation(g_sb[:], bank(2 + dt // 2, (dt % 2) * 256,
                                                      (dt % 2) * 256 + 256),
                                        AF.Silu, bias=b2t[:, dt:dt + 1]))
                sil_ev = sc.n

                @block.vector
                def _(v, sg=sg, b=b, dt=dt, sil_ev=sil_ev, prev_tp=prev_tp):
                    v.wait_ge(sc.h, sil_ev)
                    v.wait_ge(mm.h, prev_tp)
                    vq.wait(v)
                    vq.inc(v.tensor_tensor(out=sg_sb[:], in0=g_sb[:],
                                           in1=s0s[:, b * 2048 + dt * 256:b * 2048 + (dt + 1) * 256],
                                           op=ALU.mult))
                v_sg = vq.n

                for w in range(2):
                    @block.tensor
                    def _(t, w=w, v_sg=v_sg):
                        t.wait_ge(vq.h, v_sg)
                        t.wait_ge(vq.h, vq.n)
                        mm.inc(_tmm(t, ppt[:, w * 512:w * 512 + 128],
                                    sg_sb[:, w * 128:(w + 1) * 128], identf[:]))
                    m_tp = mm.n

                    @block.vector
                    def _(v, sg=sg, b=b, dt=dt, w=w, m_tp=m_tp):
                        v.wait_ge(mm.h, m_tp)
                        vq.wait(v)
                        vq.inc(v.tensor_tensor(
                            out=xacc[:, b * 2 * D + w * D + dt * 128:
                                     b * 2 * D + w * D + (dt + 1) * 128],
                            in0=xacc[:, b * 2 * D + w * D + dt * 128:
                                     b * 2 * D + w * D + (dt + 1) * 128],
                            in1=ppt[:, w * 512:w * 512 + 128], op=ALU.add))
                prev_tp = mm.n
                vg_done[(sg, dt)] = vq.n
            sg_end_v[sg] = vg_done[(sg, ND - 1)]

            @block.sync
            def _(sy, sg=sg, b=b):
                sy.wait_ge(vq.h, vg_done[(sg, ND - 1)])
                sy.dma_start(out=x1.rearrange("(w p) d -> p w d", p=128)
                             [:, 2 * sg:2 * sg + 2, :],
                             in_=xacc[:, b * 2 * D:(b + 1) * 2 * D]
                             .rearrange("p (w d) -> p w d", w=2)).then_inc(stq.h, 16)
            stq.n += 16
            out_st[sg] = stq.n

        @block.sync
        def _(sy):
            sy.wait_ge(stq.h, stq.n)
    return nc



def build_lnssm(debug=False, env=None, io=None):
    if env is None:
        nc = bass.Bass("TRN2", target_bir_lowering=False)
        x1 = nc.dram_tensor("x1", [T, D], F32, kind="ExternalInput").ap()
        B_w = nc.dram_tensor("B_w", [128, D], F32, kind="ExternalInput").ap()
        C_w = nc.dram_tensor("C_w", [D, 128], F32, kind="ExternalInput").ap()
        logA = nc.dram_tensor("logA", [1, 128], F32, kind="ExternalInput").ap()
        D_p = nc.dram_tensor("D_p", [8, 128], F32, kind="ExternalInput").ap()
        xn1f = nc.dram_tensor("xn1f", [D, T], F32, kind="ExternalOutput").ap()
        s1f = nc.dram_tensor("s1f", [D, T], F32, kind="ExternalOutput").ap()
    else:
        nc = env.nc
        x1, B_w, C_w, logA, D_p, xn1f, s1f = io
    if debug and env is None:
        dbg_u = nc.dram_tensor("dbg_u", [128, T], F32, kind="ExternalOutput").ap()
        dbg_a = nc.dram_tensor("dbg_a", [128, T], F32, kind="ExternalOutput").ap()
        dbg_hs = nc.dram_tensor("dbg_hs", [128, T], F32, kind="ExternalOutput").ap()
        dbg_dpt = nc.dram_tensor("dbg_dpt", [128, 8], F32, kind="ExternalOutput").ap()
        dbg_cwt = nc.dram_tensor("dbg_cwt", [128, ND * 128], BF16, kind="ExternalOutput").ap()
    cst = env.cst if env else declare_consts(nc, ["ident", "identf"])
    P_ = "Dp" if env else ""
    with ExitStack() as es:
        def sb(name, shape, dt):
            return es.enter_context(nc.sbuf_tensor(P_ + name, shape, dt))

        ident = sb("ident", [128, 128], BF16)
        identf = sb("identf", [128, 128], F32)
        xt = sb("xt", [128, 2 * D], F32)
        xsq = sb("xsq", [128, D], F32)
        st0 = sb("st0", [128, 8], F32)
        xnt = sb("xnt", [128, 2 * D], F32)
        tev = sb("tev", [128, 4 * 128], F32)
        xnb = sb("xnb", [128, ND * T], BF16)
        u_sb = sb("u_sb", [128, T], F32)
        afull = sb("afull", [128, T], F32)
        hs = sb("hs", [128, T], F32)
        hs_bf = sb("hs_bf", [128, T], BF16)
        bw_raw = sb("bw_raw", [128, D], BF16)
        bwT = sb("bwT", [128, ND * 128], BF16)
        cw_raw = sb("cw_raw", [128, ND * 128], BF16)
        cwT = sb("cwT", [128, ND * 128], BF16)
        la_row = sb("la_row", [1, 128], F32)
        acol = sb("acol", [128, 2], F32)
        dpr = sb("dpr", [8, 128], F32)
        dpt = sb("dpt", [128, 8], F32)
        x1c = sb("x1c", [128, 2 * 512], F32)
        s1e = sb("s1e", [128, 2 * 512], F32)
        pp = es.enter_context(nc.psum_tensor(P_ + "pp", [128, 8 * 512], F32))
        if env is None:
            block = es.enter_context(nc.Block())
            ld = SemC(nc, "ld"); vq = SemC(nc, "vq"); sc = SemC(nc, "sc")
            mm = SemC(nc, "mm"); stq = SemC(nc, "stq")
        else:
            block = env.block
            ld, vq, sc, mm, stq = env.sems

        def bank(i, lo=0, hi=512):
            return pp[:, i * 512 + lo:i * 512 + hi]

        @block.sync
        def _(sy):
            sy.dma_start(out=ident[:], in_=cst["ident"]).then_inc(ld.h, 16)
            sy.dma_start(out=identf[:], in_=cst["identf"]).then_inc(ld.h, 16)
            sy.dma_start(out=la_row[:], in_=logA).then_inc(ld.h, 16)
            sy.dma_start(out=dpr[:], in_=D_p).then_inc(ld.h, 16)
        ld.n += 16 * 4

        @block.gpsimd
        def _(g):
            g.dma_start(out=bw_raw[:], in_=B_w).then_inc(ld.h, 16)
            for i in range(ND):
                g.dma_start(out=cw_raw[:, i * 128:(i + 1) * 128],
                            in_=C_w[i * 128:(i + 1) * 128, :]).then_inc(ld.h, 16)
        ld.n += 16 * 9
        LD0 = ld.n

        tp_ev = {}
        for i in range(ND):
            @block.tensor
            def _(t, i=i):
                t.wait_ge(ld.h, LD0)
                if i >= 2:
                    t.wait_ge(vq.h, tp_ev[i - 2])
                mm.inc(_tmm(t, bank(4 + i % 2, 0, 128),
                            bw_raw[:, i * 128:(i + 1) * 128], ident[:]))
                mm.inc(_tmm(t, bank(4 + i % 2, 128, 256),
                            cw_raw[:, i * 128:(i + 1) * 128], ident[:]))
            m_tp = mm.n

            @block.vector
            def _(v, i=i, m_tp=m_tp):
                v.wait_ge(mm.h, m_tp)
                vq.wait(v)
                vq.inc(v.tensor_copy(bwT[:, i * 128:(i + 1) * 128],
                                     bank(4 + i % 2, 0, 128)))
                vq.wait(v)
                vq.inc(v.tensor_copy(cwT[:, i * 128:(i + 1) * 128],
                                     bank(4 + i % 2, 128, 256)))
            tp_ev[i] = vq.n

        @block.tensor
        def _(t):
            t.wait_ge(vq.h, tp_ev[ND - 1])
            mm.inc(_tmm(t, bank(6, 0, 1), la_row[:], identf[0:1, 0:1]))
            mm.inc(_tmm(t, bank(6, 128, 136), dpr[:], identf[0:8, 0:8]))
        m_la = mm.n

        @block.scalar
        def _(s):
            s.wait_ge(mm.h, m_la)
            sc.inc(s.activation(acol[:, 0:1], bank(6, 0, 1), AF.Exp))
        sc_a = sc.n

        @block.vector
        def _(v):
            v.wait_ge(mm.h, m_la)
            vq.wait(v)
            vq.inc(v.tensor_copy(dpt[:], bank(6, 128, 136)))
            v.wait_ge(sc.h, sc_a)
            vq.wait(v)
            vq.inc(v.memset(afull[:], 1.0))
            vq.wait(v)
            vq.inc(v.tensor_scalar(out=afull[:], in0=afull[:], scalar1=acol[:, 0:1],
                                   scalar2=None, op0=ALU.mult))
        v_a = vq.n

        v_end = [0] * NT
        s_end = [0] * NT
        ev_at = {}
        st_at = {}
        u_ev = {}
        for tt in range(NT):
            bo = (tt % 2) * D

            @block.sync
            def _(sy, tt=tt, bo=bo):
                if tt >= 2:
                    sy.wait_ge(vq.h, v_end[tt - 2])
                    sy.wait_ge(sc.h, s_end[tt - 2])
                sy.dma_start(out=xt[:, bo:bo + D],
                             in_=x1[tt * 128:(tt + 1) * 128, :]).then_inc(ld.h, 16)
            ld.n += 16
            XLD = ld.n

            @block.vector
            def _(v, tt=tt, bo=bo, XLD=XLD):
                v.wait_ge(ld.h, XLD)
                vq.wait(v)
                vq.inc(v.reduce_sum(st0[:, 0:1], xt[:, bo:bo + D], axis=AX.X))

            @block.scalar
            def _(s, tt=tt, bo=bo, XLD=XLD):
                s.wait_ge(ld.h, XLD)
                if tt >= 1:
                    s.wait_ge(vq.h, v_end[tt - 1])
                sc.inc(s.activation(xsq[:], xt[:, bo:bo + D], AF.Square,
                                    accum_out=st0[:, 2:3]))
            s_end[tt] = sc.n

            @block.vector
            def _(v, tt=tt, bo=bo):
                vq.wait(v)
                vq.inc(v.tensor_scalar(out=st0[:, 1:2], in0=st0[:, 0:1],
                                       scalar1=-1.0 / D, scalar2=None, op0=ALU.mult))
                vq.wait(v)
                vq.inc(v.tensor_tensor(out=st0[:, 3:4], in0=st0[:, 1:2],
                                       in1=st0[:, 1:2], op=ALU.mult))
                sc.wait(v)
                vq.wait(v)
                vq.inc(v.tensor_scalar(out=st0[:, 4:5], in0=st0[:, 2:3],
                                       scalar1=1.0 / D, scalar2=st0[:, 3:4],
                                       op0=ALU.mult, op1=ALU.subtract))
                vq.wait(v)
                vq.inc(v.tensor_scalar(out=st0[:, 5:6], in0=st0[:, 4:5],
                                       scalar1=1e-5, scalar2=None, op0=ALU.add))
                vq.wait(v)
                vq.inc(v.reciprocal(st0[:, 6:7], st0[:, 5:6]))

            @block.scalar
            def _(s, tt=tt):
                vq.wait(s)
                sc.inc(s.sqrt(st0[:, 7:8], st0[:, 6:7]))

            @block.vector
            def _(v, tt=tt, bo=bo):
                sc.wait(v)
                vq.inc(v.tensor_scalar(out=xnt[:, bo:bo + D], in0=xt[:, bo:bo + D],
                                       scalar1=st0[:, 1:2], scalar2=st0[:, 7:8],
                                       op0=ALU.add, op1=ALU.mult))
            v_end[tt] = vq.n

            for dc in range(ND):
                k = tt * ND + dc
                pb = k % 2
                eb = k % 4

                @block.tensor
                def _(t, tt=tt, dc=dc, pb=pb, bo=bo, k=k):
                    t.wait_ge(vq.h, v_end[tt])
                    if k >= 2:
                        t.wait_ge(vq.h, ev_at[k - 2])
                        t.wait_ge(sc.h, sc.n)
                    mm.inc(_tmm(t, bank(pb, 0, 128),
                                       xnt[:, bo + dc * 128:bo + (dc + 1) * 128],
                                       identf[:]))

                @block.scalar
                def _(s, pb=pb, eb=eb, k=k):
                    mm.wait(s)
                    if k >= 4:
                        s.wait_ge(stq.h, st_at[k - 4])
                    sc.inc(s.copy(tev[:, eb * 128:(eb + 1) * 128], bank(pb, 0, 128)))

                @block.vector
                def _(v, tt=tt, dc=dc, pb=pb, k=k):
                    v.wait_ge(mm.h, mm.n)
                    vq.wait(v)
                    vq.inc(v.tensor_copy(xnb[:, dc * T + tt * 128:dc * T + (tt + 1) * 128],
                                         bank(pb, 0, 128)))
                ev_at[k] = vq.n

                @block.sync
                def _(sy, tt=tt, dc=dc, eb=eb, k=k):
                    sc.wait(sy)
                    sy.wait_ge(vq.h, ev_at[k])
                    sy.dma_start(out=xn1f[dc * 128:(dc + 1) * 128, tt * 128:(tt + 1) * 128],
                                 in_=tev[:, eb * 128:(eb + 1) * 128]).then_inc(stq.h, 16)
                stq.n += 16
                st_at[k] = stq.n

        XN_ALL_V = vq.n
        XN_ST = stq.n

        for tg in range(NTG):
            @block.tensor
            def _(t, tg=tg):
                t.wait_ge(vq.h, XN_ALL_V)
                if tg >= 2:
                    t.wait_ge(sc.h, u_ev[tg - 2])
                for dc in range(ND):
                    mm.inc(t.matmul(bank(2 + tg % 2),
                                    bwT[:, dc * 128:(dc + 1) * 128],
                                    xnb[:, dc * T + tg * TG:dc * T + (tg + 1) * TG],
                                    start=(dc == 0), stop=(dc == ND - 1)))

            @block.scalar
            def _(s, tg=tg):
                mm.wait(s)
                sc.inc(s.copy(u_sb[:, tg * TG:(tg + 1) * TG], bank(2 + tg % 2)))
            u_ev[tg] = sc.n

        @block.vector
        def _(v):
            sc.wait(v)
            v.wait_ge(vq.h, v_a)
            vq.wait(v)
            vq.inc(v.tensor_tensor_scan(out=hs[:], data0=afull[:], data1=u_sb[:],
                                        initial=0.0, op0=ALU.mult, op1=ALU.add))
            vq.wait(v)
            vq.inc(v.tensor_copy(hs_bf[:], hs[:]))
        v_scan = vq.n

        y_ev = {}
        y_st = {}
        for dt in range(ND):
            for tg in range(NTG):
                k = dt * NTG + tg

                @block.sync
                def _(sy, dt=dt, tg=tg, k=k):
                    if k == 0:
                        sy.wait_ge(stq.h, XN_ST)
                    if k >= 2:
                        sy.wait_ge(vq.h, y_ev[k - 2])
                    sy.dma_start(out=x1c[:, (k % 2) * 512:(k % 2) * 512 + 512],
                                 in_=xn1f[dt * 128:(dt + 1) * 128, tg * TG:(tg + 1) * TG]
                                 ).then_inc(ld.h, 16)
                ld.n += 16
                YLD = ld.n

                @block.tensor
                def _(t, dt=dt, tg=tg, k=k):
                    t.wait_ge(vq.h, v_scan)
                    if k >= 2:
                        t.wait_ge(vq.h, y_ev[k - 2])
                    mm.inc(t.matmul(bank(2 + k % 2), cwT[:, dt * 128:(dt + 1) * 128],
                                    hs_bf[:, tg * TG:(tg + 1) * TG],
                                    start=True, stop=True))
                m_y = mm.n

                @block.vector
                def _(v, dt=dt, tg=tg, k=k, m_y=m_y, YLD=YLD):
                    v.wait_ge(mm.h, m_y)
                    v.wait_ge(ld.h, YLD)
                    if k >= 2:
                        v.wait_ge(stq.h, y_st[k - 2])
                    vq.wait(v)
                    vq.inc(v.scalar_tensor_tensor(
                        out=s1e[:, (k % 2) * 512:(k % 2) * 512 + 512],
                        in0=x1c[:, (k % 2) * 512:(k % 2) * 512 + 512],
                        scalar=dpt[:, dt:dt + 1], in1=bank(2 + k % 2),
                        op0=ALU.mult, op1=ALU.add))
                y_ev[k] = vq.n

                @block.sync
                def _(sy, dt=dt, tg=tg, k=k):
                    sy.wait_ge(vq.h, y_ev[k])
                    sy.dma_start(out=s1f[dt * 128:(dt + 1) * 128, tg * TG:(tg + 1) * TG],
                                 in_=s1e[:, (k % 2) * 512:(k % 2) * 512 + 512]
                                 ).then_inc(stq.h, 16)
                stq.n += 16
                y_st[k] = stq.n

        if debug and env is None:
            @block.sync
            def _(sy):
                sy.wait_ge(vq.h, v_scan)
                sy.dma_start(out=dbg_u, in_=u_sb[:]).then_inc(stq.h, 16)
                sy.dma_start(out=dbg_a, in_=afull[:]).then_inc(stq.h, 16)
                sy.dma_start(out=dbg_hs, in_=hs[:]).then_inc(stq.h, 16)
                sy.dma_start(out=dbg_dpt, in_=dpt[:]).then_inc(stq.h, 16)
                sy.dma_start(out=dbg_cwt, in_=cwT[:]).then_inc(stq.h, 16)
            stq.n += 80

        @block.sync
        def _(sy):
            sy.wait_ge(stq.h, stq.n)
    return nc



def build_subt(layers=range(6), debug=False, env=None, io=None):
    if env is None:
        nc = bass.Bass("TRN2", target_bir_lowering=False)
        xn1f = nc.dram_tensor("xn1f", [D, T], F32, kind="ExternalInput").ap()
        Wqkv = nc.dram_tensor("Wqkv", [6, D, 3072], F32, kind="ExternalInput").ap()
        q_sc = nc.dram_tensor("q_sc", [6, 64], F32, kind="ExternalInput").ap()
        k_sc = nc.dram_tensor("k_sc", [6, 64], F32, kind="ExternalInput").ap()
        Wo1 = nc.dram_tensor("Wo1", [6, D, D], F32, kind="ExternalInput").ap()
        ln_g = nc.dram_tensor("ln_g", [6, 8, 128], F32, kind="ExternalInput").ap()
        ln_b = nc.dram_tensor("ln_b", [6, 8, 128], F32, kind="ExternalInput").ap()
        W1l = nc.dram_tensor("W1l", [6, D, 5460], F32, kind="ExternalInput").ap()
        b1l = nc.dram_tensor("b1l", [6, 5460], F32, kind="ExternalInput").ap()
        W2l = nc.dram_tensor("W2l", [6, 2730, D], F32, kind="ExternalInput").ap()
        b2l = nc.dram_tensor("b2l", [6, 8, 128], F32, kind="ExternalInput").ap()
        g1f = nc.dram_tensor("g1f", [D, T], F32, kind="ExternalOutput").ap()
        cst = declare_consts(nc, ["ident", "identf", "c1q", "s1q", "c1k", "s1k", "R1",
                                  "S2", "S2T", "ID2", "Bband", "ones_1x128",
                                  "ones_128x1"])
    else:
        nc = env.nc
        cst = env.cst
        (xn1f, Wqkv, q_sc, k_sc, Wo1, ln_g, ln_b, W1l, b1l, W2l, b2l, g1f) = io
    P_ = "E" if env else ""
    dkind = "ExternalOutput" if (debug and env is None) else "Internal"
    xsP = nc.dram_tensor(P_ + "xsP", [D, T], F32, kind=dkind).ap()
    xsQ = nc.dram_tensor(P_ + "xsQ", [D, T], F32, kind=dkind).ap()
    oat1 = nc.dram_tensor(P_ + "oat1", [D, T], BF16, kind=dkind).ap()
    qscr = nc.dram_tensor(P_ + "qscr", [D, T], BF16, kind=dkind).ap()
    kscr1 = nc.dram_tensor(P_ + "kscr1", [D, T], BF16, kind=dkind).ap()
    SH = 300.0
    NS = 256
    NSG = T // NS
    layers = list(layers)
    _names = {}
    with ExitStack() as es:
        def mk_sb(stack):
            def f(name, shape, dt):
                k = _names.get(name, 0)
                _names[name] = k + 1
                if k:
                    name = f"{name}_{k}"
                return stack.enter_context(nc.sbuf_tensor(P_ + name, shape, dt))
            return f
        sb = mk_sb(es)
        identb = sb("identb", [128, 128], BF16)
        identf = sb("identf", [128, 128], F32)
        c1qt = sb("c1qt", [128, T], BF16)
        s1qt = sb("s1qt", [128, T], BF16)
        c1kt = sb("c1kt", [128, T], BF16)
        s1kt = sb("s1kt", [128, T], BF16)
        R1s = sb("R1s", [128, 128], BF16)
        S2s = sb("S2s", [2, 128], BF16)
        S2Ts = sb("S2Ts", [128, 2], BF16)
        ID2s = sb("ID2s", [64, 128], BF16)
        Bb = sb("Bb", [128, 384], F32)
        ones1x128 = sb("ones1x128", [1, 128], BF16)
        ones128x1 = sb("ones128x1", [128, 1], BF16)
        qs2 = sb("qs2", [128, 1], F32)
        ks2 = sb("ks2", [128, 1], F32)
        btri = sb("btri", [128, 1], F32)
        gt = sb("gt", [128, 8], F32)
        bt = sb("bt", [128, 8], F32)
        b1t = sb("b1t", [128, 44], F32)
        b2t = sb("b2t", [128, 8], F32)
        qsr = sb("qsr", [1, 2 * 64], F32)
        qsb = sb("qsb", [1, 2 * 64], BF16)
        gbr = sb("gbr", [8, 2 * 128], F32)
        b1ra = sb("b1ra", [21, 128], F32)
        b1rg = sb("b1rg", [21, 128], F32)
        b1rb = sb("b1rb", [1, 84], F32)
        sm = sb("sm", [1, 8], F32)
        smb = sb("smb", [1, 1], BF16)
        qcol = sb("qcol", [64, 2], BF16)
        pp = es.enter_context(nc.psum_tensor(P_ + "pp", [128, 8 * 512], F32))
        if env is None:
            block = es.enter_context(nc.Block())
            ld = SemC(nc, "ld"); vq = SemC(nc, "vq"); sc = SemC(nc, "sc")
            mm = SemC(nc, "mm"); stq = SemC(nc, "stq")
        else:
            block = env.block
            ld, vq, sc, mm, stq = env.sems

        def bank(i, lo=0, hi=512):
            return pp[:, i * 512 + lo:i * 512 + hi]

        @block.sync
        def _(sy):
            for nm, dst in (("ident", identb), ("identf", identf), ("c1q", c1qt),
                            ("s1q", s1qt), ("c1k", c1kt), ("s1k", s1kt), ("R1", R1s),
                            ("S2", S2s), ("S2T", S2Ts), ("ID2", ID2s), ("Bband", Bb),
                            ("ones_1x128", ones1x128), ("ones_128x1", ones128x1)):
                sy.dma_start(out=dst[:], in_=cst[nm]).then_inc(ld.h, 16)
        ld.n += 16 * 13

        for l in layers:
            xin = xn1f if l == layers[0] else xsQ
            fout = g1f if l == layers[-1] else xsQ

            if l != layers[0]:
                _bm, _bv, _bs, _bq, _bl = mm.n, vq.n, sc.n, stq.n, ld.n

                def _bar(e, _bm=_bm, _bv=_bv, _bs=_bs, _bq=_bq, _bl=_bl):
                    e.wait_ge(mm.h, _bm)
                    e.wait_ge(vq.h, _bv)
                    e.wait_ge(sc.h, _bs)
                    e.wait_ge(stq.h, _bq)
                    e.wait_ge(ld.h, _bl)
                block.tensor(_bar)
                block.vector(_bar)
                block.scalar(_bar)
                block.gpsimd(_bar)
                block.sync(_bar)

            @block.sync
            def _(sy, l=l):
                sy.wait_ge(mm.h, mm.n)
                sy.wait_ge(vq.h, vq.n)
                sy.wait_ge(sc.h, sc.n)
                sy.dma_start(out=qsr[:, 0:64], in_=q_sc[l:l + 1, :]).then_inc(ld.h, 16)
                sy.dma_start(out=qsr[:, 64:128], in_=k_sc[l:l + 1, :]).then_inc(ld.h, 16)
                sy.dma_start(out=gbr[:, 0:128], in_=ln_g[l]).then_inc(ld.h, 16)
                sy.dma_start(out=gbr[:, 128:256], in_=ln_b[l]).then_inc(ld.h, 16)
                sy.dma_start(out=b1ra[0:21, :],
                             in_=b1l[l:l + 1, 0:2688].rearrange("a (c p) -> (a c) p", p=128)
                             ).then_inc(ld.h, 16)
                sy.dma_start(out=b1rg[:],
                             in_=b1l[l:l + 1, 2730:5418].rearrange("a (c p) -> (a c) p", p=128)
                             ).then_inc(ld.h, 16)
                sy.dma_start(out=b1rb[:, 0:42], in_=b1l[l:l + 1, 2688:2730]).then_inc(ld.h, 16)
                sy.dma_start(out=b1rb[:, 42:84], in_=b1l[l:l + 1, 5418:5460]).then_inc(ld.h, 16)
            ld.n += 16 * 8

            @block.gpsimd
            def _(g, l=l):
                g.dma_start(out=qsb[:, 0:64], in_=q_sc[l:l + 1, :]).then_inc(ld.h, 16)
                g.dma_start(out=qsb[:, 64:128], in_=k_sc[l:l + 1, :]).then_inc(ld.h, 16)
            ld.n += 16 * 2
            SMLD = ld.n

            @block.tensor
            def _(t, l=l):
                t.wait_ge(ld.h, SMLD)
                t.wait_ge(vq.h, vq.n)
                t.wait_ge(sc.h, sc.n)
                mm.inc(_tmm(t, bank(0, 0, 8), gbr[:, 0:128], identf[0:8, 0:8]))
                mm.inc(_tmm(t, bank(0, 8, 16), gbr[:, 128:256], identf[0:8, 0:8]))
                mm.inc(_tmm(t, bank(0, 16, 37), b1ra[:], identf[0:21, 0:21]))
                mm.inc(_tmm(t, bank(0, 40, 61), b1rg[:], identf[0:21, 0:21]))
                mm.inc(_tmm(t, bank(0, 64, 65)[0:42, :], b1rb[:, 0:42], identf[0:1, 0:1]))
                mm.inc(_tmm(t, bank(0, 68, 69)[0:42, :], b1rb[:, 42:84], identf[0:1, 0:1]))
                mm.inc(_tmm(t, bank(1, 0, 1)[0:64, :], qsr[0:1, 0:64], identf[0:1, 0:1]))
                mm.inc(_tmm(t, bank(1, 4, 5)[0:64, :], qsr[0:1, 64:128], identf[0:1, 0:1]))
            mm_t0 = mm.n

            @block.sync
            def _(sy, l=l, mm_t0=mm_t0):
                sy.wait_ge(mm.h, mm_t0)
                sy.dma_start(out=gbr[:, 0:128], in_=b2l[l]).then_inc(ld.h, 16)
            ld.n += 16
            B2LD = ld.n

            @block.vector
            def _(v, l=l):
                v.wait_ge(mm.h, mm_t0)
                vq.wait(v)
                vq.inc(v.tensor_copy(gt[:], bank(0, 0, 8)))
                vq.wait(v)
                vq.inc(v.tensor_copy(bt[:], bank(0, 8, 16)))
                vq.wait(v)
                vq.inc(v.tensor_copy(b1t[:, 0:21], bank(0, 16, 37)))
                vq.wait(v)
                vq.inc(v.tensor_copy(b1t[:, 22:43], bank(0, 40, 61)))
                vq.wait(v)
                vq.inc(v.tensor_copy(b1t[0:42, 21:22], bank(0, 64, 65)[0:42, :]))
                vq.wait(v)
                vq.inc(v.tensor_copy(b1t[0:42, 43:44], bank(0, 68, 69)[0:42, :]))
                vq.wait(v)
                vq.inc(v.tensor_copy(qcol[:, 0:1], bank(1, 0, 1)[0:64, :]))
                vq.wait(v)
                vq.inc(v.tensor_copy(qcol[:, 1:2], bank(1, 4, 5)[0:64, :]))
            v_t0 = vq.n

            @block.tensor
            def _(t, l=l):
                t.wait_ge(vq.h, v_t0)
                t.wait_ge(ld.h, B2LD)
                mm.inc(t.matmul(bank(2, 0, 1), ID2s[:], qcol[:, 0:1], start=True, stop=True))
                mm.inc(t.matmul(bank(2, 4, 5), ID2s[:], qcol[:, 1:2], start=True, stop=True))
                mm.inc(_tmm(t, bank(2, 8, 16), gbr[:, 0:128], identf[0:8, 0:8]))
            mm_t1 = mm.n

            @block.vector
            def _(v, l=l):
                v.wait_ge(mm.h, mm_t1)
                vq.wait(v)
                vq.inc(v.tensor_copy(qs2[:], bank(2, 0, 1)))
                vq.wait(v)
                vq.inc(v.tensor_copy(ks2[:], bank(2, 4, 5)))
                vq.wait(v)
                vq.inc(v.tensor_copy(b2t[:], bank(2, 8, 16)))
                vq.wait(v)
                vq.inc(v.tensor_tensor(out=sm[:, 0:1].rearrange("a b -> a b") if False
                                       else qsr[:, 0:64], in0=qsr[:, 0:64],
                                       in1=qsr[:, 64:128], op=ALU.mult))
                vq.wait(v)
                vq.inc(v.reduce_max(sm[:, 0:1], qsr[:, 0:64], axis=AX.X,
                                    apply_absolute_value=True))
                vq.wait(v)
                vq.inc(v.tensor_scalar(out=sm[:, 1:2], in0=sm[:, 0:1], scalar1=8.0,
                                       scalar2=SH, op0=ALU.mult, op1=ALU.add))
                vq.wait(v)
                vq.inc(v.tensor_copy(smb[:], sm[:, 1:2]))
            v_t1 = vq.n

            @block.tensor
            def _(t):
                t.wait_ge(vq.h, v_t1)
                mm.inc(t.matmul(bank(3, 0, 1), ones1x128[:], smb[:], start=True, stop=True))
            mm_t2 = mm.n

            @block.scalar
            def _(s):
                s.wait_ge(mm.h, mm_t2)
                sc.inc(s.mul(btri[:], bank(3, 0, 1), -1.0))
            sc_btri = sc.n

            es_va = ExitStack()
            va = mk_sb(es_va)("va", [128, NT * 1040], BF16)

            @block.vector
            def _(v, _stq=None):
                v.wait_ge(mm.h, mm.n)
                v.wait_ge(sc.h, sc.n)
                v.wait_ge(stq.h, stq.n)
                v.wait_ge(ld.h, ld.n)
                vq.wait(v)
                vq.inc(v.memset(va[:].rearrange("p (t o) -> p t o", t=NT * 16)[:, :, 64:65],
                                1.0))
            V_ONES = vq.n
            es_qkv = ExitStack()
            sbq = mk_sb(es_qkv)
            wqkv_sb = sbq("wqkv_sb", [128, ND * 3072], BF16)
            xc = sbq("xc", [128, ND * 513], F32)
            xs2 = sbq("xs2", [128, ND * 512], BF16)
            rawp = sbq("rawp", [64, 0 * 512 + 512], BF16) if False else None
            rawq = sbq("rawq", [128, 2 * 512], BF16)
            sqp = sbq("sqp", [128, 512], BF16)
            n2c = sbq("n2c", [2, 512], F32)
            rn2 = sbq("rn2", [2, 512], BF16)
            rbs = sbq("rbs", [128, 512], BF16)
            v1b = sbq("v1b", [128, 512], BF16)
            v2b = sbq("v2b", [128, 2 * 512], BF16)
            rt1b = sbq("rt1b", [128, 512], BF16)
            rt2b = sbq("rt2b", [128, 512], BF16)
            qsave = sbq("qsave", [128, ND * 512], BF16)
            ksave = sbq("ksave", [128, ND * 512], BF16)

            @block.gpsimd
            def _(g, l=l):
                g.wait_ge(mm.h, mm.n)
                g.wait_ge(sc.h, sc.n)
                g.wait_ge(vq.h, vq.n)
                g.wait_ge(stq.h, stq.n)
                g.wait_ge(ld.h, ld.n)
                for i in range(ND):
                    g.dma_start(out=wqkv_sb[:, i * 3072:(i + 1) * 3072],
                                in_=Wqkv[l, i * 128:(i + 1) * 128, :]).then_inc(ld.h, 16)
            ld.n += 16 * ND
            WQKVLD = ld.n

            ev_sc = {}
            rope_done = {}
            vsc_ev = {}
            qk_st = {}
            for tc in range(NTG):
                t0 = tc * TG
                cb = (tc % 2)

                @block.sync
                def _(sy, l=l, tc=tc, t0=t0, cb=cb, xin=xin, _st=stq.n, _ld=ld.n):
                    if tc == 0:
                        sy.wait_ge(stq.h, _st)
                        sy.wait_ge(ld.h, _ld)
                    if tc >= 1:
                        sy.wait_ge(vq.h, rope_done[tc - 1])
                        sy.wait_ge(mm.h, qk_st[(tc - 1, 'mm')])
                    if tc == 0:
                        sy.dma_start(out=xc[:, 0:ND * 513]
                                     .rearrange("p (c t) -> p c t", c=ND)[:, :, 1:513],
                                     in_=xin.rearrange("(c p) t -> p c t", p=128)
                                     [:, :, 0:512]).then_inc(ld.h, 16)
                    else:
                        sy.dma_start(out=xc[:, 0:ND * 513]
                                     .rearrange("p (c t) -> p c t", c=ND),
                                     in_=xin.rearrange("(c p) t -> p c t", p=128)
                                     [:, :, t0 - 1:t0 + 512]).then_inc(ld.h, 16)
                ld.n += 16
                XLD = ld.n

                @block.vector
                def _(v, tc=tc, cb=cb, XLD=XLD):
                    v.wait_ge(ld.h, XLD)
                    vq.wait(v)
                    if tc == 0:
                        vq.inc(v.memset(xc[:, 0:ND * 513]
                                        .rearrange("p (c t) -> p c t", c=ND)[:, :, 0:1], 0.0))
                        vq.wait(v)
                    for dc in range(ND):
                        off = 1 if dc < 4 else 0
                        vq.inc(v.tensor_copy(
                            xs2[:, dc * 512:(dc + 1) * 512],
                            xc[:, dc * 513 + off:dc * 513 + off + 512]))
                        vq.wait(v)
                v_xs = vq.n

                for tt in range(4):
                    g_tt = tc * 4 + tt
                    for vg in range(2):
                        @block.tensor
                        def _(t, tc=tc, cb=cb, tt=tt, vg=vg, v_xs=v_xs):
                            t.wait_ge(vq.h, v_xs)
                            t.wait_ge(ld.h, WQKVLD)
                            t.wait_ge(sc.h, sc.n)
                            for dc in range(ND):
                                mm.inc(t.matmul(
                                    bank(2 + vg),
                                    xs2[:, dc * 512 + tt * 128:dc * 512 + tt * 128 + 128],
                                    wqkv_sb[:, dc * 3072 + 2048 + vg * 512:
                                            dc * 3072 + 2048 + (vg + 1) * 512],
                                    start=(dc == 0), stop=(dc == ND - 1)))

                        @block.scalar
                        def _(s, g_tt=g_tt, vg=vg):
                            mm.wait(s)
                            s.wait_ge(vq.h, V_ONES)
                            dst = va[:, g_tt * 1040 + vg * 520:g_tt * 1040 + (vg + 1) * 520] \
                                .rearrange("p (h o) -> p h o", h=8)[:, :, 0:64]
                            sc.inc(s.copy(dst, bank(2 + vg).rearrange("p (h o) -> p h o", h=8)))
                        vsc_ev[(tc, tt, vg)] = sc.n

                for pr in range(16):
                    is_q = pr < 8
                    fbase = pr * 128 if is_q else 1024 + (pr - 8) * 128
                    prb = pr % 2

                    @block.tensor
                    def _(t, tc=tc, cb=cb, pr=pr, prb=prb, fbase=fbase, v_xs=v_xs):
                        t.wait_ge(vq.h, v_xs)
                        t.wait_ge(ld.h, WQKVLD)
                        t.wait_ge(sc.h, sc.n)
                        for dc in range(ND):
                            mm.inc(t.matmul(
                                bank(prb),
                                wqkv_sb[:, dc * 3072 + fbase:dc * 3072 + fbase + 128],
                                xs2[:, dc * 512:(dc + 1) * 512],
                                start=(dc == 0), stop=(dc == ND - 1)))

                    @block.scalar
                    def _(s, pr=pr, prb=prb):
                        mm.wait(s)
                        s.wait_ge(vq.h, vq.n)
                        sc.inc(s.copy(rawq[:, prb * 512:(prb + 1) * 512], bank(prb)))
                    sc_raw = sc.n

                    @block.vector
                    def _(v, pr=pr, prb=prb, sc_raw=sc_raw):
                        v.wait_ge(sc.h, sc_raw)
                        vq.wait(v)
                        vq.inc(v.tensor_tensor(out=sqp[:], in0=rawq[:, prb * 512:(prb + 1) * 512],
                                               in1=rawq[:, prb * 512:(prb + 1) * 512],
                                               op=ALU.mult))
                    v_sq = vq.n

                    @block.tensor
                    def _(t, v_sq=v_sq):
                        t.wait_ge(vq.h, v_sq)
                        t.wait_ge(vq.h, vq.n)
                        mm.inc(t.matmul(bank(4, 0, 512)[0:2, :], S2Ts[:], sqp[:],
                                        start=True, stop=True))
                    m_n2 = mm.n

                    @block.vector
                    def _(v, m_n2=m_n2):
                        v.wait_ge(mm.h, m_n2)
                        vq.wait(v)
                        vq.inc(v.tensor_scalar(out=n2c[:], in0=bank(4, 0, 512)[0:2, :],
                                               scalar1=1e-24, scalar2=None, op0=ALU.max))
                        vq.wait(v)
                        with nc.allow_low_precision(reason="l2norm recip"):
                            vq.inc(v.reciprocal(n2c[:], n2c[:]))
                    v_rec = vq.n

                    @block.scalar
                    def _(s, v_rec=v_rec):
                        s.wait_ge(vq.h, v_rec)
                        with nc.allow_low_precision(reason="l2norm rsqrt bf16"):
                            sc.inc(s.sqrt(rn2[:], n2c[:]))
                    sc_rn = sc.n

                    @block.tensor
                    def _(t, sc_rn=sc_rn):
                        t.wait_ge(sc.h, sc_rn)
                        t.wait_ge(sc.h, sc.n)
                        mm.inc(t.matmul(bank(5), S2s[:], rn2[:], start=True, stop=True))
                    m_rb = mm.n

                    @block.scalar
                    def _(s, m_rb=m_rb):
                        s.wait_ge(mm.h, m_rb)
                        s.wait_ge(vq.h, vq.n)
                        sc.inc(s.copy(rbs[:], bank(5)))
                    sc_rbs = sc.n

                    @block.vector
                    def _(v, pr=pr, prb=prb, is_q=is_q, sc_rbs=sc_rbs):
                        v.wait_ge(sc.h, sc_rbs)
                        vq.wait(v)
                        vq.inc(v.tensor_tensor(out=v1b[:], in0=rawq[:, prb * 512:(prb + 1) * 512],
                                               in1=rbs[:], op=ALU.mult))
                        vq.wait(v)
                        vq.inc(v.tensor_scalar(out=v2b[:, prb * 512:(prb + 1) * 512],
                                               in0=v1b[:],
                                               scalar1=(qs2[:, 0:1] if is_q else ks2[:, 0:1]),
                                               scalar2=None, op0=ALU.mult))
                    v_v2 = vq.n

                    @block.tensor
                    def _(t, prb=prb, v_v2=v_v2):
                        t.wait_ge(vq.h, v_v2)
                        t.wait_ge(vq.h, vq.n)
                        mm.inc(t.matmul(bank(6 + prb), R1s[:],
                                        v2b[:, prb * 512:(prb + 1) * 512],
                                        start=True, stop=True))
                    m_rv = mm.n

                    @block.vector
                    def _(v, tc=tc, t0=t0, pr=pr, prb=prb, is_q=is_q, m_rv=m_rv):
                        ctab = c1qt if is_q else c1kt
                        stab = s1qt if is_q else s1kt
                        dst = qsave if is_q else ksave
                        dpr = pr if is_q else pr - 8
                        if pr == 0 and tc >= 1:
                            v.wait_ge(stq.h, qk_st[tc - 1])
                        vq.wait(v)
                        vq.inc(v.tensor_tensor(out=rt1b[:],
                                               in0=v2b[:, prb * 512:(prb + 1) * 512],
                                               in1=ctab[:, t0:t0 + 512], op=ALU.mult))
                        v.wait_ge(mm.h, m_rv)
                        vq.wait(v)
                        vq.inc(v.tensor_tensor(out=rt2b[:], in0=bank(6 + prb),
                                               in1=stab[:, t0:t0 + 512], op=ALU.mult))
                        vq.wait(v)
                        vq.inc(v.tensor_tensor(out=dst[:, dpr * 512:(dpr + 1) * 512],
                                               in0=rt1b[:], in1=rt2b[:], op=ALU.add))
                rope_done[tc] = vq.n

                @block.sync
                def _(sy, tc=tc, t0=t0):
                    sy.wait_ge(vq.h, rope_done[tc])
                    sy.dma_start(out=qscr.rearrange("(c p) t -> p c t", p=128)
                                 [:, :, t0:t0 + 512],
                                 in_=qsave[:].rearrange("p (c t) -> p c t", c=ND)
                                 ).then_inc(stq.h, 16)
                    sy.dma_start(out=kscr1.rearrange("(c p) t -> p c t", p=128)
                                 [:, :, t0:t0 + 512],
                                 in_=ksave[:].rearrange("p (c t) -> p c t", c=ND)
                                 ).then_inc(stq.h, 16)
                stq.n += 32
                qk_st[tc] = stq.n
                qk_st[(tc, 'mm')] = mm.n
            E1_ST = stq.n
            E1_SC = sc.n

            es_qkv.close()

            es_at = ExitStack()
            sba = mk_sb(es_at)
            qh = sba("qh", [64, 2 * T], BF16)
            kh = sba("kh", [64, 2 * T], BF16)
            s_sb = sba("s_sb", [128, 2 * 384], F32)
            p_sb = sba("p_sb", [128, 2 * 384], BF16)
            rr1 = sba("rr1", [1, 512], BF16)
            rb1 = sba("rb1", [64, 2 * 512], BF16)
            oh1 = sba("oh1", [64, 2 * 512], BF16)

            hld = {}
            oh_st = {}
            for hi, h in enumerate(range(16)):
                hb = hi % 2

                @block.sync
                def _(sy, h=h, hb=hb, _v=vq.n, _m=mm.n):
                    if hi == 0:
                        sy.wait_ge(stq.h, E1_ST)
                        sy.wait_ge(sc.h, E1_SC)
                        sy.wait_ge(vq.h, _v)
                        sy.wait_ge(mm.h, _m)
                    if hi >= 2:
                        sy.wait_ge(mm.h, hld[(hi - 2, 'mm')])
                    sy.dma_start(out=qh[:, hb * T:(hb + 1) * T],
                                 in_=qscr[h * 64:(h + 1) * 64, :]).then_inc(ld.h, 16)
                    sy.dma_start(out=kh[:, hb * T:(hb + 1) * T],
                                 in_=kscr1[h * 64:(h + 1) * 64, :]).then_inc(ld.h, 16)
                ld.n += 32
                hld[hi] = ld.n

                exp_v = {}
                stt_v = {}
                avst = {}
                div_v = {}
                HSV = vq.n
                HSM = mm.n
                HSS = sc.n
                for c in range(NT):
                    span = min(384, T - 128 * c)
                    pb2 = c % 2

                    @block.tensor
                    def _(t, h=h, hb=hb, c=c, span=span, pb2=pb2, HSV=HSV):
                        t.wait_ge(ld.h, hld[hi])
                        t.wait_ge(vq.h, stt_v[c - 2] if c >= 2 else HSV)
                        mm.inc(t.matmul(bank(pb2, 0, span),
                                        kh[:, hb * T + c * 128:hb * T + (c + 1) * 128],
                                        qh[:, hb * T + c * 128:hb * T + c * 128 + span],
                                        start=True, stop=True))
                    m_qk = mm.n

                    @block.vector
                    def _(v, c=c, span=span, pb2=pb2, m_qk=m_qk):
                        v.wait_ge(mm.h, m_qk)
                        if c >= 2:
                            v.wait_ge(sc.h, exp_v[c - 2])
                        vq.wait(v)
                        vq.inc(v.scalar_tensor_tensor(
                            out=s_sb[:, pb2 * 384:pb2 * 384 + span],
                            in0=bank(pb2, 0, span), scalar=SH,
                            in1=Bb[:, 0:span], op0=ALU.add, op1=ALU.mult))
                    stt_v[c] = vq.n

                    @block.scalar
                    def _(s, c=c, span=span, pb2=pb2, HSM=HSM):
                        s.wait_ge(vq.h, stt_v[c])
                        s.wait_ge(mm.h, avst[c - 2] if c >= 2 else HSM)
                        sc.inc(s.activation(p_sb[:, pb2 * 384:pb2 * 384 + span],
                                            s_sb[:, pb2 * 384:pb2 * 384 + span],
                                            AF.Exp, bias=btri[:, 0:1]))
                    exp_v[c] = sc.n

                    @block.tensor
                    def _(t, h=h, c=c, span=span, pb2=pb2):
                        t.wait_ge(sc.h, exp_v[c])
                        for mq in range(c, min(c + 3, NT)):
                            if 128 * (mq - c) >= span:
                                continue
                            gq = mq // 4
                            t.wait_ge(vq.h, div_v.get(gq - 2, HSV))
                            mm.inc(t.matmul(
                                bank(2 + gq % 2, (mq % 4) * 128, (mq % 4) * 128 + 128)
                                [0:65, :],
                                va[:, c * 1040 + (h // 8) * 520 + (h % 8) * 65:
                                   c * 1040 + (h // 8) * 520 + (h % 8) * 65 + 65],
                                p_sb[:, pb2 * 384 + (mq - c) * 128:
                                     pb2 * 384 + (mq - c) * 128 + 128],
                                start=(mq % 4 == 0 and c == max(0, mq - 2)),
                                stop=(c == mq and (mq % 4 == 3 or mq == NT - 1)),
                                skip_group_check=True))
                    avst[c] = mm.n

                    if c % 4 == 3 or c == NT - 1:
                        gq = c // 4

                        @block.vector
                        def _(v, gq=gq, c=c):
                            v.wait_ge(mm.h, avst[c])
                            vq.wait(v)
                            with nc.allow_low_precision(reason="softmax denom bf16"):
                                vq.inc(v.reciprocal(rr1[:],
                                                    bank(2 + gq % 2)[64:65, :]))
                        v_rr1 = vq.n

                        @block.tensor
                        def _(t, v_rr1=v_rr1):
                            t.wait_ge(vq.h, v_rr1)
                            t.wait_ge(sc.h, sc.n)
                            mm.inc(t.matmul(bank(6)[0:64, :], S2s[0:1, 0:64], rr1[:],
                                            start=True, stop=True))
                        m_rb1 = mm.n

                        @block.scalar
                        def _(s, gq=gq, m_rb1=m_rb1):
                            s.wait_ge(mm.h, m_rb1)
                            key = gq - 2
                            if (h, key) in oh_st:
                                s.wait_ge(stq.h, oh_st[(h, key)])
                            sc.inc(s.copy(rb1[:, (gq % 2) * 512:(gq % 2) * 512 + 512],
                                          bank(6)[0:64, :]))
                        sc_rb1 = sc.n

                        @block.vector
                        def _(v, gq=gq, sc_rb1=sc_rb1):
                            v.wait_ge(sc.h, sc_rb1)
                            vq.wait(v)
                            vq.inc(v.tensor_tensor(
                                out=oh1[:, (gq % 2) * 512:(gq % 2) * 512 + 512],
                                in0=bank(2 + gq % 2)[0:64, :],
                                in1=rb1[:, (gq % 2) * 512:(gq % 2) * 512 + 512],
                                op=ALU.mult))
                        div_v[gq] = vq.n

                        @block.sync
                        def _(sy, h=h, gq=gq):
                            sy.wait_ge(vq.h, div_v[gq])
                            sy.dma_start(out=oat1[h * 64:(h + 1) * 64,
                                                  gq * 512:(gq + 1) * 512],
                                         in_=oh1[:, (gq % 2) * 512:(gq % 2) * 512 + 512]
                                         ).then_inc(stq.h, 16)
                        stq.n += 16
                        oh_st[(h, gq)] = stq.n
                hld[(hi, 'mm')] = mm.n
            E2_ST = stq.n
            es_at.close()
            es_va.close()

            es_wo = ExitStack()
            sbo = mk_sb(es_wo)
            wo_sb = sbo("wo_sb", [128, ND * D], BF16)
            att1 = sbo("att1", [128, ND * 512], BF16)
            xres = sbo("xres", [128, ND * 512], F32)
            xo1 = sbo("xo1", [128, 2 * 512], F32)

            @block.gpsimd
            def _(g, l=l):
                g.wait_ge(mm.h, mm.n)
                g.wait_ge(sc.h, sc.n)
                g.wait_ge(vq.h, vq.n)
                g.wait_ge(stq.h, stq.n)
                g.wait_ge(ld.h, ld.n)
                for i in range(ND):
                    g.dma_start(out=wo_sb[:, i * D:(i + 1) * D],
                                in_=Wo1[l, i * 128:(i + 1) * 128, :]).then_inc(ld.h, 16)
            ld.n += 16 * ND
            WOLD = ld.n

            a_ld = {}
            xo_v = {}
            xo_stq = {}
            for tg in range(NTG):
                @block.sync
                def _(sy, tg=tg, xin=xin, _v=vq.n, _m=mm.n, _s=sc.n):
                    if tg == 0:
                        sy.wait_ge(vq.h, _v)
                        sy.wait_ge(mm.h, _m)
                        sy.wait_ge(sc.h, _s)
                        sy.wait_ge(stq.h, E2_ST)
                    if tg >= 2:
                        sy.wait_ge(mm.h, xo_v[(tg - 2, 'mm')])
                        sy.wait_ge(vq.h, xo_v[tg - 2])
                    sy.dma_start(out=att1[:].rearrange("p (c t) -> p c t", c=ND),
                                 in_=oat1.rearrange("(c p) t -> p c t", p=128)
                                 [:, :, tg * TG:(tg + 1) * TG]).then_inc(ld.h, 16)
                    sy.dma_start(out=xres[:].rearrange("p (c t) -> p c t", c=ND),
                                 in_=xin.rearrange("(c p) t -> p c t", p=128)
                                 [:, :, tg * TG:(tg + 1) * TG]).then_inc(ld.h, 16)
                ld.n += 32
                a_ld[tg] = ld.n

                for dt in range(ND):
                    k = tg * ND + dt

                    @block.tensor
                    def _(t, tg=tg, dt=dt, k=k):
                        t.wait_ge(ld.h, a_ld[tg])
                        t.wait_ge(ld.h, WOLD)
                        if k >= 2:
                            t.wait_ge(vq.h, xo_v[(k - 2, 'v')])
                        for ic in range(ND):
                            mm.inc(t.matmul(bank(k % 2),
                                            wo_sb[:, ic * D + dt * 128:ic * D + (dt + 1) * 128],
                                            att1[:, ic * 512:(ic + 1) * 512],
                                            start=(ic == 0), stop=(ic == ND - 1)))

                    @block.vector
                    def _(v, tg=tg, dt=dt, k=k):
                        v.wait_ge(mm.h, mm.n)
                        if k >= 2:
                            v.wait_ge(stq.h, xo_stq[k - 2])
                        vq.wait(v)
                        vq.inc(v.tensor_tensor(out=xo1[:, (k % 2) * 512:(k % 2) * 512 + 512],
                                               in0=bank(k % 2),
                                               in1=xres[:, dt * 512:(dt + 1) * 512],
                                               op=ALU.add))
                    xo_v[(k, 'v')] = vq.n

                    @block.sync
                    def _(sy, tg=tg, dt=dt, k=k):
                        sy.wait_ge(vq.h, xo_v[(k, 'v')])
                        sy.dma_start(out=xsP[dt * 128:(dt + 1) * 128, tg * TG:(tg + 1) * TG],
                                     in_=xo1[:, (k % 2) * 512:(k % 2) * 512 + 512]
                                     ).then_inc(stq.h, 16)
                    stq.n += 16
                    xo_stq[k] = stq.n
                xo_v[tg] = vq.n
                xo_v[(tg, 'mm')] = mm.n
            E3_ST = stq.n
            es_wo.close()

            es_ff = ExitStack()
            sbf = mk_sb(es_ff)
            w1_sb = sbf("w1f_sb", [128, ND * 5460], BF16)
            w2_sb = sbf("w2f_sb", [128, 22 * D], BF16)
            xc2 = sbf("xc2", [128, 2 * ND * 257], F32)
            xs3 = sbf("xs3", [128, 2 * ND * 256], BF16)
            sq3 = sbf("sq3", [128, 256], BF16)
            rows3 = sbf("rows3", [1, 4 * 256], F32)
            rows3b = sbf("rows3b", [1, 2 * 256], BF16)
            mb3 = sbf("mb3", [128, 256], F32)
            ib3 = sbf("ib3", [128, 256], F32)
            xnl = sbf("xnl", [128, ND * 256], BF16)
            gel = sbf("gel", [128, 2 * 256], BF16)
            hfb = sbf("hfb", [128, 2 * 256], BF16)
            xo2 = sbf("xo2", [128, 2 * 256], F32)

            @block.gpsimd
            def _(g, l=l):
                g.wait_ge(mm.h, mm.n)
                g.wait_ge(sc.h, sc.n)
                g.wait_ge(vq.h, vq.n)
                g.wait_ge(stq.h, stq.n)
                g.wait_ge(ld.h, ld.n)
                for i in range(ND):
                    g.dma_start(out=w1_sb[:, i * 5460:(i + 1) * 5460],
                                in_=W1l[l, i * 128:(i + 1) * 128, :]).then_inc(ld.h, 16)
                for i in range(21):
                    g.dma_start(out=w2_sb[:, i * D:(i + 1) * D],
                                in_=W2l[l, i * 128:(i + 1) * 128, :]).then_inc(ld.h, 16)
                g.dma_start(out=w2_sb[0:42, 21 * D:22 * D],
                            in_=W2l[l, 2688:2730, :]).then_inc(ld.h, 16)
            ld.n += 16 * (ND + 22)
            WFLD = ld.n

            FPAIR = [(j * 128, min(128, 2730 - j * 128)) for j in range(22)]
            ns_v = {}
            ns_st = {}
            for ns in range(NSG):
                t0 = ns * NS
                nb = ns % 2

                @block.sync
                def _(sy, ns=ns, t0=t0, nb=nb, E3_ST=E3_ST, _v=vq.n, _m=mm.n):
                    if ns == 0:
                        sy.wait_ge(stq.h, E3_ST)
                        sy.wait_ge(vq.h, _v)
                        sy.wait_ge(mm.h, _m)
                    if ns >= 2:
                        sy.wait_ge(vq.h, ns_v[ns - 2])
                        sy.wait_ge(mm.h, ns_v[(ns - 2, 'mm')])
                    if ns == 0:
                        sy.dma_start(out=xc2[:, nb * ND * 257:(nb + 1) * ND * 257]
                                     .rearrange("p (c t) -> p c t", c=ND)[:, :, 1:257],
                                     in_=xsP.rearrange("(c p) t -> p c t", p=128)
                                     [:, :, 0:256]).then_inc(ld.h, 16)
                    else:
                        sy.dma_start(out=xc2[:, nb * ND * 257:(nb + 1) * ND * 257]
                                     .rearrange("p (c t) -> p c t", c=ND),
                                     in_=xsP.rearrange("(c p) t -> p c t", p=128)
                                     [:, :, t0 - 1:t0 + 256]).then_inc(ld.h, 16)
                ld.n += 16
                X2LD = ld.n

                @block.vector
                def _(v, ns=ns, nb=nb, X2LD=X2LD):
                    v.wait_ge(ld.h, X2LD)
                    vq.wait(v)
                    if ns == 0:
                        vq.inc(v.memset(xc2[:, nb * ND * 257:(nb + 1) * ND * 257]
                                        .rearrange("p (c t) -> p c t", c=ND)[:, :, 0:1], 0.0))
                        vq.wait(v)
                    for dc in range(ND):
                        off = 1 if dc < 4 else 0
                        vq.inc(v.tensor_copy(
                            xs3[:, nb * ND * 256 + dc * 256:nb * ND * 256 + (dc + 1) * 256],
                            xc2[:, nb * ND * 257 + dc * 257 + off:
                                nb * ND * 257 + dc * 257 + off + 256]))
                        vq.wait(v)
                v_xs3 = vq.n

                @block.tensor
                def _(t, ns=ns, nb=nb, v_xs3=v_xs3):
                    t.wait_ge(vq.h, v_xs3)
                    t.wait_ge(ld.h, WFLD)
                    t.wait_ge(vq.h, vq.n)
                    for dc in range(ND):
                        mm.inc(t.matmul(bank(4, 0, 256)[0:1, :], ones128x1[:],
                                        xs3[:, nb * ND * 256 + dc * 256:
                                            nb * ND * 256 + (dc + 1) * 256],
                                        start=(dc == 0), stop=(dc == ND - 1)))
                m_sum = mm.n

                sqv = {}
                for dc in range(ND):
                    @block.vector
                    def _(v, ns=ns, nb=nb, dc=dc):
                        if dc >= 1:
                            v.wait_ge(mm.h, sqv[dc - 1])
                        vq.wait(v)
                        vq.inc(v.tensor_tensor(out=sq3[:],
                                               in0=xs3[:, nb * ND * 256 + dc * 256:
                                                       nb * ND * 256 + (dc + 1) * 256],
                                               in1=xs3[:, nb * ND * 256 + dc * 256:
                                                       nb * ND * 256 + (dc + 1) * 256],
                                               op=ALU.mult))
                    v_sq3 = vq.n

                    @block.tensor
                    def _(t, dc=dc, v_sq3=v_sq3):
                        t.wait_ge(vq.h, v_sq3)
                        mm.inc(t.matmul(bank(4, 256, 512)[0:1, :], ones128x1[:], sq3[:],
                                        start=(dc == 0), stop=(dc == ND - 1)))
                    sqv[dc] = mm.n
                m_sq = mm.n

                @block.vector
                def _(v, m_sq=m_sq):
                    v.wait_ge(mm.h, m_sq)
                    vq.wait(v)
                    vq.inc(v.tensor_scalar(out=rows3[:, 0:256], in0=bank(4, 0, 256)[0:1, :],
                                           scalar1=-1.0 / D, scalar2=None, op0=ALU.mult))
                    vq.wait(v)
                    vq.inc(v.tensor_tensor(out=rows3[:, 256:512], in0=rows3[:, 0:256],
                                           in1=rows3[:, 0:256], op=ALU.mult))
                    vq.wait(v)
                    vq.inc(v.scalar_tensor_tensor(out=rows3[:, 256:512],
                                                  in0=bank(4, 256, 512)[0:1, :],
                                                  scalar=1.0 / D, in1=rows3[:, 256:512],
                                                  op0=ALU.mult, op1=ALU.subtract))
                    vq.wait(v)
                    vq.inc(v.tensor_scalar(out=rows3[:, 256:512], in0=rows3[:, 256:512],
                                           scalar1=1e-5, scalar2=None, op0=ALU.add))
                    vq.wait(v)
                    vq.inc(v.reciprocal(rows3[:, 512:768], rows3[:, 256:512]))
                v_st1 = vq.n

                @block.scalar
                def _(s, v_st1=v_st1):
                    s.wait_ge(vq.h, v_st1)
                    sc.inc(s.sqrt(rows3[:, 768:1024], rows3[:, 512:768]))
                sc_st = sc.n

                @block.vector
                def _(v, sc_st=sc_st):
                    v.wait_ge(sc.h, sc_st)
                    vq.wait(v)
                    vq.inc(v.tensor_copy(rows3b[:, 0:256], rows3[:, 0:256]))
                    vq.wait(v)
                    vq.inc(v.tensor_copy(rows3b[:, 256:512], rows3[:, 768:1024]))
                v_cast = vq.n

                @block.tensor
                def _(t, v_cast=v_cast):
                    t.wait_ge(vq.h, v_cast)
                    t.wait_ge(sc.h, sc.n)
                    mm.inc(t.matmul(bank(5, 0, 256), ones1x128[:], rows3b[:, 0:256],
                                    start=True, stop=True))
                    mm.inc(t.matmul(bank(5, 256, 512), ones1x128[:], rows3b[:, 256:512],
                                    start=True, stop=True))
                m_bc = mm.n

                @block.scalar
                def _(s, m_bc=m_bc):
                    s.wait_ge(mm.h, m_bc)
                    s.wait_ge(vq.h, vq.n)
                    sc.inc(s.copy(mb3[:], bank(5, 0, 256)))
                    sc.inc(s.copy(ib3[:], bank(5, 256, 512)))
                sc_bc = sc.n

                for dc in range(ND):
                    @block.vector
                    def _(v, ns=ns, nb=nb, dc=dc, sc_bc=sc_bc):
                        v.wait_ge(sc.h, sc_bc)
                        vq.wait(v)
                        vq.inc(v.tensor_tensor(out=xo2[:, 0:256],
                                               in0=xs3[:, nb * ND * 256 + dc * 256:
                                                       nb * ND * 256 + (dc + 1) * 256],
                                               in1=mb3[:], op=ALU.add))
                        vq.wait(v)
                        vq.inc(v.tensor_tensor(out=xo2[:, 0:256], in0=xo2[:, 0:256],
                                               in1=ib3[:], op=ALU.mult))
                        vq.wait(v)
                        vq.inc(v.tensor_scalar(out=xnl[:, dc * 256:(dc + 1) * 256],
                                               in0=xo2[:, 0:256],
                                               scalar1=gt[:, dc:dc + 1],
                                               scalar2=bt[:, dc:dc + 1],
                                               op0=ALU.mult, op1=ALU.add))
                XN_V = vq.n

                hz = {}
                for jp, (c0, w) in enumerate(FPAIR):
                    jb2 = jp % 2

                    @block.tensor
                    def _(t, jp=jp, jb2=jb2, c0=c0, w=w, XN_V=XN_V):
                        t.wait_ge(vq.h, XN_V)
                        if jp >= 2:
                            t.wait_ge(vq.h, hz[(jp - 2, 'v')])
                            t.wait_ge(sc.h, hz[(jp - 2, 's')])
                        for dc in range(ND):
                            mm.inc(t.matmul(bank(jb2, 0, 256)[0:w, :],
                                            w1_sb[:, dc * 5460 + c0:dc * 5460 + c0 + w],
                                            xnl[:, dc * 256:(dc + 1) * 256],
                                            start=(dc == 0), stop=(dc == ND - 1)))
                        for dc in range(ND):
                            mm.inc(t.matmul(bank(2 + jb2, 0, 256)[0:w, :],
                                            w1_sb[:, dc * 5460 + 2730 + c0:
                                                  dc * 5460 + 2730 + c0 + w],
                                            xnl[:, dc * 256:(dc + 1) * 256],
                                            start=(dc == 0), stop=(dc == ND - 1)))
                    m_h = mm.n

                    @block.scalar
                    def _(s, jp=jp, jb2=jb2, w=w, m_h=m_h):
                        s.wait_ge(mm.h, m_h)
                        if jp >= 2:
                            s.wait_ge(vq.h, hz[(jp - 2, 'v')])
                        sc.inc(s.activation(gel[0:w, jb2 * 256:(jb2 + 1) * 256],
                                            bank(2 + jb2, 0, 256)[0:w, :],
                                            AF.Gelu, bias=b1t[0:w, 22 + jp:23 + jp]))
                    hz[(jp, 's')] = sc.n

                    @block.vector
                    def _(v, jp=jp, jb2=jb2, w=w):
                        v.wait_ge(sc.h, hz[(jp, 's')])
                        if jp >= 2:
                            v.wait_ge(mm.h, hz[(jp - 2, 'z')])
                        vq.wait(v)
                        vq.inc(v.scalar_tensor_tensor(
                            out=hfb[0:w, jb2 * 256:(jb2 + 1) * 256],
                            in0=bank(jb2, 0, 256)[0:w, :],
                            scalar=b1t[0:w, jp:jp + 1],
                            in1=gel[0:w, jb2 * 256:(jb2 + 1) * 256],
                            op0=ALU.add, op1=ALU.mult))
                    hz[(jp, 'v')] = vq.n

                    @block.tensor
                    def _(t, jp=jp, jb2=jb2, w=w):
                        t.wait_ge(vq.h, hz[(jp, 'v')])
                        for dt in range(ND):
                            mm.inc(t.matmul(bank(4 + dt // 2, (dt % 2) * 256,
                                                 (dt % 2) * 256 + 256),
                                            w2_sb[0:w, jp * D + dt * 128:
                                                  jp * D + (dt + 1) * 128],
                                            hfb[0:w, jb2 * 256:(jb2 + 1) * 256],
                                            start=(jp == 0 and dt % 2 == 0),
                                            stop=(jp == 21 and dt % 2 == 1),
                                            skip_group_check=True))
                    hz[(jp, 'z')] = mm.n
                m_z = mm.n

                zc_v = {}
                for dt in range(ND):
                    @block.vector
                    def _(v, ns=ns, nb=nb, dt=dt, m_z=m_z):
                        v.wait_ge(mm.h, m_z)
                        if dt >= 2:
                            v.wait_ge(stq.h, zc_v[(dt - 2, 'st')])
                        vq.wait(v)
                        vq.inc(v.scalar_tensor_tensor(
                            out=xo2[:, (dt % 2) * 256:(dt % 2) * 256 + 256],
                            in0=bank(4 + dt // 2, (dt % 2) * 256, (dt % 2) * 256 + 256),
                            scalar=b2t[:, dt:dt + 1],
                            in1=xc2[:, nb * ND * 257 + dt * 257 + 1:
                                    nb * ND * 257 + dt * 257 + 257],
                            op0=ALU.add, op1=ALU.add))
                    zc_v[dt] = vq.n

                    @block.sync
                    def _(sy, ns=ns, t0=t0, dt=dt, fout=fout):
                        sy.wait_ge(vq.h, zc_v[dt])
                        sy.dma_start(out=fout[dt * 128:(dt + 1) * 128, t0:t0 + 256],
                                     in_=xo2[:, (dt % 2) * 256:(dt % 2) * 256 + 256]
                                     ).then_inc(stq.h, 16)
                    stq.n += 16
                    zc_v[(dt, 'st')] = stq.n
                ns_v[ns] = vq.n
                ns_v[(ns, 'mm')] = mm.n
                ns_st[ns] = stq.n
            es_ff.close()
            prev_e4_st = stq.n
    return nc



def build_fin(env=None, io=None):
    if env is None:
        nc = bass.Bass("TRN2", target_bir_lowering=False)
        x1 = nc.dram_tensor("x1", [T, D], F32, kind="ExternalInput").ap()
        s1f = nc.dram_tensor("s1f", [D, T], F32, kind="ExternalInput").ap()
        g1f = nc.dram_tensor("g1f", [D, T], F32, kind="ExternalInput").ap()
        y = nc.dram_tensor("y", [T, D], F32, kind="ExternalOutput").ap()
        cst = declare_consts(nc, ["identf"])
    else:
        nc = env.nc
        cst = env.cst
        x1, s1f, g1f, y = io
    P_ = "Fp" if env else ""
    with ExitStack() as es:
        def sb(name, shape, dt):
            return es.enter_context(nc.sbuf_tensor(P_ + name, shape, dt))

        identf = sb("identf", [128, 128], F32)
        s1c = sb("s1c", [128, 2 * 512], F32)
        g1c = sb("g1c", [128, 2 * 512], F32)
        sgc = sb("sgc", [128, 512], F32)
        xacc = sb("xacc", [128, 2 * 4 * D], F32)
        pp = es.enter_context(nc.psum_tensor(P_ + "pp", [128, 4 * 512], F32))
        if env is None:
            block = es.enter_context(nc.Block())
            ld = SemC(nc, "ld"); vq = SemC(nc, "vq"); sc = SemC(nc, "sc")
            mm = SemC(nc, "mm"); stq = SemC(nc, "stq")
        else:
            block = env.block
            ld, vq, sc, mm, stq = env.sems

        @block.sync
        def _(sy):
            sy.dma_start(out=identf[:], in_=cst["identf"]).then_inc(ld.h, 16)
        ld.n += 16

        tg_v = {}
        tg_st = {}
        for tg in range(NTG):
            b = tg % 2

            @block.sync
            def _(sy, tg=tg, b=b):
                if tg >= 2:
                    sy.wait_ge(stq.h, tg_st[tg - 2])
                    sy.wait_ge(vq.h, tg_v[tg - 2])
                sy.dma_start(out=xacc[:, b * 4 * D:(b + 1) * 4 * D]
                             .rearrange("p (w d) -> p w d", w=4),
                             in_=x1.rearrange("(w p) d -> p w d", p=128)
                             [:, 4 * tg:4 * tg + 4, :]).then_inc(ld.h, 16)
            ld.n += 16

            dc_v = {}
            for dc in range(ND):
                cb = dc % 2

                @block.sync
                def _(sy, tg=tg, dc=dc, cb=cb):
                    if dc >= 2:
                        sy.wait_ge(vq.h, dc_v[dc - 2])
                    sy.dma_start(out=s1c[:, cb * 512:(cb + 1) * 512],
                                 in_=s1f[dc * 128:(dc + 1) * 128, tg * TG:(tg + 1) * TG]
                                 ).then_inc(ld.h, 16)
                    sy.dma_start(out=g1c[:, cb * 512:(cb + 1) * 512],
                                 in_=g1f[dc * 128:(dc + 1) * 128, tg * TG:(tg + 1) * TG]
                                 ).then_inc(ld.h, 16)
                ld.n += 32
                CLD = ld.n

                @block.scalar
                def _(s, cb=cb, CLD=CLD):
                    s.wait_ge(ld.h, CLD)
                    if True:
                        s.wait_ge(vq.h, vq.n)
                    sc.inc(s.activation(g1c[:, cb * 512:(cb + 1) * 512],
                                        g1c[:, cb * 512:(cb + 1) * 512], AF.Silu))
                sc_si = sc.n

                @block.vector
                def _(v, cb=cb, sc_si=sc_si):
                    v.wait_ge(sc.h, sc_si)
                    v.wait_ge(mm.h, mm.n)
                    vq.wait(v)
                    vq.inc(v.tensor_tensor(out=sgc[:], in0=s1c[:, cb * 512:(cb + 1) * 512],
                                           in1=g1c[:, cb * 512:(cb + 1) * 512],
                                           op=ALU.mult))
                v_sg = vq.n

                for w in range(4):
                    @block.tensor
                    def _(t, w=w, v_sg=v_sg):
                        t.wait_ge(vq.h, v_sg)
                        t.wait_ge(vq.h, vq.n)
                        mm.inc(_tmm(t, pp[:, w * 512:w * 512 + 128],
                                           sgc[:, w * 128:(w + 1) * 128], identf[:]))
                    m_tp = mm.n

                    @block.vector
                    def _(v, tg=tg, b=b, dc=dc, w=w, m_tp=m_tp):
                        v.wait_ge(mm.h, m_tp)
                        vq.wait(v)
                        dst = xacc[:, b * 4 * D + w * D + dc * 128:
                                   b * 4 * D + w * D + (dc + 1) * 128]
                        vq.inc(v.tensor_tensor(out=dst, in0=dst,
                                               in1=pp[:, w * 512:w * 512 + 128],
                                               op=ALU.add))
                dc_v[dc] = vq.n
            tg_v[tg] = vq.n

            @block.sync
            def _(sy, tg=tg, b=b):
                sy.wait_ge(vq.h, tg_v[tg])
                sy.dma_start(out=y.rearrange("(w p) d -> p w d", p=128)
                             [:, 4 * tg:4 * tg + 4, :],
                             in_=xacc[:, b * 4 * D:(b + 1) * 4 * D]
                             .rearrange("p (w d) -> p w d", w=4)).then_inc(stq.h, 16)
            stq.n += 16
            tg_st[tg] = stq.n

        @block.sync
        def _(sy):
            sy.wait_ge(stq.h, stq.n)
    return nc



def build_fused():
    nc = bass.Bass("TRN2", target_bir_lowering=False)
    EI = "ExternalInput"
    x = nc.dram_tensor("x", [T, D], F32, kind=EI).ap()
    Wq = nc.dram_tensor("Wq", [D, D], F32, kind=EI).ap()
    Wk = nc.dram_tensor("Wk", [D, 512], F32, kind=EI).ap()
    Wv = nc.dram_tensor("Wv", [D, 512], F32, kind=EI).ap()
    Wo = nc.dram_tensor("Wo", [D, D], F32, kind=EI).ap()
    bo = nc.dram_tensor("bo", [8, 128], F32, kind=EI).ap()
    Wr = nc.dram_tensor("Wr", [D, 8], F32, kind=EI).ap()
    br = nc.dram_tensor("br", [1, 8], F32, kind=EI).ap()
    W1m = nc.dram_tensor("W1m", [8, D, 4096], F32, kind=EI).ap()
    b1m = nc.dram_tensor("b1m", [8, 4096], F32, kind=EI).ap()
    W2m = nc.dram_tensor("W2m", [8, 4096, D], F32, kind=EI).ap()
    b2m = nc.dram_tensor("b2m", [8, D], F32, kind=EI).ap()
    B_w = nc.dram_tensor("B_w", [128, D], F32, kind=EI).ap()
    C_w = nc.dram_tensor("C_w", [D, 128], F32, kind=EI).ap()
    logA = nc.dram_tensor("logA", [1, 128], F32, kind=EI).ap()
    D_p = nc.dram_tensor("D_p", [8, 128], F32, kind=EI).ap()
    Wqkv = nc.dram_tensor("Wqkv", [6, D, 3072], F32, kind=EI).ap()
    q_sc = nc.dram_tensor("q_sc", [6, 64], F32, kind=EI).ap()
    k_sc = nc.dram_tensor("k_sc", [6, 64], F32, kind=EI).ap()
    Wo1 = nc.dram_tensor("Wo1", [6, D, D], F32, kind=EI).ap()
    ln_g = nc.dram_tensor("ln_g", [6, 8, 128], F32, kind=EI).ap()
    ln_b = nc.dram_tensor("ln_b", [6, 8, 128], F32, kind=EI).ap()
    W1l = nc.dram_tensor("W1l", [6, D, 5460], F32, kind=EI).ap()
    b1l = nc.dram_tensor("b1l", [6, 5460], F32, kind=EI).ap()
    W2l = nc.dram_tensor("W2l", [6, 2730, D], F32, kind=EI).ap()
    b2l = nc.dram_tensor("b2l", [6, 8, 128], F32, kind=EI).ap()
    y = nc.dram_tensor("y", [T, D], F32, kind="ExternalOutput").ap()
    xn0f = nc.dram_tensor("xn0f", [D, T], BF16, kind="Internal").ap()
    s0f = nc.dram_tensor("s0f", [D, T], F32, kind="Internal").ap()
    x1s = nc.dram_tensor("x1s", [T, D], F32, kind="Internal").ap()
    xn1f = nc.dram_tensor("xn1f", [D, T], F32, kind="Internal").ap()
    s1f = nc.dram_tensor("s1f", [D, T], F32, kind="Internal").ap()
    g1f = nc.dram_tensor("g1f", [D, T], F32, kind="Internal").ap()
    cst = declare_consts(nc, list(CONST_SPECS.keys()))
    with nc.Block() as block:
        sems = (SemC(nc, "ld"), SemC(nc, "vq"), SemC(nc, "sc"),
                SemC(nc, "mm"), SemC(nc, "stq"))
        env = FusedEnv(nc, block, sems, cst)
        build_ln0(env, (x, xn0f))
        env.barrier()
        build_l0attn(env, (xn0f, Wq, Wk, Wv, Wo, bo, s0f))
        env.barrier()
        build_l0moe(env, (xn0f, x, s0f, Wr, br, W1m, b1m, W2m, b2m, x1s))
        env.barrier()
        build_lnssm(env=env, io=(x1s, B_w, C_w, logA, D_p, xn1f, s1f))
        env.barrier()
        build_subt(env=env, io=(xn1f, Wqkv, q_sc, k_sc, Wo1, ln_g, ln_b,
                                W1l, b1l, W2l, b2l, g1f))
        env.barrier()
        build_fin(env=env, io=(x1s, s1f, g1f, y))
    return nc



_NC_CACHE = {}


def _get_nc(name, builder):
    if name not in _NC_CACHE:
        _NC_CACHE[name] = builder()
    return _NC_CACHE[name]


def kernel(**inputs):
    consts = host_consts()
    cmap = {"cst_" + k: _np_c(v) for k, v in consts.items()}
    x_full = _f32(inputs["L0_attn_x"])
    base = dict(cmap)
    base.update({
        "Wq": _f32(inputs["L0_attn_Wq"]), "Wk": _f32(inputs["L0_attn_Wk"]),
        "Wv": _f32(inputs["L0_attn_Wv"]), "Wo": _f32(inputs["L0_attn_Wo"]),
        "bo": _f32(inputs["L0_attn_bo"]).reshape(8, 128),
        "Wr": _f32(inputs["L0_ffn_Wr"]),
        "br": _f32(inputs["L0_ffn_br"]).reshape(1, 8),
        "W1m": _f32(inputs["L0_ffn_W1"]), "b1m": _f32(inputs["L0_ffn_b1"]),
        "W2m": _f32(inputs["L0_ffn_W2"]), "b2m": _f32(inputs["L0_ffn_b2"]),
        "B_w": _f32(inputs["L1_attn_B_w"]), "C_w": _f32(inputs["L1_attn_C_w"]),
        "logA": _f32(inputs["L1_attn_log_A"]).reshape(1, 128),
        "D_p": _f32(inputs["L1_attn_D_param"]).reshape(8, 128),
        "Wqkv": _f32(inputs["L1_ffn_Wqkv"]), "q_sc": _f32(inputs["L1_ffn_q_scale"]),
        "k_sc": _f32(inputs["L1_ffn_k_scale"]), "Wo1": _f32(inputs["L1_ffn_Wo"]),
        "ln_g": _f32(inputs["L1_ffn_ln_g"]).reshape(6, 8, 128),
        "ln_b": _f32(inputs["L1_ffn_ln_b"]).reshape(6, 8, 128),
        "W1l": _f32(inputs["L1_ffn_W1"]), "b1l": _f32(inputs["L1_ffn_b1"]),
        "W2l": _f32(inputs["L1_ffn_W2"]),
        "b2l": _f32(inputs["L1_ffn_b2"]).reshape(6, 8, 128),
    })
    nc = _get_nc("fused", build_fused)
    maps = [dict(base, x=np.ascontiguousarray(x_full[c])) for c in range(NCORE)]
    res = run_bass_kernel_spmd(nc, maps, core_ids=list(range(NCORE)))
    return np.stack([np.asarray(res.results[c]["y"]) for c in range(NCORE)])


def kernel_multi(**inputs):
    consts = host_consts()
    cmap = {"cst_" + k: _np_c(v) for k, v in consts.items()}

    def pick(m, names):
        return {k: m["cst_" + k] for k in names}

    x_full = _f32(inputs["L0_attn_x"])
    outs = np.empty((NCORE, T, D), np.float32)

    nc1 = _get_nc("ln0", build_ln0)
    maps1 = [{"x": np.ascontiguousarray(x_full[c]), "cst_ident": cmap["cst_ident"]}
             for c in range(NCORE)]
    r1 = run_bass_kernel_spmd(nc1, maps1, core_ids=list(range(NCORE)))
    xn0f = [r1.results[c]["xn0f"] for c in range(NCORE)]

    nc2 = _get_nc("l0attn", build_l0attn)
    w2 = {"Wq": _f32(inputs["L0_attn_Wq"]), "Wk": _f32(inputs["L0_attn_Wk"]),
          "Wv": _f32(inputs["L0_attn_Wv"]), "Wo": _f32(inputs["L0_attn_Wo"]),
          "bo": _f32(inputs["L0_attn_bo"]).reshape(8, 128)}
    c2 = {k: cmap[k] for k in ("cst_identf", "cst_c0q", "cst_s0q", "cst_c0k",
                               "cst_s0k", "cst_R0", "cst_ones_64x1",
                               "cst_ones_1x128", "cst_ones_1x64")}
    maps2 = [dict(c2, xn0f=xn0f[c], **w2) for c in range(NCORE)]
    r2 = run_bass_kernel_spmd(nc2, maps2, core_ids=list(range(NCORE)))
    s0f = [r2.results[c]["s0f"] for c in range(NCORE)]

    nc3 = _get_nc("l0moe", build_l0moe)
    w3 = {"Wr": _f32(inputs["L0_ffn_Wr"]),
          "br": _f32(inputs["L0_ffn_br"]).reshape(1, 8),
          "W1m": _f32(inputs["L0_ffn_W1"]), "b1m": _f32(inputs["L0_ffn_b1"]),
          "W2m": _f32(inputs["L0_ffn_W2"]), "b2m": _f32(inputs["L0_ffn_b2"])}
    c3 = {k: cmap[k] for k in ("cst_ident", "cst_identf", "cst_ones_1x128",
                               "cst_iota_w1", "cst_iota_w2", "cst_iota8")}
    maps3 = [dict(c3, xn0f=xn0f[c], x=np.ascontiguousarray(x_full[c]),
                  s0f=s0f[c], **w3) for c in range(NCORE)]
    r3 = run_bass_kernel_spmd(nc3, maps3, core_ids=list(range(NCORE)))
    x1 = [r3.results[c]["x1"] for c in range(NCORE)]

    nc4 = _get_nc("lnssm", build_lnssm)
    w4 = {"B_w": _f32(inputs["L1_attn_B_w"]), "C_w": _f32(inputs["L1_attn_C_w"]),
          "logA": _f32(inputs["L1_attn_log_A"]).reshape(1, 128),
          "D_p": _f32(inputs["L1_attn_D_param"]).reshape(8, 128)}
    c4 = {k: cmap[k] for k in ("cst_ident", "cst_identf")}
    maps4 = [dict(c4, x1=x1[c], **w4) for c in range(NCORE)]
    r4 = run_bass_kernel_spmd(nc4, maps4, core_ids=list(range(NCORE)))
    xn1f = [r4.results[c]["xn1f"] for c in range(NCORE)]
    s1f = [r4.results[c]["s1f"] for c in range(NCORE)]

    nc5 = _get_nc("subt", build_subt)
    w5 = {"Wqkv": _f32(inputs["L1_ffn_Wqkv"]), "q_sc": _f32(inputs["L1_ffn_q_scale"]),
          "k_sc": _f32(inputs["L1_ffn_k_scale"]), "Wo1": _f32(inputs["L1_ffn_Wo"]),
          "ln_g": _f32(inputs["L1_ffn_ln_g"]).reshape(6, 8, 128),
          "ln_b": _f32(inputs["L1_ffn_ln_b"]).reshape(6, 8, 128),
          "W1l": _f32(inputs["L1_ffn_W1"]), "b1l": _f32(inputs["L1_ffn_b1"]),
          "W2l": _f32(inputs["L1_ffn_W2"]),
          "b2l": _f32(inputs["L1_ffn_b2"]).reshape(6, 8, 128)}
    c5 = {k: cmap[k] for k in ("cst_ident", "cst_identf", "cst_c1q", "cst_s1q",
                               "cst_c1k", "cst_s1k", "cst_R1", "cst_S2", "cst_S2T",
                               "cst_ID2", "cst_Bband", "cst_ones_1x128",
                               "cst_ones_128x1")}
    maps5 = [dict(c5, xn1f=xn1f[c], **w5) for c in range(NCORE)]
    r5 = run_bass_kernel_spmd(nc5, maps5, core_ids=list(range(NCORE)))
    g1f = [r5.results[c]["g1f"] for c in range(NCORE)]

    nc6 = _get_nc("fin", build_fin)
    maps6 = [{"cst_identf": cmap["cst_identf"], "x1": x1[c], "s1f": s1f[c],
              "g1f": g1f[c]} for c in range(NCORE)]
    r6 = run_bass_kernel_spmd(nc6, maps6, core_ids=list(range(NCORE)))
    for c in range(NCORE):
        outs[c] = np.asarray(r6.results[c]["y"])
    return outs


def _np_c(v):
    return np.ascontiguousarray(v)


# version 05
# speedup vs baseline: 1.5582x; 1.5582x over previous
import numpy as np
import ml_dtypes
from contextlib import ExitStack
import concourse.bass as bass
import concourse.mybir as mybir
from concourse.bass_utils import run_bass_kernel_spmd

F32 = mybir.dt.float32
BF16 = mybir.dt.bfloat16
U32 = mybir.dt.uint32
AF = mybir.ActivationFunctionType
ALU = mybir.AluOpType
AX = mybir.AxisListType

T = 4096
D = 1024
NCORE = 8
NT = T // 128
ND = D // 128
TG = 512
NTG = T // TG

bfdt = ml_dtypes.bfloat16


def _bf(x):
    return np.ascontiguousarray(np.asarray(x, dtype=bfdt))


def _f32(x):
    return np.ascontiguousarray(np.asarray(x, dtype=np.float32))


class SemC:
    def __init__(self, nc, name):
        self.h = nc.semaphore(name).__enter__()
        self.n = 0

    def inc(self, instr, k=1):
        instr.then_inc(self.h, k)
        self.n += k
        return self.n

    def wait(self, eng, v=None):
        eng.wait_ge(self.h, self.n if v is None else v)



def host_consts():
    c = {}
    ident = np.eye(128, dtype=np.float32)
    c["ident"] = _bf(ident)
    c["identf"] = _f32(ident)
    ROT = 32
    inv_freq = 1.0 / (10000.0 ** (np.arange(0, ROT, 2, dtype=np.float64) / ROT))
    t_ar = np.arange(T, dtype=np.float64)
    freqs = t_ar[:, None] * inv_freq[None, :]
    cosr = np.repeat(np.cos(freqs), 2, axis=-1)
    sinr = np.repeat(np.sin(freqs), 2, axis=-1)
    SCALE = 64.0 ** (-0.5)
    cq = np.ones((64, T)); sq_ = np.zeros((64, T))
    cq[:32, :] = cosr.T; sq_[:32, :] = sinr.T
    c["c0q"] = _bf(cq * SCALE); c["s0q"] = _bf(sq_ * SCALE)
    c["c0k"] = _bf(cq); c["s0k"] = _bf(sq_)
    R0 = np.zeros((64, 64), dtype=np.float32)
    for i in range(16):
        R0[2 * i + 1, 2 * i] = -1.0
        R0[2 * i, 2 * i + 1] = 1.0
    c["R0"] = _bf(R0)
    DH = 64
    inv_f1 = 1.0 / (10000.0 ** (np.arange(0, DH, 2, dtype=np.float64) / DH))
    fr1 = t_ar[:, None] * inv_f1[None, :]
    fr1 = np.concatenate([fr1, fr1], axis=-1)
    base_scale = (np.arange(0, DH, 2, dtype=np.float64) + 0.4 * DH) / (1.4 * DH)
    power = (t_ar - T // 2) / (256 // 2)
    scal = base_scale[None, :] ** power[:, None]
    scal = np.concatenate([scal, scal], axis=-1)
    cos1 = np.cos(fr1); sin1 = np.sin(fr1)
    QKS = 8.0
    csq = (cos1 * scal * QKS).T; ssq = (sin1 * scal * QKS).T
    csk = (cos1 / scal).T; ssk = (sin1 / scal).T
    c["c1q"] = _bf(np.concatenate([csq, csq], axis=0))
    c["s1q"] = _bf(np.concatenate([ssq, ssq], axis=0))
    c["c1k"] = _bf(np.concatenate([csk, csk], axis=0))
    c["s1k"] = _bf(np.concatenate([ssk, ssk], axis=0))
    R1 = np.zeros((64, 64), dtype=np.float32)
    for i in range(32):
        R1[i + 32, i] = -1.0
        R1[i, i + 32] = 1.0
    R1s = np.zeros((128, 128), dtype=np.float32)
    R1s[:64, :64] = R1; R1s[64:, 64:] = R1
    c["R1"] = _bf(R1s)
    ii = np.arange(128)
    c["mLOW"] = _f32((ii[:, None] >= ii[None, :]).astype(np.float32))
    c["mUP"] = _f32((ii[:, None] <= ii[None, :]).astype(np.float32))
    c["ones_1x128"] = _bf(np.ones((1, 128), np.float32))
    c["ones_64x1"] = _bf(np.ones((64, 1), np.float32))
    c["ones_128x1"] = _bf(np.ones((128, 1), np.float32))
    c["ones_1x64"] = _bf(np.ones((1, 64), np.float32))
    S2 = np.zeros((2, 128), np.float32)
    S2[0, :64] = 1.0; S2[1, 64:] = 1.0
    c["S2"] = _bf(S2)
    iw1 = np.zeros((128, 16), np.float32)
    for dc in range(8):
        for hh in range(2):
            iw1[:, dc * 2 + hh] = (dc * 128 + ii) * 2 + hh
    c["iota_w1"] = _f32(iw1)
    iw2 = np.zeros((128, 32), np.float32)
    for fc in range(32):
        iw2[:, fc] = fc * 128 + ii
    c["iota_w2"] = _f32(iw2)
    c["iota8"] = _f32(np.arange(8, dtype=np.float32).reshape(1, 8))
    S2T = np.zeros((128, 2), np.float32)
    S2T[:64, 0] = 1.0; S2T[64:, 1] = 1.0
    c["S2T"] = _bf(S2T)
    ID2 = np.zeros((64, 128), np.float32)
    for p in range(128):
        ID2[p % 64, p] = 1.0
    c["ID2"] = _bf(ID2)
    jj = np.arange(384)
    rr_ = np.arange(128)
    c["Bband"] = _f32(((jj[None, :] >= rr_[:, None]) &
                       (jj[None, :] <= rr_[:, None] + 256)).astype(np.float32))
    return c


CONST_SPECS = {
    "ident": ([128, 128], BF16), "identf": ([128, 128], F32),
    "c0q": ([64, T], BF16), "s0q": ([64, T], BF16),
    "c0k": ([64, T], BF16), "s0k": ([64, T], BF16), "R0": ([64, 64], BF16),
    "c1q": ([128, T], BF16), "s1q": ([128, T], BF16), "c1k": ([128, T], BF16),
    "s1k": ([128, T], BF16), "R1": ([128, 128], BF16),
    "mLOW": ([128, 128], F32), "mUP": ([128, 128], F32),
    "ones_1x128": ([1, 128], BF16), "ones_64x1": ([64, 1], BF16),
    "ones_128x1": ([128, 1], BF16), "ones_1x64": ([1, 64], BF16), "S2": ([2, 128], BF16),
    "iota_w1": ([128, 16], F32), "iota_w2": ([128, 32], F32), "iota8": ([1, 8], F32),
    "S2T": ([128, 2], BF16), "ID2": ([64, 128], BF16), "Bband": ([128, 384], F32),
}


class FusedEnv:
    def __init__(self, nc, block, sems, cst):
        self.nc = nc
        self.block = block
        self.sems = sems
        self.cst = cst
        self.pref = ""

    def barrier(self):
        ld, vq, sc, mm, stq = self.sems
        vals = (mm.n, vq.n, sc.n, stq.n, ld.n)

        def _bar(e, vals=vals):
            e.wait_ge(mm.h, vals[0])
            e.wait_ge(vq.h, vals[1])
            e.wait_ge(sc.h, vals[2])
            e.wait_ge(stq.h, vals[3])
            e.wait_ge(ld.h, vals[4])
        self.block.tensor(_bar)
        self.block.vector(_bar)
        self.block.scalar(_bar)
        self.block.gpsimd(_bar)
        self.block.sync(_bar)


def _tmm(t, out, in_, ident):
    return t.matmul(out, in_, ident, start=True, stop=True)


def declare_consts(nc, names):
    return {n: nc.dram_tensor("cst_" + n, CONST_SPECS[n][0], CONST_SPECS[n][1],
                              kind="ExternalInput").ap() for n in names}



def build_ln0(env=None, io=None):
    if env is None:
        nc = bass.Bass("TRN2", target_bir_lowering=False)
        x = nc.dram_tensor("x", [T, D], F32, kind="ExternalInput").ap()
        xn0f = nc.dram_tensor("xn0f", [D, T], BF16, kind="ExternalOutput").ap()
        cst = declare_consts(nc, ["ident"])
    else:
        nc = env.nc
        cst = env.cst
        x, xn0f = io
    P_ = "A" if env else ""
    with ExitStack() as es:
        ident = es.enter_context(nc.sbuf_tensor(P_ + "ident", [128, 128], BF16))
        xt = es.enter_context(nc.sbuf_tensor(P_ + "xt", [128, 2 * D], F32))
        xsq = es.enter_context(nc.sbuf_tensor(P_ + "xsq", [128, D], F32))
        st0 = es.enter_context(nc.sbuf_tensor(P_ + "st0", [128, 8], F32))
        xnt = es.enter_context(nc.sbuf_tensor(P_ + "xnt", [128, 2 * D], BF16))
        tev = es.enter_context(nc.sbuf_tensor(P_ + "tev", [128, 4 * 128], BF16))
        pt = es.enter_context(nc.psum_tensor(P_ + "pt", [128, 2 * 512], F32))
        if env is None:
            block = es.enter_context(nc.Block())
            ld = SemC(nc, "ld"); vq = SemC(nc, "vq"); sc = SemC(nc, "sc")
            stq = SemC(nc, "stq")
        else:
            block = env.block
            ld, vq, sc, _mm, stq = env.sems
        tp = SemC(nc, P_ + "tp"); ev = SemC(nc, P_ + "ev")

        @block.sync
        def _(sy):
            sy.dma_start(out=ident[:], in_=cst["ident"]).then_inc(ld.h, 16)
        ld.n += 16
        LDB = ld.n

        v_end = [0] * NT
        s_end = [0] * NT
        ev_at = {}
        st_at = {}
        for tt in range(NT):
            bo = (tt % 2) * D

            @block.sync
            def _(sy, tt=tt, bo=bo):
                if tt >= 2:
                    sy.wait_ge(vq.h, v_end[tt - 2])
                    sy.wait_ge(sc.h, s_end[tt - 2])
                sy.dma_start(out=xt[:, bo:bo + D],
                             in_=x[tt * 128:(tt + 1) * 128, :]).then_inc(ld.h, 16)
            ld.n += 16

            @block.vector
            def _(v, tt=tt, bo=bo):
                v.wait_ge(ld.h, LDB + 16 * (tt + 1))
                vq.inc(v.reduce_sum(st0[:, 0:1], xt[:, bo:bo + D], axis=AX.X))

            @block.scalar
            def _(s, tt=tt, bo=bo):
                s.wait_ge(ld.h, LDB + 16 * (tt + 1))
                if tt >= 1:
                    s.wait_ge(vq.h, v_end[tt - 1])
                sc.inc(s.activation(xsq[:], xt[:, bo:bo + D], AF.Square,
                                    accum_out=st0[:, 2:3]))
            s_end[tt] = sc.n

            @block.vector
            def _(v, tt=tt, bo=bo):
                vq.wait(v)
                vq.inc(v.tensor_scalar(out=st0[:, 1:2], in0=st0[:, 0:1],
                                       scalar1=-1.0 / D, scalar2=None, op0=ALU.mult))
                vq.wait(v)
                vq.inc(v.tensor_tensor(out=st0[:, 3:4], in0=st0[:, 1:2],
                                       in1=st0[:, 1:2], op=ALU.mult))
                sc.wait(v)
                vq.wait(v)
                vq.inc(v.tensor_scalar(out=st0[:, 4:5], in0=st0[:, 2:3],
                                       scalar1=1.0 / D, scalar2=st0[:, 3:4],
                                       op0=ALU.mult, op1=ALU.subtract))
                vq.wait(v)
                vq.inc(v.tensor_scalar(out=st0[:, 5:6], in0=st0[:, 4:5],
                                       scalar1=1e-5, scalar2=None, op0=ALU.add))

            @block.scalar
            def _(s, tt=tt):
                vq.wait(s)
                sc.inc(s.activation(st0[:, 6:7], st0[:, 5:6], AF.Ln))
                sc.inc(s.activation(st0[:, 7:8], st0[:, 6:7], AF.Exp, scale=-0.5))

            @block.vector
            def _(v, tt=tt, bo=bo):
                sc.wait(v)
                vq.inc(v.tensor_scalar(out=xnt[:, bo:bo + D], in0=xt[:, bo:bo + D],
                                       scalar1=st0[:, 1:2], scalar2=st0[:, 7:8],
                                       op0=ALU.add, op1=ALU.mult))
            v_end[tt] = vq.n

            for dc in range(ND):
                k = tt * ND + dc
                pb = k % 2
                eb = k % 4

                @block.tensor
                def _(t, tt=tt, dc=dc, pb=pb, bo=bo, k=k):
                    t.wait_ge(vq.h, v_end[tt])
                    if k >= 2:
                        t.wait_ge(ev.h, ev_at[k - 2])
                    tp.inc(_tmm(t, pt[:, pb * 512:pb * 512 + 128],
                                xnt[:, bo + dc * 128:bo + (dc + 1) * 128], ident[:]))

                @block.scalar
                def _(s, pb=pb, eb=eb, k=k):
                    tp.wait(s)
                    if k >= 4:
                        s.wait_ge(stq.h, st_at[k - 4])
                    ev.inc(s.copy(tev[:, eb * 128:(eb + 1) * 128],
                                  pt[:, pb * 512:pb * 512 + 128]))
                ev_at[k] = ev.n

                @block.sync
                def _(sy, tt=tt, dc=dc, eb=eb):
                    ev.wait(sy)
                    sy.dma_start(out=xn0f[dc * 128:(dc + 1) * 128, tt * 128:(tt + 1) * 128],
                                 in_=tev[:, eb * 128:(eb + 1) * 128]).then_inc(stq.h, 16)
                stq.n += 16
                st_at[k] = stq.n

        @block.sync
        def _(sy):
            sy.wait_ge(stq.h, stq.n)
    return nc



def build_l0attn(env=None, io=None):
    if env is None:
        nc = bass.Bass("TRN2", target_bir_lowering=False)
        xn0f = nc.dram_tensor("xn0f", [D, T], BF16, kind="ExternalInput").ap()
        Wq = nc.dram_tensor("Wq", [D, D], F32, kind="ExternalInput").ap()
        Wk = nc.dram_tensor("Wk", [D, 512], F32, kind="ExternalInput").ap()
        Wv = nc.dram_tensor("Wv", [D, 512], F32, kind="ExternalInput").ap()
        Wo = nc.dram_tensor("Wo", [D, D], F32, kind="ExternalInput").ap()
        bo = nc.dram_tensor("bo", [8, 128], F32, kind="ExternalInput").ap()
        s0f = nc.dram_tensor("s0f", [D, T], F32, kind="ExternalOutput").ap()
        cst = declare_consts(nc, ["identf", "c0q", "s0q", "c0k", "s0k", "R0",
                                  "ones_64x1", "ones_1x128", "ones_1x64"])
    else:
        nc = env.nc
        cst = env.cst
        xn0f, Wq, Wk, Wv, Wo, bo, s0f = io
    P_ = "B" if env else ""
    oat = nc.dram_tensor(P_ + "oat", [D, T], BF16, kind="Internal").ap()
    kscr = nc.dram_tensor(P_ + "kscr", [512, T], BF16, kind="Internal").ap()
    NH, NKV = 16, 8
    with ExitStack() as es:
        def mk_sb(stack):
            def sb(name, shape, dt):
                return stack.enter_context(nc.sbuf_tensor(P_ + name, shape, dt))
            return sb

        sb = mk_sb(es)
        xn = sb("xn", [128, ND * T], BF16)
        cqt = sb("cq", [64, T], BF16)
        sqt = sb("sq", [64, T], BF16)
        ckt = sb("ck", [64, T], BF16)
        skt = sb("sk", [64, T], BF16)
        R0s = sb("R0s", [64, 64], BF16)
        identf = sb("identf", [128, 128], F32)
        ones64 = sb("ones64", [64, 1], BF16)
        ones1x128 = sb("ones1x128", [1, 128], BF16)
        ones1x64 = sb("ones1x64", [1, 64], BF16)
        vaug = sb("vaug", [128, NT * 520], BF16)
        mk2_all = sb("mk2_all", [1, NKV], F32)
        macc = sb("macc", [1, 4], F32)
        chbf = sb("chbf", [1, 1], BF16)
        bias_h = sb("bias_h", [128, 1], F32)
        rr = sb("rr", [1, 512], BF16)
        rrl = sb("rrl", [1, 512], F32)
        rb = sb("rb", [64, 2 * 512], BF16)
        oh = sb("oh", [64, 2 * 512], BF16)
        bo_raw = sb("bo_raw", [8, 128], F32)
        bo_t = sb("bo_t", [128, 8], F32)
        pp = es.enter_context(nc.psum_tensor(P_ + "pp", [128, 8 * 512], F32))
        if env is None:
            block = es.enter_context(nc.Block())
            ld = SemC(nc, "ld"); vq = SemC(nc, "vq"); sc = SemC(nc, "sc")
            mm = SemC(nc, "mm"); stq = SemC(nc, "stq")
        else:
            block = env.block
            ld, vq, sc, mm, stq = env.sems

        es23 = ExitStack()
        sb23 = mk_sb(es23)
        kcur = sb23("kcur", [64, T], BF16)
        qf = sb23("qf", [64, T], BF16)
        raw = sb23("raw", [64, 2 * 512], BF16)
        rt1 = sb23("rt1", [64, 2 * 512], BF16)
        rt2 = sb23("rt2", [64, 2 * 512], BF16)
        psb = sb23("psb", [128, 4 * 512], BF16)

        es_wv = ExitStack()
        wv_sb = mk_sb(es_wv)("wv_sb", [128, ND * 512], BF16)

        def bank(i, lo=0, hi=512):
            return pp[:, i * 512 + lo:i * 512 + hi]

        @block.sync
        def _(sy):
            for nm, dst in (("c0q", cqt), ("s0q", sqt), ("c0k", ckt), ("s0k", skt),
                            ("R0", R0s), ("ones_64x1", ones64), ("identf", identf),
                            ("ones_1x128", ones1x128), ("ones_1x64", ones1x64)):
                sy.dma_start(out=dst[:], in_=cst[nm]).then_inc(ld.h, 16)
            sy.dma_start(out=xn[:].rearrange("p (c t) -> p c t", c=ND),
                         in_=xn0f.rearrange("(c p) t -> p c t", p=128)).then_inc(ld.h, 16)
            sy.dma_start(out=bo_raw[:], in_=bo).then_inc(ld.h, 16)
        ld.n += 16 * 11

        @block.gpsimd
        def _(g):
            for i in range(ND):
                g.dma_start(out=wv_sb[:, i * 512:(i + 1) * 512],
                            in_=Wv[i * 128:(i + 1) * 128, :]).then_inc(ld.h, 16)
        ld.n += 16 * ND
        LDALL = ld.n

        @block.tensor
        def _(t):
            t.wait_ge(ld.h, LDALL)
            mm.inc(t.matmul(bank(7, 8, 16), bo_raw[:], identf[0:8, 0:8],
                            start=True, stop=True))
        mm_bo = mm.n

        @block.vector
        def _(v):
            v.wait_ge(mm.h, mm_bo)
            vq.inc(v.tensor_copy(bo_t[:], bank(7, 8, 16)))
            vq.inc(v.memset(vaug[:], 0.0))
        v_init = vq.n

        B1S = sc.n
        sc_v = {tt: B1S + tt + 1 for tt in range(NT)}
        B1M = mm.n
        mm_v1 = {tt: B1M + 8 * (tt + 1) for tt in range(NT)}

        @block.tensor
        def _(t):
            t.wait_ge(ld.h, LDALL)
            for tt in range(NT):
                if tt >= 2:
                    t.wait_ge(sc.h, sc_v[tt - 2])
                for dc in range(ND):
                    mm.inc(t.matmul(bank(tt % 2),
                                    xn[:, dc * T + tt * 128:dc * T + (tt + 1) * 128],
                                    wv_sb[:, dc * 512:(dc + 1) * 512],
                                    start=(dc == 0), stop=(dc == ND - 1)))

        @block.scalar
        def _(s):
            s.wait_ge(vq.h, v_init)
            for tt in range(NT):
                s.wait_ge(mm.h, mm_v1[tt])
                dst = vaug[:, tt * 520:(tt + 1) * 520].rearrange("p (h o) -> p h o", h=8)[:, :, 0:64]
                srcp = bank(tt % 2).rearrange("p (h o) -> p h o", h=8)
                sc.inc(s.copy(dst, srcp))
            assert sc.n == sc_v[NT - 1]

        @block.vector
        def _(v):
            sc.wait(v)
            vq.inc(v.memset(
                vaug[:].rearrange("p (t o) -> p t o", t=NT * 8)[:, :, 64:65], 1.0))
        v_vaug_done = vq.n

        def proj_rope_head(w_sb, stride, off, ct, st_, dst_sl, wld_v):
            rope_v = {}
            raw_ev = {}
            for tg in range(NTG):
                b = tg % 2

                @block.tensor
                def _(t, tg=tg, b=b):
                    t.wait_ge(ld.h, wld_v)
                    t.wait_ge(vq.h, vq.n)
                    if tg >= 2:
                        t.wait_ge(vq.h, rope_v[tg - 2])
                        t.wait_ge(vq.h, raw_ev[tg - 2])
                    for dc in range(ND):
                        mm.inc(t.matmul(bank(4 + b)[0:64, :],
                                        w_sb[:, dc * stride + off:dc * stride + off + 64],
                                        xn[:, dc * T + tg * TG:dc * T + (tg + 1) * TG],
                                        start=(dc == 0), stop=(dc == ND - 1)))

                @block.vector
                def _(v, tg=tg, b=b):
                    v.wait_ge(mm.h, mm.n)
                    if tg >= 2:
                        v.wait_ge(vq.h, rope_v[tg - 2])
                    vq.wait(v)
                    vq.inc(v.tensor_copy(raw[:, b * 512:(b + 1) * 512],
                                         bank(4 + b)[0:64, :]))
                raw_ev[tg] = vq.n

                @block.tensor
                def _(t, tg=tg, b=b):
                    t.wait_ge(vq.h, raw_ev[tg])
                    mm.inc(t.matmul(bank(6 + b)[0:64, :], R0s[:],
                                    raw[:, b * 512:(b + 1) * 512], start=True, stop=True))
                r_done = mm.n

                @block.vector
                def _(v, tg=tg, b=b, r_done=r_done, raw_done=raw_ev[tg]):
                    v.wait_ge(vq.h, raw_done)
                    vq.wait(v)
                    vq.inc(v.tensor_tensor(out=rt1[:, b * 512:(b + 1) * 512],
                                           in0=raw[:, b * 512:(b + 1) * 512],
                                           in1=ct[:, tg * TG:(tg + 1) * TG], op=ALU.mult))
                    v.wait_ge(mm.h, r_done)
                    vq.wait(v)
                    vq.inc(v.tensor_tensor(out=rt2[:, b * 512:(b + 1) * 512],
                                           in0=bank(6 + b)[0:64, :],
                                           in1=st_[:, tg * TG:(tg + 1) * TG], op=ALU.mult))
                    vq.wait(v)
                    vq.inc(v.tensor_tensor(
                        out=dst_sl(tg),
                        in0=rt1[:, b * 512:(b + 1) * 512],
                        in1=rt2[:, b * 512:(b + 1) * 512], op=ALU.add))
                rope_v[tg] = vq.n

        def head_norm_bound(src_fn):
            @block.vector
            def _(v):
                vq.wait(v)
                vq.inc(v.memset(macc[:, 0:1], 0.0))
            for tg in range(NTG):
                @block.vector
                def _(v, tg=tg):
                    if tg >= 1:
                        v.wait_ge(mm.h, mm.n)
                    vq.wait(v)
                    vq.inc(v.tensor_tensor(out=rt1[:, 0:512], in0=src_fn(tg),
                                           in1=src_fn(tg), op=ALU.mult))
                v_sq = vq.n

                @block.tensor
                def _(t, v_sq=v_sq):
                    t.wait_ge(vq.h, v_sq)
                    mm.inc(t.matmul(bank(7)[0:1, :], ones64[:], rt1[:, 0:512],
                                    start=True, stop=True))
                m_mm = mm.n

                @block.vector
                def _(v, m_mm=m_mm):
                    v.wait_ge(mm.h, m_mm)
                    vq.wait(v)
                    vq.inc(v.reduce_max(macc[:, 1:2], bank(7)[0:1, :], axis=AX.X))
                    vq.wait(v)
                    vq.inc(v.tensor_tensor(out=macc[:, 0:1], in0=macc[:, 0:1],
                                           in1=macc[:, 1:2], op=ALU.max))

        es_wv.close()
        es_wk = ExitStack()
        wk_sb = mk_sb(es_wk)("wk_sb", [128, ND * 512], BF16)

        @block.gpsimd
        def _(g):
            g.wait_ge(mm.h, mm.n)
            for i in range(ND):
                g.dma_start(out=wk_sb[:, i * 512:(i + 1) * 512],
                            in_=Wk[i * 128:(i + 1) * 128, :]).then_inc(ld.h, 16)
        ld.n += 16 * ND
        WKLD = ld.n

        kscr_w = {}
        for j in range(NKV):
            if j >= 1:
                @block.vector
                def _(v, j=j):
                    v.wait_ge(stq.h, kscr_w[j - 1])
            proj_rope_head(wk_sb, 512, j * 64, ckt, skt,
                           lambda tg: kcur[:, tg * TG:(tg + 1) * TG], WKLD)
            head_norm_bound(lambda tg: kcur[:, tg * TG:(tg + 1) * TG])

            @block.vector
            def _(v, j=j):
                vq.wait(v)
                vq.inc(v.tensor_copy(mk2_all[:, j:j + 1], macc[:, 0:1]))
            v_kd = vq.n

            @block.sync
            def _(sy, j=j, v_kd=v_kd):
                sy.wait_ge(vq.h, v_kd)
                sy.dma_start(out=kscr[j * 64:(j + 1) * 64, :],
                             in_=kcur[:]).then_inc(stq.h, 16)
            stq.n += 16
            kscr_w[j] = stq.n

        es_wk.close()

        es_wq = ExitStack()
        wq_sb = mk_sb(es_wq)("wq_sb", [128, ND * D], BF16)

        @block.gpsimd
        def _(g):
            g.wait_ge(mm.h, mm.n)
            for i in range(ND):
                g.dma_start(out=wq_sb[:, i * D:(i + 1) * D],
                            in_=Wq[i * 128:(i + 1) * 128, :]).then_inc(ld.h, 16)
        ld.n += 16 * ND
        WQLD = ld.n

        oat_w = {}
        for h in range(NH):
            j = h // 2
            if h % 2 == 0:
                @block.sync
                def _(sy, j=j):
                    sy.wait_ge(mm.h, mm.n)
                    sy.wait_ge(stq.h, kscr_w[NKV - 1])
                    sy.dma_start(out=kcur[:], in_=kscr[j * 64:(j + 1) * 64, :]
                                 ).then_inc(ld.h, 16)
                ld.n += 16
            KLD = ld.n
            proj_rope_head(wq_sb, D, h * 64, cqt, sqt,
                           lambda tg: qf[:, tg * TG:(tg + 1) * TG], WQLD)

            head_norm_bound(lambda tg: qf[:, tg * TG:(tg + 1) * TG])

            @block.vector
            def _(v, j=j):
                vq.wait(v)
                vq.inc(v.tensor_tensor(out=macc[:, 2:3], in0=macc[:, 0:1],
                                       in1=mk2_all[:, j:j + 1], op=ALU.mult))
            v_c2 = vq.n

            @block.scalar
            def _(s, v_c2=v_c2):
                s.wait_ge(vq.h, v_c2)
                sc.inc(s.sqrt(macc[:, 3:4], macc[:, 2:3]))
            sc_cs = sc.n

            @block.vector
            def _(v, sc_cs=sc_cs):
                v.wait_ge(sc.h, sc_cs)
                vq.wait(v)
                vq.inc(v.tensor_copy(chbf[:], macc[:, 3:4]))
            v_mq = vq.n

            @block.tensor
            def _(t, v_mq=v_mq):
                t.wait_ge(vq.h, v_mq)
                t.wait_ge(sc.h, sc.n)
                mm.inc(t.matmul(bank(7, 4, 5), ones1x128[:], chbf[:],
                                start=True, stop=True))
            mm_c2 = mm.n

            @block.scalar
            def _(s, mm_c2=mm_c2):
                s.wait_ge(mm.h, mm_c2)
                sc.inc(s.mul(bias_h[:], bank(7, 4, 5), -1.0))
            sc_bias = sc.n

            for qg in range(NTG):
                sc_base = sc.n
                sc_exp = [sc_base + kt + 1 for kt in range(NT)]

                mm_qk = [0] * NT
                mm_av = [0] * NT
                cnt = mm.n
                for kt in range(NT):
                    cnt += 1; mm_qk[kt] = cnt
                    if kt >= 1:
                        cnt += 1; mm_av[kt - 1] = cnt
                cnt += 1; mm_av[NT - 1] = cnt

                @block.tensor
                def _(t, h=h, j=j, qg=qg, sc_exp=sc_exp, sc_bias=sc_bias, KLD=KLD):
                    t.wait_ge(sc.h, sc_bias)
                    t.wait_ge(ld.h, KLD)
                    for kt in range(NT):
                        if kt >= 3:
                            t.wait_ge(sc.h, sc_exp[kt - 3])
                        mm.inc(t.matmul(bank(kt % 3),
                                        kcur[:, kt * 128:(kt + 1) * 128],
                                        qf[:, qg * TG:(qg + 1) * TG],
                                        start=True, stop=True))
                        if kt >= 1:
                            t.wait_ge(sc.h, sc_exp[kt - 1])
                            mm.inc(t.matmul(
                                bank(3)[0:65, :],
                                vaug[:, (kt - 1) * 520 + j * 65:(kt - 1) * 520 + j * 65 + 65],
                                psb[:, ((kt - 1) % 4) * 512:((kt - 1) % 4) * 512 + 512],
                                start=(kt - 1 == 0), stop=False))
                    t.wait_ge(sc.h, sc_exp[NT - 1])
                    mm.inc(t.matmul(bank(3)[0:65, :],
                                    vaug[:, (NT - 1) * 520 + j * 65:(NT - 1) * 520 + j * 65 + 65],
                                    psb[:, ((NT - 1) % 4) * 512:((NT - 1) % 4) * 512 + 512],
                                    start=False, stop=True))
                assert mm.n == cnt, (mm.n, cnt)

                @block.scalar
                def _(s, qg=qg, mm_qk=mm_qk, mm_av=mm_av):
                    for kt in range(NT):
                        s.wait_ge(mm.h, mm_qk[kt])
                        if kt >= 4:
                            s.wait_ge(mm.h, mm_av[kt - 4])
                        sc.inc(s.activation(psb[:, (kt % 4) * 512:(kt % 4) * 512 + 512],
                                            bank(kt % 3), AF.Exp, bias=bias_h[:, 0:1]))
                    s.wait_ge(mm.h, mm_av[NT - 1])
                    sc.inc(s.activation(rrl[:], bank(3)[64:65, :], AF.Ln))
                    with nc.allow_low_precision(reason="softmax denom bf16"):
                        sc.inc(s.activation(rr[:], rrl[:], AF.Exp, scale=-1.0))
                assert sc.n == sc_exp[-1] + 2, (sc.n,)
                sc_rr = sc.n

                @block.tensor
                def _(t, sc_rr=sc_rr):
                    t.wait_ge(sc.h, sc_rr)
                    mm.inc(t.matmul(bank(5)[0:64, :], ones1x64[:], rr[:],
                                    start=True, stop=True))
                mm_rb = mm.n

                @block.vector
                def _(v, qg=qg, h=h, mm_rb=mm_rb):
                    v.wait_ge(mm.h, mm_rb)
                    key = (h, qg - 2) if qg >= 2 else (h - 1, qg + NTG - 2)
                    if key in oat_w:
                        v.wait_ge(stq.h, oat_w[key])
                    vq.wait(v)
                    vq.inc(v.tensor_copy(rb[:, (qg % 2) * 512:(qg % 2) * 512 + 512],
                                         bank(5)[0:64, :]))
                sc_rb = vq.n

                @block.vector
                def _(v, qg=qg, sc_rb=sc_rb):
                    v.wait_ge(vq.h, sc_rb)
                    vq.wait(v)
                    vq.inc(v.tensor_tensor(out=oh[:, (qg % 2) * 512:(qg % 2) * 512 + 512],
                                           in0=bank(3)[0:64, :],
                                           in1=rb[:, (qg % 2) * 512:(qg % 2) * 512 + 512],
                                           op=ALU.mult))
                v_oh = vq.n

                @block.sync
                def _(sy, h=h, qg=qg, v_oh=v_oh):
                    sy.wait_ge(vq.h, v_oh)
                    sy.dma_start(out=oat[h * 64:(h + 1) * 64, qg * TG:(qg + 1) * TG],
                                 in_=oh[:, (qg % 2) * 512:(qg % 2) * 512 + 512]
                                 ).then_inc(stq.h, 16)
                stq.n += 16
                oat_w[(h, qg)] = stq.n

                @block.tensor
                def _(t, v_oh=v_oh):
                    t.wait_ge(vq.h, v_oh)
        B3_DONE = oat_w[(NH - 1, NTG - 1)]
        B3_MM = mm.n

        es_wq.close()
        es23.close()
        es4 = ExitStack()
        sb4 = mk_sb(es4)
        att = sb4("att", [128, ND * 512], BF16)
        wo_sb = sb4("wo_sb", [128, ND * D], BF16)
        s0e = sb4("s0e", [128, 2 * 512], F32)

        @block.gpsimd
        def _(g):
            g.wait_ge(mm.h, B3_MM)
            g.wait_ge(stq.h, B3_DONE)
            for i in range(ND):
                g.dma_start(out=wo_sb[:, i * D:(i + 1) * D],
                            in_=Wo[i * 128:(i + 1) * 128, :]).then_inc(ld.h, 16)
        ld.n += 16 * ND
        WOLD = ld.n

        s0_ev = {}
        s0_st = {}
        att_ld = {}
        for tg in range(NTG):
            @block.sync
            def _(sy, tg=tg):
                if tg == 0:
                    sy.wait_ge(stq.h, B3_DONE)
                sy.dma_start(out=att[:].rearrange("p (c t) -> p c t", c=ND),
                             in_=oat.rearrange("(c p) t -> p c t", p=128)
                             [:, :, tg * TG:(tg + 1) * TG]).then_inc(ld.h, 16)
            ld.n += 16
            att_ld[tg] = ld.n

            for dt in range(ND):
                k = tg * ND + dt

                @block.tensor
                def _(t, tg=tg, dt=dt, k=k):
                    t.wait_ge(ld.h, att_ld[tg])
                    if k >= 2:
                        t.wait_ge(vq.h, s0_ev[k - 2])
                    for ic in range(ND):
                        mm.inc(t.matmul(bank(k % 2),
                                        wo_sb[:, ic * D + dt * 128:ic * D + (dt + 1) * 128],
                                        att[:, ic * 512:(ic + 1) * 512],
                                        start=(ic == 0), stop=(ic == ND - 1)))

                @block.vector
                def _(v, dt=dt, k=k):
                    v.wait_ge(mm.h, mm.n)
                    if k >= 4:
                        v.wait_ge(stq.h, s0_st[k - 4])
                    vq.wait(v)
                    vq.inc(v.tensor_scalar(out=s0e[:, (k % 2) * 512:(k % 2) * 512 + 512],
                                           in0=bank(k % 2), scalar1=bo_t[:, dt:dt + 1],
                                           scalar2=None, op0=ALU.add))
                s0_ev[k] = vq.n

                @block.sync
                def _(sy, tg=tg, dt=dt, k=k):
                    sy.wait_ge(vq.h, s0_ev[k])
                    sy.dma_start(out=s0f[dt * 128:(dt + 1) * 128, tg * TG:(tg + 1) * TG],
                                 in_=s0e[:, (k % 2) * 512:(k % 2) * 512 + 512]
                                 ).then_inc(stq.h, 16)
                stq.n += 16
                s0_st[k] = stq.n

            @block.sync
            def _(sy):
                sy.wait_ge(mm.h, mm.n)

        @block.sync
        def _(sy):
            sy.wait_ge(stq.h, stq.n)
        es4.close()
    return nc



def build_l0moe(env=None, io=None):
    if env is None:
        nc = bass.Bass("TRN2", target_bir_lowering=False)
        xn0f = nc.dram_tensor("xn0f", [D, T], BF16, kind="ExternalInput").ap()
        x = nc.dram_tensor("x", [T, D], F32, kind="ExternalInput").ap()
        s0f = nc.dram_tensor("s0f", [D, T], F32, kind="ExternalInput").ap()
        Wr = nc.dram_tensor("Wr", [D, 8], F32, kind="ExternalInput").ap()
        br = nc.dram_tensor("br", [1, 8], F32, kind="ExternalInput").ap()
        W1m = nc.dram_tensor("W1m", [8, D, 4096], F32, kind="ExternalInput").ap()
        b1m = nc.dram_tensor("b1m", [8, 4096], F32, kind="ExternalInput").ap()
        W2m = nc.dram_tensor("W2m", [8, 4096, D], F32, kind="ExternalInput").ap()
        b2m = nc.dram_tensor("b2m", [8, D], F32, kind="ExternalInput").ap()
        x1 = nc.dram_tensor("x1", [T, D], F32, kind="ExternalOutput").ap()
        cst = declare_consts(nc, ["ident", "identf", "ones_1x128", "iota_w1",
                                  "iota_w2", "iota8"])
    else:
        nc = env.nc
        cst = env.cst
        xn0f, x, s0f, Wr, br, W1m, b1m, W2m, b2m, x1 = io
    P_ = "C" if env else ""
    NS = 256
    NSG = T // NS
    with ExitStack() as es:
        def sb(name, shape, dt):
            return es.enter_context(nc.sbuf_tensor(P_ + name, shape, dt))

        w1_sb = sb("w1_sb", [128, ND * 4096], BF16)
        w2_sb = sb("w2_sb", [128, 32 * D], BF16)
        xns = sb("xns", [128, 2 * ND * NS], BF16)
        s0s = sb("s0s", [128, 2 * ND * NS], F32)
        xacc = sb("xacc", [128, 4 * D], F32)
        h_sb = sb("h_sb", [128, 2 * NS], BF16)
        g_sb = sb("g_sb", [128, NS], F32)
        gs2 = sb("gs2", [128, ND * NS], F32)
        sg_sb = sb("sg_sb", [128, NS], F32)
        mean_bf = sb("mean_bf", [128, ND], BF16)
        wr_sb = sb("wr_sb", [128, ND * 8], BF16)
        br_sb = sb("br_sb", [1, 8], F32)
        io8 = sb("io8", [1, 8], F32)
        iow1 = sb("iow1", [128, 16], F32)
        iow2 = sb("iow2", [128, 32], F32)
        onesb = sb("onesb", [1, 128], BF16)
        identf = sb("identf", [128, 128], F32)
        sc8 = sb("sc8", [1, 8], F32)
        mx1 = sb("mx1", [1, 1], F32)
        ohot = sb("ohot", [1, 8], F32)
        e_val = sb("e_val", [1, 1], F32)
        e_bf = sb("e_bf", [1, 1], BF16)
        e_f = sb("e_f", [128, 1], F32)
        e_sc = sb("e_sc", [128, 2], F32)
        idx1f = sb("idx1f", [128, 16], F32)
        idx2f = sb("idx2f", [128, 32], F32)
        idx1u = sb("idx1u", [128, 16], U32)
        idx2u = sb("idx2u", [128, 32], U32)
        b1bf = sb("b1bf", [8, 4096], BF16)
        b2bf = sb("b2bf", [8, D], BF16)
        b1t = sb("b1t", [128, 32], F32)
        b2t = sb("b2t", [128, 8], F32)
        xnc = sb("xnc", [128, ND], F32)
        oh_bfr = sb("oh_bfr", [1, 8], BF16)
        oh8 = sb("oh8", [8, 1], BF16)
        identb = sb("identb", [1, 1], BF16)
        pp = es.enter_context(nc.psum_tensor(P_ + "pp", [128, 6 * 512], F32))
        ppt = es.enter_context(nc.psum_tensor(P_ + "ppt", [128, 2 * 512], F32))
        if env is None:
            block = es.enter_context(nc.Block())
            ld = SemC(nc, "ld"); vq = SemC(nc, "vq"); sc = SemC(nc, "sc")
            mm = SemC(nc, "mm"); stq = SemC(nc, "stq")
        else:
            block = env.block
            ld, vq, sc, mm, stq = env.sems
        gat = SemC(nc, P_ + "gat")

        def bank(i, lo=0, hi=512):
            return pp[:, i * 512 + lo:i * 512 + hi]

        @block.sync
        def _(sy):
            for nm, dst in (("identf", identf), ("ones_1x128", onesb),
                            ("iota_w1", iow1), ("iota_w2", iow2), ("iota8", io8)):
                sy.dma_start(out=dst[:], in_=cst[nm]).then_inc(ld.h, 16)
            sy.dma_start(out=br_sb[:], in_=br).then_inc(ld.h, 16)
            sy.dma_start(out=identb[:], in_=cst["ident"][0:1, 0:1]).then_inc(ld.h, 16)
        ld.n += 16 * 7

        @block.gpsimd
        def _(g):
            for i in range(ND):
                g.dma_start(out=wr_sb[:, i * 8:(i + 1) * 8],
                            in_=Wr[i * 128:(i + 1) * 128, :]).then_inc(ld.h, 16)
            g.dma_start(out=b1bf[:], in_=b1m).then_inc(ld.h, 16)
            g.dma_start(out=b2bf[:], in_=b2m).then_inc(ld.h, 16)
        ld.n += 16 * 10
        LD0 = ld.n

        mean_v = {}
        for dc in range(ND):
            @block.sync
            def _(sy, dc=dc):
                if dc >= 1:
                    sy.wait_ge(vq.h, mean_v[dc - 1])
                sy.dma_start(out=xns[:, 0:T],
                             in_=xn0f[dc * 128:(dc + 1) * 128, :]).then_inc(ld.h, 16)
            ld.n += 16

            @block.vector
            def _(v, dc=dc):
                v.wait_ge(ld.h, LD0 + 16 * (dc + 1))
                vq.wait(v)
                vq.inc(v.reduce_sum(xnc[:, dc:dc + 1], xns[:, 0:T], axis=AX.X))
                vq.wait(v)
                vq.inc(v.tensor_scalar(out=mean_bf[:, dc:dc + 1], in0=xnc[:, dc:dc + 1],
                                       scalar1=1.0 / T, scalar2=None, op0=ALU.mult))
            mean_v[dc] = vq.n
        MEAN_V = vq.n

        @block.tensor
        def _(t):
            t.wait_ge(vq.h, MEAN_V)
            for dc in range(ND):
                mm.inc(t.matmul(bank(0, 0, 8)[0:1, :], mean_bf[:, dc:dc + 1],
                                wr_sb[:, dc * 8:(dc + 1) * 8],
                                start=(dc == 0), stop=(dc == ND - 1)))
        mm_sc = mm.n

        @block.vector
        def _(v):
            v.wait_ge(mm.h, mm_sc)
            vq.wait(v)
            vq.inc(v.tensor_tensor(out=sc8[:], in0=bank(0, 0, 8)[0:1, :],
                                   in1=br_sb[:], op=ALU.add))
            vq.wait(v)
            vq.inc(v.reduce_max(mx1[:], sc8[:], axis=AX.X))
            vq.wait(v)
            vq.inc(v.tensor_scalar(out=ohot[:], in0=sc8[:], scalar1=mx1[:, 0:1],
                                   scalar2=None, op0=ALU.is_ge))
            vq.wait(v)
            vq.inc(v.tensor_tensor(out=ohot[:], in0=ohot[:], in1=io8[:], op=ALU.mult))
            vq.wait(v)
            vq.inc(v.reduce_sum(e_val[:], ohot[:], axis=AX.X))
            vq.wait(v)
            vq.inc(v.tensor_copy(e_bf[:], e_val[:]))
        v_e = vq.n

        @block.tensor
        def _(t):
            t.wait_ge(vq.h, v_e)
            mm.inc(t.matmul(bank(1, 0, 1), onesb[:], e_bf[:], start=True, stop=True))
        mm_e = mm.n

        @block.vector
        def _(v):
            v.wait_ge(mm.h, mm_e)
            vq.wait(v)
            vq.inc(v.tensor_copy(e_f[:], bank(1, 0, 1)))
            vq.wait(v)
            vq.inc(v.tensor_scalar(out=e_sc[:, 0:1], in0=e_f[:], scalar1=2048.0,
                                   scalar2=None, op0=ALU.mult))
            vq.wait(v)
            vq.inc(v.tensor_scalar(out=e_sc[:, 1:2], in0=e_f[:], scalar1=4096.0,
                                   scalar2=None, op0=ALU.mult))
            vq.wait(v)
            vq.inc(v.tensor_scalar(out=idx1f[:], in0=iow1[:], scalar1=e_sc[:, 0:1],
                                   scalar2=None, op0=ALU.add))
            vq.wait(v)
            vq.inc(v.tensor_scalar(out=idx2f[:], in0=iow2[:], scalar1=e_sc[:, 1:2],
                                   scalar2=None, op0=ALU.add))
            vq.wait(v)
            vq.inc(v.tensor_copy(idx1u[:], idx1f[:]))
            vq.wait(v)
            vq.inc(v.tensor_copy(idx2u[:], idx2f[:]))
        v_idx = vq.n

        w1v = W1m.rearrange("e d (h c) -> (e d h) c", c=2048)
        w2v = W2m.rearrange("e f c -> (e f) c")
        @block.gpsimd
        def _(g):
            g.wait_ge(vq.h, v_idx)
            for dc in range(ND):
                for hh in range(2):
                    g.indirect_dma_start(
                        out=w1_sb[:, dc * 4096 + hh * 2048:dc * 4096 + (hh + 1) * 2048],
                        out_offset=None, in_=w1v,
                        in_offset=bass.IndirectOffsetOnAxis(
                            ap=idx1u[:, dc * 2 + hh:dc * 2 + hh + 1], axis=0),
                    ).then_inc(gat.h, 16)
            for fc in range(32):
                g.indirect_dma_start(
                    out=w2_sb[:, fc * D:(fc + 1) * D],
                    out_offset=None, in_=w2v,
                    in_offset=bass.IndirectOffsetOnAxis(
                        ap=idx2u[:, fc:fc + 1], axis=0),
                ).then_inc(gat.h, 16)
        gat.n += 16 * 48
        GAT_ALL = gat.n

        @block.vector
        def _(v):
            vq.wait(v)
            vq.inc(v.tensor_scalar(out=ohot[:], in0=sc8[:], scalar1=mx1[:, 0:1],
                                   scalar2=None, op0=ALU.is_ge))
            vq.wait(v)
            vq.inc(v.tensor_copy(oh_bfr[:], ohot[:]))
        v_ohbf = vq.n

        @block.tensor
        def _(t):
            t.wait_ge(vq.h, v_ohbf)
            mm.inc(t.matmul(bank(1, 8, 9)[0:8, :], oh_bfr[:], identb[0:1, 0:1],
                            start=True, stop=True))
        mm_oht = mm.n

        @block.vector
        def _(v):
            v.wait_ge(mm.h, mm_oht)
            vq.wait(v)
            vq.inc(v.tensor_copy(oh8[:], bank(1, 8, 9)[0:8, :]))
        v_oh8 = vq.n

        @block.tensor
        def _(t):
            t.wait_ge(vq.h, v_oh8)
            t.wait_ge(ld.h, LD0)
            for c in range(32):
                mm.inc(t.matmul(bank(2, c, c + 1), b1bf[:, c * 128:(c + 1) * 128],
                                oh8[:], start=True, stop=True))
            for c in range(ND):
                mm.inc(t.matmul(bank(3, c, c + 1), b2bf[:, c * 128:(c + 1) * 128],
                                oh8[:], start=True, stop=True))
        mm_bias = mm.n

        @block.vector
        def _(v):
            v.wait_ge(mm.h, mm_bias)
            vq.wait(v)
            vq.inc(v.tensor_copy(b1t[:], bank(2, 0, 32)))
            vq.wait(v)
            vq.inc(v.tensor_copy(b2t[:], bank(3, 0, 8)))
        v_bias = vq.n

        xs_ld = {}
        out_st = {}
        hv_ev = {}
        sg_end_v = {}
        for sg in range(NSG):
            b = sg % 2
            t0 = sg * NS

            @block.sync
            def _(sy, sg=sg, b=b, t0=t0):
                if sg < 2:
                    sy.wait_ge(vq.h, MEAN_V)
                if sg >= 2:
                    sy.wait_ge(stq.h, out_st[sg - 2])
                    sy.wait_ge(vq.h, sg_end_v[sg - 2])
                sy.dma_start(out=xns[:, b * 2048:(b + 1) * 2048]
                             .rearrange("p (c t) -> p c t", c=ND),
                             in_=xn0f.rearrange("(c p) t -> p c t", p=128)
                             [:, :, t0:t0 + NS]).then_inc(ld.h, 16)
                sy.dma_start(out=s0s[:, b * 2048:(b + 1) * 2048]
                             .rearrange("p (c t) -> p c t", c=ND),
                             in_=s0f.rearrange("(c p) t -> p c t", p=128)
                             [:, :, t0:t0 + NS]).then_inc(ld.h, 16)
                sy.dma_start(out=xacc[:, b * 2 * D:(b + 1) * 2 * D]
                             .rearrange("p (w d) -> p w d", w=2),
                             in_=x.rearrange("(w p) d -> p w d", p=128)
                             [:, 2 * sg:2 * sg + 2, :]).then_inc(ld.h, 16)
            ld.n += 48
            xs_ld[sg] = ld.n

            CSB = sc.n
            hv = {fc: CSB + fc + 1 for fc in range(32)}
            for fc in range(32):
                hv_ev[(sg, fc)] = hv[fc]
            CMB = mm.n
            mm_h1 = {fc: CMB + 16 * fc + 8 for fc in range(32)}

            @block.tensor
            def _(t, sg=sg, b=b):
                t.wait_ge(ld.h, xs_ld[sg])
                if sg == 0:
                    t.wait_ge(gat.h, GAT_ALL)
                    t.wait_ge(vq.h, v_bias)
                for fc in range(32):
                    if fc >= 2:
                        t.wait_ge(sc.h, hv[fc - 2])
                    elif sg >= 1:
                        t.wait_ge(sc.h, hv_ev[(sg - 1, 30 + fc)])
                    for dc in range(ND):
                        mm.inc(t.matmul(bank(fc % 2, 0, 256),
                                        w1_sb[:, dc * 4096 + fc * 128:dc * 4096 + (fc + 1) * 128],
                                        xns[:, b * 2048 + dc * 256:b * 2048 + (dc + 1) * 256],
                                        start=(dc == 0), stop=(dc == ND - 1)))
                    t.wait_ge(sc.h, hv[fc])
                    for dt in range(ND):
                        mm.inc(t.matmul(bank(2 + dt // 2, (dt % 2) * 256, (dt % 2) * 256 + 256),
                                        w2_sb[:, fc * D + dt * 128:fc * D + (dt + 1) * 128],
                                        h_sb[:, (fc % 2) * 256:(fc % 2) * 256 + 256],
                                        start=(fc == 0 and dt % 2 == 0),
                                        stop=(fc == 31 and dt % 2 == 1),
                                        skip_group_check=True))

            @block.scalar
            def _(s, sg=sg):
                for fc in range(32):
                    s.wait_ge(mm.h, mm_h1[fc])
                    sc.inc(s.activation(h_sb[:, (fc % 2) * 256:(fc % 2) * 256 + 256],
                                        bank(fc % 2, 0, 256),
                                        AF.Relu, bias=b1t[:, fc:fc + 1]))
                assert sc.n == hv[31]
            mm_z = mm.n

            vg_done = {}
            prev_tp = 0
            CVB = vq.n
            vgp = {dt: CVB + 3 * dt + 1 for dt in range(ND)}
            sil = {dt: sc.n + dt + 1 for dt in range(ND)}

            @block.scalar
            def _(s, sg=sg):
                s.wait_ge(mm.h, mm_z)
                if sg >= 1:
                    s.wait_ge(vq.h, sg_end_v[sg - 1])
                for dt in range(ND):
                    if dt >= 1:
                        s.wait_ge(vq.h, vgp[dt - 1])
                    sc.inc(s.activation(gs2[:, dt * 256:(dt + 1) * 256],
                                        bank(2 + dt // 2, (dt % 2) * 256,
                                             (dt % 2) * 256 + 256),
                                        AF.Silu, bias=b2t[:, dt:dt + 1]))
                assert sc.n == sil[ND - 1]

            for dt in range(ND):
                sil_ev = sil[dt]

                @block.vector
                def _(v, sg=sg, b=b, dt=dt, sil_ev=sil_ev, prev_tp=prev_tp):
                    v.wait_ge(sc.h, sil_ev)
                    v.wait_ge(mm.h, prev_tp)
                    vq.wait(v)
                    vq.inc(v.tensor_tensor(out=sg_sb[:], in0=gs2[:, dt * 256:(dt + 1) * 256],
                                           in1=s0s[:, b * 2048 + dt * 256:b * 2048 + (dt + 1) * 256],
                                           op=ALU.mult))
                v_sg = vq.n

                for w in range(2):
                    @block.tensor
                    def _(t, w=w, v_sg=v_sg):
                        t.wait_ge(vq.h, v_sg)
                        t.wait_ge(vq.h, vq.n)
                        mm.inc(_tmm(t, ppt[:, w * 512:w * 512 + 128],
                                    sg_sb[:, w * 128:(w + 1) * 128], identf[:]))
                    m_tp = mm.n

                    @block.vector
                    def _(v, sg=sg, b=b, dt=dt, w=w, m_tp=m_tp):
                        v.wait_ge(mm.h, m_tp)
                        vq.wait(v)
                        vq.inc(v.tensor_tensor(
                            out=xacc[:, b * 2 * D + w * D + dt * 128:
                                     b * 2 * D + w * D + (dt + 1) * 128],
                            in0=xacc[:, b * 2 * D + w * D + dt * 128:
                                     b * 2 * D + w * D + (dt + 1) * 128],
                            in1=ppt[:, w * 512:w * 512 + 128], op=ALU.add))
                prev_tp = mm.n
                vg_done[(sg, dt)] = vq.n
                assert vq.n == vgp[dt] + 2, (vq.n, vgp[dt])
            sg_end_v[sg] = vg_done[(sg, ND - 1)]

            @block.sync
            def _(sy, sg=sg, b=b):
                sy.wait_ge(vq.h, vg_done[(sg, ND - 1)])
                sy.dma_start(out=x1.rearrange("(w p) d -> p w d", p=128)
                             [:, 2 * sg:2 * sg + 2, :],
                             in_=xacc[:, b * 2 * D:(b + 1) * 2 * D]
                             .rearrange("p (w d) -> p w d", w=2)).then_inc(stq.h, 16)
            stq.n += 16
            out_st[sg] = stq.n

        @block.sync
        def _(sy):
            sy.wait_ge(stq.h, stq.n)
    return nc



def build_lnssm(debug=False, env=None, io=None):
    if env is None:
        nc = bass.Bass("TRN2", target_bir_lowering=False)
        x1 = nc.dram_tensor("x1", [T, D], F32, kind="ExternalInput").ap()
        B_w = nc.dram_tensor("B_w", [128, D], F32, kind="ExternalInput").ap()
        C_w = nc.dram_tensor("C_w", [D, 128], F32, kind="ExternalInput").ap()
        logA = nc.dram_tensor("logA", [1, 128], F32, kind="ExternalInput").ap()
        D_p = nc.dram_tensor("D_p", [8, 128], F32, kind="ExternalInput").ap()
        xn1f = nc.dram_tensor("xn1f", [D, T], F32, kind="ExternalOutput").ap()
        s1f = nc.dram_tensor("s1f", [D, T], F32, kind="ExternalOutput").ap()
    else:
        nc = env.nc
        x1, B_w, C_w, logA, D_p, xn1f, s1f = io
    if debug and env is None:
        dbg_u = nc.dram_tensor("dbg_u", [128, T], F32, kind="ExternalOutput").ap()
        dbg_a = nc.dram_tensor("dbg_a", [128, T], F32, kind="ExternalOutput").ap()
        dbg_hs = nc.dram_tensor("dbg_hs", [128, T], F32, kind="ExternalOutput").ap()
        dbg_dpt = nc.dram_tensor("dbg_dpt", [128, 8], F32, kind="ExternalOutput").ap()
        dbg_cwt = nc.dram_tensor("dbg_cwt", [128, ND * 128], BF16, kind="ExternalOutput").ap()
    cst = env.cst if env else declare_consts(nc, ["ident", "identf"])
    P_ = "Dp" if env else ""
    with ExitStack() as es:
        def sb(name, shape, dt):
            return es.enter_context(nc.sbuf_tensor(P_ + name, shape, dt))

        ident = sb("ident", [128, 128], BF16)
        identf = sb("identf", [128, 128], F32)
        xt = sb("xt", [128, 2 * D], F32)
        xsq = sb("xsq", [128, D], F32)
        st0 = sb("st0", [128, 8], F32)
        xnt = sb("xnt", [128, 2 * D], F32)
        tev = sb("tev", [128, 4 * 128], F32)
        xnb = sb("xnb", [128, ND * T], BF16)
        u_sb = sb("u_sb", [128, T], F32)
        afull = sb("afull", [128, T], F32)
        hs = sb("hs", [128, T], F32)
        hs_bf = sb("hs_bf", [128, T], BF16)
        bw_raw = sb("bw_raw", [128, D], BF16)
        bwT = sb("bwT", [128, ND * 128], BF16)
        cw_raw = sb("cw_raw", [128, ND * 128], BF16)
        cwT = sb("cwT", [128, ND * 128], BF16)
        la_row = sb("la_row", [1, 128], F32)
        acol = sb("acol", [128, 2], F32)
        dpr = sb("dpr", [8, 128], F32)
        dpt = sb("dpt", [128, 8], F32)
        x1c = sb("x1c", [128, 2 * 512], F32)
        s1e = sb("s1e", [128, 2 * 512], F32)
        pp = es.enter_context(nc.psum_tensor(P_ + "pp", [128, 8 * 512], F32))
        if env is None:
            block = es.enter_context(nc.Block())
            ld = SemC(nc, "ld"); vq = SemC(nc, "vq"); sc = SemC(nc, "sc")
            mm = SemC(nc, "mm"); stq = SemC(nc, "stq")
        else:
            block = env.block
            ld, vq, sc, mm, stq = env.sems

        def bank(i, lo=0, hi=512):
            return pp[:, i * 512 + lo:i * 512 + hi]

        @block.sync
        def _(sy):
            sy.dma_start(out=ident[:], in_=cst["ident"]).then_inc(ld.h, 16)
            sy.dma_start(out=identf[:], in_=cst["identf"]).then_inc(ld.h, 16)
            sy.dma_start(out=la_row[:], in_=logA).then_inc(ld.h, 16)
            sy.dma_start(out=dpr[:], in_=D_p).then_inc(ld.h, 16)
        ld.n += 16 * 4

        @block.gpsimd
        def _(g):
            g.dma_start(out=bw_raw[:], in_=B_w).then_inc(ld.h, 16)
            for i in range(ND):
                g.dma_start(out=cw_raw[:, i * 128:(i + 1) * 128],
                            in_=C_w[i * 128:(i + 1) * 128, :]).then_inc(ld.h, 16)
        ld.n += 16 * 9
        LD0 = ld.n

        tp_ev = {}
        for i in range(ND):
            @block.tensor
            def _(t, i=i):
                t.wait_ge(ld.h, LD0)
                if i >= 2:
                    t.wait_ge(vq.h, tp_ev[i - 2])
                mm.inc(_tmm(t, bank(4 + i % 2, 0, 128),
                            bw_raw[:, i * 128:(i + 1) * 128], ident[:]))
                mm.inc(_tmm(t, bank(4 + i % 2, 128, 256),
                            cw_raw[:, i * 128:(i + 1) * 128], ident[:]))
            m_tp = mm.n

            @block.vector
            def _(v, i=i, m_tp=m_tp):
                v.wait_ge(mm.h, m_tp)
                vq.wait(v)
                vq.inc(v.tensor_copy(bwT[:, i * 128:(i + 1) * 128],
                                     bank(4 + i % 2, 0, 128)))
                vq.wait(v)
                vq.inc(v.tensor_copy(cwT[:, i * 128:(i + 1) * 128],
                                     bank(4 + i % 2, 128, 256)))
            tp_ev[i] = vq.n

        @block.tensor
        def _(t):
            t.wait_ge(vq.h, tp_ev[ND - 1])
            mm.inc(_tmm(t, bank(6, 0, 1), la_row[:], identf[0:1, 0:1]))
            mm.inc(_tmm(t, bank(6, 128, 136), dpr[:], identf[0:8, 0:8]))
        m_la = mm.n

        @block.scalar
        def _(s):
            s.wait_ge(mm.h, m_la)
            sc.inc(s.activation(acol[:, 0:1], bank(6, 0, 1), AF.Exp))
        sc_a = sc.n

        @block.vector
        def _(v):
            v.wait_ge(mm.h, m_la)
            vq.wait(v)
            vq.inc(v.tensor_copy(dpt[:], bank(6, 128, 136)))
            v.wait_ge(sc.h, sc_a)
            vq.wait(v)
            vq.inc(v.memset(afull[:], 1.0))
            vq.wait(v)
            vq.inc(v.tensor_scalar(out=afull[:], in0=afull[:], scalar1=acol[:, 0:1],
                                   scalar2=None, op0=ALU.mult))
        v_a = vq.n

        v_end = [0] * NT
        s_end = [0] * NT
        ev_at = {}
        st_at = {}
        u_ev = {}
        for tt in range(NT):
            bo = (tt % 2) * D

            @block.sync
            def _(sy, tt=tt, bo=bo):
                if tt >= 2:
                    sy.wait_ge(vq.h, v_end[tt - 2])
                    sy.wait_ge(sc.h, s_end[tt - 2])
                sy.dma_start(out=xt[:, bo:bo + D],
                             in_=x1[tt * 128:(tt + 1) * 128, :]).then_inc(ld.h, 16)
            ld.n += 16
            XLD = ld.n

            @block.vector
            def _(v, tt=tt, bo=bo, XLD=XLD):
                v.wait_ge(ld.h, XLD)
                vq.wait(v)
                vq.inc(v.reduce_sum(st0[:, 0:1], xt[:, bo:bo + D], axis=AX.X))

            @block.scalar
            def _(s, tt=tt, bo=bo, XLD=XLD):
                s.wait_ge(ld.h, XLD)
                if tt >= 1:
                    s.wait_ge(vq.h, v_end[tt - 1])
                sc.inc(s.activation(xsq[:], xt[:, bo:bo + D], AF.Square,
                                    accum_out=st0[:, 2:3]))
            s_end[tt] = sc.n

            @block.vector
            def _(v, tt=tt, bo=bo):
                vq.wait(v)
                vq.inc(v.tensor_scalar(out=st0[:, 1:2], in0=st0[:, 0:1],
                                       scalar1=-1.0 / D, scalar2=None, op0=ALU.mult))
                vq.wait(v)
                vq.inc(v.tensor_tensor(out=st0[:, 3:4], in0=st0[:, 1:2],
                                       in1=st0[:, 1:2], op=ALU.mult))
                sc.wait(v)
                vq.wait(v)
                vq.inc(v.tensor_scalar(out=st0[:, 4:5], in0=st0[:, 2:3],
                                       scalar1=1.0 / D, scalar2=st0[:, 3:4],
                                       op0=ALU.mult, op1=ALU.subtract))
                vq.wait(v)
                vq.inc(v.tensor_scalar(out=st0[:, 5:6], in0=st0[:, 4:5],
                                       scalar1=1e-5, scalar2=None, op0=ALU.add))

            @block.scalar
            def _(s, tt=tt):
                vq.wait(s)
                sc.inc(s.activation(st0[:, 6:7], st0[:, 5:6], AF.Ln))
                sc.inc(s.activation(st0[:, 7:8], st0[:, 6:7], AF.Exp, scale=-0.5))

            @block.vector
            def _(v, tt=tt, bo=bo):
                sc.wait(v)
                vq.inc(v.tensor_scalar(out=xnt[:, bo:bo + D], in0=xt[:, bo:bo + D],
                                       scalar1=st0[:, 1:2], scalar2=st0[:, 7:8],
                                       op0=ALU.add, op1=ALU.mult))
            v_end[tt] = vq.n

            for dc in range(ND):
                k = tt * ND + dc
                pb = k % 2
                eb = k % 4

                @block.tensor
                def _(t, tt=tt, dc=dc, pb=pb, bo=bo, k=k):
                    t.wait_ge(vq.h, v_end[tt])
                    if k >= 2:
                        t.wait_ge(vq.h, ev_at[k - 2])
                        t.wait_ge(sc.h, sc.n)
                    mm.inc(_tmm(t, bank(pb, 0, 128),
                                       xnt[:, bo + dc * 128:bo + (dc + 1) * 128],
                                       identf[:]))

                @block.scalar
                def _(s, pb=pb, eb=eb, k=k):
                    mm.wait(s)
                    if k >= 4:
                        s.wait_ge(stq.h, st_at[k - 4])
                    sc.inc(s.copy(tev[:, eb * 128:(eb + 1) * 128], bank(pb, 0, 128)))

                @block.vector
                def _(v, tt=tt, dc=dc, pb=pb, k=k):
                    v.wait_ge(mm.h, mm.n)
                    vq.wait(v)
                    vq.inc(v.tensor_copy(xnb[:, dc * T + tt * 128:dc * T + (tt + 1) * 128],
                                         bank(pb, 0, 128)))
                ev_at[k] = vq.n

                @block.sync
                def _(sy, tt=tt, dc=dc, eb=eb, k=k):
                    sc.wait(sy)
                    sy.wait_ge(vq.h, ev_at[k])
                    sy.dma_start(out=xn1f[dc * 128:(dc + 1) * 128, tt * 128:(tt + 1) * 128],
                                 in_=tev[:, eb * 128:(eb + 1) * 128]).then_inc(stq.h, 16)
                stq.n += 16
                st_at[k] = stq.n

        XN_ALL_V = vq.n
        XN_ST = stq.n

        for tg in range(NTG):
            @block.tensor
            def _(t, tg=tg):
                t.wait_ge(vq.h, XN_ALL_V)
                if tg >= 2:
                    t.wait_ge(sc.h, u_ev[tg - 2])
                for dc in range(ND):
                    mm.inc(t.matmul(bank(2 + tg % 2),
                                    bwT[:, dc * 128:(dc + 1) * 128],
                                    xnb[:, dc * T + tg * TG:dc * T + (tg + 1) * TG],
                                    start=(dc == 0), stop=(dc == ND - 1)))

            @block.scalar
            def _(s, tg=tg):
                mm.wait(s)
                sc.inc(s.copy(u_sb[:, tg * TG:(tg + 1) * TG], bank(2 + tg % 2)))
            u_ev[tg] = sc.n

        @block.vector
        def _(v):
            sc.wait(v)
            v.wait_ge(vq.h, v_a)
            vq.wait(v)
            vq.inc(v.tensor_tensor_scan(out=hs[:], data0=afull[:], data1=u_sb[:],
                                        initial=0.0, op0=ALU.mult, op1=ALU.add))
            vq.wait(v)
            vq.inc(v.tensor_copy(hs_bf[:], hs[:]))
        v_scan = vq.n

        y_ev = {}
        y_st = {}
        for dt in range(ND):
            for tg in range(NTG):
                k = dt * NTG + tg

                @block.sync
                def _(sy, dt=dt, tg=tg, k=k):
                    if k == 0:
                        sy.wait_ge(stq.h, XN_ST)
                    if k >= 2:
                        sy.wait_ge(vq.h, y_ev[k - 2])
                    sy.dma_start(out=x1c[:, (k % 2) * 512:(k % 2) * 512 + 512],
                                 in_=xn1f[dt * 128:(dt + 1) * 128, tg * TG:(tg + 1) * TG]
                                 ).then_inc(ld.h, 16)
                ld.n += 16
                YLD = ld.n

                @block.tensor
                def _(t, dt=dt, tg=tg, k=k):
                    t.wait_ge(vq.h, v_scan)
                    if k >= 2:
                        t.wait_ge(vq.h, y_ev[k - 2])
                    mm.inc(t.matmul(bank(2 + k % 2), cwT[:, dt * 128:(dt + 1) * 128],
                                    hs_bf[:, tg * TG:(tg + 1) * TG],
                                    start=True, stop=True))
                m_y = mm.n

                @block.vector
                def _(v, dt=dt, tg=tg, k=k, m_y=m_y, YLD=YLD):
                    v.wait_ge(mm.h, m_y)
                    v.wait_ge(ld.h, YLD)
                    if k >= 2:
                        v.wait_ge(stq.h, y_st[k - 2])
                    vq.wait(v)
                    vq.inc(v.scalar_tensor_tensor(
                        out=s1e[:, (k % 2) * 512:(k % 2) * 512 + 512],
                        in0=x1c[:, (k % 2) * 512:(k % 2) * 512 + 512],
                        scalar=dpt[:, dt:dt + 1], in1=bank(2 + k % 2),
                        op0=ALU.mult, op1=ALU.add))
                y_ev[k] = vq.n

                @block.sync
                def _(sy, dt=dt, tg=tg, k=k):
                    sy.wait_ge(vq.h, y_ev[k])
                    sy.dma_start(out=s1f[dt * 128:(dt + 1) * 128, tg * TG:(tg + 1) * TG],
                                 in_=s1e[:, (k % 2) * 512:(k % 2) * 512 + 512]
                                 ).then_inc(stq.h, 16)
                stq.n += 16
                y_st[k] = stq.n

        if debug and env is None:
            @block.sync
            def _(sy):
                sy.wait_ge(vq.h, v_scan)
                sy.dma_start(out=dbg_u, in_=u_sb[:]).then_inc(stq.h, 16)
                sy.dma_start(out=dbg_a, in_=afull[:]).then_inc(stq.h, 16)
                sy.dma_start(out=dbg_hs, in_=hs[:]).then_inc(stq.h, 16)
                sy.dma_start(out=dbg_dpt, in_=dpt[:]).then_inc(stq.h, 16)
                sy.dma_start(out=dbg_cwt, in_=cwT[:]).then_inc(stq.h, 16)
            stq.n += 80

        @block.sync
        def _(sy):
            sy.wait_ge(stq.h, stq.n)
    return nc



def build_subt(layers=range(6), debug=False, env=None, io=None):
    if env is None:
        nc = bass.Bass("TRN2", target_bir_lowering=False)
        xn1f = nc.dram_tensor("xn1f", [D, T], F32, kind="ExternalInput").ap()
        Wqkv = nc.dram_tensor("Wqkv", [6, D, 3072], F32, kind="ExternalInput").ap()
        q_sc = nc.dram_tensor("q_sc", [6, 64], F32, kind="ExternalInput").ap()
        k_sc = nc.dram_tensor("k_sc", [6, 64], F32, kind="ExternalInput").ap()
        Wo1 = nc.dram_tensor("Wo1", [6, D, D], F32, kind="ExternalInput").ap()
        ln_g = nc.dram_tensor("ln_g", [6, 8, 128], F32, kind="ExternalInput").ap()
        ln_b = nc.dram_tensor("ln_b", [6, 8, 128], F32, kind="ExternalInput").ap()
        W1l = nc.dram_tensor("W1l", [6, D, 5460], F32, kind="ExternalInput").ap()
        b1l = nc.dram_tensor("b1l", [6, 5460], F32, kind="ExternalInput").ap()
        W2l = nc.dram_tensor("W2l", [6, 2730, D], F32, kind="ExternalInput").ap()
        b2l = nc.dram_tensor("b2l", [6, 8, 128], F32, kind="ExternalInput").ap()
        g1f = nc.dram_tensor("g1f", [D, T], F32, kind="ExternalOutput").ap()
        cst = declare_consts(nc, ["ident", "identf", "c1q", "s1q", "c1k", "s1k", "R1",
                                  "S2", "S2T", "ID2", "Bband", "ones_1x128",
                                  "ones_128x1"])
    else:
        nc = env.nc
        cst = env.cst
        (xn1f, Wqkv, q_sc, k_sc, Wo1, ln_g, ln_b, W1l, b1l, W2l, b2l, g1f) = io
    P_ = "E" if env else ""
    dkind = "ExternalOutput" if (debug and env is None) else "Internal"
    xsP = nc.dram_tensor(P_ + "xsP", [D, T], F32, kind=dkind).ap()
    xsQ = nc.dram_tensor(P_ + "xsQ", [D, T], F32, kind=dkind).ap()
    oat1 = nc.dram_tensor(P_ + "oat1", [D, T], BF16, kind=dkind).ap()
    qscr = nc.dram_tensor(P_ + "qscr", [D, T], BF16, kind=dkind).ap()
    kscr1 = nc.dram_tensor(P_ + "kscr1", [D, T], BF16, kind=dkind).ap()
    SH = 300.0
    NS = 256
    NSG = T // NS
    layers = list(layers)
    _names = {}
    with ExitStack() as es:
        def mk_sb(stack):
            def f(name, shape, dt):
                k = _names.get(name, 0)
                _names[name] = k + 1
                if k:
                    name = f"{name}_{k}"
                return stack.enter_context(nc.sbuf_tensor(P_ + name, shape, dt))
            return f
        sb = mk_sb(es)
        identb = sb("identb", [128, 128], BF16)
        identf = sb("identf", [128, 128], F32)
        c1qt = sb("c1qt", [128, T], BF16)
        s1qt = sb("s1qt", [128, T], BF16)
        c1kt = sb("c1kt", [128, T], BF16)
        s1kt = sb("s1kt", [128, T], BF16)
        R1s = sb("R1s", [128, 128], BF16)
        S2s = sb("S2s", [2, 128], BF16)
        S2Ts = sb("S2Ts", [128, 2], BF16)
        ID2s = sb("ID2s", [64, 128], BF16)
        Bb = sb("Bb", [128, 384], F32)
        ones1x128 = sb("ones1x128", [1, 128], BF16)
        ones128x1 = sb("ones128x1", [128, 1], BF16)
        qs2 = sb("qs2", [128, 1], F32)
        ks2 = sb("ks2", [128, 1], F32)
        btri = sb("btri", [128, 1], F32)
        gt = sb("gt", [128, 8], F32)
        bt = sb("bt", [128, 8], F32)
        b1t = sb("b1t", [128, 44], F32)
        b2t = sb("b2t", [128, 8], F32)
        qsr = sb("qsr", [1, 2 * 64], F32)
        qsb = sb("qsb", [1, 2 * 64], BF16)
        gbr = sb("gbr", [8, 2 * 128], F32)
        b1ra = sb("b1ra", [21, 128], F32)
        b1rg = sb("b1rg", [21, 128], F32)
        b1rb = sb("b1rb", [1, 84], F32)
        sm = sb("sm", [1, 8], F32)
        smb = sb("smb", [1, 1], BF16)
        qcol = sb("qcol", [64, 2], BF16)
        pp = es.enter_context(nc.psum_tensor(P_ + "pp", [128, 8 * 512], F32))
        if env is None:
            block = es.enter_context(nc.Block())
            ld = SemC(nc, "ld"); vq = SemC(nc, "vq"); sc = SemC(nc, "sc")
            mm = SemC(nc, "mm"); stq = SemC(nc, "stq")
        else:
            block = env.block
            ld, vq, sc, mm, stq = env.sems

        def bank(i, lo=0, hi=512):
            return pp[:, i * 512 + lo:i * 512 + hi]

        @block.sync
        def _(sy):
            for nm, dst in (("ident", identb), ("identf", identf), ("c1q", c1qt),
                            ("s1q", s1qt), ("c1k", c1kt), ("s1k", s1kt), ("R1", R1s),
                            ("S2", S2s), ("S2T", S2Ts), ("ID2", ID2s), ("Bband", Bb),
                            ("ones_1x128", ones1x128), ("ones_128x1", ones128x1)):
                sy.dma_start(out=dst[:], in_=cst[nm]).then_inc(ld.h, 16)
        ld.n += 16 * 13

        for l in layers:
            xin = xn1f if l == layers[0] else xsQ
            fout = g1f if l == layers[-1] else xsQ

            if l != layers[0]:
                _bm, _bv, _bs, _bq, _bl = mm.n, vq.n, sc.n, stq.n, ld.n

                def _bar(e, _bm=_bm, _bv=_bv, _bs=_bs, _bq=_bq, _bl=_bl):
                    e.wait_ge(mm.h, _bm)
                    e.wait_ge(vq.h, _bv)
                    e.wait_ge(sc.h, _bs)
                    e.wait_ge(stq.h, _bq)
                    e.wait_ge(ld.h, _bl)
                block.tensor(_bar)
                block.vector(_bar)
                block.scalar(_bar)
                block.gpsimd(_bar)
                block.sync(_bar)

            @block.sync
            def _(sy, l=l):
                sy.wait_ge(mm.h, mm.n)
                sy.wait_ge(vq.h, vq.n)
                sy.wait_ge(sc.h, sc.n)
                sy.dma_start(out=qsr[:, 0:64], in_=q_sc[l:l + 1, :]).then_inc(ld.h, 16)
                sy.dma_start(out=qsr[:, 64:128], in_=k_sc[l:l + 1, :]).then_inc(ld.h, 16)
                sy.dma_start(out=gbr[:, 0:128], in_=ln_g[l]).then_inc(ld.h, 16)
                sy.dma_start(out=gbr[:, 128:256], in_=ln_b[l]).then_inc(ld.h, 16)
                sy.dma_start(out=b1ra[0:21, :],
                             in_=b1l[l:l + 1, 0:2688].rearrange("a (c p) -> (a c) p", p=128)
                             ).then_inc(ld.h, 16)
                sy.dma_start(out=b1rg[:],
                             in_=b1l[l:l + 1, 2730:5418].rearrange("a (c p) -> (a c) p", p=128)
                             ).then_inc(ld.h, 16)
                sy.dma_start(out=b1rb[:, 0:42], in_=b1l[l:l + 1, 2688:2730]).then_inc(ld.h, 16)
                sy.dma_start(out=b1rb[:, 42:84], in_=b1l[l:l + 1, 5418:5460]).then_inc(ld.h, 16)
            ld.n += 16 * 8

            @block.gpsimd
            def _(g, l=l):
                g.dma_start(out=qsb[:, 0:64], in_=q_sc[l:l + 1, :]).then_inc(ld.h, 16)
                g.dma_start(out=qsb[:, 64:128], in_=k_sc[l:l + 1, :]).then_inc(ld.h, 16)
            ld.n += 16 * 2
            SMLD = ld.n

            @block.tensor
            def _(t, l=l):
                t.wait_ge(ld.h, SMLD)
                t.wait_ge(vq.h, vq.n)
                t.wait_ge(sc.h, sc.n)
                mm.inc(_tmm(t, bank(0, 0, 8), gbr[:, 0:128], identf[0:8, 0:8]))
                mm.inc(_tmm(t, bank(0, 8, 16), gbr[:, 128:256], identf[0:8, 0:8]))
                mm.inc(_tmm(t, bank(0, 16, 37), b1ra[:], identf[0:21, 0:21]))
                mm.inc(_tmm(t, bank(0, 40, 61), b1rg[:], identf[0:21, 0:21]))
                mm.inc(_tmm(t, bank(0, 64, 65)[0:42, :], b1rb[:, 0:42], identf[0:1, 0:1]))
                mm.inc(_tmm(t, bank(0, 68, 69)[0:42, :], b1rb[:, 42:84], identf[0:1, 0:1]))
                mm.inc(_tmm(t, bank(1, 0, 1)[0:64, :], qsr[0:1, 0:64], identf[0:1, 0:1]))
                mm.inc(_tmm(t, bank(1, 4, 5)[0:64, :], qsr[0:1, 64:128], identf[0:1, 0:1]))
            mm_t0 = mm.n

            @block.sync
            def _(sy, l=l, mm_t0=mm_t0):
                sy.wait_ge(mm.h, mm_t0)
                sy.dma_start(out=gbr[:, 0:128], in_=b2l[l]).then_inc(ld.h, 16)
            ld.n += 16
            B2LD = ld.n

            @block.vector
            def _(v, l=l):
                v.wait_ge(mm.h, mm_t0)
                vq.wait(v)
                vq.inc(v.tensor_copy(gt[:], bank(0, 0, 8)))
                vq.wait(v)
                vq.inc(v.tensor_copy(bt[:], bank(0, 8, 16)))
                vq.wait(v)
                vq.inc(v.tensor_copy(b1t[:, 0:21], bank(0, 16, 37)))
                vq.wait(v)
                vq.inc(v.tensor_copy(b1t[:, 22:43], bank(0, 40, 61)))
                vq.wait(v)
                vq.inc(v.tensor_copy(b1t[0:42, 21:22], bank(0, 64, 65)[0:42, :]))
                vq.wait(v)
                vq.inc(v.tensor_copy(b1t[0:42, 43:44], bank(0, 68, 69)[0:42, :]))
                vq.wait(v)
                vq.inc(v.tensor_copy(qcol[:, 0:1], bank(1, 0, 1)[0:64, :]))
                vq.wait(v)
                vq.inc(v.tensor_copy(qcol[:, 1:2], bank(1, 4, 5)[0:64, :]))
            v_t0 = vq.n

            @block.tensor
            def _(t, l=l):
                t.wait_ge(vq.h, v_t0)
                t.wait_ge(ld.h, B2LD)
                mm.inc(t.matmul(bank(2, 0, 1), ID2s[:], qcol[:, 0:1], start=True, stop=True))
                mm.inc(t.matmul(bank(2, 4, 5), ID2s[:], qcol[:, 1:2], start=True, stop=True))
                mm.inc(_tmm(t, bank(2, 8, 16), gbr[:, 0:128], identf[0:8, 0:8]))
            mm_t1 = mm.n

            @block.vector
            def _(v, l=l):
                v.wait_ge(mm.h, mm_t1)
                vq.wait(v)
                vq.inc(v.tensor_copy(qs2[:], bank(2, 0, 1)))
                vq.wait(v)
                vq.inc(v.tensor_copy(ks2[:], bank(2, 4, 5)))
                vq.wait(v)
                vq.inc(v.tensor_copy(b2t[:], bank(2, 8, 16)))
                vq.wait(v)
                vq.inc(v.tensor_tensor(out=sm[:, 0:1].rearrange("a b -> a b") if False
                                       else qsr[:, 0:64], in0=qsr[:, 0:64],
                                       in1=qsr[:, 64:128], op=ALU.mult))
                vq.wait(v)
                vq.inc(v.reduce_max(sm[:, 0:1], qsr[:, 0:64], axis=AX.X,
                                    apply_absolute_value=True))
                vq.wait(v)
                vq.inc(v.tensor_scalar(out=sm[:, 1:2], in0=sm[:, 0:1], scalar1=8.0,
                                       scalar2=SH, op0=ALU.mult, op1=ALU.add))
                vq.wait(v)
                vq.inc(v.tensor_copy(smb[:], sm[:, 1:2]))
            v_t1 = vq.n

            @block.tensor
            def _(t):
                t.wait_ge(vq.h, v_t1)
                mm.inc(t.matmul(bank(3, 0, 1), ones1x128[:], smb[:], start=True, stop=True))
            mm_t2 = mm.n

            @block.scalar
            def _(s):
                s.wait_ge(mm.h, mm_t2)
                sc.inc(s.mul(btri[:], bank(3, 0, 1), -1.0))
            sc_btri = sc.n

            es_va = ExitStack()
            va = mk_sb(es_va)("va", [128, NT * 1040], BF16)

            @block.vector
            def _(v, _stq=None):
                v.wait_ge(mm.h, mm.n)
                v.wait_ge(sc.h, sc.n)
                v.wait_ge(stq.h, stq.n)
                v.wait_ge(ld.h, ld.n)
                vq.wait(v)
                vq.inc(v.memset(va[:].rearrange("p (t o) -> p t o", t=NT * 16)[:, :, 64:65],
                                1.0))
            V_ONES = vq.n
            es_qkv = ExitStack()
            sbq = mk_sb(es_qkv)
            wqkv_sb = sbq("wqkv_sb", [128, ND * 3072], BF16)
            xc = sbq("xc", [128, ND * 513], F32)
            xs2 = sbq("xs2", [128, ND * 512], BF16)
            rawp = sbq("rawp", [64, 0 * 512 + 512], BF16) if False else None
            rawq = sbq("rawq", [128, 2 * 512], BF16)
            sqp = sbq("sqp", [128, 512], BF16)
            n2c = sbq("n2c", [2, 512], F32)
            n2l = sbq("n2l", [2, 512], F32)
            rn2 = sbq("rn2", [2, 512], BF16)
            rbs = sbq("rbs", [128, 512], BF16)
            v1b = sbq("v1b", [128, 512], BF16)
            v2b = sbq("v2b", [128, 2 * 512], BF16)
            rt1b = sbq("rt1b", [128, 512], BF16)
            rt2b = sbq("rt2b", [128, 512], BF16)
            qsave = sbq("qsave", [128, ND * 512], BF16)
            ksave = sbq("ksave", [128, ND * 512], BF16)

            @block.gpsimd
            def _(g, l=l):
                g.wait_ge(mm.h, mm.n)
                g.wait_ge(sc.h, sc.n)
                g.wait_ge(vq.h, vq.n)
                g.wait_ge(stq.h, stq.n)
                g.wait_ge(ld.h, ld.n)
                for i in range(ND):
                    g.dma_start(out=wqkv_sb[:, i * 3072:(i + 1) * 3072],
                                in_=Wqkv[l, i * 128:(i + 1) * 128, :]).then_inc(ld.h, 16)
            ld.n += 16 * ND
            WQKVLD = ld.n

            ev_sc = {}
            rope_done = {}
            vsc_ev = {}
            qk_st = {}
            for tc in range(NTG):
                t0 = tc * TG
                cb = (tc % 2)

                @block.sync
                def _(sy, l=l, tc=tc, t0=t0, cb=cb, xin=xin, _st=stq.n, _ld=ld.n):
                    if tc == 0:
                        sy.wait_ge(stq.h, _st)
                        sy.wait_ge(ld.h, _ld)
                    if tc >= 1:
                        sy.wait_ge(vq.h, rope_done[tc - 1])
                        sy.wait_ge(mm.h, qk_st[(tc - 1, 'mm')])
                    if tc == 0:
                        sy.dma_start(out=xc[:, 0:ND * 513]
                                     .rearrange("p (c t) -> p c t", c=ND)[:, :, 1:513],
                                     in_=xin.rearrange("(c p) t -> p c t", p=128)
                                     [:, :, 0:512]).then_inc(ld.h, 16)
                    else:
                        sy.dma_start(out=xc[:, 0:ND * 513]
                                     .rearrange("p (c t) -> p c t", c=ND),
                                     in_=xin.rearrange("(c p) t -> p c t", p=128)
                                     [:, :, t0 - 1:t0 + 512]).then_inc(ld.h, 16)
                ld.n += 16
                XLD = ld.n

                @block.vector
                def _(v, tc=tc, cb=cb, XLD=XLD):
                    v.wait_ge(ld.h, XLD)
                    vq.wait(v)
                    if tc == 0:
                        vq.inc(v.memset(xc[:, 0:ND * 513]
                                        .rearrange("p (c t) -> p c t", c=ND)[:, :, 0:1], 0.0))
                        vq.wait(v)
                    for dc in range(ND):
                        off = 1 if dc < 4 else 0
                        vq.inc(v.tensor_copy(
                            xs2[:, dc * 512:(dc + 1) * 512],
                            xc[:, dc * 513 + off:dc * 513 + off + 512]))
                        vq.wait(v)
                v_xs = vq.n

                for tt in range(4):
                    g_tt = tc * 4 + tt
                    for vg in range(2):
                        @block.tensor
                        def _(t, tc=tc, cb=cb, tt=tt, vg=vg, v_xs=v_xs):
                            t.wait_ge(vq.h, v_xs)
                            t.wait_ge(ld.h, WQKVLD)
                            t.wait_ge(sc.h, sc.n)
                            for dc in range(ND):
                                mm.inc(t.matmul(
                                    bank(2 + vg),
                                    xs2[:, dc * 512 + tt * 128:dc * 512 + tt * 128 + 128],
                                    wqkv_sb[:, dc * 3072 + 2048 + vg * 512:
                                            dc * 3072 + 2048 + (vg + 1) * 512],
                                    start=(dc == 0), stop=(dc == ND - 1)))

                        @block.scalar
                        def _(s, g_tt=g_tt, vg=vg):
                            mm.wait(s)
                            s.wait_ge(vq.h, V_ONES)
                            dst = va[:, g_tt * 1040 + vg * 520:g_tt * 1040 + (vg + 1) * 520] \
                                .rearrange("p (h o) -> p h o", h=8)[:, :, 0:64]
                            sc.inc(s.copy(dst, bank(2 + vg).rearrange("p (h o) -> p h o", h=8)))
                        vsc_ev[(tc, tt, vg)] = sc.n

                for pr in range(16):
                    is_q = pr < 8
                    fbase = pr * 128 if is_q else 1024 + (pr - 8) * 128
                    prb = pr % 2

                    @block.tensor
                    def _(t, tc=tc, cb=cb, pr=pr, prb=prb, fbase=fbase, v_xs=v_xs):
                        t.wait_ge(vq.h, v_xs)
                        t.wait_ge(ld.h, WQKVLD)
                        t.wait_ge(sc.h, sc.n)
                        for dc in range(ND):
                            mm.inc(t.matmul(
                                bank(prb),
                                wqkv_sb[:, dc * 3072 + fbase:dc * 3072 + fbase + 128],
                                xs2[:, dc * 512:(dc + 1) * 512],
                                start=(dc == 0), stop=(dc == ND - 1)))

                    SCP = sc.n
                    sc_raw = SCP + 1
                    sc_rn = SCP + 3
                    sc_rbs = SCP + 4

                    @block.scalar
                    def _(s, pr=pr, prb=prb, _v=vq.n):
                        mm.wait(s)
                        s.wait_ge(vq.h, _v)
                        sc.inc(s.copy(rawq[:, prb * 512:(prb + 1) * 512], bank(prb)))

                    @block.vector
                    def _(v, pr=pr, prb=prb, sc_raw=sc_raw):
                        v.wait_ge(sc.h, sc_raw)
                        vq.wait(v)
                        vq.inc(v.tensor_tensor(out=sqp[:], in0=rawq[:, prb * 512:(prb + 1) * 512],
                                               in1=rawq[:, prb * 512:(prb + 1) * 512],
                                               op=ALU.mult))
                    v_sq = vq.n

                    @block.tensor
                    def _(t, v_sq=v_sq):
                        t.wait_ge(vq.h, v_sq)
                        t.wait_ge(vq.h, vq.n)
                        mm.inc(t.matmul(bank(4, 0, 512)[0:2, :], S2Ts[:], sqp[:],
                                        start=True, stop=True))
                    m_n2 = mm.n

                    @block.vector
                    def _(v, m_n2=m_n2):
                        v.wait_ge(mm.h, m_n2)
                        vq.wait(v)
                        vq.inc(v.tensor_scalar(out=n2c[:], in0=bank(4, 0, 512)[0:2, :],
                                               scalar1=1e-24, scalar2=None, op0=ALU.max))
                    v_cl = vq.n

                    @block.scalar
                    def _(s, v_cl=v_cl):
                        s.wait_ge(vq.h, v_cl)
                        sc.inc(s.activation(n2c[:, 512:1024] if False else n2l[:],
                                            n2c[:], AF.Ln))
                        with nc.allow_low_precision(reason="l2norm rsqrt bf16"):
                            sc.inc(s.activation(rn2[:], n2l[:], AF.Exp, scale=-0.5))
                    assert sc.n == sc_rn, (sc.n, sc_rn)

                    @block.tensor
                    def _(t, sc_rn=sc_rn, _v=vq.n):
                        t.wait_ge(sc.h, sc_rn)
                        t.wait_ge(vq.h, _v)
                        mm.inc(t.matmul(bank(5), S2s[:], rn2[:], start=True, stop=True))
                    m_rb = mm.n

                    @block.scalar
                    def _(s, m_rb=m_rb, _v=vq.n):
                        s.wait_ge(mm.h, m_rb)
                        s.wait_ge(vq.h, _v)
                        sc.inc(s.copy(rbs[:], bank(5)))
                    assert sc.n == sc_rbs, (sc.n, sc_rbs)

                    @block.vector
                    def _(v, pr=pr, prb=prb, is_q=is_q, sc_rbs=sc_rbs):
                        v.wait_ge(sc.h, sc_rbs)
                        vq.wait(v)
                        vq.inc(v.tensor_tensor(out=v1b[:], in0=rawq[:, prb * 512:(prb + 1) * 512],
                                               in1=rbs[:], op=ALU.mult))
                        vq.wait(v)
                        vq.inc(v.tensor_scalar(out=v2b[:, prb * 512:(prb + 1) * 512],
                                               in0=v1b[:],
                                               scalar1=(qs2[:, 0:1] if is_q else ks2[:, 0:1]),
                                               scalar2=None, op0=ALU.mult))
                    v_v2 = vq.n

                    @block.tensor
                    def _(t, prb=prb, v_v2=v_v2):
                        t.wait_ge(vq.h, v_v2)
                        t.wait_ge(vq.h, vq.n)
                        mm.inc(t.matmul(bank(6 + prb), R1s[:],
                                        v2b[:, prb * 512:(prb + 1) * 512],
                                        start=True, stop=True))
                    m_rv = mm.n

                    @block.vector
                    def _(v, tc=tc, t0=t0, pr=pr, prb=prb, is_q=is_q, m_rv=m_rv):
                        ctab = c1qt if is_q else c1kt
                        stab = s1qt if is_q else s1kt
                        dst = qsave if is_q else ksave
                        dpr = pr if is_q else pr - 8
                        if pr == 0 and tc >= 1:
                            v.wait_ge(stq.h, qk_st[tc - 1])
                        vq.wait(v)
                        vq.inc(v.tensor_tensor(out=rt1b[:],
                                               in0=v2b[:, prb * 512:(prb + 1) * 512],
                                               in1=ctab[:, t0:t0 + 512], op=ALU.mult))
                        v.wait_ge(mm.h, m_rv)
                        vq.wait(v)
                        vq.inc(v.tensor_tensor(out=rt2b[:], in0=bank(6 + prb),
                                               in1=stab[:, t0:t0 + 512], op=ALU.mult))
                        vq.wait(v)
                        vq.inc(v.tensor_tensor(out=dst[:, dpr * 512:(dpr + 1) * 512],
                                               in0=rt1b[:], in1=rt2b[:], op=ALU.add))
                rope_done[tc] = vq.n

                @block.sync
                def _(sy, tc=tc, t0=t0):
                    sy.wait_ge(vq.h, rope_done[tc])
                    sy.dma_start(out=qscr.rearrange("(c p) t -> p c t", p=128)
                                 [:, :, t0:t0 + 512],
                                 in_=qsave[:].rearrange("p (c t) -> p c t", c=ND)
                                 ).then_inc(stq.h, 16)
                    sy.dma_start(out=kscr1.rearrange("(c p) t -> p c t", p=128)
                                 [:, :, t0:t0 + 512],
                                 in_=ksave[:].rearrange("p (c t) -> p c t", c=ND)
                                 ).then_inc(stq.h, 16)
                stq.n += 32
                qk_st[tc] = stq.n
                qk_st[(tc, 'mm')] = mm.n
            E1_ST = stq.n
            E1_SC = sc.n

            es_qkv.close()

            es_at = ExitStack()
            sba = mk_sb(es_at)
            qh = sba("qh", [64, 2 * T], BF16)
            kh = sba("kh", [64, 2 * T], BF16)
            s_sb = sba("s_sb", [128, 2 * 384], F32)
            p_sb = sba("p_sb", [128, 2 * 384], BF16)
            rr1 = sba("rr1", [1, 512], BF16)
            rrf = sba("rrf", [1, 512], F32)
            rb1 = sba("rb1", [64, 2 * 512], BF16)
            oh1 = sba("oh1", [64, 2 * 512], BF16)

            hld = {}
            oh_st = {}
            for hi, h in enumerate(range(16)):
                hb = hi % 2

                @block.sync
                def _(sy, h=h, hb=hb, _v=vq.n, _m=mm.n):
                    if hi == 0:
                        sy.wait_ge(stq.h, E1_ST)
                        sy.wait_ge(sc.h, E1_SC)
                        sy.wait_ge(vq.h, _v)
                        sy.wait_ge(mm.h, _m)
                    if hi >= 2:
                        sy.wait_ge(mm.h, hld[(hi - 2, 'mm')])
                    sy.dma_start(out=qh[:, hb * T:(hb + 1) * T],
                                 in_=qscr[h * 64:(h + 1) * 64, :]).then_inc(ld.h, 16)
                    sy.dma_start(out=kh[:, hb * T:(hb + 1) * T],
                                 in_=kscr1[h * 64:(h + 1) * 64, :]).then_inc(ld.h, 16)
                ld.n += 32
                hld[hi] = ld.n

                HSV = vq.n
                HSM = mm.n
                HSS = sc.n
                HST = stq.n
                spans = [min(384, T - 128 * c) for c in range(NT)]
                navs = []
                for c in range(NT):
                    n = 0
                    for mq in range(c, min(c + 3, NT)):
                        if 128 * (mq - c) < spans[c]:
                            n += 1
                    navs.append(n)
                mm_qk = {}; mm_av = {}; mm_rb = {}
                cnt = HSM
                for c in range(NT):
                    cnt += 1; mm_qk[c] = cnt
                    cnt += navs[c]; mm_av[c] = cnt
                    if c % 4 == 3:
                        cnt += 1; mm_rb[c // 4] = cnt
                sc_exp = {}; sc_rcp = {}
                sn = HSS
                for c in range(NT):
                    sn += 1; sc_exp[c] = sn
                    if c % 4 == 3:
                        sn += 2; sc_rcp[c // 4] = sn
                vq_stt = {}; vq_div = {}
                vn = HSV
                for c in range(NT):
                    vn += 1; vq_stt[c] = vn
                    if c % 4 == 3:
                        vn += 2; vq_div[c // 4] = vn
                st_oh = {g: HST + 16 * (g + 1) for g in range(8)}

                @block.tensor
                def _(t, h=h, hb=hb, hi=hi):
                    t.wait_ge(ld.h, hld[hi])
                    for c in range(NT):
                        span = spans[c]
                        pb2 = c % 2
                        t.wait_ge(vq.h, vq_stt[c - 2] if c >= 2 else HSV)
                        mm.inc(t.matmul(bank(pb2, 0, span),
                                        kh[:, hb * T + c * 128:hb * T + (c + 1) * 128],
                                        qh[:, hb * T + c * 128:hb * T + c * 128 + span],
                                        start=True, stop=True))
                        t.wait_ge(sc.h, sc_exp[c])
                        for mq in range(c, min(c + 3, NT)):
                            if 128 * (mq - c) >= span:
                                continue
                            gq = mq // 4
                            t.wait_ge(vq.h, vq_div.get(gq - 2, HSV))
                            mm.inc(t.matmul(
                                bank(2 + gq % 2, (mq % 4) * 128, (mq % 4) * 128 + 128)
                                [0:65, :],
                                va[:, c * 1040 + (h // 8) * 520 + (h % 8) * 65:
                                   c * 1040 + (h // 8) * 520 + (h % 8) * 65 + 65],
                                p_sb[:, pb2 * 384 + (mq - c) * 128:
                                     pb2 * 384 + (mq - c) * 128 + 128],
                                start=(mq % 4 == 0 and c == max(0, mq - 2)),
                                stop=(c == mq and (mq % 4 == 3 or mq == NT - 1)),
                                skip_group_check=True))
                        if c % 4 == 3:
                            gq = c // 4
                            t.wait_ge(sc.h, sc_rcp[gq])
                            mm.inc(t.matmul(bank(6)[0:64, :], S2s[0:1, 0:64],
                                            rr1[:, 0:512], start=True, stop=True))
                    assert mm.n == cnt, (mm.n, cnt)

                @block.vector
                def _(v, h=h, hi=hi):
                    for c in range(NT):
                        span = spans[c]
                        pb2 = c % 2
                        v.wait_ge(mm.h, mm_qk[c])
                        if c >= 2:
                            v.wait_ge(sc.h, sc_exp[c - 2])
                        vq.inc(v.scalar_tensor_tensor(
                            out=s_sb[:, pb2 * 384:pb2 * 384 + span],
                            in0=bank(pb2, 0, span), scalar=SH,
                            in1=Bb[:, 0:span], op0=ALU.add, op1=ALU.mult))
                        if c % 4 == 3:
                            gq = c // 4
                            v.wait_ge(mm.h, mm_rb[gq])
                            key = (h, gq - 2)
                            if key in oh_st:
                                v.wait_ge(stq.h, oh_st[key])
                            vq.inc(v.tensor_copy(rb1[:, (gq % 2) * 512:(gq % 2) * 512 + 512],
                                                 bank(6)[0:64, :]))
                            vq.inc(v.tensor_tensor(
                                out=oh1[:, (gq % 2) * 512:(gq % 2) * 512 + 512],
                                in0=bank(2 + gq % 2)[0:64, :],
                                in1=rb1[:, (gq % 2) * 512:(gq % 2) * 512 + 512],
                                op=ALU.mult))
                    assert vq.n == vn, (vq.n, vn)

                @block.scalar
                def _(s, h=h):
                    for c in range(NT):
                        span = spans[c]
                        pb2 = c % 2
                        s.wait_ge(vq.h, vq_stt[c])
                        s.wait_ge(mm.h, mm_av[c - 2] if c >= 2 else HSM)
                        sc.inc(s.activation(p_sb[:, pb2 * 384:pb2 * 384 + span],
                                            s_sb[:, pb2 * 384:pb2 * 384 + span],
                                            AF.Exp, bias=btri[:, 0:1]))
                        if c % 4 == 3:
                            gq = c // 4
                            s.wait_ge(mm.h, mm_av[c])
                            sc.inc(s.activation(rrf[:], bank(2 + gq % 2)[64:65, :],
                                                AF.Ln))
                            with nc.allow_low_precision(reason="softmax denom bf16"):
                                sc.inc(s.activation(rr1[:, 0:512], rrf[:],
                                                    AF.Exp, scale=-1.0))
                    assert sc.n == sn, (sc.n, sn)

                @block.sync
                def _(sy, h=h):
                    for g in range(8):
                        sy.wait_ge(vq.h, vq_div[g])
                        sy.dma_start(out=oat1[h * 64:(h + 1) * 64,
                                              g * 512:(g + 1) * 512],
                                     in_=oh1[:, (g % 2) * 512:(g % 2) * 512 + 512]
                                     ).then_inc(stq.h, 16)
                stq.n += 16 * 8
                for g in range(8):
                    oh_st[(h, g)] = st_oh[g]
                hld[(hi, 'mm')] = mm.n
            E2_ST = stq.n
            es_at.close()
            es_va.close()

            es_wo = ExitStack()
            sbo = mk_sb(es_wo)
            wo_sb = sbo("wo_sb", [128, ND * D], BF16)
            att1 = sbo("att1", [128, ND * 512], BF16)
            xres = sbo("xres", [128, ND * 512], F32)
            xo1 = sbo("xo1", [128, 2 * 512], F32)

            @block.gpsimd
            def _(g, l=l):
                g.wait_ge(mm.h, mm.n)
                g.wait_ge(sc.h, sc.n)
                g.wait_ge(vq.h, vq.n)
                g.wait_ge(stq.h, stq.n)
                g.wait_ge(ld.h, ld.n)
                for i in range(ND):
                    g.dma_start(out=wo_sb[:, i * D:(i + 1) * D],
                                in_=Wo1[l, i * 128:(i + 1) * 128, :]).then_inc(ld.h, 16)
            ld.n += 16 * ND
            WOLD = ld.n

            a_ld = {}
            xo_v = {}
            xo_stq = {}
            for tg in range(NTG):
                @block.sync
                def _(sy, tg=tg, xin=xin, _v=vq.n, _m=mm.n, _s=sc.n):
                    if tg == 0:
                        sy.wait_ge(vq.h, _v)
                        sy.wait_ge(mm.h, _m)
                        sy.wait_ge(sc.h, _s)
                        sy.wait_ge(stq.h, E2_ST)
                    if tg >= 2:
                        sy.wait_ge(mm.h, xo_v[(tg - 2, 'mm')])
                        sy.wait_ge(vq.h, xo_v[tg - 2])
                    sy.dma_start(out=att1[:].rearrange("p (c t) -> p c t", c=ND),
                                 in_=oat1.rearrange("(c p) t -> p c t", p=128)
                                 [:, :, tg * TG:(tg + 1) * TG]).then_inc(ld.h, 16)
                    sy.dma_start(out=xres[:].rearrange("p (c t) -> p c t", c=ND),
                                 in_=xin.rearrange("(c p) t -> p c t", p=128)
                                 [:, :, tg * TG:(tg + 1) * TG]).then_inc(ld.h, 16)
                ld.n += 32
                a_ld[tg] = ld.n

                for dt in range(ND):
                    k = tg * ND + dt

                    @block.tensor
                    def _(t, tg=tg, dt=dt, k=k):
                        t.wait_ge(ld.h, a_ld[tg])
                        t.wait_ge(ld.h, WOLD)
                        if k >= 2:
                            t.wait_ge(vq.h, xo_v[(k - 2, 'v')])
                        for ic in range(ND):
                            mm.inc(t.matmul(bank(k % 2),
                                            wo_sb[:, ic * D + dt * 128:ic * D + (dt + 1) * 128],
                                            att1[:, ic * 512:(ic + 1) * 512],
                                            start=(ic == 0), stop=(ic == ND - 1)))

                    @block.vector
                    def _(v, tg=tg, dt=dt, k=k):
                        v.wait_ge(mm.h, mm.n)
                        if k >= 2:
                            v.wait_ge(stq.h, xo_stq[k - 2])
                        vq.wait(v)
                        vq.inc(v.tensor_tensor(out=xo1[:, (k % 2) * 512:(k % 2) * 512 + 512],
                                               in0=bank(k % 2),
                                               in1=xres[:, dt * 512:(dt + 1) * 512],
                                               op=ALU.add))
                    xo_v[(k, 'v')] = vq.n

                    @block.sync
                    def _(sy, tg=tg, dt=dt, k=k):
                        sy.wait_ge(vq.h, xo_v[(k, 'v')])
                        sy.dma_start(out=xsP[dt * 128:(dt + 1) * 128, tg * TG:(tg + 1) * TG],
                                     in_=xo1[:, (k % 2) * 512:(k % 2) * 512 + 512]
                                     ).then_inc(stq.h, 16)
                    stq.n += 16
                    xo_stq[k] = stq.n
                xo_v[tg] = vq.n
                xo_v[(tg, 'mm')] = mm.n
            E3_ST = stq.n
            es_wo.close()

            es_ff = ExitStack()
            sbf = mk_sb(es_ff)
            w1_sb = sbf("w1f_sb", [128, ND * 5460], BF16)
            w2_sb = sbf("w2f_sb", [128, 22 * D], BF16)
            xc2 = sbf("xc2", [128, 2 * ND * 257], F32)
            xs3 = sbf("xs3", [128, 2 * ND * 256], BF16)
            sq3 = sbf("sq3", [128, 256], BF16)
            rows3 = sbf("rows3", [1, 4 * 256], F32)
            rows3b = sbf("rows3b", [1, 2 * 256], BF16)
            mb3 = sbf("mb3", [128, 256], F32)
            ib3 = sbf("ib3", [128, 256], F32)
            xnl = sbf("xnl", [128, ND * 256], BF16)
            gel = sbf("gel", [128, 2 * 256], BF16)
            hfb = sbf("hfb", [128, 2 * 256], BF16)
            xo2 = sbf("xo2", [128, 2 * 256], F32)

            @block.gpsimd
            def _(g, l=l):
                g.wait_ge(mm.h, mm.n)
                g.wait_ge(sc.h, sc.n)
                g.wait_ge(vq.h, vq.n)
                g.wait_ge(stq.h, stq.n)
                g.wait_ge(ld.h, ld.n)
                for i in range(ND):
                    g.dma_start(out=w1_sb[:, i * 5460:(i + 1) * 5460],
                                in_=W1l[l, i * 128:(i + 1) * 128, :]).then_inc(ld.h, 16)
                for i in range(21):
                    g.dma_start(out=w2_sb[:, i * D:(i + 1) * D],
                                in_=W2l[l, i * 128:(i + 1) * 128, :]).then_inc(ld.h, 16)
                g.dma_start(out=w2_sb[0:42, 21 * D:22 * D],
                            in_=W2l[l, 2688:2730, :]).then_inc(ld.h, 16)
            ld.n += 16 * (ND + 22)
            WFLD = ld.n

            FPAIR = [(j * 128, min(128, 2730 - j * 128)) for j in range(22)]
            ns_v = {}
            ns_st = {}
            for ns in range(NSG):
                t0 = ns * NS
                nb = ns % 2

                @block.sync
                def _(sy, ns=ns, t0=t0, nb=nb, E3_ST=E3_ST, _v=vq.n, _m=mm.n):
                    if ns == 0:
                        sy.wait_ge(stq.h, E3_ST)
                        sy.wait_ge(vq.h, _v)
                        sy.wait_ge(mm.h, _m)
                    if ns >= 2:
                        sy.wait_ge(vq.h, ns_v[ns - 2])
                        sy.wait_ge(mm.h, ns_v[(ns - 2, 'mm')])
                    if ns == 0:
                        sy.dma_start(out=xc2[:, nb * ND * 257:(nb + 1) * ND * 257]
                                     .rearrange("p (c t) -> p c t", c=ND)[:, :, 1:257],
                                     in_=xsP.rearrange("(c p) t -> p c t", p=128)
                                     [:, :, 0:256]).then_inc(ld.h, 16)
                    else:
                        sy.dma_start(out=xc2[:, nb * ND * 257:(nb + 1) * ND * 257]
                                     .rearrange("p (c t) -> p c t", c=ND),
                                     in_=xsP.rearrange("(c p) t -> p c t", p=128)
                                     [:, :, t0 - 1:t0 + 256]).then_inc(ld.h, 16)
                ld.n += 16
                X2LD = ld.n

                @block.vector
                def _(v, ns=ns, nb=nb, X2LD=X2LD):
                    v.wait_ge(ld.h, X2LD)
                    vq.wait(v)
                    if ns == 0:
                        vq.inc(v.memset(xc2[:, nb * ND * 257:(nb + 1) * ND * 257]
                                        .rearrange("p (c t) -> p c t", c=ND)[:, :, 0:1], 0.0))
                        vq.wait(v)
                    for dc in range(ND):
                        off = 1 if dc < 4 else 0
                        vq.inc(v.tensor_copy(
                            xs3[:, nb * ND * 256 + dc * 256:nb * ND * 256 + (dc + 1) * 256],
                            xc2[:, nb * ND * 257 + dc * 257 + off:
                                nb * ND * 257 + dc * 257 + off + 256]))
                        vq.wait(v)
                v_xs3 = vq.n

                @block.tensor
                def _(t, ns=ns, nb=nb, v_xs3=v_xs3):
                    t.wait_ge(vq.h, v_xs3)
                    t.wait_ge(ld.h, WFLD)
                    t.wait_ge(vq.h, vq.n)
                    for dc in range(ND):
                        mm.inc(t.matmul(bank(4, 0, 256)[0:1, :], ones128x1[:],
                                        xs3[:, nb * ND * 256 + dc * 256:
                                            nb * ND * 256 + (dc + 1) * 256],
                                        start=(dc == 0), stop=(dc == ND - 1)))
                m_sum = mm.n

                sqv = {}
                for dc in range(ND):
                    @block.vector
                    def _(v, ns=ns, nb=nb, dc=dc):
                        if dc >= 1:
                            v.wait_ge(mm.h, sqv[dc - 1])
                        vq.wait(v)
                        vq.inc(v.tensor_tensor(out=sq3[:],
                                               in0=xs3[:, nb * ND * 256 + dc * 256:
                                                       nb * ND * 256 + (dc + 1) * 256],
                                               in1=xs3[:, nb * ND * 256 + dc * 256:
                                                       nb * ND * 256 + (dc + 1) * 256],
                                               op=ALU.mult))
                    v_sq3 = vq.n

                    @block.tensor
                    def _(t, dc=dc, v_sq3=v_sq3):
                        t.wait_ge(vq.h, v_sq3)
                        mm.inc(t.matmul(bank(4, 256, 512)[0:1, :], ones128x1[:], sq3[:],
                                        start=(dc == 0), stop=(dc == ND - 1)))
                    sqv[dc] = mm.n
                m_sq = mm.n

                @block.vector
                def _(v, m_sq=m_sq):
                    v.wait_ge(mm.h, m_sq)
                    vq.wait(v)
                    vq.inc(v.tensor_scalar(out=rows3[:, 0:256], in0=bank(4, 0, 256)[0:1, :],
                                           scalar1=-1.0 / D, scalar2=None, op0=ALU.mult))
                    vq.wait(v)
                    vq.inc(v.tensor_tensor(out=rows3[:, 256:512], in0=rows3[:, 0:256],
                                           in1=rows3[:, 0:256], op=ALU.mult))
                    vq.wait(v)
                    vq.inc(v.scalar_tensor_tensor(out=rows3[:, 256:512],
                                                  in0=bank(4, 256, 512)[0:1, :],
                                                  scalar=1.0 / D, in1=rows3[:, 256:512],
                                                  op0=ALU.mult, op1=ALU.subtract))
                    vq.wait(v)
                    vq.inc(v.tensor_scalar(out=rows3[:, 256:512], in0=rows3[:, 256:512],
                                           scalar1=1e-5, scalar2=None, op0=ALU.add))
                v_st1 = vq.n
                SB4 = sc.n
                sc_rsq = SB4 + 2
                sc_bc = SB4 + 4
                sc_gelu = {jp: SB4 + 5 + jp for jp in range(22)}

                @block.vector
                def _(v, sc_rsq=sc_rsq):
                    v.wait_ge(sc.h, sc_rsq)
                    vq.wait(v)
                    vq.inc(v.tensor_copy(rows3b[:, 0:256], rows3[:, 0:256]))
                    vq.wait(v)
                    vq.inc(v.tensor_copy(rows3b[:, 256:512], rows3[:, 768:1024]))
                v_cast = vq.n

                @block.tensor
                def _(t, v_cast=v_cast):
                    t.wait_ge(vq.h, v_cast)
                    t.wait_ge(sc.h, sc.n)
                    mm.inc(t.matmul(bank(5, 0, 256), ones1x128[:], rows3b[:, 0:256],
                                    start=True, stop=True))
                    mm.inc(t.matmul(bank(5, 256, 512), ones1x128[:], rows3b[:, 256:512],
                                    start=True, stop=True))
                m_bc = mm.n


                for dc in range(ND):
                    @block.vector
                    def _(v, ns=ns, nb=nb, dc=dc, sc_bc=sc_bc):
                        v.wait_ge(sc.h, sc_bc)
                        vq.wait(v)
                        vq.inc(v.tensor_tensor(out=xo2[:, 0:256],
                                               in0=xs3[:, nb * ND * 256 + dc * 256:
                                                       nb * ND * 256 + (dc + 1) * 256],
                                               in1=mb3[:], op=ALU.add))
                        vq.wait(v)
                        vq.inc(v.tensor_tensor(out=xo2[:, 0:256], in0=xo2[:, 0:256],
                                               in1=ib3[:], op=ALU.mult))
                        vq.wait(v)
                        vq.inc(v.tensor_scalar(out=xnl[:, dc * 256:(dc + 1) * 256],
                                               in0=xo2[:, 0:256],
                                               scalar1=gt[:, dc:dc + 1],
                                               scalar2=bt[:, dc:dc + 1],
                                               op0=ALU.mult, op1=ALU.add))
                XN_V = vq.n

                TB4 = mm.n
                m_h = {jp: TB4 + 24 * jp + 16 for jp in range(22)}
                m_zz = {jp: TB4 + 24 * (jp + 1) for jp in range(22)}
                hz_v = {jp: vq.n + jp + 1 for jp in range(22)}

                @block.scalar
                def _(s, ns=ns, v_st1=v_st1, m_bc=m_bc):
                    s.wait_ge(vq.h, v_st1)
                    sc.inc(s.activation(rows3[:, 512:768], rows3[:, 256:512], AF.Ln))
                    sc.inc(s.activation(rows3[:, 768:1024], rows3[:, 512:768],
                                        AF.Exp, scale=-0.5))
                    s.wait_ge(mm.h, m_bc)
                    sc.inc(s.copy(mb3[:], bank(5, 0, 256)))
                    sc.inc(s.copy(ib3[:], bank(5, 256, 512)))
                    for jp, (c0, w) in enumerate(FPAIR):
                        jb2 = jp % 2
                        s.wait_ge(mm.h, m_h[jp])
                        if jp >= 2:
                            s.wait_ge(vq.h, hz_v[jp - 2])
                        sc.inc(s.activation(gel[0:w, jb2 * 256:(jb2 + 1) * 256],
                                            bank(2 + jb2, 0, 256)[0:w, :],
                                            AF.Gelu, bias=b1t[0:w, 22 + jp:23 + jp]))
                assert sc.n == sc_gelu[21] + 0 * ns, (sc.n,)

                @block.vector
                def _(v, ns=ns):
                    for jp, (c0, w) in enumerate(FPAIR):
                        jb2 = jp % 2
                        v.wait_ge(sc.h, sc_gelu[jp])
                        if jp >= 2:
                            v.wait_ge(mm.h, m_zz[jp - 2])
                        vq.inc(v.scalar_tensor_tensor(
                            out=hfb[0:w, jb2 * 256:(jb2 + 1) * 256],
                            in0=bank(jb2, 0, 256)[0:w, :],
                            scalar=b1t[0:w, jp:jp + 1],
                            in1=gel[0:w, jb2 * 256:(jb2 + 1) * 256],
                            op0=ALU.add, op1=ALU.mult))
                assert vq.n == hz_v[21], (vq.n,)

                @block.tensor
                def _(t, ns=ns, XN_V=XN_V):
                    for jp, (c0, w) in enumerate(FPAIR):
                        jb2 = jp % 2
                        t.wait_ge(vq.h, XN_V)
                        if jp >= 2:
                            t.wait_ge(vq.h, hz_v[jp - 2])
                            t.wait_ge(sc.h, sc_gelu[jp - 2])
                        for dc in range(ND):
                            mm.inc(t.matmul(bank(jb2, 0, 256)[0:w, :],
                                            w1_sb[:, dc * 5460 + c0:dc * 5460 + c0 + w],
                                            xnl[:, dc * 256:(dc + 1) * 256],
                                            start=(dc == 0), stop=(dc == ND - 1)))
                        for dc in range(ND):
                            mm.inc(t.matmul(bank(2 + jb2, 0, 256)[0:w, :],
                                            w1_sb[:, dc * 5460 + 2730 + c0:
                                                  dc * 5460 + 2730 + c0 + w],
                                            xnl[:, dc * 256:(dc + 1) * 256],
                                            start=(dc == 0), stop=(dc == ND - 1)))
                        t.wait_ge(vq.h, hz_v[jp])
                        for dt in range(ND):
                            mm.inc(t.matmul(bank(4 + dt // 2, (dt % 2) * 256,
                                                 (dt % 2) * 256 + 256),
                                            w2_sb[0:w, jp * D + dt * 128:
                                                  jp * D + (dt + 1) * 128],
                                            hfb[0:w, jb2 * 256:(jb2 + 1) * 256],
                                            start=(jp == 0 and dt % 2 == 0),
                                            stop=(jp == 21 and dt % 2 == 1),
                                            skip_group_check=True))
                assert mm.n == m_zz[21], (mm.n,)
                m_z = mm.n

                zc_v = {}
                for dt in range(ND):
                    @block.vector
                    def _(v, ns=ns, nb=nb, dt=dt, m_z=m_z):
                        v.wait_ge(mm.h, m_z)
                        if dt >= 2:
                            v.wait_ge(stq.h, zc_v[(dt - 2, 'st')])
                        vq.wait(v)
                        vq.inc(v.scalar_tensor_tensor(
                            out=xo2[:, (dt % 2) * 256:(dt % 2) * 256 + 256],
                            in0=bank(4 + dt // 2, (dt % 2) * 256, (dt % 2) * 256 + 256),
                            scalar=b2t[:, dt:dt + 1],
                            in1=xc2[:, nb * ND * 257 + dt * 257 + 1:
                                    nb * ND * 257 + dt * 257 + 257],
                            op0=ALU.add, op1=ALU.add))
                    zc_v[dt] = vq.n

                    @block.sync
                    def _(sy, ns=ns, t0=t0, dt=dt, fout=fout):
                        sy.wait_ge(vq.h, zc_v[dt])
                        sy.dma_start(out=fout[dt * 128:(dt + 1) * 128, t0:t0 + 256],
                                     in_=xo2[:, (dt % 2) * 256:(dt % 2) * 256 + 256]
                                     ).then_inc(stq.h, 16)
                    stq.n += 16
                    zc_v[(dt, 'st')] = stq.n
                ns_v[ns] = vq.n
                ns_v[(ns, 'mm')] = mm.n
                ns_st[ns] = stq.n
            es_ff.close()
            prev_e4_st = stq.n
    return nc



def build_fin(env=None, io=None):
    if env is None:
        nc = bass.Bass("TRN2", target_bir_lowering=False)
        x1 = nc.dram_tensor("x1", [T, D], F32, kind="ExternalInput").ap()
        s1f = nc.dram_tensor("s1f", [D, T], F32, kind="ExternalInput").ap()
        g1f = nc.dram_tensor("g1f", [D, T], F32, kind="ExternalInput").ap()
        y = nc.dram_tensor("y", [T, D], F32, kind="ExternalOutput").ap()
        cst = declare_consts(nc, ["identf"])
    else:
        nc = env.nc
        cst = env.cst
        x1, s1f, g1f, y = io
    P_ = "Fp" if env else ""
    with ExitStack() as es:
        def sb(name, shape, dt):
            return es.enter_context(nc.sbuf_tensor(P_ + name, shape, dt))

        identf = sb("identf", [128, 128], F32)
        s1c = sb("s1c", [128, 2 * 512], F32)
        g1c = sb("g1c", [128, 2 * 512], F32)
        sgc = sb("sgc", [128, 512], F32)
        xacc = sb("xacc", [128, 2 * 4 * D], F32)
        pp = es.enter_context(nc.psum_tensor(P_ + "pp", [128, 4 * 512], F32))
        if env is None:
            block = es.enter_context(nc.Block())
            ld = SemC(nc, "ld"); vq = SemC(nc, "vq"); sc = SemC(nc, "sc")
            mm = SemC(nc, "mm"); stq = SemC(nc, "stq")
        else:
            block = env.block
            ld, vq, sc, mm, stq = env.sems

        @block.sync
        def _(sy):
            sy.dma_start(out=identf[:], in_=cst["identf"]).then_inc(ld.h, 16)
        ld.n += 16

        tg_v = {}
        tg_st = {}
        for tg in range(NTG):
            b = tg % 2

            @block.sync
            def _(sy, tg=tg, b=b):
                if tg >= 2:
                    sy.wait_ge(stq.h, tg_st[tg - 2])
                    sy.wait_ge(vq.h, tg_v[tg - 2])
                sy.dma_start(out=xacc[:, b * 4 * D:(b + 1) * 4 * D]
                             .rearrange("p (w d) -> p w d", w=4),
                             in_=x1.rearrange("(w p) d -> p w d", p=128)
                             [:, 4 * tg:4 * tg + 4, :]).then_inc(ld.h, 16)
            ld.n += 16

            dc_v = {}
            for dc in range(ND):
                cb = dc % 2

                @block.sync
                def _(sy, tg=tg, dc=dc, cb=cb):
                    if dc >= 2:
                        sy.wait_ge(vq.h, dc_v[dc - 2])
                    sy.dma_start(out=s1c[:, cb * 512:(cb + 1) * 512],
                                 in_=s1f[dc * 128:(dc + 1) * 128, tg * TG:(tg + 1) * TG]
                                 ).then_inc(ld.h, 16)
                    sy.dma_start(out=g1c[:, cb * 512:(cb + 1) * 512],
                                 in_=g1f[dc * 128:(dc + 1) * 128, tg * TG:(tg + 1) * TG]
                                 ).then_inc(ld.h, 16)
                ld.n += 32
                CLD = ld.n

                @block.scalar
                def _(s, cb=cb, CLD=CLD):
                    s.wait_ge(ld.h, CLD)
                    if True:
                        s.wait_ge(vq.h, vq.n)
                    sc.inc(s.activation(g1c[:, cb * 512:(cb + 1) * 512],
                                        g1c[:, cb * 512:(cb + 1) * 512], AF.Silu))
                sc_si = sc.n

                @block.vector
                def _(v, cb=cb, sc_si=sc_si):
                    v.wait_ge(sc.h, sc_si)
                    v.wait_ge(mm.h, mm.n)
                    vq.wait(v)
                    vq.inc(v.tensor_tensor(out=sgc[:], in0=s1c[:, cb * 512:(cb + 1) * 512],
                                           in1=g1c[:, cb * 512:(cb + 1) * 512],
                                           op=ALU.mult))
                v_sg = vq.n

                for w in range(4):
                    @block.tensor
                    def _(t, w=w, v_sg=v_sg):
                        t.wait_ge(vq.h, v_sg)
                        t.wait_ge(vq.h, vq.n)
                        mm.inc(_tmm(t, pp[:, w * 512:w * 512 + 128],
                                           sgc[:, w * 128:(w + 1) * 128], identf[:]))
                    m_tp = mm.n

                    @block.vector
                    def _(v, tg=tg, b=b, dc=dc, w=w, m_tp=m_tp):
                        v.wait_ge(mm.h, m_tp)
                        vq.wait(v)
                        dst = xacc[:, b * 4 * D + w * D + dc * 128:
                                   b * 4 * D + w * D + (dc + 1) * 128]
                        vq.inc(v.tensor_tensor(out=dst, in0=dst,
                                               in1=pp[:, w * 512:w * 512 + 128],
                                               op=ALU.add))
                dc_v[dc] = vq.n
            tg_v[tg] = vq.n

            @block.sync
            def _(sy, tg=tg, b=b):
                sy.wait_ge(vq.h, tg_v[tg])
                sy.dma_start(out=y.rearrange("(w p) d -> p w d", p=128)
                             [:, 4 * tg:4 * tg + 4, :],
                             in_=xacc[:, b * 4 * D:(b + 1) * 4 * D]
                             .rearrange("p (w d) -> p w d", w=4)).then_inc(stq.h, 16)
            stq.n += 16
            tg_st[tg] = stq.n

        @block.sync
        def _(sy):
            sy.wait_ge(stq.h, stq.n)
    return nc



def build_fused():
    nc = bass.Bass("TRN2", target_bir_lowering=False)
    EI = "ExternalInput"
    x = nc.dram_tensor("x", [T, D], F32, kind=EI).ap()
    Wq = nc.dram_tensor("Wq", [D, D], F32, kind=EI).ap()
    Wk = nc.dram_tensor("Wk", [D, 512], F32, kind=EI).ap()
    Wv = nc.dram_tensor("Wv", [D, 512], F32, kind=EI).ap()
    Wo = nc.dram_tensor("Wo", [D, D], F32, kind=EI).ap()
    bo = nc.dram_tensor("bo", [8, 128], F32, kind=EI).ap()
    Wr = nc.dram_tensor("Wr", [D, 8], F32, kind=EI).ap()
    br = nc.dram_tensor("br", [1, 8], F32, kind=EI).ap()
    W1m = nc.dram_tensor("W1m", [8, D, 4096], F32, kind=EI).ap()
    b1m = nc.dram_tensor("b1m", [8, 4096], F32, kind=EI).ap()
    W2m = nc.dram_tensor("W2m", [8, 4096, D], F32, kind=EI).ap()
    b2m = nc.dram_tensor("b2m", [8, D], F32, kind=EI).ap()
    B_w = nc.dram_tensor("B_w", [128, D], F32, kind=EI).ap()
    C_w = nc.dram_tensor("C_w", [D, 128], F32, kind=EI).ap()
    logA = nc.dram_tensor("logA", [1, 128], F32, kind=EI).ap()
    D_p = nc.dram_tensor("D_p", [8, 128], F32, kind=EI).ap()
    Wqkv = nc.dram_tensor("Wqkv", [6, D, 3072], F32, kind=EI).ap()
    q_sc = nc.dram_tensor("q_sc", [6, 64], F32, kind=EI).ap()
    k_sc = nc.dram_tensor("k_sc", [6, 64], F32, kind=EI).ap()
    Wo1 = nc.dram_tensor("Wo1", [6, D, D], F32, kind=EI).ap()
    ln_g = nc.dram_tensor("ln_g", [6, 8, 128], F32, kind=EI).ap()
    ln_b = nc.dram_tensor("ln_b", [6, 8, 128], F32, kind=EI).ap()
    W1l = nc.dram_tensor("W1l", [6, D, 5460], F32, kind=EI).ap()
    b1l = nc.dram_tensor("b1l", [6, 5460], F32, kind=EI).ap()
    W2l = nc.dram_tensor("W2l", [6, 2730, D], F32, kind=EI).ap()
    b2l = nc.dram_tensor("b2l", [6, 8, 128], F32, kind=EI).ap()
    y = nc.dram_tensor("y", [T, D], F32, kind="ExternalOutput").ap()
    xn0f = nc.dram_tensor("xn0f", [D, T], BF16, kind="Internal").ap()
    s0f = nc.dram_tensor("s0f", [D, T], F32, kind="Internal").ap()
    x1s = nc.dram_tensor("x1s", [T, D], F32, kind="Internal").ap()
    xn1f = nc.dram_tensor("xn1f", [D, T], F32, kind="Internal").ap()
    s1f = nc.dram_tensor("s1f", [D, T], F32, kind="Internal").ap()
    g1f = nc.dram_tensor("g1f", [D, T], F32, kind="Internal").ap()
    cst = declare_consts(nc, list(CONST_SPECS.keys()))
    with nc.Block() as block:
        sems = (SemC(nc, "ld"), SemC(nc, "vq"), SemC(nc, "sc"),
                SemC(nc, "mm"), SemC(nc, "stq"))
        env = FusedEnv(nc, block, sems, cst)
        build_ln0(env, (x, xn0f))
        env.barrier()
        build_l0attn(env, (xn0f, Wq, Wk, Wv, Wo, bo, s0f))
        env.barrier()
        build_l0moe(env, (xn0f, x, s0f, Wr, br, W1m, b1m, W2m, b2m, x1s))
        env.barrier()
        build_lnssm(env=env, io=(x1s, B_w, C_w, logA, D_p, xn1f, s1f))
        env.barrier()
        build_subt(env=env, io=(xn1f, Wqkv, q_sc, k_sc, Wo1, ln_g, ln_b,
                                W1l, b1l, W2l, b2l, g1f))
        env.barrier()
        build_fin(env=env, io=(x1s, s1f, g1f, y))
    return nc



_NC_CACHE = {}


def _get_nc(name, builder):
    if name not in _NC_CACHE:
        _NC_CACHE[name] = builder()
    return _NC_CACHE[name]


def kernel(**inputs):
    consts = host_consts()
    cmap = {"cst_" + k: _np_c(v) for k, v in consts.items()}
    x_full = _f32(inputs["L0_attn_x"])
    base = dict(cmap)
    base.update({
        "Wq": _f32(inputs["L0_attn_Wq"]), "Wk": _f32(inputs["L0_attn_Wk"]),
        "Wv": _f32(inputs["L0_attn_Wv"]), "Wo": _f32(inputs["L0_attn_Wo"]),
        "bo": _f32(inputs["L0_attn_bo"]).reshape(8, 128),
        "Wr": _f32(inputs["L0_ffn_Wr"]),
        "br": _f32(inputs["L0_ffn_br"]).reshape(1, 8),
        "W1m": _f32(inputs["L0_ffn_W1"]), "b1m": _f32(inputs["L0_ffn_b1"]),
        "W2m": _f32(inputs["L0_ffn_W2"]), "b2m": _f32(inputs["L0_ffn_b2"]),
        "B_w": _f32(inputs["L1_attn_B_w"]), "C_w": _f32(inputs["L1_attn_C_w"]),
        "logA": _f32(inputs["L1_attn_log_A"]).reshape(1, 128),
        "D_p": _f32(inputs["L1_attn_D_param"]).reshape(8, 128),
        "Wqkv": _f32(inputs["L1_ffn_Wqkv"]), "q_sc": _f32(inputs["L1_ffn_q_scale"]),
        "k_sc": _f32(inputs["L1_ffn_k_scale"]), "Wo1": _f32(inputs["L1_ffn_Wo"]),
        "ln_g": _f32(inputs["L1_ffn_ln_g"]).reshape(6, 8, 128),
        "ln_b": _f32(inputs["L1_ffn_ln_b"]).reshape(6, 8, 128),
        "W1l": _f32(inputs["L1_ffn_W1"]), "b1l": _f32(inputs["L1_ffn_b1"]),
        "W2l": _f32(inputs["L1_ffn_W2"]),
        "b2l": _f32(inputs["L1_ffn_b2"]).reshape(6, 8, 128),
    })
    nc = _get_nc("fused", build_fused)
    maps = [dict(base, x=np.ascontiguousarray(x_full[c])) for c in range(NCORE)]
    res = run_bass_kernel_spmd(nc, maps, core_ids=list(range(NCORE)))
    return np.stack([np.asarray(res.results[c]["y"]) for c in range(NCORE)])


def kernel_multi(**inputs):
    consts = host_consts()
    cmap = {"cst_" + k: _np_c(v) for k, v in consts.items()}

    def pick(m, names):
        return {k: m["cst_" + k] for k in names}

    x_full = _f32(inputs["L0_attn_x"])
    outs = np.empty((NCORE, T, D), np.float32)

    nc1 = _get_nc("ln0", build_ln0)
    maps1 = [{"x": np.ascontiguousarray(x_full[c]), "cst_ident": cmap["cst_ident"]}
             for c in range(NCORE)]
    r1 = run_bass_kernel_spmd(nc1, maps1, core_ids=list(range(NCORE)))
    xn0f = [r1.results[c]["xn0f"] for c in range(NCORE)]

    nc2 = _get_nc("l0attn", build_l0attn)
    w2 = {"Wq": _f32(inputs["L0_attn_Wq"]), "Wk": _f32(inputs["L0_attn_Wk"]),
          "Wv": _f32(inputs["L0_attn_Wv"]), "Wo": _f32(inputs["L0_attn_Wo"]),
          "bo": _f32(inputs["L0_attn_bo"]).reshape(8, 128)}
    c2 = {k: cmap[k] for k in ("cst_identf", "cst_c0q", "cst_s0q", "cst_c0k",
                               "cst_s0k", "cst_R0", "cst_ones_64x1",
                               "cst_ones_1x128", "cst_ones_1x64")}
    maps2 = [dict(c2, xn0f=xn0f[c], **w2) for c in range(NCORE)]
    r2 = run_bass_kernel_spmd(nc2, maps2, core_ids=list(range(NCORE)))
    s0f = [r2.results[c]["s0f"] for c in range(NCORE)]

    nc3 = _get_nc("l0moe", build_l0moe)
    w3 = {"Wr": _f32(inputs["L0_ffn_Wr"]),
          "br": _f32(inputs["L0_ffn_br"]).reshape(1, 8),
          "W1m": _f32(inputs["L0_ffn_W1"]), "b1m": _f32(inputs["L0_ffn_b1"]),
          "W2m": _f32(inputs["L0_ffn_W2"]), "b2m": _f32(inputs["L0_ffn_b2"])}
    c3 = {k: cmap[k] for k in ("cst_ident", "cst_identf", "cst_ones_1x128",
                               "cst_iota_w1", "cst_iota_w2", "cst_iota8")}
    maps3 = [dict(c3, xn0f=xn0f[c], x=np.ascontiguousarray(x_full[c]),
                  s0f=s0f[c], **w3) for c in range(NCORE)]
    r3 = run_bass_kernel_spmd(nc3, maps3, core_ids=list(range(NCORE)))
    x1 = [r3.results[c]["x1"] for c in range(NCORE)]

    nc4 = _get_nc("lnssm", build_lnssm)
    w4 = {"B_w": _f32(inputs["L1_attn_B_w"]), "C_w": _f32(inputs["L1_attn_C_w"]),
          "logA": _f32(inputs["L1_attn_log_A"]).reshape(1, 128),
          "D_p": _f32(inputs["L1_attn_D_param"]).reshape(8, 128)}
    c4 = {k: cmap[k] for k in ("cst_ident", "cst_identf")}
    maps4 = [dict(c4, x1=x1[c], **w4) for c in range(NCORE)]
    r4 = run_bass_kernel_spmd(nc4, maps4, core_ids=list(range(NCORE)))
    xn1f = [r4.results[c]["xn1f"] for c in range(NCORE)]
    s1f = [r4.results[c]["s1f"] for c in range(NCORE)]

    nc5 = _get_nc("subt", build_subt)
    w5 = {"Wqkv": _f32(inputs["L1_ffn_Wqkv"]), "q_sc": _f32(inputs["L1_ffn_q_scale"]),
          "k_sc": _f32(inputs["L1_ffn_k_scale"]), "Wo1": _f32(inputs["L1_ffn_Wo"]),
          "ln_g": _f32(inputs["L1_ffn_ln_g"]).reshape(6, 8, 128),
          "ln_b": _f32(inputs["L1_ffn_ln_b"]).reshape(6, 8, 128),
          "W1l": _f32(inputs["L1_ffn_W1"]), "b1l": _f32(inputs["L1_ffn_b1"]),
          "W2l": _f32(inputs["L1_ffn_W2"]),
          "b2l": _f32(inputs["L1_ffn_b2"]).reshape(6, 8, 128)}
    c5 = {k: cmap[k] for k in ("cst_ident", "cst_identf", "cst_c1q", "cst_s1q",
                               "cst_c1k", "cst_s1k", "cst_R1", "cst_S2", "cst_S2T",
                               "cst_ID2", "cst_Bband", "cst_ones_1x128",
                               "cst_ones_128x1")}
    maps5 = [dict(c5, xn1f=xn1f[c], **w5) for c in range(NCORE)]
    r5 = run_bass_kernel_spmd(nc5, maps5, core_ids=list(range(NCORE)))
    g1f = [r5.results[c]["g1f"] for c in range(NCORE)]

    nc6 = _get_nc("fin", build_fin)
    maps6 = [{"cst_identf": cmap["cst_identf"], "x1": x1[c], "s1f": s1f[c],
              "g1f": g1f[c]} for c in range(NCORE)]
    r6 = run_bass_kernel_spmd(nc6, maps6, core_ids=list(range(NCORE)))
    for c in range(NCORE):
        outs[c] = np.asarray(r6.results[c]["y"])
    return outs


def _np_c(v):
    return np.ascontiguousarray(v)


# version 06
# speedup vs baseline: 1.6602x; 1.0655x over previous
import numpy as np
import ml_dtypes
from contextlib import ExitStack
import concourse.bass as bass
import concourse.mybir as mybir
from concourse.bass_utils import run_bass_kernel_spmd

F32 = mybir.dt.float32
BF16 = mybir.dt.bfloat16
U32 = mybir.dt.uint32
AF = mybir.ActivationFunctionType
ALU = mybir.AluOpType
AX = mybir.AxisListType

T = 4096
D = 1024
NCORE = 8
NT = T // 128
ND = D // 128
TG = 512
NTG = T // TG

bfdt = ml_dtypes.bfloat16


def _bf(x):
    return np.ascontiguousarray(np.asarray(x, dtype=bfdt))


def _f32(x):
    return np.ascontiguousarray(np.asarray(x, dtype=np.float32))


class SemC:
    def __init__(self, nc, name):
        self.h = nc.semaphore(name).__enter__()
        self.n = 0

    def inc(self, instr, k=1):
        instr.then_inc(self.h, k)
        self.n += k
        return self.n

    def wait(self, eng, v=None):
        eng.wait_ge(self.h, self.n if v is None else v)



def host_consts():
    c = {}
    ident = np.eye(128, dtype=np.float32)
    c["ident"] = _bf(ident)
    c["identf"] = _f32(ident)
    ROT = 32
    inv_freq = 1.0 / (10000.0 ** (np.arange(0, ROT, 2, dtype=np.float64) / ROT))
    t_ar = np.arange(T, dtype=np.float64)
    freqs = t_ar[:, None] * inv_freq[None, :]
    cosr = np.repeat(np.cos(freqs), 2, axis=-1)
    sinr = np.repeat(np.sin(freqs), 2, axis=-1)
    SCALE = 64.0 ** (-0.5)
    cq = np.ones((64, T)); sq_ = np.zeros((64, T))
    cq[:32, :] = cosr.T; sq_[:32, :] = sinr.T
    c["c0q"] = _bf(cq * SCALE); c["s0q"] = _bf(sq_ * SCALE)
    c["c0k"] = _bf(cq); c["s0k"] = _bf(sq_)
    R0 = np.zeros((64, 64), dtype=np.float32)
    for i in range(16):
        R0[2 * i + 1, 2 * i] = -1.0
        R0[2 * i, 2 * i + 1] = 1.0
    c["R0"] = _bf(R0)
    DH = 64
    inv_f1 = 1.0 / (10000.0 ** (np.arange(0, DH, 2, dtype=np.float64) / DH))
    fr1 = t_ar[:, None] * inv_f1[None, :]
    fr1 = np.concatenate([fr1, fr1], axis=-1)
    base_scale = (np.arange(0, DH, 2, dtype=np.float64) + 0.4 * DH) / (1.4 * DH)
    power = (t_ar - T // 2) / (256 // 2)
    scal = base_scale[None, :] ** power[:, None]
    scal = np.concatenate([scal, scal], axis=-1)
    cos1 = np.cos(fr1); sin1 = np.sin(fr1)
    QKS = 8.0
    csq = (cos1 * scal * QKS).T; ssq = (sin1 * scal * QKS).T
    csk = (cos1 / scal).T; ssk = (sin1 / scal).T
    c["c1q"] = _bf(np.concatenate([csq, csq], axis=0))
    c["s1q"] = _bf(np.concatenate([ssq, ssq], axis=0))
    c["c1k"] = _bf(np.concatenate([csk, csk], axis=0))
    c["s1k"] = _bf(np.concatenate([ssk, ssk], axis=0))
    R1 = np.zeros((64, 64), dtype=np.float32)
    for i in range(32):
        R1[i + 32, i] = -1.0
        R1[i, i + 32] = 1.0
    R1s = np.zeros((128, 128), dtype=np.float32)
    R1s[:64, :64] = R1; R1s[64:, 64:] = R1
    c["R1"] = _bf(R1s)
    ii = np.arange(128)
    c["mLOW"] = _f32((ii[:, None] >= ii[None, :]).astype(np.float32))
    c["mUP"] = _f32((ii[:, None] <= ii[None, :]).astype(np.float32))
    c["ones_1x128"] = _bf(np.ones((1, 128), np.float32))
    c["ones_64x1"] = _bf(np.ones((64, 1), np.float32))
    c["ones_128x1"] = _bf(np.ones((128, 1), np.float32))
    c["ones_1x64"] = _bf(np.ones((1, 64), np.float32))
    S2 = np.zeros((2, 128), np.float32)
    S2[0, :64] = 1.0; S2[1, 64:] = 1.0
    c["S2"] = _bf(S2)
    iw1 = np.zeros((128, 16), np.float32)
    for dc in range(8):
        for hh in range(2):
            iw1[:, dc * 2 + hh] = (dc * 128 + ii) * 2 + hh
    c["iota_w1"] = _f32(iw1)
    iw2 = np.zeros((128, 32), np.float32)
    for fc in range(32):
        iw2[:, fc] = fc * 128 + ii
    c["iota_w2"] = _f32(iw2)
    c["iota8"] = _f32(np.arange(8, dtype=np.float32).reshape(1, 8))
    S2T = np.zeros((128, 2), np.float32)
    S2T[:64, 0] = 1.0; S2T[64:, 1] = 1.0
    c["S2T"] = _bf(S2T)
    ID2 = np.zeros((64, 128), np.float32)
    for p in range(128):
        ID2[p % 64, p] = 1.0
    c["ID2"] = _bf(ID2)
    jj = np.arange(384)
    rr_ = np.arange(128)
    c["Bband"] = _f32(((jj[None, :] >= rr_[:, None]) &
                       (jj[None, :] <= rr_[:, None] + 256)).astype(np.float32))
    return c


CONST_SPECS = {
    "ident": ([128, 128], BF16), "identf": ([128, 128], F32),
    "c0q": ([64, T], BF16), "s0q": ([64, T], BF16),
    "c0k": ([64, T], BF16), "s0k": ([64, T], BF16), "R0": ([64, 64], BF16),
    "c1q": ([128, T], BF16), "s1q": ([128, T], BF16), "c1k": ([128, T], BF16),
    "s1k": ([128, T], BF16), "R1": ([128, 128], BF16),
    "mLOW": ([128, 128], F32), "mUP": ([128, 128], F32),
    "ones_1x128": ([1, 128], BF16), "ones_64x1": ([64, 1], BF16),
    "ones_128x1": ([128, 1], BF16), "ones_1x64": ([1, 64], BF16), "S2": ([2, 128], BF16),
    "iota_w1": ([128, 16], F32), "iota_w2": ([128, 32], F32), "iota8": ([1, 8], F32),
    "S2T": ([128, 2], BF16), "ID2": ([64, 128], BF16), "Bband": ([128, 384], F32),
}


class FusedEnv:
    def __init__(self, nc, block, sems, cst):
        self.nc = nc
        self.block = block
        self.sems = sems
        self.cst = cst
        self.pref = ""

    def barrier(self):
        ld, vq, sc, mm, stq = self.sems
        vals = (mm.n, vq.n, sc.n, stq.n, ld.n)

        def _bar(e, vals=vals):
            e.wait_ge(mm.h, vals[0])
            e.wait_ge(vq.h, vals[1])
            e.wait_ge(sc.h, vals[2])
            e.wait_ge(stq.h, vals[3])
            e.wait_ge(ld.h, vals[4])
        self.block.tensor(_bar)
        self.block.vector(_bar)
        self.block.scalar(_bar)
        self.block.gpsimd(_bar)
        self.block.sync(_bar)


def _tmm(t, out, in_, ident):
    return t.matmul(out, in_, ident, start=True, stop=True)


def declare_consts(nc, names):
    return {n: nc.dram_tensor("cst_" + n, CONST_SPECS[n][0], CONST_SPECS[n][1],
                              kind="ExternalInput").ap() for n in names}



def build_ln0(env=None, io=None):
    if env is None:
        nc = bass.Bass("TRN2", target_bir_lowering=False)
        x = nc.dram_tensor("x", [T, D], F32, kind="ExternalInput").ap()
        xn0f = nc.dram_tensor("xn0f", [D, T], BF16, kind="ExternalOutput").ap()
        cst = declare_consts(nc, ["ident"])
    else:
        nc = env.nc
        cst = env.cst
        x, xn0f = io
    P_ = "A" if env else ""
    with ExitStack() as es:
        ident = es.enter_context(nc.sbuf_tensor(P_ + "ident", [128, 128], BF16))
        xt = es.enter_context(nc.sbuf_tensor(P_ + "xt", [128, 2 * D], F32))
        xsq = es.enter_context(nc.sbuf_tensor(P_ + "xsq", [128, D], F32))
        st0 = es.enter_context(nc.sbuf_tensor(P_ + "st0", [128, 8], F32))
        xnt = es.enter_context(nc.sbuf_tensor(P_ + "xnt", [128, 2 * D], BF16))
        tev = es.enter_context(nc.sbuf_tensor(P_ + "tev", [128, 2 * ND * 128], BF16))
        pt = es.enter_context(nc.psum_tensor(P_ + "pt", [128, 2 * 512], F32))
        if env is None:
            block = es.enter_context(nc.Block())
            ld = SemC(nc, "ld"); vq = SemC(nc, "vq"); sc = SemC(nc, "sc")
            stq = SemC(nc, "stq")
        else:
            block = env.block
            ld, vq, sc, _mm, stq = env.sems
        tp = SemC(nc, P_ + "tp"); ev = SemC(nc, P_ + "ev")

        @block.sync
        def _(sy):
            sy.dma_start(out=ident[:], in_=cst["ident"]).then_inc(ld.h, 16)
        ld.n += 16
        LDB = ld.n

        v_end = [0] * NT
        s_end = [0] * NT
        ev_at = {}
        st_at = {}
        for tt in range(NT):
            bo = (tt % 2) * D

            @block.sync
            def _(sy, tt=tt, bo=bo):
                if tt >= 2:
                    sy.wait_ge(vq.h, v_end[tt - 2])
                    sy.wait_ge(sc.h, s_end[tt - 2])
                sy.dma_start(out=xt[:, bo:bo + D],
                             in_=x[tt * 128:(tt + 1) * 128, :]).then_inc(ld.h, 16)
            ld.n += 16

            @block.vector
            def _(v, tt=tt, bo=bo):
                v.wait_ge(ld.h, LDB + 16 * (tt + 1))
                vq.inc(v.reduce_sum(st0[:, 0:1], xt[:, bo:bo + D], axis=AX.X))

            @block.scalar
            def _(s, tt=tt, bo=bo):
                s.wait_ge(ld.h, LDB + 16 * (tt + 1))
                if tt >= 1:
                    s.wait_ge(vq.h, v_end[tt - 1])
                sc.inc(s.activation(xsq[:], xt[:, bo:bo + D], AF.Square,
                                    accum_out=st0[:, 2:3]))
            s_end[tt] = sc.n

            @block.vector
            def _(v, tt=tt, bo=bo):
                vq.wait(v)
                vq.inc(v.tensor_scalar(out=st0[:, 1:2], in0=st0[:, 0:1],
                                       scalar1=-1.0 / D, scalar2=None, op0=ALU.mult))
                vq.wait(v)
                vq.inc(v.tensor_tensor(out=st0[:, 3:4], in0=st0[:, 1:2],
                                       in1=st0[:, 1:2], op=ALU.mult))
                sc.wait(v)
                vq.wait(v)
                vq.inc(v.tensor_scalar(out=st0[:, 4:5], in0=st0[:, 2:3],
                                       scalar1=1.0 / D, scalar2=st0[:, 3:4],
                                       op0=ALU.mult, op1=ALU.subtract))
                vq.wait(v)
                vq.inc(v.tensor_scalar(out=st0[:, 5:6], in0=st0[:, 4:5],
                                       scalar1=1e-5, scalar2=None, op0=ALU.add))

            @block.scalar
            def _(s, tt=tt):
                vq.wait(s)
                sc.inc(s.activation(st0[:, 6:7], st0[:, 5:6], AF.Ln))
                sc.inc(s.activation(st0[:, 7:8], st0[:, 6:7], AF.Exp, scale=-0.5))

            @block.vector
            def _(v, tt=tt, bo=bo):
                sc.wait(v)
                vq.inc(v.tensor_scalar(out=xnt[:, bo:bo + D], in0=xt[:, bo:bo + D],
                                       scalar1=st0[:, 1:2], scalar2=st0[:, 7:8],
                                       op0=ALU.add, op1=ALU.mult))
            v_end[tt] = vq.n

            tb = (tt % 2) * ND
            for dc in range(ND):
                k = tt * ND + dc
                pb = k % 2

                @block.tensor
                def _(t, tt=tt, dc=dc, pb=pb, bo=bo, k=k):
                    t.wait_ge(vq.h, v_end[tt])
                    if k >= 2:
                        t.wait_ge(ev.h, ev_at[k - 2])
                    tp.inc(_tmm(t, pt[:, pb * 512:pb * 512 + 128],
                                xnt[:, bo + dc * 128:bo + (dc + 1) * 128], ident[:]))

                @block.scalar
                def _(s, pb=pb, tb=tb, dc=dc, tt=tt):
                    tp.wait(s)
                    if dc == 0 and tt >= 2:
                        s.wait_ge(stq.h, st_at[tt - 2])
                    ev.inc(s.copy(tev[:, (tb + dc) * 128:(tb + dc + 1) * 128],
                                  pt[:, pb * 512:pb * 512 + 128]))
                ev_at[k] = ev.n

            @block.sync
            def _(sy, tt=tt, tb=tb):
                ev.wait(sy)
                sy.dma_start(out=xn0f.rearrange("(c p) t -> p c t", p=128)
                             [:, :, tt * 128:(tt + 1) * 128],
                             in_=tev[:, tb * 128:(tb + ND) * 128]
                             .rearrange("p (c t) -> p c t", c=ND)).then_inc(stq.h, 16)
            stq.n += 16
            st_at[tt] = stq.n

        @block.sync
        def _(sy):
            sy.wait_ge(stq.h, stq.n)
    return nc



def build_l0attn(env=None, io=None):
    if env is None:
        nc = bass.Bass("TRN2", target_bir_lowering=False)
        xn0f = nc.dram_tensor("xn0f", [D, T], BF16, kind="ExternalInput").ap()
        Wq = nc.dram_tensor("Wq", [D, D], F32, kind="ExternalInput").ap()
        Wk = nc.dram_tensor("Wk", [D, 512], F32, kind="ExternalInput").ap()
        Wv = nc.dram_tensor("Wv", [D, 512], F32, kind="ExternalInput").ap()
        Wo = nc.dram_tensor("Wo", [D, D], F32, kind="ExternalInput").ap()
        bo = nc.dram_tensor("bo", [8, 128], F32, kind="ExternalInput").ap()
        s0f = nc.dram_tensor("s0f", [D, T], F32, kind="ExternalOutput").ap()
        cst = declare_consts(nc, ["identf", "c0q", "s0q", "c0k", "s0k", "R0",
                                  "ones_64x1", "ones_1x128", "ones_1x64"])
    else:
        nc = env.nc
        cst = env.cst
        xn0f, Wq, Wk, Wv, Wo, bo, s0f = io
    P_ = "B" if env else ""
    oat = nc.dram_tensor(P_ + "oat", [D, T], BF16, kind="Internal").ap()
    kscr = nc.dram_tensor(P_ + "kscr", [512, T], BF16, kind="Internal").ap()
    NH, NKV = 16, 8
    with ExitStack() as es:
        def mk_sb(stack):
            def sb(name, shape, dt):
                return stack.enter_context(nc.sbuf_tensor(P_ + name, shape, dt))
            return sb

        sb = mk_sb(es)
        xn = sb("xn", [128, ND * T], BF16)
        cqt = sb("cq", [64, T], BF16)
        sqt = sb("sq", [64, T], BF16)
        ckt = sb("ck", [64, T], BF16)
        skt = sb("sk", [64, T], BF16)
        R0s = sb("R0s", [64, 64], BF16)
        identf = sb("identf", [128, 128], F32)
        ones64 = sb("ones64", [64, 1], BF16)
        ones1x128 = sb("ones1x128", [1, 128], BF16)
        ones1x64 = sb("ones1x64", [1, 64], BF16)
        vaug = sb("vaug", [128, NT * 520], BF16)
        mk2_all = sb("mk2_all", [1, NKV], F32)
        macc = sb("macc", [1, 4], F32)
        chbf = sb("chbf", [1, 1], BF16)
        bias_h = sb("bias_h", [128, 1], F32)
        rr = sb("rr", [1, 512], BF16)
        rrl = sb("rrl", [1, 512], F32)
        rb = sb("rb", [64, 2 * 512], BF16)
        oh = sb("oh", [64, 2 * T], BF16)
        bo_raw = sb("bo_raw", [8, 128], F32)
        bo_t = sb("bo_t", [128, 8], F32)
        pp = es.enter_context(nc.psum_tensor(P_ + "pp", [128, 8 * 512], F32))
        if env is None:
            block = es.enter_context(nc.Block())
            ld = SemC(nc, "ld"); vq = SemC(nc, "vq"); sc = SemC(nc, "sc")
            mm = SemC(nc, "mm"); stq = SemC(nc, "stq")
        else:
            block = env.block
            ld, vq, sc, mm, stq = env.sems

        es23 = ExitStack()
        sb23 = mk_sb(es23)
        kcur = sb23("kcur", [64, T], BF16)
        qf = sb23("qf", [64, T], BF16)
        raw = sb23("raw", [64, 2 * 512], BF16)
        rt1 = sb23("rt1", [64, 2 * 512], BF16)
        rt2 = sb23("rt2", [64, 2 * 512], BF16)
        psb = sb23("psb", [128, 4 * 512], BF16)

        es_wv = ExitStack()
        wv_sb = mk_sb(es_wv)("wv_sb", [128, ND * 512], BF16)

        def bank(i, lo=0, hi=512):
            return pp[:, i * 512 + lo:i * 512 + hi]

        @block.sync
        def _(sy):
            for nm, dst in (("c0q", cqt), ("s0q", sqt), ("c0k", ckt), ("s0k", skt),
                            ("R0", R0s), ("ones_64x1", ones64), ("identf", identf),
                            ("ones_1x128", ones1x128), ("ones_1x64", ones1x64)):
                sy.dma_start(out=dst[:], in_=cst[nm]).then_inc(ld.h, 16)
            sy.dma_start(out=xn[:].rearrange("p (c t) -> p c t", c=ND),
                         in_=xn0f.rearrange("(c p) t -> p c t", p=128)).then_inc(ld.h, 16)
            sy.dma_start(out=bo_raw[:], in_=bo).then_inc(ld.h, 16)
        ld.n += 16 * 11

        @block.gpsimd
        def _(g):
            for i in range(ND):
                g.dma_start(out=wv_sb[:, i * 512:(i + 1) * 512],
                            in_=Wv[i * 128:(i + 1) * 128, :]).then_inc(ld.h, 16)
        ld.n += 16 * ND
        LDALL = ld.n

        @block.tensor
        def _(t):
            t.wait_ge(ld.h, LDALL)
            mm.inc(t.matmul(bank(7, 8, 16), bo_raw[:], identf[0:8, 0:8],
                            start=True, stop=True))
        mm_bo = mm.n

        @block.vector
        def _(v):
            v.wait_ge(mm.h, mm_bo)
            vq.inc(v.tensor_copy(bo_t[:], bank(7, 8, 16)))
            vq.inc(v.memset(vaug[:], 0.0))
        v_init = vq.n

        B1S = sc.n
        sc_v = {tt: B1S + tt + 1 for tt in range(NT)}
        B1M = mm.n
        mm_v1 = {tt: B1M + 8 * (tt + 1) for tt in range(NT)}

        @block.tensor
        def _(t):
            t.wait_ge(ld.h, LDALL)
            for tt in range(NT):
                if tt >= 2:
                    t.wait_ge(sc.h, sc_v[tt - 2])
                for dc in range(ND):
                    mm.inc(t.matmul(bank(tt % 2),
                                    xn[:, dc * T + tt * 128:dc * T + (tt + 1) * 128],
                                    wv_sb[:, dc * 512:(dc + 1) * 512],
                                    start=(dc == 0), stop=(dc == ND - 1)))

        @block.scalar
        def _(s):
            s.wait_ge(vq.h, v_init)
            for tt in range(NT):
                s.wait_ge(mm.h, mm_v1[tt])
                dst = vaug[:, tt * 520:(tt + 1) * 520].rearrange("p (h o) -> p h o", h=8)[:, :, 0:64]
                srcp = bank(tt % 2).rearrange("p (h o) -> p h o", h=8)
                sc.inc(s.copy(dst, srcp))
            assert sc.n == sc_v[NT - 1]

        @block.vector
        def _(v):
            sc.wait(v)
            vq.inc(v.memset(
                vaug[:].rearrange("p (t o) -> p t o", t=NT * 8)[:, :, 64:65], 1.0))
        v_vaug_done = vq.n

        def proj_rope_head(w_sb, stride, off, ct, st_, dst_sl, wld_v):
            rope_v = {}
            raw_ev = {}
            for tg in range(NTG):
                b = tg % 2

                @block.tensor
                def _(t, tg=tg, b=b):
                    t.wait_ge(ld.h, wld_v)
                    t.wait_ge(vq.h, vq.n)
                    if tg >= 2:
                        t.wait_ge(vq.h, rope_v[tg - 2])
                        t.wait_ge(vq.h, raw_ev[tg - 2])
                    for dc in range(ND):
                        mm.inc(t.matmul(bank(4 + b)[0:64, :],
                                        w_sb[:, dc * stride + off:dc * stride + off + 64],
                                        xn[:, dc * T + tg * TG:dc * T + (tg + 1) * TG],
                                        start=(dc == 0), stop=(dc == ND - 1)))

                @block.vector
                def _(v, tg=tg, b=b):
                    v.wait_ge(mm.h, mm.n)
                    if tg >= 2:
                        v.wait_ge(vq.h, rope_v[tg - 2])
                    vq.wait(v)
                    vq.inc(v.tensor_copy(raw[:, b * 512:(b + 1) * 512],
                                         bank(4 + b)[0:64, :]))
                raw_ev[tg] = vq.n

                @block.tensor
                def _(t, tg=tg, b=b):
                    t.wait_ge(vq.h, raw_ev[tg])
                    mm.inc(t.matmul(bank(6 + b)[0:64, :], R0s[:],
                                    raw[:, b * 512:(b + 1) * 512], start=True, stop=True))
                r_done = mm.n

                @block.vector
                def _(v, tg=tg, b=b, r_done=r_done, raw_done=raw_ev[tg]):
                    v.wait_ge(vq.h, raw_done)
                    vq.wait(v)
                    vq.inc(v.tensor_tensor(out=rt1[:, b * 512:(b + 1) * 512],
                                           in0=raw[:, b * 512:(b + 1) * 512],
                                           in1=ct[:, tg * TG:(tg + 1) * TG], op=ALU.mult))
                    v.wait_ge(mm.h, r_done)
                    vq.wait(v)
                    vq.inc(v.tensor_tensor(out=rt2[:, b * 512:(b + 1) * 512],
                                           in0=bank(6 + b)[0:64, :],
                                           in1=st_[:, tg * TG:(tg + 1) * TG], op=ALU.mult))
                    vq.wait(v)
                    vq.inc(v.tensor_tensor(
                        out=dst_sl(tg),
                        in0=rt1[:, b * 512:(b + 1) * 512],
                        in1=rt2[:, b * 512:(b + 1) * 512], op=ALU.add))
                rope_v[tg] = vq.n

        def head_norm_bound(src_fn):
            @block.vector
            def _(v):
                vq.wait(v)
                vq.inc(v.memset(macc[:, 0:1], 0.0))
            for tg in range(NTG):
                @block.vector
                def _(v, tg=tg):
                    if tg >= 1:
                        v.wait_ge(mm.h, mm.n)
                    vq.wait(v)
                    vq.inc(v.tensor_tensor(out=rt1[:, 0:512], in0=src_fn(tg),
                                           in1=src_fn(tg), op=ALU.mult))
                v_sq = vq.n

                @block.tensor
                def _(t, v_sq=v_sq):
                    t.wait_ge(vq.h, v_sq)
                    mm.inc(t.matmul(bank(7)[0:1, :], ones64[:], rt1[:, 0:512],
                                    start=True, stop=True))
                m_mm = mm.n

                @block.vector
                def _(v, m_mm=m_mm):
                    v.wait_ge(mm.h, m_mm)
                    vq.wait(v)
                    vq.inc(v.reduce_max(macc[:, 1:2], bank(7)[0:1, :], axis=AX.X))
                    vq.wait(v)
                    vq.inc(v.tensor_tensor(out=macc[:, 0:1], in0=macc[:, 0:1],
                                           in1=macc[:, 1:2], op=ALU.max))

        es_wv.close()
        es_wk = ExitStack()
        wk_sb = mk_sb(es_wk)("wk_sb", [128, ND * 512], BF16)

        @block.gpsimd
        def _(g):
            g.wait_ge(mm.h, mm.n)
            for i in range(ND):
                g.dma_start(out=wk_sb[:, i * 512:(i + 1) * 512],
                            in_=Wk[i * 128:(i + 1) * 128, :]).then_inc(ld.h, 16)
        ld.n += 16 * ND
        WKLD = ld.n

        kscr_w = {}
        for j in range(NKV):
            if j >= 1:
                @block.vector
                def _(v, j=j):
                    v.wait_ge(stq.h, kscr_w[j - 1])
            proj_rope_head(wk_sb, 512, j * 64, ckt, skt,
                           lambda tg: kcur[:, tg * TG:(tg + 1) * TG], WKLD)
            head_norm_bound(lambda tg: kcur[:, tg * TG:(tg + 1) * TG])

            @block.vector
            def _(v, j=j):
                vq.wait(v)
                vq.inc(v.tensor_copy(mk2_all[:, j:j + 1], macc[:, 0:1]))
            v_kd = vq.n

            @block.sync
            def _(sy, j=j, v_kd=v_kd):
                sy.wait_ge(vq.h, v_kd)
                sy.dma_start(out=kscr[j * 64:(j + 1) * 64, :],
                             in_=kcur[:]).then_inc(stq.h, 16)
            stq.n += 16
            kscr_w[j] = stq.n

        es_wk.close()

        es_wq = ExitStack()
        wq_sb = mk_sb(es_wq)("wq_sb", [128, ND * D], BF16)

        @block.gpsimd
        def _(g):
            g.wait_ge(mm.h, mm.n)
            for i in range(ND):
                g.dma_start(out=wq_sb[:, i * D:(i + 1) * D],
                            in_=Wq[i * 128:(i + 1) * 128, :]).then_inc(ld.h, 16)
        ld.n += 16 * ND
        WQLD = ld.n

        oat_w = {}
        for h in range(NH):
            j = h // 2
            if h % 2 == 0:
                @block.sync
                def _(sy, j=j):
                    sy.wait_ge(mm.h, mm.n)
                    sy.wait_ge(stq.h, kscr_w[NKV - 1])
                    sy.dma_start(out=kcur[:], in_=kscr[j * 64:(j + 1) * 64, :]
                                 ).then_inc(ld.h, 16)
                ld.n += 16
            KLD = ld.n
            proj_rope_head(wq_sb, D, h * 64, cqt, sqt,
                           lambda tg: qf[:, tg * TG:(tg + 1) * TG], WQLD)

            head_norm_bound(lambda tg: qf[:, tg * TG:(tg + 1) * TG])

            @block.vector
            def _(v, j=j):
                vq.wait(v)
                vq.inc(v.tensor_tensor(out=macc[:, 2:3], in0=macc[:, 0:1],
                                       in1=mk2_all[:, j:j + 1], op=ALU.mult))
            v_c2 = vq.n

            @block.scalar
            def _(s, v_c2=v_c2):
                s.wait_ge(vq.h, v_c2)
                sc.inc(s.sqrt(macc[:, 3:4], macc[:, 2:3]))
            sc_cs = sc.n

            @block.vector
            def _(v, sc_cs=sc_cs):
                v.wait_ge(sc.h, sc_cs)
                vq.wait(v)
                vq.inc(v.tensor_copy(chbf[:], macc[:, 3:4]))
            v_mq = vq.n

            @block.tensor
            def _(t, v_mq=v_mq):
                t.wait_ge(vq.h, v_mq)
                t.wait_ge(sc.h, sc.n)
                mm.inc(t.matmul(bank(7, 4, 5), ones1x128[:], chbf[:],
                                start=True, stop=True))
            mm_c2 = mm.n

            @block.scalar
            def _(s, mm_c2=mm_c2):
                s.wait_ge(mm.h, mm_c2)
                sc.inc(s.mul(bias_h[:], bank(7, 4, 5), -1.0))
            sc_bias = sc.n

            for qg in range(NTG):
                sc_base = sc.n
                sc_exp = [sc_base + kt + 1 for kt in range(NT)]

                mm_qk = [0] * NT
                mm_av = [0] * NT
                cnt = mm.n
                for kt in range(NT):
                    cnt += 1; mm_qk[kt] = cnt
                    if kt >= 1:
                        cnt += 1; mm_av[kt - 1] = cnt
                cnt += 1; mm_av[NT - 1] = cnt

                @block.tensor
                def _(t, h=h, j=j, qg=qg, sc_exp=sc_exp, sc_bias=sc_bias, KLD=KLD):
                    t.wait_ge(sc.h, sc_bias)
                    t.wait_ge(ld.h, KLD)
                    for kt in range(NT):
                        if kt >= 3:
                            t.wait_ge(sc.h, sc_exp[kt - 3])
                        mm.inc(t.matmul(bank(kt % 3),
                                        kcur[:, kt * 128:(kt + 1) * 128],
                                        qf[:, qg * TG:(qg + 1) * TG],
                                        start=True, stop=True))
                        if kt >= 1:
                            t.wait_ge(sc.h, sc_exp[kt - 1])
                            mm.inc(t.matmul(
                                bank(3)[0:65, :],
                                vaug[:, (kt - 1) * 520 + j * 65:(kt - 1) * 520 + j * 65 + 65],
                                psb[:, ((kt - 1) % 4) * 512:((kt - 1) % 4) * 512 + 512],
                                start=(kt - 1 == 0), stop=False))
                    t.wait_ge(sc.h, sc_exp[NT - 1])
                    mm.inc(t.matmul(bank(3)[0:65, :],
                                    vaug[:, (NT - 1) * 520 + j * 65:(NT - 1) * 520 + j * 65 + 65],
                                    psb[:, ((NT - 1) % 4) * 512:((NT - 1) % 4) * 512 + 512],
                                    start=False, stop=True))
                assert mm.n == cnt, (mm.n, cnt)

                @block.scalar
                def _(s, qg=qg, mm_qk=mm_qk, mm_av=mm_av):
                    for kt in range(NT):
                        s.wait_ge(mm.h, mm_qk[kt])
                        if kt >= 4:
                            s.wait_ge(mm.h, mm_av[kt - 4])
                        sc.inc(s.activation(psb[:, (kt % 4) * 512:(kt % 4) * 512 + 512],
                                            bank(kt % 3), AF.Exp, bias=bias_h[:, 0:1]))
                    s.wait_ge(mm.h, mm_av[NT - 1])
                    sc.inc(s.activation(rrl[:], bank(3)[64:65, :], AF.Ln))
                    with nc.allow_low_precision(reason="softmax denom bf16"):
                        sc.inc(s.activation(rr[:], rrl[:], AF.Exp, scale=-1.0))
                assert sc.n == sc_exp[-1] + 2, (sc.n,)
                sc_rr = sc.n

                @block.tensor
                def _(t, sc_rr=sc_rr):
                    t.wait_ge(sc.h, sc_rr)
                    mm.inc(t.matmul(bank(5)[0:64, :], ones1x64[:], rr[:],
                                    start=True, stop=True))
                mm_rb = mm.n

                @block.vector
                def _(v, qg=qg, h=h, mm_rb=mm_rb):
                    v.wait_ge(mm.h, mm_rb)
                    if qg == 0 and (h - 2) in oat_w:
                        v.wait_ge(stq.h, oat_w[h - 2])
                    vq.wait(v)
                    vq.inc(v.tensor_copy(rb[:, (qg % 2) * 512:(qg % 2) * 512 + 512],
                                         bank(5)[0:64, :]))
                sc_rb = vq.n

                @block.vector
                def _(v, h=h, qg=qg, sc_rb=sc_rb):
                    v.wait_ge(vq.h, sc_rb)
                    vq.wait(v)
                    vq.inc(v.tensor_tensor(out=oh[:, (h % 2) * T + qg * TG:
                                               (h % 2) * T + (qg + 1) * TG],
                                           in0=bank(3)[0:64, :],
                                           in1=rb[:, (qg % 2) * 512:(qg % 2) * 512 + 512],
                                           op=ALU.mult))
                v_oh = vq.n

                @block.tensor
                def _(t, v_oh=v_oh):
                    t.wait_ge(vq.h, v_oh)

            @block.sync
            def _(sy, h=h, v_oh=v_oh):
                sy.wait_ge(vq.h, v_oh)
                sy.dma_start(out=oat[h * 64:(h + 1) * 64, :],
                             in_=oh[:, (h % 2) * T:(h % 2 + 1) * T]).then_inc(stq.h, 16)
            stq.n += 16
            oat_w[h] = stq.n
        B3_DONE = oat_w[NH - 1]
        B3_MM = mm.n

        es_wq.close()
        es23.close()
        es4 = ExitStack()
        sb4 = mk_sb(es4)
        att = sb4("att", [128, ND * 512], BF16)
        wo_sb = sb4("wo_sb", [128, ND * D], BF16)
        s0e = sb4("s0e", [128, 2 * 512], F32)

        @block.gpsimd
        def _(g):
            g.wait_ge(mm.h, B3_MM)
            g.wait_ge(stq.h, B3_DONE)
            for i in range(ND):
                g.dma_start(out=wo_sb[:, i * D:(i + 1) * D],
                            in_=Wo[i * 128:(i + 1) * 128, :]).then_inc(ld.h, 16)
        ld.n += 16 * ND
        WOLD = ld.n

        s0_ev = {}
        s0_st = {}
        att_ld = {}
        for tg in range(NTG):
            @block.sync
            def _(sy, tg=tg):
                if tg == 0:
                    sy.wait_ge(stq.h, B3_DONE)
                sy.dma_start(out=att[:].rearrange("p (c t) -> p c t", c=ND),
                             in_=oat.rearrange("(c p) t -> p c t", p=128)
                             [:, :, tg * TG:(tg + 1) * TG]).then_inc(ld.h, 16)
            ld.n += 16
            att_ld[tg] = ld.n

            for dt in range(ND):
                k = tg * ND + dt

                @block.tensor
                def _(t, tg=tg, dt=dt, k=k):
                    t.wait_ge(ld.h, att_ld[tg])
                    if k >= 2:
                        t.wait_ge(vq.h, s0_ev[k - 2])
                    for ic in range(ND):
                        mm.inc(t.matmul(bank(k % 2),
                                        wo_sb[:, ic * D + dt * 128:ic * D + (dt + 1) * 128],
                                        att[:, ic * 512:(ic + 1) * 512],
                                        start=(ic == 0), stop=(ic == ND - 1)))

                @block.vector
                def _(v, dt=dt, k=k):
                    v.wait_ge(mm.h, mm.n)
                    if k >= 4:
                        v.wait_ge(stq.h, s0_st[k - 4])
                    vq.wait(v)
                    vq.inc(v.tensor_scalar(out=s0e[:, (k % 2) * 512:(k % 2) * 512 + 512],
                                           in0=bank(k % 2), scalar1=bo_t[:, dt:dt + 1],
                                           scalar2=None, op0=ALU.add))
                s0_ev[k] = vq.n

                @block.sync
                def _(sy, tg=tg, dt=dt, k=k):
                    sy.wait_ge(vq.h, s0_ev[k])
                    sy.dma_start(out=s0f[dt * 128:(dt + 1) * 128, tg * TG:(tg + 1) * TG],
                                 in_=s0e[:, (k % 2) * 512:(k % 2) * 512 + 512]
                                 ).then_inc(stq.h, 16)
                stq.n += 16
                s0_st[k] = stq.n

            @block.sync
            def _(sy):
                sy.wait_ge(mm.h, mm.n)

        @block.sync
        def _(sy):
            sy.wait_ge(stq.h, stq.n)
        es4.close()
    return nc



def build_l0moe(env=None, io=None):
    if env is None:
        nc = bass.Bass("TRN2", target_bir_lowering=False)
        xn0f = nc.dram_tensor("xn0f", [D, T], BF16, kind="ExternalInput").ap()
        x = nc.dram_tensor("x", [T, D], F32, kind="ExternalInput").ap()
        s0f = nc.dram_tensor("s0f", [D, T], F32, kind="ExternalInput").ap()
        Wr = nc.dram_tensor("Wr", [D, 8], F32, kind="ExternalInput").ap()
        br = nc.dram_tensor("br", [1, 8], F32, kind="ExternalInput").ap()
        W1m = nc.dram_tensor("W1m", [8, D, 4096], F32, kind="ExternalInput").ap()
        b1m = nc.dram_tensor("b1m", [8, 4096], F32, kind="ExternalInput").ap()
        W2m = nc.dram_tensor("W2m", [8, 4096, D], F32, kind="ExternalInput").ap()
        b2m = nc.dram_tensor("b2m", [8, D], F32, kind="ExternalInput").ap()
        x1 = nc.dram_tensor("x1", [T, D], F32, kind="ExternalOutput").ap()
        cst = declare_consts(nc, ["ident", "identf", "ones_1x128", "iota_w1",
                                  "iota_w2", "iota8"])
    else:
        nc = env.nc
        cst = env.cst
        xn0f, x, s0f, Wr, br, W1m, b1m, W2m, b2m, x1 = io
    P_ = "C" if env else ""
    NS = 256
    NSG = T // NS
    with ExitStack() as es:
        def sb(name, shape, dt):
            return es.enter_context(nc.sbuf_tensor(P_ + name, shape, dt))

        w1_sb = sb("w1_sb", [128, ND * 4096], BF16)
        w2_sb = sb("w2_sb", [128, 32 * D], BF16)
        xns = sb("xns", [128, 2 * ND * NS], BF16)
        s0s = sb("s0s", [128, 2 * ND * NS], F32)
        xacc = sb("xacc", [128, 4 * D], F32)
        h_sb = sb("h_sb", [128, 2 * NS], BF16)
        g_sb = sb("g_sb", [128, NS], F32)
        gs2 = sb("gs2", [128, ND * NS], F32)
        sg_sb = sb("sg_sb", [128, NS], F32)
        mean_bf = sb("mean_bf", [128, ND], BF16)
        wr_sb = sb("wr_sb", [128, ND * 8], BF16)
        br_sb = sb("br_sb", [1, 8], F32)
        io8 = sb("io8", [1, 8], F32)
        iow1 = sb("iow1", [128, 16], F32)
        iow2 = sb("iow2", [128, 32], F32)
        onesb = sb("onesb", [1, 128], BF16)
        identf = sb("identf", [128, 128], F32)
        sc8 = sb("sc8", [1, 8], F32)
        mx1 = sb("mx1", [1, 1], F32)
        ohot = sb("ohot", [1, 8], F32)
        e_val = sb("e_val", [1, 1], F32)
        e_bf = sb("e_bf", [1, 1], BF16)
        e_f = sb("e_f", [128, 1], F32)
        e_sc = sb("e_sc", [128, 2], F32)
        idx1f = sb("idx1f", [128, 16], F32)
        idx2f = sb("idx2f", [128, 32], F32)
        idx1u = sb("idx1u", [128, 16], U32)
        idx2u = sb("idx2u", [128, 32], U32)
        b1bf = sb("b1bf", [8, 4096], BF16)
        b2bf = sb("b2bf", [8, D], BF16)
        b1t = sb("b1t", [128, 32], F32)
        b2t = sb("b2t", [128, 8], F32)
        xnc = sb("xnc", [128, ND], F32)
        oh_bfr = sb("oh_bfr", [1, 8], BF16)
        oh8 = sb("oh8", [8, 1], BF16)
        identb = sb("identb", [1, 1], BF16)
        pp = es.enter_context(nc.psum_tensor(P_ + "pp", [128, 6 * 512], F32))
        ppt = es.enter_context(nc.psum_tensor(P_ + "ppt", [128, 2 * 512], F32))
        if env is None:
            block = es.enter_context(nc.Block())
            ld = SemC(nc, "ld"); vq = SemC(nc, "vq"); sc = SemC(nc, "sc")
            mm = SemC(nc, "mm"); stq = SemC(nc, "stq")
        else:
            block = env.block
            ld, vq, sc, mm, stq = env.sems
        gat = SemC(nc, P_ + "gat")

        def bank(i, lo=0, hi=512):
            return pp[:, i * 512 + lo:i * 512 + hi]

        @block.sync
        def _(sy):
            for nm, dst in (("identf", identf), ("ones_1x128", onesb),
                            ("iota_w1", iow1), ("iota_w2", iow2), ("iota8", io8)):
                sy.dma_start(out=dst[:], in_=cst[nm]).then_inc(ld.h, 16)
            sy.dma_start(out=br_sb[:], in_=br).then_inc(ld.h, 16)
            sy.dma_start(out=identb[:], in_=cst["ident"][0:1, 0:1]).then_inc(ld.h, 16)
        ld.n += 16 * 7

        @block.gpsimd
        def _(g):
            for i in range(ND):
                g.dma_start(out=wr_sb[:, i * 8:(i + 1) * 8],
                            in_=Wr[i * 128:(i + 1) * 128, :]).then_inc(ld.h, 16)
            g.dma_start(out=b1bf[:], in_=b1m).then_inc(ld.h, 16)
            g.dma_start(out=b2bf[:], in_=b2m).then_inc(ld.h, 16)
        ld.n += 16 * 10
        LD0 = ld.n

        mean_v = {}
        for dc in range(ND):
            @block.sync
            def _(sy, dc=dc):
                if dc >= 1:
                    sy.wait_ge(vq.h, mean_v[dc - 1])
                sy.dma_start(out=xns[:, 0:T],
                             in_=xn0f[dc * 128:(dc + 1) * 128, :]).then_inc(ld.h, 16)
            ld.n += 16

            @block.vector
            def _(v, dc=dc):
                v.wait_ge(ld.h, LD0 + 16 * (dc + 1))
                vq.wait(v)
                vq.inc(v.reduce_sum(xnc[:, dc:dc + 1], xns[:, 0:T], axis=AX.X))
                vq.wait(v)
                vq.inc(v.tensor_scalar(out=mean_bf[:, dc:dc + 1], in0=xnc[:, dc:dc + 1],
                                       scalar1=1.0 / T, scalar2=None, op0=ALU.mult))
            mean_v[dc] = vq.n
        MEAN_V = vq.n

        @block.tensor
        def _(t):
            t.wait_ge(vq.h, MEAN_V)
            for dc in range(ND):
                mm.inc(t.matmul(bank(0, 0, 8)[0:1, :], mean_bf[:, dc:dc + 1],
                                wr_sb[:, dc * 8:(dc + 1) * 8],
                                start=(dc == 0), stop=(dc == ND - 1)))
        mm_sc = mm.n

        @block.vector
        def _(v):
            v.wait_ge(mm.h, mm_sc)
            vq.wait(v)
            vq.inc(v.tensor_tensor(out=sc8[:], in0=bank(0, 0, 8)[0:1, :],
                                   in1=br_sb[:], op=ALU.add))
            vq.wait(v)
            vq.inc(v.reduce_max(mx1[:], sc8[:], axis=AX.X))
            vq.wait(v)
            vq.inc(v.tensor_scalar(out=ohot[:], in0=sc8[:], scalar1=mx1[:, 0:1],
                                   scalar2=None, op0=ALU.is_ge))
            vq.wait(v)
            vq.inc(v.tensor_tensor(out=ohot[:], in0=ohot[:], in1=io8[:], op=ALU.mult))
            vq.wait(v)
            vq.inc(v.reduce_sum(e_val[:], ohot[:], axis=AX.X))
            vq.wait(v)
            vq.inc(v.tensor_copy(e_bf[:], e_val[:]))
        v_e = vq.n

        @block.tensor
        def _(t):
            t.wait_ge(vq.h, v_e)
            mm.inc(t.matmul(bank(1, 0, 1), onesb[:], e_bf[:], start=True, stop=True))
        mm_e = mm.n

        @block.vector
        def _(v):
            v.wait_ge(mm.h, mm_e)
            vq.wait(v)
            vq.inc(v.tensor_copy(e_f[:], bank(1, 0, 1)))
            vq.wait(v)
            vq.inc(v.tensor_scalar(out=e_sc[:, 0:1], in0=e_f[:], scalar1=2048.0,
                                   scalar2=None, op0=ALU.mult))
            vq.wait(v)
            vq.inc(v.tensor_scalar(out=e_sc[:, 1:2], in0=e_f[:], scalar1=4096.0,
                                   scalar2=None, op0=ALU.mult))
            vq.wait(v)
            vq.inc(v.tensor_scalar(out=idx1f[:], in0=iow1[:], scalar1=e_sc[:, 0:1],
                                   scalar2=None, op0=ALU.add))
            vq.wait(v)
            vq.inc(v.tensor_scalar(out=idx2f[:], in0=iow2[:], scalar1=e_sc[:, 1:2],
                                   scalar2=None, op0=ALU.add))
            vq.wait(v)
            vq.inc(v.tensor_copy(idx1u[:], idx1f[:]))
            vq.wait(v)
            vq.inc(v.tensor_copy(idx2u[:], idx2f[:]))
        v_idx = vq.n

        w1v = W1m.rearrange("e d (h c) -> (e d h) c", c=2048)
        w2v = W2m.rearrange("e f c -> (e f) c")
        @block.gpsimd
        def _(g):
            g.wait_ge(vq.h, v_idx)
            for dc in range(ND):
                for hh in range(2):
                    g.indirect_dma_start(
                        out=w1_sb[:, dc * 4096 + hh * 2048:dc * 4096 + (hh + 1) * 2048],
                        out_offset=None, in_=w1v,
                        in_offset=bass.IndirectOffsetOnAxis(
                            ap=idx1u[:, dc * 2 + hh:dc * 2 + hh + 1], axis=0),
                    ).then_inc(gat.h, 16)
            for fc in range(32):
                g.indirect_dma_start(
                    out=w2_sb[:, fc * D:(fc + 1) * D],
                    out_offset=None, in_=w2v,
                    in_offset=bass.IndirectOffsetOnAxis(
                        ap=idx2u[:, fc:fc + 1], axis=0),
                ).then_inc(gat.h, 16)
        gat.n += 16 * 48
        GAT_ALL = gat.n

        @block.vector
        def _(v):
            vq.wait(v)
            vq.inc(v.tensor_scalar(out=ohot[:], in0=sc8[:], scalar1=mx1[:, 0:1],
                                   scalar2=None, op0=ALU.is_ge))
            vq.wait(v)
            vq.inc(v.tensor_copy(oh_bfr[:], ohot[:]))
        v_ohbf = vq.n

        @block.tensor
        def _(t):
            t.wait_ge(vq.h, v_ohbf)
            mm.inc(t.matmul(bank(1, 8, 9)[0:8, :], oh_bfr[:], identb[0:1, 0:1],
                            start=True, stop=True))
        mm_oht = mm.n

        @block.vector
        def _(v):
            v.wait_ge(mm.h, mm_oht)
            vq.wait(v)
            vq.inc(v.tensor_copy(oh8[:], bank(1, 8, 9)[0:8, :]))
        v_oh8 = vq.n

        @block.tensor
        def _(t):
            t.wait_ge(vq.h, v_oh8)
            t.wait_ge(ld.h, LD0)
            for c in range(32):
                mm.inc(t.matmul(bank(2, c, c + 1), b1bf[:, c * 128:(c + 1) * 128],
                                oh8[:], start=True, stop=True))
            for c in range(ND):
                mm.inc(t.matmul(bank(3, c, c + 1), b2bf[:, c * 128:(c + 1) * 128],
                                oh8[:], start=True, stop=True))
        mm_bias = mm.n

        @block.vector
        def _(v):
            v.wait_ge(mm.h, mm_bias)
            vq.wait(v)
            vq.inc(v.tensor_copy(b1t[:], bank(2, 0, 32)))
            vq.wait(v)
            vq.inc(v.tensor_copy(b2t[:], bank(3, 0, 8)))
        v_bias = vq.n

        xs_ld = {}
        out_st = {}
        hv_ev = {}
        sg_end_v = {}
        for sg in range(NSG):
            b = sg % 2
            t0 = sg * NS

            @block.sync
            def _(sy, sg=sg, b=b, t0=t0):
                if sg < 2:
                    sy.wait_ge(vq.h, MEAN_V)
                if sg >= 2:
                    sy.wait_ge(stq.h, out_st[sg - 2])
                    sy.wait_ge(vq.h, sg_end_v[sg - 2])
                sy.dma_start(out=xns[:, b * 2048:(b + 1) * 2048]
                             .rearrange("p (c t) -> p c t", c=ND),
                             in_=xn0f.rearrange("(c p) t -> p c t", p=128)
                             [:, :, t0:t0 + NS]).then_inc(ld.h, 16)
                sy.dma_start(out=s0s[:, b * 2048:(b + 1) * 2048]
                             .rearrange("p (c t) -> p c t", c=ND),
                             in_=s0f.rearrange("(c p) t -> p c t", p=128)
                             [:, :, t0:t0 + NS]).then_inc(ld.h, 16)
                sy.dma_start(out=xacc[:, b * 2 * D:(b + 1) * 2 * D]
                             .rearrange("p (w d) -> p w d", w=2),
                             in_=x.rearrange("(w p) d -> p w d", p=128)
                             [:, 2 * sg:2 * sg + 2, :]).then_inc(ld.h, 16)
            ld.n += 48
            xs_ld[sg] = ld.n

            CSB = sc.n
            hv = {fc: CSB + fc + 1 for fc in range(32)}
            for fc in range(32):
                hv_ev[(sg, fc)] = hv[fc]
            CMB = mm.n
            mm_h1 = {fc: CMB + 16 * fc + 8 for fc in range(32)}

            @block.tensor
            def _(t, sg=sg, b=b):
                t.wait_ge(ld.h, xs_ld[sg])
                if sg == 0:
                    t.wait_ge(gat.h, GAT_ALL)
                    t.wait_ge(vq.h, v_bias)
                for fc in range(32):
                    if fc >= 2:
                        t.wait_ge(sc.h, hv[fc - 2])
                    elif sg >= 1:
                        t.wait_ge(sc.h, hv_ev[(sg - 1, 30 + fc)])
                    for dc in range(ND):
                        mm.inc(t.matmul(bank(fc % 2, 0, 256),
                                        w1_sb[:, dc * 4096 + fc * 128:dc * 4096 + (fc + 1) * 128],
                                        xns[:, b * 2048 + dc * 256:b * 2048 + (dc + 1) * 256],
                                        start=(dc == 0), stop=(dc == ND - 1)))
                    t.wait_ge(sc.h, hv[fc])
                    for dt in range(ND):
                        mm.inc(t.matmul(bank(2 + dt // 2, (dt % 2) * 256, (dt % 2) * 256 + 256),
                                        w2_sb[:, fc * D + dt * 128:fc * D + (dt + 1) * 128],
                                        h_sb[:, (fc % 2) * 256:(fc % 2) * 256 + 256],
                                        start=(fc == 0 and dt % 2 == 0),
                                        stop=(fc == 31 and dt % 2 == 1),
                                        skip_group_check=True))

            @block.scalar
            def _(s, sg=sg):
                for fc in range(32):
                    s.wait_ge(mm.h, mm_h1[fc])
                    sc.inc(s.activation(h_sb[:, (fc % 2) * 256:(fc % 2) * 256 + 256],
                                        bank(fc % 2, 0, 256),
                                        AF.Relu, bias=b1t[:, fc:fc + 1]))
                assert sc.n == hv[31]
            mm_z = mm.n

            vg_done = {}
            prev_tp = 0
            CVB = vq.n
            vgp = {dt: CVB + 3 * dt + 1 for dt in range(ND)}
            sil = {dt: sc.n + dt + 1 for dt in range(ND)}

            @block.scalar
            def _(s, sg=sg):
                s.wait_ge(mm.h, mm_z)
                if sg >= 1:
                    s.wait_ge(vq.h, sg_end_v[sg - 1])
                for dt in range(ND):
                    if dt >= 1:
                        s.wait_ge(vq.h, vgp[dt - 1])
                    sc.inc(s.activation(gs2[:, dt * 256:(dt + 1) * 256],
                                        bank(2 + dt // 2, (dt % 2) * 256,
                                             (dt % 2) * 256 + 256),
                                        AF.Silu, bias=b2t[:, dt:dt + 1]))
                assert sc.n == sil[ND - 1]

            for dt in range(ND):
                sil_ev = sil[dt]

                @block.vector
                def _(v, sg=sg, b=b, dt=dt, sil_ev=sil_ev, prev_tp=prev_tp):
                    v.wait_ge(sc.h, sil_ev)
                    v.wait_ge(mm.h, prev_tp)
                    vq.wait(v)
                    vq.inc(v.tensor_tensor(out=sg_sb[:], in0=gs2[:, dt * 256:(dt + 1) * 256],
                                           in1=s0s[:, b * 2048 + dt * 256:b * 2048 + (dt + 1) * 256],
                                           op=ALU.mult))
                v_sg = vq.n

                for w in range(2):
                    @block.tensor
                    def _(t, w=w, v_sg=v_sg):
                        t.wait_ge(vq.h, v_sg)
                        t.wait_ge(vq.h, vq.n)
                        mm.inc(_tmm(t, ppt[:, w * 512:w * 512 + 128],
                                    sg_sb[:, w * 128:(w + 1) * 128], identf[:]))
                    m_tp = mm.n

                    @block.vector
                    def _(v, sg=sg, b=b, dt=dt, w=w, m_tp=m_tp):
                        v.wait_ge(mm.h, m_tp)
                        vq.wait(v)
                        vq.inc(v.tensor_tensor(
                            out=xacc[:, b * 2 * D + w * D + dt * 128:
                                     b * 2 * D + w * D + (dt + 1) * 128],
                            in0=xacc[:, b * 2 * D + w * D + dt * 128:
                                     b * 2 * D + w * D + (dt + 1) * 128],
                            in1=ppt[:, w * 512:w * 512 + 128], op=ALU.add))
                prev_tp = mm.n
                vg_done[(sg, dt)] = vq.n
                assert vq.n == vgp[dt] + 2, (vq.n, vgp[dt])
            sg_end_v[sg] = vg_done[(sg, ND - 1)]

            @block.sync
            def _(sy, sg=sg, b=b):
                sy.wait_ge(vq.h, vg_done[(sg, ND - 1)])
                sy.dma_start(out=x1.rearrange("(w p) d -> p w d", p=128)
                             [:, 2 * sg:2 * sg + 2, :],
                             in_=xacc[:, b * 2 * D:(b + 1) * 2 * D]
                             .rearrange("p (w d) -> p w d", w=2)).then_inc(stq.h, 16)
            stq.n += 16
            out_st[sg] = stq.n

        @block.sync
        def _(sy):
            sy.wait_ge(stq.h, stq.n)
    return nc



def build_lnssm(debug=False, env=None, io=None):
    if env is None:
        nc = bass.Bass("TRN2", target_bir_lowering=False)
        x1 = nc.dram_tensor("x1", [T, D], F32, kind="ExternalInput").ap()
        B_w = nc.dram_tensor("B_w", [128, D], F32, kind="ExternalInput").ap()
        C_w = nc.dram_tensor("C_w", [D, 128], F32, kind="ExternalInput").ap()
        logA = nc.dram_tensor("logA", [1, 128], F32, kind="ExternalInput").ap()
        D_p = nc.dram_tensor("D_p", [8, 128], F32, kind="ExternalInput").ap()
        xn1f = nc.dram_tensor("xn1f", [D, T], F32, kind="ExternalOutput").ap()
        s1f = nc.dram_tensor("s1f", [D, T], F32, kind="ExternalOutput").ap()
    else:
        nc = env.nc
        x1, B_w, C_w, logA, D_p, xn1f, s1f = io
    if debug and env is None:
        dbg_u = nc.dram_tensor("dbg_u", [128, T], F32, kind="ExternalOutput").ap()
        dbg_a = nc.dram_tensor("dbg_a", [128, T], F32, kind="ExternalOutput").ap()
        dbg_hs = nc.dram_tensor("dbg_hs", [128, T], F32, kind="ExternalOutput").ap()
        dbg_dpt = nc.dram_tensor("dbg_dpt", [128, 8], F32, kind="ExternalOutput").ap()
        dbg_cwt = nc.dram_tensor("dbg_cwt", [128, ND * 128], BF16, kind="ExternalOutput").ap()
    cst = env.cst if env else declare_consts(nc, ["ident", "identf"])
    P_ = "Dp" if env else ""
    with ExitStack() as es:
        def sb(name, shape, dt):
            return es.enter_context(nc.sbuf_tensor(P_ + name, shape, dt))

        ident = sb("ident", [128, 128], BF16)
        identf = sb("identf", [128, 128], F32)
        xt = sb("xt", [128, 2 * D], F32)
        xsq = sb("xsq", [128, D], F32)
        st0 = sb("st0", [128, 8], F32)
        xnt = sb("xnt", [128, 2 * D], F32)
        tev = sb("tev", [128, 2 * ND * 128], F32)
        xnb = sb("xnb", [128, ND * T], BF16)
        u_sb = sb("u_sb", [128, T], F32)
        afull = sb("afull", [128, T], F32)
        hs = sb("hs", [128, T], F32)
        hs_bf = sb("hs_bf", [128, T], BF16)
        bw_raw = sb("bw_raw", [128, D], BF16)
        bwT = sb("bwT", [128, ND * 128], BF16)
        cw_raw = sb("cw_raw", [128, ND * 128], BF16)
        cwT = sb("cwT", [128, ND * 128], BF16)
        la_row = sb("la_row", [1, 128], F32)
        acol = sb("acol", [128, 2], F32)
        dpr = sb("dpr", [8, 128], F32)
        dpt = sb("dpt", [128, 8], F32)
        x1c = sb("x1c", [128, 2 * 512], F32)
        s1e = sb("s1e", [128, 2 * 512], F32)
        pp = es.enter_context(nc.psum_tensor(P_ + "pp", [128, 8 * 512], F32))
        if env is None:
            block = es.enter_context(nc.Block())
            ld = SemC(nc, "ld"); vq = SemC(nc, "vq"); sc = SemC(nc, "sc")
            mm = SemC(nc, "mm"); stq = SemC(nc, "stq")
        else:
            block = env.block
            ld, vq, sc, mm, stq = env.sems

        def bank(i, lo=0, hi=512):
            return pp[:, i * 512 + lo:i * 512 + hi]

        @block.sync
        def _(sy):
            sy.dma_start(out=ident[:], in_=cst["ident"]).then_inc(ld.h, 16)
            sy.dma_start(out=identf[:], in_=cst["identf"]).then_inc(ld.h, 16)
            sy.dma_start(out=la_row[:], in_=logA).then_inc(ld.h, 16)
            sy.dma_start(out=dpr[:], in_=D_p).then_inc(ld.h, 16)
        ld.n += 16 * 4

        @block.gpsimd
        def _(g):
            g.dma_start(out=bw_raw[:], in_=B_w).then_inc(ld.h, 16)
            for i in range(ND):
                g.dma_start(out=cw_raw[:, i * 128:(i + 1) * 128],
                            in_=C_w[i * 128:(i + 1) * 128, :]).then_inc(ld.h, 16)
        ld.n += 16 * 9
        LD0 = ld.n

        tp_ev = {}
        for i in range(ND):
            @block.tensor
            def _(t, i=i):
                t.wait_ge(ld.h, LD0)
                if i >= 2:
                    t.wait_ge(vq.h, tp_ev[i - 2])
                mm.inc(_tmm(t, bank(4 + i % 2, 0, 128),
                            bw_raw[:, i * 128:(i + 1) * 128], ident[:]))
                mm.inc(_tmm(t, bank(4 + i % 2, 128, 256),
                            cw_raw[:, i * 128:(i + 1) * 128], ident[:]))
            m_tp = mm.n

            @block.vector
            def _(v, i=i, m_tp=m_tp):
                v.wait_ge(mm.h, m_tp)
                vq.wait(v)
                vq.inc(v.tensor_copy(bwT[:, i * 128:(i + 1) * 128],
                                     bank(4 + i % 2, 0, 128)))
                vq.wait(v)
                vq.inc(v.tensor_copy(cwT[:, i * 128:(i + 1) * 128],
                                     bank(4 + i % 2, 128, 256)))
            tp_ev[i] = vq.n

        @block.tensor
        def _(t):
            t.wait_ge(vq.h, tp_ev[ND - 1])
            mm.inc(_tmm(t, bank(6, 0, 1), la_row[:], identf[0:1, 0:1]))
            mm.inc(_tmm(t, bank(6, 128, 136), dpr[:], identf[0:8, 0:8]))
        m_la = mm.n

        @block.scalar
        def _(s):
            s.wait_ge(mm.h, m_la)
            sc.inc(s.activation(acol[:, 0:1], bank(6, 0, 1), AF.Exp))
        sc_a = sc.n

        @block.vector
        def _(v):
            v.wait_ge(mm.h, m_la)
            vq.wait(v)
            vq.inc(v.tensor_copy(dpt[:], bank(6, 128, 136)))
            v.wait_ge(sc.h, sc_a)
            vq.wait(v)
            vq.inc(v.memset(afull[:], 1.0))
            vq.wait(v)
            vq.inc(v.tensor_scalar(out=afull[:], in0=afull[:], scalar1=acol[:, 0:1],
                                   scalar2=None, op0=ALU.mult))
        v_a = vq.n

        v_end = [0] * NT
        s_end = [0] * NT
        ev_at = {}
        st_at = {}
        u_ev = {}
        for tt in range(NT):
            bo = (tt % 2) * D

            @block.sync
            def _(sy, tt=tt, bo=bo):
                if tt >= 2:
                    sy.wait_ge(vq.h, v_end[tt - 2])
                    sy.wait_ge(sc.h, s_end[tt - 2])
                sy.dma_start(out=xt[:, bo:bo + D],
                             in_=x1[tt * 128:(tt + 1) * 128, :]).then_inc(ld.h, 16)
            ld.n += 16
            XLD = ld.n

            @block.vector
            def _(v, tt=tt, bo=bo, XLD=XLD):
                v.wait_ge(ld.h, XLD)
                vq.wait(v)
                vq.inc(v.reduce_sum(st0[:, 0:1], xt[:, bo:bo + D], axis=AX.X))

            @block.scalar
            def _(s, tt=tt, bo=bo, XLD=XLD):
                s.wait_ge(ld.h, XLD)
                if tt >= 1:
                    s.wait_ge(vq.h, v_end[tt - 1])
                sc.inc(s.activation(xsq[:], xt[:, bo:bo + D], AF.Square,
                                    accum_out=st0[:, 2:3]))
            s_end[tt] = sc.n

            @block.vector
            def _(v, tt=tt, bo=bo):
                vq.wait(v)
                vq.inc(v.tensor_scalar(out=st0[:, 1:2], in0=st0[:, 0:1],
                                       scalar1=-1.0 / D, scalar2=None, op0=ALU.mult))
                vq.wait(v)
                vq.inc(v.tensor_tensor(out=st0[:, 3:4], in0=st0[:, 1:2],
                                       in1=st0[:, 1:2], op=ALU.mult))
                sc.wait(v)
                vq.wait(v)
                vq.inc(v.tensor_scalar(out=st0[:, 4:5], in0=st0[:, 2:3],
                                       scalar1=1.0 / D, scalar2=st0[:, 3:4],
                                       op0=ALU.mult, op1=ALU.subtract))
                vq.wait(v)
                vq.inc(v.tensor_scalar(out=st0[:, 5:6], in0=st0[:, 4:5],
                                       scalar1=1e-5, scalar2=None, op0=ALU.add))

            @block.scalar
            def _(s, tt=tt):
                vq.wait(s)
                sc.inc(s.activation(st0[:, 6:7], st0[:, 5:6], AF.Ln))
                sc.inc(s.activation(st0[:, 7:8], st0[:, 6:7], AF.Exp, scale=-0.5))

            @block.vector
            def _(v, tt=tt, bo=bo):
                sc.wait(v)
                vq.inc(v.tensor_scalar(out=xnt[:, bo:bo + D], in0=xt[:, bo:bo + D],
                                       scalar1=st0[:, 1:2], scalar2=st0[:, 7:8],
                                       op0=ALU.add, op1=ALU.mult))
            v_end[tt] = vq.n

            tb = (tt % 2) * ND
            for dc in range(ND):
                k = tt * ND + dc
                pb = k % 2

                @block.tensor
                def _(t, tt=tt, dc=dc, pb=pb, bo=bo, k=k):
                    t.wait_ge(vq.h, v_end[tt])
                    if k >= 2:
                        t.wait_ge(vq.h, ev_at[k - 2])
                        t.wait_ge(sc.h, sc.n)
                    mm.inc(_tmm(t, bank(pb, 0, 128),
                                       xnt[:, bo + dc * 128:bo + (dc + 1) * 128],
                                       identf[:]))

                @block.scalar
                def _(s, pb=pb, tb=tb, dc=dc, tt=tt):
                    mm.wait(s)
                    if dc == 0 and tt >= 2:
                        s.wait_ge(stq.h, st_at[tt - 2])
                    sc.inc(s.copy(tev[:, (tb + dc) * 128:(tb + dc + 1) * 128],
                                  bank(pb, 0, 128)))

                @block.vector
                def _(v, tt=tt, dc=dc, pb=pb, k=k):
                    v.wait_ge(mm.h, mm.n)
                    vq.wait(v)
                    vq.inc(v.tensor_copy(xnb[:, dc * T + tt * 128:dc * T + (tt + 1) * 128],
                                         bank(pb, 0, 128)))
                ev_at[k] = vq.n

            @block.sync
            def _(sy, tt=tt, tb=tb):
                sc.wait(sy)
                sy.dma_start(out=xn1f.rearrange("(c p) t -> p c t", p=128)
                             [:, :, tt * 128:(tt + 1) * 128],
                             in_=tev[:, tb * 128:(tb + ND) * 128]
                             .rearrange("p (c t) -> p c t", c=ND)).then_inc(stq.h, 16)
            stq.n += 16
            st_at[tt] = stq.n

        XN_ALL_V = vq.n
        XN_ST = stq.n

        for tg in range(NTG):
            @block.tensor
            def _(t, tg=tg):
                t.wait_ge(vq.h, XN_ALL_V)
                if tg >= 2:
                    t.wait_ge(sc.h, u_ev[tg - 2])
                for dc in range(ND):
                    mm.inc(t.matmul(bank(2 + tg % 2),
                                    bwT[:, dc * 128:(dc + 1) * 128],
                                    xnb[:, dc * T + tg * TG:dc * T + (tg + 1) * TG],
                                    start=(dc == 0), stop=(dc == ND - 1)))

            @block.scalar
            def _(s, tg=tg):
                mm.wait(s)
                sc.inc(s.copy(u_sb[:, tg * TG:(tg + 1) * TG], bank(2 + tg % 2)))
            u_ev[tg] = sc.n

        @block.vector
        def _(v):
            sc.wait(v)
            v.wait_ge(vq.h, v_a)
            vq.wait(v)
            vq.inc(v.tensor_tensor_scan(out=hs[:], data0=afull[:], data1=u_sb[:],
                                        initial=0.0, op0=ALU.mult, op1=ALU.add))
            vq.wait(v)
            vq.inc(v.tensor_copy(hs_bf[:], hs[:]))
        v_scan = vq.n

        y_ev = {}
        y_st = {}
        for dt in range(ND):
            for tg in range(NTG):
                k = dt * NTG + tg

                @block.sync
                def _(sy, dt=dt, tg=tg, k=k):
                    if k == 0:
                        sy.wait_ge(stq.h, XN_ST)
                    if k >= 2:
                        sy.wait_ge(vq.h, y_ev[k - 2])
                    sy.dma_start(out=x1c[:, (k % 2) * 512:(k % 2) * 512 + 512],
                                 in_=xn1f[dt * 128:(dt + 1) * 128, tg * TG:(tg + 1) * TG]
                                 ).then_inc(ld.h, 16)
                ld.n += 16
                YLD = ld.n

                @block.tensor
                def _(t, dt=dt, tg=tg, k=k):
                    t.wait_ge(vq.h, v_scan)
                    if k >= 2:
                        t.wait_ge(vq.h, y_ev[k - 2])
                    mm.inc(t.matmul(bank(2 + k % 2), cwT[:, dt * 128:(dt + 1) * 128],
                                    hs_bf[:, tg * TG:(tg + 1) * TG],
                                    start=True, stop=True))
                m_y = mm.n

                @block.vector
                def _(v, dt=dt, tg=tg, k=k, m_y=m_y, YLD=YLD):
                    v.wait_ge(mm.h, m_y)
                    v.wait_ge(ld.h, YLD)
                    if k >= 2:
                        v.wait_ge(stq.h, y_st[k - 2])
                    vq.wait(v)
                    vq.inc(v.scalar_tensor_tensor(
                        out=s1e[:, (k % 2) * 512:(k % 2) * 512 + 512],
                        in0=x1c[:, (k % 2) * 512:(k % 2) * 512 + 512],
                        scalar=dpt[:, dt:dt + 1], in1=bank(2 + k % 2),
                        op0=ALU.mult, op1=ALU.add))
                y_ev[k] = vq.n

                @block.sync
                def _(sy, dt=dt, tg=tg, k=k):
                    sy.wait_ge(vq.h, y_ev[k])
                    sy.dma_start(out=s1f[dt * 128:(dt + 1) * 128, tg * TG:(tg + 1) * TG],
                                 in_=s1e[:, (k % 2) * 512:(k % 2) * 512 + 512]
                                 ).then_inc(stq.h, 16)
                stq.n += 16
                y_st[k] = stq.n

        if debug and env is None:
            @block.sync
            def _(sy):
                sy.wait_ge(vq.h, v_scan)
                sy.dma_start(out=dbg_u, in_=u_sb[:]).then_inc(stq.h, 16)
                sy.dma_start(out=dbg_a, in_=afull[:]).then_inc(stq.h, 16)
                sy.dma_start(out=dbg_hs, in_=hs[:]).then_inc(stq.h, 16)
                sy.dma_start(out=dbg_dpt, in_=dpt[:]).then_inc(stq.h, 16)
                sy.dma_start(out=dbg_cwt, in_=cwT[:]).then_inc(stq.h, 16)
            stq.n += 80

        @block.sync
        def _(sy):
            sy.wait_ge(stq.h, stq.n)
    return nc



def build_subt(layers=range(6), debug=False, env=None, io=None):
    if env is None:
        nc = bass.Bass("TRN2", target_bir_lowering=False)
        xn1f = nc.dram_tensor("xn1f", [D, T], F32, kind="ExternalInput").ap()
        Wqkv = nc.dram_tensor("Wqkv", [6, D, 3072], F32, kind="ExternalInput").ap()
        q_sc = nc.dram_tensor("q_sc", [6, 64], F32, kind="ExternalInput").ap()
        k_sc = nc.dram_tensor("k_sc", [6, 64], F32, kind="ExternalInput").ap()
        Wo1 = nc.dram_tensor("Wo1", [6, D, D], F32, kind="ExternalInput").ap()
        ln_g = nc.dram_tensor("ln_g", [6, 8, 128], F32, kind="ExternalInput").ap()
        ln_b = nc.dram_tensor("ln_b", [6, 8, 128], F32, kind="ExternalInput").ap()
        W1l = nc.dram_tensor("W1l", [6, D, 5460], F32, kind="ExternalInput").ap()
        b1l = nc.dram_tensor("b1l", [6, 5460], F32, kind="ExternalInput").ap()
        W2l = nc.dram_tensor("W2l", [6, 2730, D], F32, kind="ExternalInput").ap()
        b2l = nc.dram_tensor("b2l", [6, 8, 128], F32, kind="ExternalInput").ap()
        g1f = nc.dram_tensor("g1f", [D, T], F32, kind="ExternalOutput").ap()
        cst = declare_consts(nc, ["ident", "identf", "c1q", "s1q", "c1k", "s1k", "R1",
                                  "S2", "S2T", "ID2", "Bband", "ones_1x128",
                                  "ones_128x1"])
    else:
        nc = env.nc
        cst = env.cst
        (xn1f, Wqkv, q_sc, k_sc, Wo1, ln_g, ln_b, W1l, b1l, W2l, b2l, g1f) = io
    P_ = "E" if env else ""
    dkind = "ExternalOutput" if (debug and env is None) else "Internal"
    xsP = nc.dram_tensor(P_ + "xsP", [D, T], F32, kind=dkind).ap()
    xsQ = nc.dram_tensor(P_ + "xsQ", [D, T], F32, kind=dkind).ap()
    oat1 = nc.dram_tensor(P_ + "oat1", [D, T], BF16, kind=dkind).ap()
    qscr = nc.dram_tensor(P_ + "qscr", [D, T], BF16, kind=dkind).ap()
    kscr1 = nc.dram_tensor(P_ + "kscr1", [D, T], BF16, kind=dkind).ap()
    SH = 300.0
    NS = 256
    NSG = T // NS
    layers = list(layers)
    _names = {}
    with ExitStack() as es:
        def mk_sb(stack):
            def f(name, shape, dt):
                k = _names.get(name, 0)
                _names[name] = k + 1
                if k:
                    name = f"{name}_{k}"
                return stack.enter_context(nc.sbuf_tensor(P_ + name, shape, dt))
            return f
        sb = mk_sb(es)
        identb = sb("identb", [128, 128], BF16)
        identf = sb("identf", [128, 128], F32)
        c1qt = sb("c1qt", [128, T], BF16)
        s1qt = sb("s1qt", [128, T], BF16)
        c1kt = sb("c1kt", [128, T], BF16)
        s1kt = sb("s1kt", [128, T], BF16)
        R1s = sb("R1s", [128, 128], BF16)
        S2s = sb("S2s", [2, 128], BF16)
        S2Ts = sb("S2Ts", [128, 2], BF16)
        ID2s = sb("ID2s", [64, 128], BF16)
        Bb = sb("Bb", [128, 384], F32)
        ones1x128 = sb("ones1x128", [1, 128], BF16)
        ones128x1 = sb("ones128x1", [128, 1], BF16)
        qs2 = sb("qs2", [128, 1], F32)
        ks2 = sb("ks2", [128, 1], F32)
        btri = sb("btri", [128, 1], F32)
        gt = sb("gt", [128, 8], F32)
        bt = sb("bt", [128, 8], F32)
        b1t = sb("b1t", [128, 44], F32)
        b2t = sb("b2t", [128, 8], F32)
        qsr = sb("qsr", [1, 2 * 64], F32)
        qsb = sb("qsb", [1, 2 * 64], BF16)
        gbr = sb("gbr", [8, 2 * 128], F32)
        b1ra = sb("b1ra", [21, 128], F32)
        b1rg = sb("b1rg", [21, 128], F32)
        b1rb = sb("b1rb", [1, 84], F32)
        sm = sb("sm", [1, 8], F32)
        smb = sb("smb", [1, 1], BF16)
        qcol = sb("qcol", [64, 2], BF16)
        pp = es.enter_context(nc.psum_tensor(P_ + "pp", [128, 8 * 512], F32))
        if env is None:
            block = es.enter_context(nc.Block())
            ld = SemC(nc, "ld"); vq = SemC(nc, "vq"); sc = SemC(nc, "sc")
            mm = SemC(nc, "mm"); stq = SemC(nc, "stq")
        else:
            block = env.block
            ld, vq, sc, mm, stq = env.sems

        def bank(i, lo=0, hi=512):
            return pp[:, i * 512 + lo:i * 512 + hi]

        @block.sync
        def _(sy):
            for nm, dst in (("ident", identb), ("identf", identf), ("c1q", c1qt),
                            ("s1q", s1qt), ("c1k", c1kt), ("s1k", s1kt), ("R1", R1s),
                            ("S2", S2s), ("S2T", S2Ts), ("ID2", ID2s), ("Bband", Bb),
                            ("ones_1x128", ones1x128), ("ones_128x1", ones128x1)):
                sy.dma_start(out=dst[:], in_=cst[nm]).then_inc(ld.h, 16)
        ld.n += 16 * 13

        for l in layers:
            xin = xn1f if l == layers[0] else xsQ
            fout = g1f if l == layers[-1] else xsQ

            if l != layers[0]:
                _bm, _bv, _bs, _bq, _bl = mm.n, vq.n, sc.n, stq.n, ld.n

                def _bar(e, _bm=_bm, _bv=_bv, _bs=_bs, _bq=_bq, _bl=_bl):
                    e.wait_ge(mm.h, _bm)
                    e.wait_ge(vq.h, _bv)
                    e.wait_ge(sc.h, _bs)
                    e.wait_ge(stq.h, _bq)
                    e.wait_ge(ld.h, _bl)
                block.tensor(_bar)
                block.vector(_bar)
                block.scalar(_bar)
                block.gpsimd(_bar)
                block.sync(_bar)

            @block.sync
            def _(sy, l=l):
                sy.wait_ge(mm.h, mm.n)
                sy.wait_ge(vq.h, vq.n)
                sy.wait_ge(sc.h, sc.n)
                sy.dma_start(out=qsr[:, 0:64], in_=q_sc[l:l + 1, :]).then_inc(ld.h, 16)
                sy.dma_start(out=qsr[:, 64:128], in_=k_sc[l:l + 1, :]).then_inc(ld.h, 16)
                sy.dma_start(out=gbr[:, 0:128], in_=ln_g[l]).then_inc(ld.h, 16)
                sy.dma_start(out=gbr[:, 128:256], in_=ln_b[l]).then_inc(ld.h, 16)
                sy.dma_start(out=b1ra[0:21, :],
                             in_=b1l[l:l + 1, 0:2688].rearrange("a (c p) -> (a c) p", p=128)
                             ).then_inc(ld.h, 16)
                sy.dma_start(out=b1rg[:],
                             in_=b1l[l:l + 1, 2730:5418].rearrange("a (c p) -> (a c) p", p=128)
                             ).then_inc(ld.h, 16)
                sy.dma_start(out=b1rb[:, 0:42], in_=b1l[l:l + 1, 2688:2730]).then_inc(ld.h, 16)
                sy.dma_start(out=b1rb[:, 42:84], in_=b1l[l:l + 1, 5418:5460]).then_inc(ld.h, 16)
            ld.n += 16 * 8

            @block.gpsimd
            def _(g, l=l):
                g.dma_start(out=qsb[:, 0:64], in_=q_sc[l:l + 1, :]).then_inc(ld.h, 16)
                g.dma_start(out=qsb[:, 64:128], in_=k_sc[l:l + 1, :]).then_inc(ld.h, 16)
            ld.n += 16 * 2
            SMLD = ld.n

            @block.tensor
            def _(t, l=l):
                t.wait_ge(ld.h, SMLD)
                t.wait_ge(vq.h, vq.n)
                t.wait_ge(sc.h, sc.n)
                mm.inc(_tmm(t, bank(0, 0, 8), gbr[:, 0:128], identf[0:8, 0:8]))
                mm.inc(_tmm(t, bank(0, 8, 16), gbr[:, 128:256], identf[0:8, 0:8]))
                mm.inc(_tmm(t, bank(0, 16, 37), b1ra[:], identf[0:21, 0:21]))
                mm.inc(_tmm(t, bank(0, 40, 61), b1rg[:], identf[0:21, 0:21]))
                mm.inc(_tmm(t, bank(0, 64, 65)[0:42, :], b1rb[:, 0:42], identf[0:1, 0:1]))
                mm.inc(_tmm(t, bank(0, 68, 69)[0:42, :], b1rb[:, 42:84], identf[0:1, 0:1]))
                mm.inc(_tmm(t, bank(1, 0, 1)[0:64, :], qsr[0:1, 0:64], identf[0:1, 0:1]))
                mm.inc(_tmm(t, bank(1, 4, 5)[0:64, :], qsr[0:1, 64:128], identf[0:1, 0:1]))
            mm_t0 = mm.n

            @block.sync
            def _(sy, l=l, mm_t0=mm_t0):
                sy.wait_ge(mm.h, mm_t0)
                sy.dma_start(out=gbr[:, 0:128], in_=b2l[l]).then_inc(ld.h, 16)
            ld.n += 16
            B2LD = ld.n

            @block.vector
            def _(v, l=l):
                v.wait_ge(mm.h, mm_t0)
                vq.wait(v)
                vq.inc(v.tensor_copy(gt[:], bank(0, 0, 8)))
                vq.wait(v)
                vq.inc(v.tensor_copy(bt[:], bank(0, 8, 16)))
                vq.wait(v)
                vq.inc(v.tensor_copy(b1t[:, 0:21], bank(0, 16, 37)))
                vq.wait(v)
                vq.inc(v.tensor_copy(b1t[:, 22:43], bank(0, 40, 61)))
                vq.wait(v)
                vq.inc(v.tensor_copy(b1t[0:42, 21:22], bank(0, 64, 65)[0:42, :]))
                vq.wait(v)
                vq.inc(v.tensor_copy(b1t[0:42, 43:44], bank(0, 68, 69)[0:42, :]))
                vq.wait(v)
                vq.inc(v.tensor_copy(qcol[:, 0:1], bank(1, 0, 1)[0:64, :]))
                vq.wait(v)
                vq.inc(v.tensor_copy(qcol[:, 1:2], bank(1, 4, 5)[0:64, :]))
            v_t0 = vq.n

            @block.tensor
            def _(t, l=l):
                t.wait_ge(vq.h, v_t0)
                t.wait_ge(ld.h, B2LD)
                mm.inc(t.matmul(bank(2, 0, 1), ID2s[:], qcol[:, 0:1], start=True, stop=True))
                mm.inc(t.matmul(bank(2, 4, 5), ID2s[:], qcol[:, 1:2], start=True, stop=True))
                mm.inc(_tmm(t, bank(2, 8, 16), gbr[:, 0:128], identf[0:8, 0:8]))
            mm_t1 = mm.n

            @block.vector
            def _(v, l=l):
                v.wait_ge(mm.h, mm_t1)
                vq.wait(v)
                vq.inc(v.tensor_copy(qs2[:], bank(2, 0, 1)))
                vq.wait(v)
                vq.inc(v.tensor_copy(ks2[:], bank(2, 4, 5)))
                vq.wait(v)
                vq.inc(v.tensor_copy(b2t[:], bank(2, 8, 16)))
                vq.wait(v)
                vq.inc(v.tensor_tensor(out=sm[:, 0:1].rearrange("a b -> a b") if False
                                       else qsr[:, 0:64], in0=qsr[:, 0:64],
                                       in1=qsr[:, 64:128], op=ALU.mult))
                vq.wait(v)
                vq.inc(v.reduce_max(sm[:, 0:1], qsr[:, 0:64], axis=AX.X,
                                    apply_absolute_value=True))
                vq.wait(v)
                vq.inc(v.tensor_scalar(out=sm[:, 1:2], in0=sm[:, 0:1], scalar1=8.0,
                                       scalar2=SH, op0=ALU.mult, op1=ALU.add))
                vq.wait(v)
                vq.inc(v.tensor_copy(smb[:], sm[:, 1:2]))
            v_t1 = vq.n

            @block.tensor
            def _(t):
                t.wait_ge(vq.h, v_t1)
                mm.inc(t.matmul(bank(3, 0, 1), ones1x128[:], smb[:], start=True, stop=True))
            mm_t2 = mm.n

            @block.scalar
            def _(s):
                s.wait_ge(mm.h, mm_t2)
                sc.inc(s.mul(btri[:], bank(3, 0, 1), -1.0))
            sc_btri = sc.n

            es_va = ExitStack()
            va = mk_sb(es_va)("va", [128, NT * 1040], BF16)

            @block.vector
            def _(v, _stq=None):
                v.wait_ge(mm.h, mm.n)
                v.wait_ge(sc.h, sc.n)
                v.wait_ge(stq.h, stq.n)
                v.wait_ge(ld.h, ld.n)
                vq.wait(v)
                vq.inc(v.memset(va[:].rearrange("p (t o) -> p t o", t=NT * 16)[:, :, 64:65],
                                1.0))
            V_ONES = vq.n
            es_qkv = ExitStack()
            sbq = mk_sb(es_qkv)
            wqkv_sb = sbq("wqkv_sb", [128, ND * 3072], BF16)
            xc = sbq("xc", [128, ND * 513], F32)
            xs2 = sbq("xs2", [128, ND * 512], BF16)
            rawp = sbq("rawp", [64, 0 * 512 + 512], BF16) if False else None
            rawq = sbq("rawq", [128, 2 * 512], BF16)
            sqp = sbq("sqp", [128, 512], BF16)
            n2c = sbq("n2c", [2, 512], F32)
            n2l = sbq("n2l", [2, 512], F32)
            rn2 = sbq("rn2", [2, 512], BF16)
            rbs = sbq("rbs", [128, 2 * 512], BF16)
            v1b = sbq("v1b", [128, 512], BF16)
            v2b = sbq("v2b", [128, 2 * 512], BF16)
            rt1b = sbq("rt1b", [128, 512], BF16)
            rt2b = sbq("rt2b", [128, 512], BF16)
            qsave = sbq("qsave", [128, ND * 512], BF16)
            ksave = sbq("ksave", [128, ND * 512], BF16)

            @block.gpsimd
            def _(g, l=l):
                g.wait_ge(mm.h, mm.n)
                g.wait_ge(sc.h, sc.n)
                g.wait_ge(vq.h, vq.n)
                g.wait_ge(stq.h, stq.n)
                g.wait_ge(ld.h, ld.n)
                for i in range(ND):
                    g.dma_start(out=wqkv_sb[:, i * 3072:(i + 1) * 3072],
                                in_=Wqkv[l, i * 128:(i + 1) * 128, :]).then_inc(ld.h, 16)
            ld.n += 16 * ND
            WQKVLD = ld.n

            ev_sc = {}
            rope_done = {}
            vsc_ev = {}
            qk_st = {}
            for tc in range(NTG):
                t0 = tc * TG
                cb = (tc % 2)

                @block.sync
                def _(sy, l=l, tc=tc, t0=t0, cb=cb, xin=xin, _st=stq.n, _ld=ld.n):
                    if tc == 0:
                        sy.wait_ge(stq.h, _st)
                        sy.wait_ge(ld.h, _ld)
                    if tc >= 1:
                        sy.wait_ge(vq.h, rope_done[tc - 1])
                        sy.wait_ge(mm.h, qk_st[(tc - 1, 'mm')])
                    if tc == 0:
                        sy.dma_start(out=xc[:, 0:ND * 513]
                                     .rearrange("p (c t) -> p c t", c=ND)[:, :, 1:513],
                                     in_=xin.rearrange("(c p) t -> p c t", p=128)
                                     [:, :, 0:512]).then_inc(ld.h, 16)
                    else:
                        sy.dma_start(out=xc[:, 0:ND * 513]
                                     .rearrange("p (c t) -> p c t", c=ND),
                                     in_=xin.rearrange("(c p) t -> p c t", p=128)
                                     [:, :, t0 - 1:t0 + 512]).then_inc(ld.h, 16)
                ld.n += 16
                XLD = ld.n

                @block.vector
                def _(v, tc=tc, cb=cb, XLD=XLD):
                    v.wait_ge(ld.h, XLD)
                    vq.wait(v)
                    if tc == 0:
                        vq.inc(v.memset(xc[:, 0:ND * 513]
                                        .rearrange("p (c t) -> p c t", c=ND)[:, :, 0:1], 0.0))
                        vq.wait(v)
                    for dc in range(ND):
                        off = 1 if dc < 4 else 0
                        vq.inc(v.tensor_copy(
                            xs2[:, dc * 512:(dc + 1) * 512],
                            xc[:, dc * 513 + off:dc * 513 + off + 512]))
                        vq.wait(v)
                v_xs = vq.n

                for tt in range(4):
                    g_tt = tc * 4 + tt
                    for vg in range(2):
                        @block.tensor
                        def _(t, tc=tc, cb=cb, tt=tt, vg=vg, v_xs=v_xs):
                            t.wait_ge(vq.h, v_xs)
                            t.wait_ge(ld.h, WQKVLD)
                            t.wait_ge(sc.h, sc.n)
                            for dc in range(ND):
                                mm.inc(t.matmul(
                                    bank(2 + vg),
                                    xs2[:, dc * 512 + tt * 128:dc * 512 + tt * 128 + 128],
                                    wqkv_sb[:, dc * 3072 + 2048 + vg * 512:
                                            dc * 3072 + 2048 + (vg + 1) * 512],
                                    start=(dc == 0), stop=(dc == ND - 1)))

                        @block.scalar
                        def _(s, g_tt=g_tt, vg=vg):
                            mm.wait(s)
                            s.wait_ge(vq.h, V_ONES)
                            dst = va[:, g_tt * 1040 + vg * 520:g_tt * 1040 + (vg + 1) * 520] \
                                .rearrange("p (h o) -> p h o", h=8)[:, :, 0:64]
                            sc.inc(s.copy(dst, bank(2 + vg).rearrange("p (h o) -> p h o", h=8)))
                        vsc_ev[(tc, tt, vg)] = sc.n

                E1T = mm.n
                E1V = vq.n
                E1S = sc.n
                t_proj = {pr: E1T + 11 * pr + 8 for pr in range(16)}
                t_n2 = {pr: E1T + 11 * pr + 9 for pr in range(16)}
                t_rb = {pr: E1T + 11 * pr + 10 for pr in range(16)}
                t_R = {pr: E1T + 11 * pr + 11 for pr in range(16)}
                v_sq = {pr: E1V + 7 * pr + 1 for pr in range(16)}
                v_cl = {pr: E1V + 7 * pr + 2 for pr in range(16)}
                v_v1 = {pr: E1V + 7 * pr + 3 for pr in range(16)}
                v_v2 = {pr: E1V + 7 * pr + 4 for pr in range(16)}
                v_t2 = {pr: E1V + 7 * pr + 6 for pr in range(16)}
                v_dst = {pr: E1V + 7 * pr + 7 for pr in range(16)}
                s_raw = {pr: E1S + 4 * pr + 1 for pr in range(16)}
                s_rn = {pr: E1S + 4 * pr + 3 for pr in range(16)}
                s_rbs = {pr: E1S + 4 * pr + 4 for pr in range(16)}

                @block.tensor
                def _(t, tc=tc, v_xs=v_xs):
                    t.wait_ge(ld.h, WQKVLD)
                    t.wait_ge(vq.h, v_xs)
                    for pr in range(16):
                        is_q = pr < 8
                        fbase = pr * 128 if is_q else 1024 + (pr - 8) * 128
                        prb = pr % 2
                        if pr >= 2:
                            t.wait_ge(sc.h, s_raw[pr - 2])
                        for dc in range(ND):
                            mm.inc(t.matmul(
                                bank(prb),
                                wqkv_sb[:, dc * 3072 + fbase:dc * 3072 + fbase + 128],
                                xs2[:, dc * 512:(dc + 1) * 512],
                                start=(dc == 0), stop=(dc == ND - 1)))
                        t.wait_ge(vq.h, v_sq[pr])
                        if pr >= 1:
                            t.wait_ge(vq.h, v_cl[pr - 1])
                        mm.inc(t.matmul(bank(4, 0, 512)[0:2, :], S2Ts[:], sqp[:],
                                        start=True, stop=True))
                        t.wait_ge(sc.h, s_rn[pr])
                        if pr >= 1:
                            t.wait_ge(sc.h, s_rbs[pr - 1])
                        mm.inc(t.matmul(bank(5), S2s[:], rn2[:], start=True, stop=True))
                        t.wait_ge(vq.h, v_v2[pr])
                        if pr >= 2:
                            t.wait_ge(vq.h, v_t2[pr - 2])
                        mm.inc(t.matmul(bank(6 + prb), R1s[:],
                                        v2b[:, prb * 512:(prb + 1) * 512],
                                        start=True, stop=True))
                    assert mm.n == t_R[15], (mm.n,)

                @block.scalar
                def _(s, tc=tc):
                    for pr in range(16):
                        prb = pr % 2
                        s.wait_ge(mm.h, t_proj[pr])
                        if pr >= 2:
                            s.wait_ge(vq.h, v_v1[pr - 2])
                        sc.inc(s.copy(rawq[:, prb * 512:(prb + 1) * 512], bank(prb)))
                        s.wait_ge(vq.h, v_cl[pr])
                        sc.inc(s.activation(n2l[:], n2c[:], AF.Ln))
                        with nc.allow_low_precision(reason="l2norm rsqrt bf16"):
                            sc.inc(s.activation(rn2[:], n2l[:], AF.Exp, scale=-0.5))
                        s.wait_ge(mm.h, t_rb[pr])
                        if pr >= 2:
                            s.wait_ge(vq.h, v_v1[pr - 1])
                        sc.inc(s.copy(rbs[:, prb * 512:(prb + 1) * 512], bank(5)))
                    assert sc.n == s_rbs[15], (sc.n,)

                @block.vector
                def _(v, tc=tc, t0=t0):
                    for pr in range(16):
                        is_q = pr < 8
                        prb = pr % 2
                        ctab = c1qt if is_q else c1kt
                        stab = s1qt if is_q else s1kt
                        dst = qsave if is_q else ksave
                        dpr2 = pr if is_q else pr - 8
                        v.wait_ge(sc.h, s_raw[pr])
                        if pr >= 1:
                            v.wait_ge(mm.h, t_n2[pr - 1])
                        vq.inc(v.tensor_tensor(out=sqp[:],
                                               in0=rawq[:, prb * 512:(prb + 1) * 512],
                                               in1=rawq[:, prb * 512:(prb + 1) * 512],
                                               op=ALU.mult))
                        v.wait_ge(mm.h, t_n2[pr])
                        if pr >= 1:
                            v.wait_ge(sc.h, s_rn[pr - 1])
                        vq.inc(v.tensor_scalar(out=n2c[:], in0=bank(4, 0, 512)[0:2, :],
                                               scalar1=1e-24, scalar2=None, op0=ALU.max))
                        v.wait_ge(sc.h, s_rbs[pr])
                        vq.inc(v.tensor_tensor(out=v1b[:],
                                               in0=rawq[:, prb * 512:(prb + 1) * 512],
                                               in1=rbs[:, prb * 512:(prb + 1) * 512],
                                               op=ALU.mult))
                        if pr >= 2:
                            v.wait_ge(mm.h, t_R[pr - 2])
                        vq.inc(v.tensor_scalar(out=v2b[:, prb * 512:(prb + 1) * 512],
                                               in0=v1b[:],
                                               scalar1=(qs2[:, 0:1] if is_q else ks2[:, 0:1]),
                                               scalar2=None, op0=ALU.mult))
                        vq.inc(v.tensor_tensor(out=rt1b[:],
                                               in0=v2b[:, prb * 512:(prb + 1) * 512],
                                               in1=ctab[:, t0:t0 + 512], op=ALU.mult))
                        v.wait_ge(mm.h, t_R[pr])
                        vq.inc(v.tensor_tensor(out=rt2b[:], in0=bank(6 + prb),
                                               in1=stab[:, t0:t0 + 512], op=ALU.mult))
                        if pr == 0 and tc >= 1:
                            v.wait_ge(stq.h, qk_st[tc - 1])
                        vq.inc(v.tensor_tensor(out=dst[:, dpr2 * 512:(dpr2 + 1) * 512],
                                               in0=rt1b[:], in1=rt2b[:], op=ALU.add))
                    assert vq.n == v_dst[15], (vq.n,)
                rope_done[tc] = vq.n

                @block.sync
                def _(sy, tc=tc, t0=t0):
                    sy.wait_ge(vq.h, rope_done[tc])
                    sy.dma_start(out=qscr.rearrange("(c p) t -> p c t", p=128)
                                 [:, :, t0:t0 + 512],
                                 in_=qsave[:].rearrange("p (c t) -> p c t", c=ND)
                                 ).then_inc(stq.h, 16)
                    sy.dma_start(out=kscr1.rearrange("(c p) t -> p c t", p=128)
                                 [:, :, t0:t0 + 512],
                                 in_=ksave[:].rearrange("p (c t) -> p c t", c=ND)
                                 ).then_inc(stq.h, 16)
                stq.n += 32
                qk_st[tc] = stq.n
                qk_st[(tc, 'mm')] = mm.n
            E1_ST = stq.n
            E1_SC = sc.n

            es_qkv.close()

            es_at = ExitStack()
            sba = mk_sb(es_at)
            qh = sba("qh", [64, 2 * T], BF16)
            kh = sba("kh", [64, 2 * T], BF16)
            s_sb = sba("s_sb", [128, 2 * 384], F32)
            p_sb = sba("p_sb", [128, 2 * 384], BF16)
            rr1 = sba("rr1", [1, 512], BF16)
            rrf = sba("rrf", [1, 512], F32)
            rb1 = sba("rb1", [64, 2 * 512], BF16)
            oh1 = sba("oh1", [64, 2 * T], BF16)

            hld = {}
            oh_st = {}
            for hi, h in enumerate(range(16)):
                hb = hi % 2

                @block.sync
                def _(sy, h=h, hb=hb, _v=vq.n, _m=mm.n):
                    if hi == 0:
                        sy.wait_ge(stq.h, E1_ST)
                        sy.wait_ge(sc.h, E1_SC)
                        sy.wait_ge(vq.h, _v)
                        sy.wait_ge(mm.h, _m)
                    if hi >= 2:
                        sy.wait_ge(mm.h, hld[(hi - 2, 'mm')])
                    sy.dma_start(out=qh[:, hb * T:(hb + 1) * T],
                                 in_=qscr[h * 64:(h + 1) * 64, :]).then_inc(ld.h, 16)
                    sy.dma_start(out=kh[:, hb * T:(hb + 1) * T],
                                 in_=kscr1[h * 64:(h + 1) * 64, :]).then_inc(ld.h, 16)
                ld.n += 32
                hld[hi] = ld.n

                HSV = vq.n
                HSM = mm.n
                HSS = sc.n
                HST = stq.n
                spans = [min(384, T - 128 * c) for c in range(NT)]
                navs = []
                for c in range(NT):
                    n = 0
                    for mq in range(c, min(c + 3, NT)):
                        if 128 * (mq - c) < spans[c]:
                            n += 1
                    navs.append(n)
                mm_qk = {}; mm_av = {}; mm_rb = {}
                cnt = HSM
                for c in range(NT):
                    cnt += 1; mm_qk[c] = cnt
                    cnt += navs[c]; mm_av[c] = cnt
                    if c % 4 == 3:
                        cnt += 1; mm_rb[c // 4] = cnt
                sc_exp = {}; sc_rcp = {}
                sn = HSS
                for c in range(NT):
                    sn += 1; sc_exp[c] = sn
                    if c % 4 == 3:
                        sn += 2; sc_rcp[c // 4] = sn
                vq_stt = {}; vq_div = {}
                vn = HSV
                for c in range(NT):
                    vn += 1; vq_stt[c] = vn
                    if c % 4 == 3:
                        vn += 2; vq_div[c // 4] = vn

                @block.tensor
                def _(t, h=h, hb=hb, hi=hi):
                    t.wait_ge(ld.h, hld[hi])
                    for c in range(NT):
                        span = spans[c]
                        pb2 = c % 2
                        t.wait_ge(vq.h, vq_stt[c - 2] if c >= 2 else HSV)
                        mm.inc(t.matmul(bank(pb2, 0, span),
                                        kh[:, hb * T + c * 128:hb * T + (c + 1) * 128],
                                        qh[:, hb * T + c * 128:hb * T + c * 128 + span],
                                        start=True, stop=True))
                        t.wait_ge(sc.h, sc_exp[c])
                        for mq in range(c, min(c + 3, NT)):
                            if 128 * (mq - c) >= span:
                                continue
                            gq = mq // 4
                            t.wait_ge(vq.h, vq_div.get(gq - 2, HSV))
                            mm.inc(t.matmul(
                                bank(2 + gq % 2, (mq % 4) * 128, (mq % 4) * 128 + 128)
                                [0:65, :],
                                va[:, c * 1040 + (h // 8) * 520 + (h % 8) * 65:
                                   c * 1040 + (h // 8) * 520 + (h % 8) * 65 + 65],
                                p_sb[:, pb2 * 384 + (mq - c) * 128:
                                     pb2 * 384 + (mq - c) * 128 + 128],
                                start=(mq % 4 == 0 and c == max(0, mq - 2)),
                                stop=(c == mq and (mq % 4 == 3 or mq == NT - 1)),
                                skip_group_check=True))
                        if c % 4 == 3:
                            gq = c // 4
                            t.wait_ge(sc.h, sc_rcp[gq])
                            mm.inc(t.matmul(bank(6)[0:64, :], S2s[0:1, 0:64],
                                            rr1[:, 0:512], start=True, stop=True))
                    assert mm.n == cnt, (mm.n, cnt)

                @block.vector
                def _(v, h=h, hi=hi):
                    for c in range(NT):
                        span = spans[c]
                        pb2 = c % 2
                        v.wait_ge(mm.h, mm_qk[c])
                        if c >= 2:
                            v.wait_ge(sc.h, sc_exp[c - 2])
                        vq.inc(v.scalar_tensor_tensor(
                            out=s_sb[:, pb2 * 384:pb2 * 384 + span],
                            in0=bank(pb2, 0, span), scalar=SH,
                            in1=Bb[:, 0:span], op0=ALU.add, op1=ALU.mult))
                        if c % 4 == 3:
                            gq = c // 4
                            v.wait_ge(mm.h, mm_rb[gq])
                            if gq == 0 and (hi - 2) in oh_st:
                                v.wait_ge(stq.h, oh_st[hi - 2])
                            vq.inc(v.tensor_copy(rb1[:, (gq % 2) * 512:(gq % 2) * 512 + 512],
                                                 bank(6)[0:64, :]))
                            vq.inc(v.tensor_tensor(
                                out=oh1[:, (hi % 2) * T + gq * 512:
                                         (hi % 2) * T + (gq + 1) * 512],
                                in0=bank(2 + gq % 2)[0:64, :],
                                in1=rb1[:, (gq % 2) * 512:(gq % 2) * 512 + 512],
                                op=ALU.mult))
                    assert vq.n == vn, (vq.n, vn)

                @block.scalar
                def _(s, h=h):
                    for c in range(NT):
                        span = spans[c]
                        pb2 = c % 2
                        s.wait_ge(vq.h, vq_stt[c])
                        s.wait_ge(mm.h, mm_av[c - 2] if c >= 2 else HSM)
                        sc.inc(s.activation(p_sb[:, pb2 * 384:pb2 * 384 + span],
                                            s_sb[:, pb2 * 384:pb2 * 384 + span],
                                            AF.Exp, bias=btri[:, 0:1]))
                        if c % 4 == 3:
                            gq = c // 4
                            s.wait_ge(mm.h, mm_av[c])
                            sc.inc(s.activation(rrf[:], bank(2 + gq % 2)[64:65, :],
                                                AF.Ln))
                            with nc.allow_low_precision(reason="softmax denom bf16"):
                                sc.inc(s.activation(rr1[:, 0:512], rrf[:],
                                                    AF.Exp, scale=-1.0))
                    assert sc.n == sn, (sc.n, sn)

                @block.sync
                def _(sy, h=h, hi=hi):
                    sy.wait_ge(vq.h, vq_div[7])
                    sy.dma_start(out=oat1[h * 64:(h + 1) * 64, :],
                                 in_=oh1[:, (hi % 2) * T:(hi % 2 + 1) * T]
                                 ).then_inc(stq.h, 16)
                stq.n += 16
                oh_st[hi] = stq.n
                hld[(hi, 'mm')] = mm.n
            E2_ST = stq.n
            es_at.close()
            es_va.close()

            es_wo = ExitStack()
            sbo = mk_sb(es_wo)
            wo_sb = sbo("wo_sb", [128, ND * D], BF16)
            att1 = sbo("att1", [128, ND * 512], BF16)
            xres = sbo("xres", [128, ND * 512], F32)
            xo1 = sbo("xo1", [128, 2 * 512], F32)

            @block.gpsimd
            def _(g, l=l):
                g.wait_ge(mm.h, mm.n)
                g.wait_ge(sc.h, sc.n)
                g.wait_ge(vq.h, vq.n)
                g.wait_ge(stq.h, stq.n)
                g.wait_ge(ld.h, ld.n)
                for i in range(ND):
                    g.dma_start(out=wo_sb[:, i * D:(i + 1) * D],
                                in_=Wo1[l, i * 128:(i + 1) * 128, :]).then_inc(ld.h, 16)
            ld.n += 16 * ND
            WOLD = ld.n

            a_ld = {}
            xo_v = {}
            xo_stq = {}
            for tg in range(NTG):
                @block.sync
                def _(sy, tg=tg, xin=xin, _v=vq.n, _m=mm.n, _s=sc.n):
                    if tg == 0:
                        sy.wait_ge(vq.h, _v)
                        sy.wait_ge(mm.h, _m)
                        sy.wait_ge(sc.h, _s)
                        sy.wait_ge(stq.h, E2_ST)
                    if tg >= 2:
                        sy.wait_ge(mm.h, xo_v[(tg - 2, 'mm')])
                        sy.wait_ge(vq.h, xo_v[tg - 2])
                    sy.dma_start(out=att1[:].rearrange("p (c t) -> p c t", c=ND),
                                 in_=oat1.rearrange("(c p) t -> p c t", p=128)
                                 [:, :, tg * TG:(tg + 1) * TG]).then_inc(ld.h, 16)
                    sy.dma_start(out=xres[:].rearrange("p (c t) -> p c t", c=ND),
                                 in_=xin.rearrange("(c p) t -> p c t", p=128)
                                 [:, :, tg * TG:(tg + 1) * TG]).then_inc(ld.h, 16)
                ld.n += 32
                a_ld[tg] = ld.n

                for dt in range(ND):
                    k = tg * ND + dt

                    @block.tensor
                    def _(t, tg=tg, dt=dt, k=k):
                        t.wait_ge(ld.h, a_ld[tg])
                        t.wait_ge(ld.h, WOLD)
                        if k >= 2:
                            t.wait_ge(vq.h, xo_v[(k - 2, 'v')])
                        for ic in range(ND):
                            mm.inc(t.matmul(bank(k % 2),
                                            wo_sb[:, ic * D + dt * 128:ic * D + (dt + 1) * 128],
                                            att1[:, ic * 512:(ic + 1) * 512],
                                            start=(ic == 0), stop=(ic == ND - 1)))

                    @block.vector
                    def _(v, tg=tg, dt=dt, k=k):
                        v.wait_ge(mm.h, mm.n)
                        if k >= 2:
                            v.wait_ge(stq.h, xo_stq[k - 2])
                        vq.wait(v)
                        vq.inc(v.tensor_tensor(out=xo1[:, (k % 2) * 512:(k % 2) * 512 + 512],
                                               in0=bank(k % 2),
                                               in1=xres[:, dt * 512:(dt + 1) * 512],
                                               op=ALU.add))
                    xo_v[(k, 'v')] = vq.n

                    @block.sync
                    def _(sy, tg=tg, dt=dt, k=k):
                        sy.wait_ge(vq.h, xo_v[(k, 'v')])
                        sy.dma_start(out=xsP[dt * 128:(dt + 1) * 128, tg * TG:(tg + 1) * TG],
                                     in_=xo1[:, (k % 2) * 512:(k % 2) * 512 + 512]
                                     ).then_inc(stq.h, 16)
                    stq.n += 16
                    xo_stq[k] = stq.n
                xo_v[tg] = vq.n
                xo_v[(tg, 'mm')] = mm.n
            E3_ST = stq.n
            es_wo.close()

            es_ff = ExitStack()
            sbf = mk_sb(es_ff)
            w1_sb = sbf("w1f_sb", [128, ND * 5460], BF16)
            w2_sb = sbf("w2f_sb", [128, 22 * D], BF16)
            xc2 = sbf("xc2", [128, 2 * ND * 257], F32)
            xs3 = sbf("xs3", [128, 2 * ND * 256], BF16)
            sq3 = sbf("sq3", [128, 256], BF16)
            rows3 = sbf("rows3", [1, 4 * 256], F32)
            rows3b = sbf("rows3b", [1, 2 * 256], BF16)
            mb3 = sbf("mb3", [128, 256], F32)
            ib3 = sbf("ib3", [128, 256], F32)
            xnl = sbf("xnl", [128, ND * 256], BF16)
            gel = sbf("gel", [128, 2 * 256], BF16)
            hfb = sbf("hfb", [128, 2 * 256], BF16)
            xo2 = sbf("xo2", [128, 2 * 256], F32)

            @block.gpsimd
            def _(g, l=l):
                g.wait_ge(mm.h, mm.n)
                g.wait_ge(sc.h, sc.n)
                g.wait_ge(vq.h, vq.n)
                g.wait_ge(stq.h, stq.n)
                g.wait_ge(ld.h, ld.n)
                for i in range(ND):
                    g.dma_start(out=w1_sb[:, i * 5460:(i + 1) * 5460],
                                in_=W1l[l, i * 128:(i + 1) * 128, :]).then_inc(ld.h, 16)
                for i in range(21):
                    g.dma_start(out=w2_sb[:, i * D:(i + 1) * D],
                                in_=W2l[l, i * 128:(i + 1) * 128, :]).then_inc(ld.h, 16)
                g.dma_start(out=w2_sb[0:42, 21 * D:22 * D],
                            in_=W2l[l, 2688:2730, :]).then_inc(ld.h, 16)
            ld.n += 16 * (ND + 22)
            WFLD = ld.n

            FPAIR = [(j * 128, min(128, 2730 - j * 128)) for j in range(22)]
            ns_v = {}
            ns_st = {}
            for ns in range(NSG):
                t0 = ns * NS
                nb = ns % 2

                @block.sync
                def _(sy, ns=ns, t0=t0, nb=nb, E3_ST=E3_ST, _v=vq.n, _m=mm.n):
                    if ns == 0:
                        sy.wait_ge(stq.h, E3_ST)
                        sy.wait_ge(vq.h, _v)
                        sy.wait_ge(mm.h, _m)
                    if ns >= 2:
                        sy.wait_ge(vq.h, ns_v[ns - 2])
                        sy.wait_ge(mm.h, ns_v[(ns - 2, 'mm')])
                    if ns == 0:
                        sy.dma_start(out=xc2[:, nb * ND * 257:(nb + 1) * ND * 257]
                                     .rearrange("p (c t) -> p c t", c=ND)[:, :, 1:257],
                                     in_=xsP.rearrange("(c p) t -> p c t", p=128)
                                     [:, :, 0:256]).then_inc(ld.h, 16)
                    else:
                        sy.dma_start(out=xc2[:, nb * ND * 257:(nb + 1) * ND * 257]
                                     .rearrange("p (c t) -> p c t", c=ND),
                                     in_=xsP.rearrange("(c p) t -> p c t", p=128)
                                     [:, :, t0 - 1:t0 + 256]).then_inc(ld.h, 16)
                ld.n += 16
                X2LD = ld.n

                @block.vector
                def _(v, ns=ns, nb=nb, X2LD=X2LD):
                    v.wait_ge(ld.h, X2LD)
                    vq.wait(v)
                    if ns == 0:
                        vq.inc(v.memset(xc2[:, nb * ND * 257:(nb + 1) * ND * 257]
                                        .rearrange("p (c t) -> p c t", c=ND)[:, :, 0:1], 0.0))
                        vq.wait(v)
                    for dc in range(ND):
                        off = 1 if dc < 4 else 0
                        vq.inc(v.tensor_copy(
                            xs3[:, nb * ND * 256 + dc * 256:nb * ND * 256 + (dc + 1) * 256],
                            xc2[:, nb * ND * 257 + dc * 257 + off:
                                nb * ND * 257 + dc * 257 + off + 256]))
                        vq.wait(v)
                v_xs3 = vq.n

                @block.tensor
                def _(t, ns=ns, nb=nb, v_xs3=v_xs3):
                    t.wait_ge(vq.h, v_xs3)
                    t.wait_ge(ld.h, WFLD)
                    t.wait_ge(vq.h, vq.n)
                    for dc in range(ND):
                        mm.inc(t.matmul(bank(4, 0, 256)[0:1, :], ones128x1[:],
                                        xs3[:, nb * ND * 256 + dc * 256:
                                            nb * ND * 256 + (dc + 1) * 256],
                                        start=(dc == 0), stop=(dc == ND - 1)))
                m_sum = mm.n

                sqv = {}
                for dc in range(ND):
                    @block.vector
                    def _(v, ns=ns, nb=nb, dc=dc):
                        if dc >= 1:
                            v.wait_ge(mm.h, sqv[dc - 1])
                        vq.wait(v)
                        vq.inc(v.tensor_tensor(out=sq3[:],
                                               in0=xs3[:, nb * ND * 256 + dc * 256:
                                                       nb * ND * 256 + (dc + 1) * 256],
                                               in1=xs3[:, nb * ND * 256 + dc * 256:
                                                       nb * ND * 256 + (dc + 1) * 256],
                                               op=ALU.mult))
                    v_sq3 = vq.n

                    @block.tensor
                    def _(t, dc=dc, v_sq3=v_sq3):
                        t.wait_ge(vq.h, v_sq3)
                        mm.inc(t.matmul(bank(4, 256, 512)[0:1, :], ones128x1[:], sq3[:],
                                        start=(dc == 0), stop=(dc == ND - 1)))
                    sqv[dc] = mm.n
                m_sq = mm.n

                @block.vector
                def _(v, m_sq=m_sq):
                    v.wait_ge(mm.h, m_sq)
                    vq.wait(v)
                    vq.inc(v.tensor_scalar(out=rows3[:, 0:256], in0=bank(4, 0, 256)[0:1, :],
                                           scalar1=-1.0 / D, scalar2=None, op0=ALU.mult))
                    vq.wait(v)
                    vq.inc(v.tensor_tensor(out=rows3[:, 256:512], in0=rows3[:, 0:256],
                                           in1=rows3[:, 0:256], op=ALU.mult))
                    vq.wait(v)
                    vq.inc(v.scalar_tensor_tensor(out=rows3[:, 256:512],
                                                  in0=bank(4, 256, 512)[0:1, :],
                                                  scalar=1.0 / D, in1=rows3[:, 256:512],
                                                  op0=ALU.mult, op1=ALU.subtract))
                    vq.wait(v)
                    vq.inc(v.tensor_scalar(out=rows3[:, 256:512], in0=rows3[:, 256:512],
                                           scalar1=1e-5, scalar2=None, op0=ALU.add))
                v_st1 = vq.n
                SB4 = sc.n
                sc_rsq = SB4 + 2
                sc_bc = SB4 + 4
                sc_gelu = {jp: SB4 + 5 + jp for jp in range(22)}

                @block.vector
                def _(v, sc_rsq=sc_rsq):
                    v.wait_ge(sc.h, sc_rsq)
                    vq.wait(v)
                    vq.inc(v.tensor_copy(rows3b[:, 0:256], rows3[:, 0:256]))
                    vq.wait(v)
                    vq.inc(v.tensor_copy(rows3b[:, 256:512], rows3[:, 768:1024]))
                v_cast = vq.n

                @block.tensor
                def _(t, v_cast=v_cast):
                    t.wait_ge(vq.h, v_cast)
                    t.wait_ge(sc.h, sc.n)
                    mm.inc(t.matmul(bank(5, 0, 256), ones1x128[:], rows3b[:, 0:256],
                                    start=True, stop=True))
                    mm.inc(t.matmul(bank(5, 256, 512), ones1x128[:], rows3b[:, 256:512],
                                    start=True, stop=True))
                m_bc = mm.n


                for dc in range(ND):
                    @block.vector
                    def _(v, ns=ns, nb=nb, dc=dc, sc_bc=sc_bc):
                        v.wait_ge(sc.h, sc_bc)
                        vq.wait(v)
                        vq.inc(v.tensor_tensor(out=xo2[:, 0:256],
                                               in0=xs3[:, nb * ND * 256 + dc * 256:
                                                       nb * ND * 256 + (dc + 1) * 256],
                                               in1=mb3[:], op=ALU.add))
                        vq.wait(v)
                        vq.inc(v.tensor_tensor(out=xo2[:, 0:256], in0=xo2[:, 0:256],
                                               in1=ib3[:], op=ALU.mult))
                        vq.wait(v)
                        vq.inc(v.tensor_scalar(out=xnl[:, dc * 256:(dc + 1) * 256],
                                               in0=xo2[:, 0:256],
                                               scalar1=gt[:, dc:dc + 1],
                                               scalar2=bt[:, dc:dc + 1],
                                               op0=ALU.mult, op1=ALU.add))
                XN_V = vq.n

                TB4 = mm.n
                m_h = {jp: TB4 + 24 * jp + 16 for jp in range(22)}
                m_zz = {jp: TB4 + 24 * (jp + 1) for jp in range(22)}
                hz_v = {jp: vq.n + jp + 1 for jp in range(22)}

                @block.scalar
                def _(s, ns=ns, v_st1=v_st1, m_bc=m_bc):
                    s.wait_ge(vq.h, v_st1)
                    sc.inc(s.activation(rows3[:, 512:768], rows3[:, 256:512], AF.Ln))
                    sc.inc(s.activation(rows3[:, 768:1024], rows3[:, 512:768],
                                        AF.Exp, scale=-0.5))
                    s.wait_ge(mm.h, m_bc)
                    sc.inc(s.copy(mb3[:], bank(5, 0, 256)))
                    sc.inc(s.copy(ib3[:], bank(5, 256, 512)))
                    for jp, (c0, w) in enumerate(FPAIR):
                        jb2 = jp % 2
                        s.wait_ge(mm.h, m_h[jp])
                        if jp >= 2:
                            s.wait_ge(vq.h, hz_v[jp - 2])
                        sc.inc(s.activation(gel[0:w, jb2 * 256:(jb2 + 1) * 256],
                                            bank(2 + jb2, 0, 256)[0:w, :],
                                            AF.Gelu, bias=b1t[0:w, 22 + jp:23 + jp]))
                assert sc.n == sc_gelu[21] + 0 * ns, (sc.n,)

                @block.vector
                def _(v, ns=ns):
                    for jp, (c0, w) in enumerate(FPAIR):
                        jb2 = jp % 2
                        v.wait_ge(sc.h, sc_gelu[jp])
                        if jp >= 2:
                            v.wait_ge(mm.h, m_zz[jp - 2])
                        vq.inc(v.scalar_tensor_tensor(
                            out=hfb[0:w, jb2 * 256:(jb2 + 1) * 256],
                            in0=bank(jb2, 0, 256)[0:w, :],
                            scalar=b1t[0:w, jp:jp + 1],
                            in1=gel[0:w, jb2 * 256:(jb2 + 1) * 256],
                            op0=ALU.add, op1=ALU.mult))
                assert vq.n == hz_v[21], (vq.n,)

                @block.tensor
                def _(t, ns=ns, XN_V=XN_V):
                    for jp, (c0, w) in enumerate(FPAIR):
                        jb2 = jp % 2
                        t.wait_ge(vq.h, XN_V)
                        if jp >= 2:
                            t.wait_ge(vq.h, hz_v[jp - 2])
                            t.wait_ge(sc.h, sc_gelu[jp - 2])
                        for dc in range(ND):
                            mm.inc(t.matmul(bank(jb2, 0, 256)[0:w, :],
                                            w1_sb[:, dc * 5460 + c0:dc * 5460 + c0 + w],
                                            xnl[:, dc * 256:(dc + 1) * 256],
                                            start=(dc == 0), stop=(dc == ND - 1)))
                        for dc in range(ND):
                            mm.inc(t.matmul(bank(2 + jb2, 0, 256)[0:w, :],
                                            w1_sb[:, dc * 5460 + 2730 + c0:
                                                  dc * 5460 + 2730 + c0 + w],
                                            xnl[:, dc * 256:(dc + 1) * 256],
                                            start=(dc == 0), stop=(dc == ND - 1)))
                        t.wait_ge(vq.h, hz_v[jp])
                        for dt in range(ND):
                            mm.inc(t.matmul(bank(4 + dt // 2, (dt % 2) * 256,
                                                 (dt % 2) * 256 + 256),
                                            w2_sb[0:w, jp * D + dt * 128:
                                                  jp * D + (dt + 1) * 128],
                                            hfb[0:w, jb2 * 256:(jb2 + 1) * 256],
                                            start=(jp == 0 and dt % 2 == 0),
                                            stop=(jp == 21 and dt % 2 == 1),
                                            skip_group_check=True))
                assert mm.n == m_zz[21], (mm.n,)
                m_z = mm.n

                zc_v = {}
                for dt in range(ND):
                    @block.vector
                    def _(v, ns=ns, nb=nb, dt=dt, m_z=m_z):
                        v.wait_ge(mm.h, m_z)
                        if dt >= 2:
                            v.wait_ge(stq.h, zc_v[(dt - 2, 'st')])
                        vq.wait(v)
                        vq.inc(v.scalar_tensor_tensor(
                            out=xo2[:, (dt % 2) * 256:(dt % 2) * 256 + 256],
                            in0=bank(4 + dt // 2, (dt % 2) * 256, (dt % 2) * 256 + 256),
                            scalar=b2t[:, dt:dt + 1],
                            in1=xc2[:, nb * ND * 257 + dt * 257 + 1:
                                    nb * ND * 257 + dt * 257 + 257],
                            op0=ALU.add, op1=ALU.add))
                    zc_v[dt] = vq.n

                    @block.sync
                    def _(sy, ns=ns, t0=t0, dt=dt, fout=fout):
                        sy.wait_ge(vq.h, zc_v[dt])
                        sy.dma_start(out=fout[dt * 128:(dt + 1) * 128, t0:t0 + 256],
                                     in_=xo2[:, (dt % 2) * 256:(dt % 2) * 256 + 256]
                                     ).then_inc(stq.h, 16)
                    stq.n += 16
                    zc_v[(dt, 'st')] = stq.n
                ns_v[ns] = vq.n
                ns_v[(ns, 'mm')] = mm.n
                ns_st[ns] = stq.n
            es_ff.close()
            prev_e4_st = stq.n
    return nc



def build_fin(env=None, io=None):
    if env is None:
        nc = bass.Bass("TRN2", target_bir_lowering=False)
        x1 = nc.dram_tensor("x1", [T, D], F32, kind="ExternalInput").ap()
        s1f = nc.dram_tensor("s1f", [D, T], F32, kind="ExternalInput").ap()
        g1f = nc.dram_tensor("g1f", [D, T], F32, kind="ExternalInput").ap()
        y = nc.dram_tensor("y", [T, D], F32, kind="ExternalOutput").ap()
        cst = declare_consts(nc, ["identf"])
    else:
        nc = env.nc
        cst = env.cst
        x1, s1f, g1f, y = io
    P_ = "Fp" if env else ""
    with ExitStack() as es:
        def sb(name, shape, dt):
            return es.enter_context(nc.sbuf_tensor(P_ + name, shape, dt))

        identf = sb("identf", [128, 128], F32)
        s1c = sb("s1c", [128, 2 * 512], F32)
        g1c = sb("g1c", [128, 2 * 512], F32)
        sgc = sb("sgc", [128, 512], F32)
        xacc = sb("xacc", [128, 2 * 4 * D], F32)
        pp = es.enter_context(nc.psum_tensor(P_ + "pp", [128, 4 * 512], F32))
        if env is None:
            block = es.enter_context(nc.Block())
            ld = SemC(nc, "ld"); vq = SemC(nc, "vq"); sc = SemC(nc, "sc")
            mm = SemC(nc, "mm"); stq = SemC(nc, "stq")
        else:
            block = env.block
            ld, vq, sc, mm, stq = env.sems

        @block.sync
        def _(sy):
            sy.dma_start(out=identf[:], in_=cst["identf"]).then_inc(ld.h, 16)
        ld.n += 16

        tg_v = {}
        tg_st = {}
        for tg in range(NTG):
            b = tg % 2

            @block.sync
            def _(sy, tg=tg, b=b):
                if tg >= 2:
                    sy.wait_ge(stq.h, tg_st[tg - 2])
                    sy.wait_ge(vq.h, tg_v[tg - 2])
                sy.dma_start(out=xacc[:, b * 4 * D:(b + 1) * 4 * D]
                             .rearrange("p (w d) -> p w d", w=4),
                             in_=x1.rearrange("(w p) d -> p w d", p=128)
                             [:, 4 * tg:4 * tg + 4, :]).then_inc(ld.h, 16)
            ld.n += 16

            dc_v = {}
            for dc in range(ND):
                cb = dc % 2

                @block.sync
                def _(sy, tg=tg, dc=dc, cb=cb):
                    if dc >= 2:
                        sy.wait_ge(vq.h, dc_v[dc - 2])
                    sy.dma_start(out=s1c[:, cb * 512:(cb + 1) * 512],
                                 in_=s1f[dc * 128:(dc + 1) * 128, tg * TG:(tg + 1) * TG]
                                 ).then_inc(ld.h, 16)
                    sy.dma_start(out=g1c[:, cb * 512:(cb + 1) * 512],
                                 in_=g1f[dc * 128:(dc + 1) * 128, tg * TG:(tg + 1) * TG]
                                 ).then_inc(ld.h, 16)
                ld.n += 32
                CLD = ld.n

                @block.scalar
                def _(s, cb=cb, CLD=CLD):
                    s.wait_ge(ld.h, CLD)
                    if True:
                        s.wait_ge(vq.h, vq.n)
                    sc.inc(s.activation(g1c[:, cb * 512:(cb + 1) * 512],
                                        g1c[:, cb * 512:(cb + 1) * 512], AF.Silu))
                sc_si = sc.n

                @block.vector
                def _(v, cb=cb, sc_si=sc_si):
                    v.wait_ge(sc.h, sc_si)
                    v.wait_ge(mm.h, mm.n)
                    vq.wait(v)
                    vq.inc(v.tensor_tensor(out=sgc[:], in0=s1c[:, cb * 512:(cb + 1) * 512],
                                           in1=g1c[:, cb * 512:(cb + 1) * 512],
                                           op=ALU.mult))
                v_sg = vq.n

                for w in range(4):
                    @block.tensor
                    def _(t, w=w, v_sg=v_sg):
                        t.wait_ge(vq.h, v_sg)
                        t.wait_ge(vq.h, vq.n)
                        mm.inc(_tmm(t, pp[:, w * 512:w * 512 + 128],
                                           sgc[:, w * 128:(w + 1) * 128], identf[:]))
                    m_tp = mm.n

                    @block.vector
                    def _(v, tg=tg, b=b, dc=dc, w=w, m_tp=m_tp):
                        v.wait_ge(mm.h, m_tp)
                        vq.wait(v)
                        dst = xacc[:, b * 4 * D + w * D + dc * 128:
                                   b * 4 * D + w * D + (dc + 1) * 128]
                        vq.inc(v.tensor_tensor(out=dst, in0=dst,
                                               in1=pp[:, w * 512:w * 512 + 128],
                                               op=ALU.add))
                dc_v[dc] = vq.n
            tg_v[tg] = vq.n

            @block.sync
            def _(sy, tg=tg, b=b):
                sy.wait_ge(vq.h, tg_v[tg])
                sy.dma_start(out=y.rearrange("(w p) d -> p w d", p=128)
                             [:, 4 * tg:4 * tg + 4, :],
                             in_=xacc[:, b * 4 * D:(b + 1) * 4 * D]
                             .rearrange("p (w d) -> p w d", w=4)).then_inc(stq.h, 16)
            stq.n += 16
            tg_st[tg] = stq.n

        @block.sync
        def _(sy):
            sy.wait_ge(stq.h, stq.n)
    return nc



def build_fused():
    nc = bass.Bass("TRN2", target_bir_lowering=False)
    EI = "ExternalInput"
    x = nc.dram_tensor("x", [T, D], F32, kind=EI).ap()
    Wq = nc.dram_tensor("Wq", [D, D], F32, kind=EI).ap()
    Wk = nc.dram_tensor("Wk", [D, 512], F32, kind=EI).ap()
    Wv = nc.dram_tensor("Wv", [D, 512], F32, kind=EI).ap()
    Wo = nc.dram_tensor("Wo", [D, D], F32, kind=EI).ap()
    bo = nc.dram_tensor("bo", [8, 128], F32, kind=EI).ap()
    Wr = nc.dram_tensor("Wr", [D, 8], F32, kind=EI).ap()
    br = nc.dram_tensor("br", [1, 8], F32, kind=EI).ap()
    W1m = nc.dram_tensor("W1m", [8, D, 4096], F32, kind=EI).ap()
    b1m = nc.dram_tensor("b1m", [8, 4096], F32, kind=EI).ap()
    W2m = nc.dram_tensor("W2m", [8, 4096, D], F32, kind=EI).ap()
    b2m = nc.dram_tensor("b2m", [8, D], F32, kind=EI).ap()
    B_w = nc.dram_tensor("B_w", [128, D], F32, kind=EI).ap()
    C_w = nc.dram_tensor("C_w", [D, 128], F32, kind=EI).ap()
    logA = nc.dram_tensor("logA", [1, 128], F32, kind=EI).ap()
    D_p = nc.dram_tensor("D_p", [8, 128], F32, kind=EI).ap()
    Wqkv = nc.dram_tensor("Wqkv", [6, D, 3072], F32, kind=EI).ap()
    q_sc = nc.dram_tensor("q_sc", [6, 64], F32, kind=EI).ap()
    k_sc = nc.dram_tensor("k_sc", [6, 64], F32, kind=EI).ap()
    Wo1 = nc.dram_tensor("Wo1", [6, D, D], F32, kind=EI).ap()
    ln_g = nc.dram_tensor("ln_g", [6, 8, 128], F32, kind=EI).ap()
    ln_b = nc.dram_tensor("ln_b", [6, 8, 128], F32, kind=EI).ap()
    W1l = nc.dram_tensor("W1l", [6, D, 5460], F32, kind=EI).ap()
    b1l = nc.dram_tensor("b1l", [6, 5460], F32, kind=EI).ap()
    W2l = nc.dram_tensor("W2l", [6, 2730, D], F32, kind=EI).ap()
    b2l = nc.dram_tensor("b2l", [6, 8, 128], F32, kind=EI).ap()
    y = nc.dram_tensor("y", [T, D], F32, kind="ExternalOutput").ap()
    xn0f = nc.dram_tensor("xn0f", [D, T], BF16, kind="Internal").ap()
    s0f = nc.dram_tensor("s0f", [D, T], F32, kind="Internal").ap()
    x1s = nc.dram_tensor("x1s", [T, D], F32, kind="Internal").ap()
    xn1f = nc.dram_tensor("xn1f", [D, T], F32, kind="Internal").ap()
    s1f = nc.dram_tensor("s1f", [D, T], F32, kind="Internal").ap()
    g1f = nc.dram_tensor("g1f", [D, T], F32, kind="Internal").ap()
    cst = declare_consts(nc, list(CONST_SPECS.keys()))
    with nc.Block() as block:
        sems = (SemC(nc, "ld"), SemC(nc, "vq"), SemC(nc, "sc"),
                SemC(nc, "mm"), SemC(nc, "stq"))
        env = FusedEnv(nc, block, sems, cst)
        build_ln0(env, (x, xn0f))
        env.barrier()
        build_l0attn(env, (xn0f, Wq, Wk, Wv, Wo, bo, s0f))
        env.barrier()
        build_l0moe(env, (xn0f, x, s0f, Wr, br, W1m, b1m, W2m, b2m, x1s))
        env.barrier()
        build_lnssm(env=env, io=(x1s, B_w, C_w, logA, D_p, xn1f, s1f))
        env.barrier()
        build_subt(env=env, io=(xn1f, Wqkv, q_sc, k_sc, Wo1, ln_g, ln_b,
                                W1l, b1l, W2l, b2l, g1f))
        env.barrier()
        build_fin(env=env, io=(x1s, s1f, g1f, y))
    return nc



_NC_CACHE = {}


def _get_nc(name, builder):
    if name not in _NC_CACHE:
        _NC_CACHE[name] = builder()
    return _NC_CACHE[name]


def kernel(**inputs):
    consts = host_consts()
    cmap = {"cst_" + k: _np_c(v) for k, v in consts.items()}
    x_full = _f32(inputs["L0_attn_x"])
    base = dict(cmap)
    base.update({
        "Wq": _f32(inputs["L0_attn_Wq"]), "Wk": _f32(inputs["L0_attn_Wk"]),
        "Wv": _f32(inputs["L0_attn_Wv"]), "Wo": _f32(inputs["L0_attn_Wo"]),
        "bo": _f32(inputs["L0_attn_bo"]).reshape(8, 128),
        "Wr": _f32(inputs["L0_ffn_Wr"]),
        "br": _f32(inputs["L0_ffn_br"]).reshape(1, 8),
        "W1m": _f32(inputs["L0_ffn_W1"]), "b1m": _f32(inputs["L0_ffn_b1"]),
        "W2m": _f32(inputs["L0_ffn_W2"]), "b2m": _f32(inputs["L0_ffn_b2"]),
        "B_w": _f32(inputs["L1_attn_B_w"]), "C_w": _f32(inputs["L1_attn_C_w"]),
        "logA": _f32(inputs["L1_attn_log_A"]).reshape(1, 128),
        "D_p": _f32(inputs["L1_attn_D_param"]).reshape(8, 128),
        "Wqkv": _f32(inputs["L1_ffn_Wqkv"]), "q_sc": _f32(inputs["L1_ffn_q_scale"]),
        "k_sc": _f32(inputs["L1_ffn_k_scale"]), "Wo1": _f32(inputs["L1_ffn_Wo"]),
        "ln_g": _f32(inputs["L1_ffn_ln_g"]).reshape(6, 8, 128),
        "ln_b": _f32(inputs["L1_ffn_ln_b"]).reshape(6, 8, 128),
        "W1l": _f32(inputs["L1_ffn_W1"]), "b1l": _f32(inputs["L1_ffn_b1"]),
        "W2l": _f32(inputs["L1_ffn_W2"]),
        "b2l": _f32(inputs["L1_ffn_b2"]).reshape(6, 8, 128),
    })
    nc = _get_nc("fused", build_fused)
    maps = [dict(base, x=np.ascontiguousarray(x_full[c])) for c in range(NCORE)]
    res = run_bass_kernel_spmd(nc, maps, core_ids=list(range(NCORE)))
    return np.stack([np.asarray(res.results[c]["y"]) for c in range(NCORE)])


def kernel_multi(**inputs):
    consts = host_consts()
    cmap = {"cst_" + k: _np_c(v) for k, v in consts.items()}

    def pick(m, names):
        return {k: m["cst_" + k] for k in names}

    x_full = _f32(inputs["L0_attn_x"])
    outs = np.empty((NCORE, T, D), np.float32)

    nc1 = _get_nc("ln0", build_ln0)
    maps1 = [{"x": np.ascontiguousarray(x_full[c]), "cst_ident": cmap["cst_ident"]}
             for c in range(NCORE)]
    r1 = run_bass_kernel_spmd(nc1, maps1, core_ids=list(range(NCORE)))
    xn0f = [r1.results[c]["xn0f"] for c in range(NCORE)]

    nc2 = _get_nc("l0attn", build_l0attn)
    w2 = {"Wq": _f32(inputs["L0_attn_Wq"]), "Wk": _f32(inputs["L0_attn_Wk"]),
          "Wv": _f32(inputs["L0_attn_Wv"]), "Wo": _f32(inputs["L0_attn_Wo"]),
          "bo": _f32(inputs["L0_attn_bo"]).reshape(8, 128)}
    c2 = {k: cmap[k] for k in ("cst_identf", "cst_c0q", "cst_s0q", "cst_c0k",
                               "cst_s0k", "cst_R0", "cst_ones_64x1",
                               "cst_ones_1x128", "cst_ones_1x64")}
    maps2 = [dict(c2, xn0f=xn0f[c], **w2) for c in range(NCORE)]
    r2 = run_bass_kernel_spmd(nc2, maps2, core_ids=list(range(NCORE)))
    s0f = [r2.results[c]["s0f"] for c in range(NCORE)]

    nc3 = _get_nc("l0moe", build_l0moe)
    w3 = {"Wr": _f32(inputs["L0_ffn_Wr"]),
          "br": _f32(inputs["L0_ffn_br"]).reshape(1, 8),
          "W1m": _f32(inputs["L0_ffn_W1"]), "b1m": _f32(inputs["L0_ffn_b1"]),
          "W2m": _f32(inputs["L0_ffn_W2"]), "b2m": _f32(inputs["L0_ffn_b2"])}
    c3 = {k: cmap[k] for k in ("cst_ident", "cst_identf", "cst_ones_1x128",
                               "cst_iota_w1", "cst_iota_w2", "cst_iota8")}
    maps3 = [dict(c3, xn0f=xn0f[c], x=np.ascontiguousarray(x_full[c]),
                  s0f=s0f[c], **w3) for c in range(NCORE)]
    r3 = run_bass_kernel_spmd(nc3, maps3, core_ids=list(range(NCORE)))
    x1 = [r3.results[c]["x1"] for c in range(NCORE)]

    nc4 = _get_nc("lnssm", build_lnssm)
    w4 = {"B_w": _f32(inputs["L1_attn_B_w"]), "C_w": _f32(inputs["L1_attn_C_w"]),
          "logA": _f32(inputs["L1_attn_log_A"]).reshape(1, 128),
          "D_p": _f32(inputs["L1_attn_D_param"]).reshape(8, 128)}
    c4 = {k: cmap[k] for k in ("cst_ident", "cst_identf")}
    maps4 = [dict(c4, x1=x1[c], **w4) for c in range(NCORE)]
    r4 = run_bass_kernel_spmd(nc4, maps4, core_ids=list(range(NCORE)))
    xn1f = [r4.results[c]["xn1f"] for c in range(NCORE)]
    s1f = [r4.results[c]["s1f"] for c in range(NCORE)]

    nc5 = _get_nc("subt", build_subt)
    w5 = {"Wqkv": _f32(inputs["L1_ffn_Wqkv"]), "q_sc": _f32(inputs["L1_ffn_q_scale"]),
          "k_sc": _f32(inputs["L1_ffn_k_scale"]), "Wo1": _f32(inputs["L1_ffn_Wo"]),
          "ln_g": _f32(inputs["L1_ffn_ln_g"]).reshape(6, 8, 128),
          "ln_b": _f32(inputs["L1_ffn_ln_b"]).reshape(6, 8, 128),
          "W1l": _f32(inputs["L1_ffn_W1"]), "b1l": _f32(inputs["L1_ffn_b1"]),
          "W2l": _f32(inputs["L1_ffn_W2"]),
          "b2l": _f32(inputs["L1_ffn_b2"]).reshape(6, 8, 128)}
    c5 = {k: cmap[k] for k in ("cst_ident", "cst_identf", "cst_c1q", "cst_s1q",
                               "cst_c1k", "cst_s1k", "cst_R1", "cst_S2", "cst_S2T",
                               "cst_ID2", "cst_Bband", "cst_ones_1x128",
                               "cst_ones_128x1")}
    maps5 = [dict(c5, xn1f=xn1f[c], **w5) for c in range(NCORE)]
    r5 = run_bass_kernel_spmd(nc5, maps5, core_ids=list(range(NCORE)))
    g1f = [r5.results[c]["g1f"] for c in range(NCORE)]

    nc6 = _get_nc("fin", build_fin)
    maps6 = [{"cst_identf": cmap["cst_identf"], "x1": x1[c], "s1f": s1f[c],
              "g1f": g1f[c]} for c in range(NCORE)]
    r6 = run_bass_kernel_spmd(nc6, maps6, core_ids=list(range(NCORE)))
    for c in range(NCORE):
        outs[c] = np.asarray(r6.results[c]["y"])
    return outs


def _np_c(v):
    return np.ascontiguousarray(v)
